# version 08
# baseline (speedup 1.0000x reference)
from contextlib import ExitStack

import numpy as np
import concourse.bass as bass
import concourse.mybir as mybir

F32 = mybir.dt.float32
BF16 = mybir.dt.bfloat16
I32 = mybir.dt.int32
ALU = mybir.AluOpType
AF = mybir.ActivationFunctionType
AX = mybir.AxisListType.X


class Sem:
    def __init__(self, nc, stack, name):
        self.h = stack.enter_context(nc.semaphore(name))
        self.n = 0

    def inc(self, k=1):
        self.n += k
        return self.n


class Item:
    __slots__ = ("eng", "emit", "waits", "inc_sem", "inc_amt")

    def __init__(self, eng, emit, waits, inc_sem, inc_amt):
        self.eng = eng
        self.emit = emit
        self.waits = waits
        self.inc_sem = inc_sem
        self.inc_amt = inc_amt


class Phase:
    ENGINES = ("sync", "scalar", "vector", "tensor", "gpsimd")

    def __init__(self, name):
        self.name = name
        self.items = []

    def add(self, eng, emit, waits=(), sem=None, inc=1):
        assert eng in Phase.ENGINES, eng
        waits = list(waits)
        if sem is not None and not waits:
            waits = [(sem, sem.n)]
        if sem is not None:
            sem.inc(inc)
        self.items.append(Item(eng, emit, waits, sem, inc))

    def emit(self, nc):
        with nc.Block() as blk:
            for eng_name in Phase.ENGINES:
                items = [it for it in self.items if it.eng == eng_name]
                if not items:
                    continue

                def body(e, items=items):
                    for it in items:
                        for (s, v) in it.waits:
                            e.wait_ge(s.h, v)
                    raise RuntimeError("unused")

                def body2(e, items=items):
                    for it in items:
                        for (s, v) in it.waits:
                            e.wait_ge(s.h, v)
                        inst = it.emit(e)
                        if it.inc_sem is not None:
                            inst.then_inc(it.inc_sem.h, it.inc_amt)

                getattr(blk, eng_name)(body2)


class Ctx:
    def __init__(self, nc, stack):
        self.nc = nc
        self.stack = stack
        self.sems = {}

    def sem(self, name):
        if name not in self.sems:
            self.sems[name] = Sem(self.nc, self.stack, name)
        return self.sems[name]

    def sbuf(self, name, shape, dtype):
        return self.stack.enter_context(self.nc.sbuf_tensor(name, shape, dtype))

    def psum(self, name, shape, dtype=F32):
        return self.stack.enter_context(self.nc.psum_tensor(name, shape, dtype))


"""adaln megakernel builder. Phases:
P0: constants, modulation (mod = silu(c)@wmod), b' = b + sa@W, transforms
P1: f = (LN(x)*(1+ca))@W + b' for all S tokens -> f_feat (SBUF) + f_tok (HBM)
P2: flash attention + residual + LN2 -> x', h2 spilled; h2 absmax
P3: weight quant+transpose (overlapped later), collective max #1
P4: gate/up + silu*up -> hmid spilled; absmax; collective max #2
P5: down + residual -> out
"""
import numpy as np
import concourse.bass as bass
import concourse.mybir as mybir

D = 1024
DI = 2816
NKC = 8
EPS = 1e-5


def build_nc(S=8192, dev=0, n_cores=8):
    nc = bass.Bass("TRN2", target_bir_lowering=False)
    Q = S // 2
    n_xt = S // 128
    n_qt = Q // 128
    stack = ExitStack()
    ctx = Ctx(nc, stack)
    sb, ps, sem = ctx.sbuf, ctx.psum, ctx.sem

    x_hbm = nc.dram_tensor("x_full", [S, D], F32, kind="ExternalInput").ap()
    W_hbm = nc.dram_tensor("W", [D, D], F32, kind="ExternalInput").ap()
    b_hbm = nc.dram_tensor("attn_b", [1, D], F32, kind="ExternalInput").ap()
    c_hbm = nc.dram_tensor("c_row", [1, D], F32, kind="ExternalInput").ap()
    wmod_hbm = nc.dram_tensor("wmod", [D, 6 * D], F32, kind="ExternalInput").ap()
    clipx_hbm = nc.dram_tensor("clip_x", [1, 1], F32, kind="ExternalInput").ap()
    clipd_hbm = nc.dram_tensor("clip_down", [1, 1], F32, kind="ExternalInput").ap()
    wg_hbm = nc.dram_tensor("w_gate", [DI, D], F32, kind="ExternalInput").ap()
    wu_hbm = nc.dram_tensor("w_up", [DI, D], F32, kind="ExternalInput").ap()
    wd_hbm = nc.dram_tensor("w_down", [D, DI], F32, kind="ExternalInput").ap()
    out_hbm = nc.dram_tensor("out", [Q, D], F32, kind="ExternalOutput").ap()

    ftok_hbm = nc.dram_tensor("ftok_hbm", [S, D], BF16).ap()
    mod_hbm = nc.dram_tensor("mod_hbm", [8, D], F32).ap()

    dev_f = dev_ftk = None
    if dev & 1:
        dev_f = nc.dram_tensor("dev_ffeat", [NKC * 128, S], BF16, kind="ExternalOutput").ap()

    ident_bf = sb("ident_bf", [128, 128], BF16)
    ident_f32 = sb("ident_f32", [128, 128], F32)
    band = sb("band", [128, 160], F32)
    amax1 = sb("amax1", [128, 1], F32)
    amax2 = sb("amax2", [128, 1], F32)
    one_bf = sb("one_bf", [1, 1], BF16)
    one_f32 = sb("one_f32", [1, 1], F32)
    clip_sb = sb("clip_sb", [1, 2], F32)
    scl_g = sb("scl_g", [128, 22], F32)
    scl_u = sb("scl_u", [128, 22], F32)
    scl_d = sb("scl_d", [128, 8], F32)
    ffstack = ExitStack()
    f_feat = [ffstack.enter_context(nc.sbuf_tensor(f"f_feat{k}", [128, S], BF16))
              for k in range(NKC)]

    dC = sem("dC")
    dW = [sem("dW0"), sem("dW1")]
    dWM = [sem("dWM0"), sem("dWM1")]
    dBC = sem("dBC")
    dX = [sem("dX0"), sem("dX1")]
    dSF = [sem("dSF0"), sem("dSF1")]
    dMW = sem("dMW")
    sS = sem("sS")
    sV = sem("sV")
    sT = sem("sT")
    sG = sem("sG")

    p0 = Phase("p0p1")
    marks = {}

    p0a = ExitStack()
    p0b = ExitStack()
    p0early = ExitStack()
    p0sb = lambda n, sh, dt: p0b.enter_context(nc.sbuf_tensor(n, sh, dt))
    p0ps = lambda n, sh: p0early.enter_context(nc.psum_tensor(n, sh, F32))
    W_bf = p0a.enter_context(nc.sbuf_tensor("W_bf", [128, NKC * D], BF16))
    opca_feat = p0a.enter_context(nc.sbuf_tensor("opca_feat", [128, NKC], F32))
    bp_tile = p0a.enter_context(nc.sbuf_tensor("bp_tile", [128, D], F32))

    c_sb = p0sb("c_sb", [1, D], F32)
    b_sb = p0sb("b_sb", [1, D], F32)
    sgfeat = p0sb("sgfeat", [128, NKC], F32)
    iota_t = p0sb("iota_t", [128, 160], F32)
    wmbuf = [p0sb(f"wmbuf{j}", [128, 2 * D], F32) for j in range(2)]
    wbuf = wmbuf
    mod_acc = p0sb("mod_acc", [1, 6 * D], F32)
    t1r = p0sb("t1r", [1, D], F32)
    sa_bf = p0sb("sa_bf", [1, D], BF16)
    safeat = p0sb("safeat", [128, NKC], BF16)

    psC = p0ps("psC", [128, NKC])
    psM = [p0ps(f"psM{j}", [1, 512]) for j in range(2)]
    psR = [p0ps(f"psR{j}", [1, 512]) for j in range(2)]

    p0.add("sync", lambda e: e.dma_start(out=c_sb[:], in_=c_hbm), sem=dC, inc=16)
    p0.add("sync", lambda e: e.dma_start(out=b_sb[:], in_=b_hbm), sem=dC, inc=16)
    p0.add("sync", lambda e: e.dma_start(out=clip_sb[:, 0:1], in_=clipx_hbm), sem=dC, inc=16)
    p0.add("sync", lambda e: e.dma_start(out=clip_sb[:, 1:2], in_=clipd_hbm), sem=dC, inc=16)
    dC_all = dC.n

    p0.add("gpsimd", lambda e: e.memset(ident_bf[:], 0.0), sem=sG)
    p0.add("gpsimd", lambda e: e.affine_select(
        out=ident_bf[:], in_=ident_bf[:], compare_op=ALU.not_equal,
        fill=1.0, base=0, pattern=[[-1, 128]], channel_multiplier=1),
        waits=[(sG, sG.n)], sem=sG)
    g_identbf = sG.n
    p0.add("gpsimd", lambda e: e.memset(ident_f32[:], 0.0), sem=sG)
    p0.add("gpsimd", lambda e: e.affine_select(
        out=ident_f32[:], in_=ident_f32[:], compare_op=ALU.not_equal,
        fill=1.0, base=0, pattern=[[-1, 128]], channel_multiplier=1),
        waits=[(sG, sG.n)], sem=sG)
    p0.add("gpsimd", lambda e: e.memset(one_bf[:], 1.0), sem=sG)
    g_onebf = sG.n
    p0.add("gpsimd", lambda e: e.memset(one_f32[:], 1.0), sem=sG)
    g_onef32 = sG.n
    p0.add("gpsimd", lambda e: e.iota(
        iota_t[:], [[1, 160]], base=-9, channel_multiplier=-1,
        allow_small_or_imprecise_dtypes=True), sem=sG)
    g_iota = sG.n
    p0.add("scalar", lambda e: e.activation(band[:], iota_t[:], AF.Abs),
           waits=[(sG, g_iota)], sem=sS)
    ss_b = sS.n
    p0.add("vector", lambda e: e.tensor_scalar(
        band[:], band[:], -1.0, 10.0, ALU.mult, ALU.add), waits=[(sS, ss_b)], sem=sV)
    p0.add("vector", lambda e: e.tensor_scalar(
        band[:], band[:], 0.0, -0.01, ALU.max, ALU.mult), waits=[(sV, sV.n)], sem=sV)

    p0.add("scalar", lambda e: e.activation(c_sb[:], c_sb[:], AF.Silu),
           waits=[(dC, dC_all)], sem=sS)
    ss_siluc = sS.n
    for k in range(NKC):
        p0.add("tensor", lambda e, k=k: e.matmul(
            psC[:, k:k + 1], c_sb[0:1, 128 * k:128 * (k + 1)], one_f32[0:1, 0:1],
            start=True, stop=True),
            waits=[(sS, ss_siluc), (sG, g_onef32)] if k == 0 else [], sem=sT)
    st_sgf = sT.n
    p0.add("scalar", lambda e: e.copy(sgfeat[:], psC[:]), waits=[(sT, st_sgf)], sem=sS)
    ss_sgf = sS.n

    sv_wcast = []
    for k in range(NKC):
        w_ = [(sV, sv_wcast[k - 2])] if k >= 2 else []
        p0.add("sync", lambda e, k=k: e.dma_start(
            out=wbuf[k % 2][:, 0:D], in_=W_hbm[128 * k:128 * (k + 1), :]),
            waits=w_, sem=dW[k % 2], inc=16)
        dval = dW[k % 2].n
        p0.add("vector", lambda e, k=k: e.tensor_copy(
            W_bf[:, D * k:D * (k + 1)], wbuf[k % 2][:, 0:D]),
            waits=[(dW[k % 2], dval)], sem=sV)
        sv_wcast.append(sV.n)
    sv_Wbf = sV.n

    wmm_cons = {}
    addmark = {}
    nsub = 3
    subs = [(k, c4) for k in range(NKC) for c4 in range(nsub)]
    for idx, (k, c4) in enumerate(subs):
        pbuf = idx % 2
        w_ = [wmm_cons[idx - 2]] if idx >= 2 else [(sV, sv_wcast[6 + idx])]
        p0.add("sync", lambda e, k=k, c4=c4, pbuf=pbuf: e.dma_start(
            out=wmbuf[pbuf][:], in_=wmod_hbm[128 * k:128 * (k + 1),
                                             2 * D * c4:2 * D * (c4 + 1)]),
            waits=w_, sem=dWM[pbuf], inc=16)
        dval = dWM[pbuf].n
        for h in range(4):
            s_ = 4 * c4 + h
            bank = s_ % 2
            w_mm = [(dWM[pbuf], dval), (sS, ss_sgf)]
            if bank in addmark:
                w_mm.append(addmark[bank])
            p0.add("tensor", lambda e, k=k, h=h, pbuf=pbuf, bank=bank: e.matmul(
                psM[bank][:], sgfeat[:, k:k + 1], wmbuf[pbuf][0:128, 512 * h:512 * (h + 1)],
                start=True, stop=True), waits=w_mm, sem=sT)
            mmv = sT.n
            wmm_cons[idx] = (sT, mmv)
            eng = "vector"
            esem = sV
            if k == 0:
                p0.add(eng, lambda e, s_=s_, bank=bank: e.tensor_copy(
                    mod_acc[0:1, 512 * s_:512 * (s_ + 1)], psM[bank][0:1, :]),
                    waits=[(sT, mmv)], sem=esem)
            else:
                p0.add(eng, lambda e, s_=s_, bank=bank: e.tensor_add(
                    mod_acc[0:1, 512 * s_:512 * (s_ + 1)],
                    mod_acc[0:1, 512 * s_:512 * (s_ + 1)], psM[bank][0:1, :]),
                    waits=[(sT, mmv)], sem=esem)
            addmark[bank] = (esem, esem.n)
    sv_mod = sV.n
    sg_mod = sG.n

    def msl(i):
        return mod_acc[0:1, D * i:D * (i + 1)]
    mod_ready = [(sV, sv_mod), (sG, sg_mod)]

    p0.add("vector", lambda e: e.tensor_copy(sa_bf[:], msl(0)), waits=mod_ready, sem=sV)
    sv_sabf = sV.n
    for k in range(NKC):
        p0.add("tensor", lambda e, k=k: e.matmul(
            psC[:, k:k + 1], sa_bf[0:1, 128 * k:128 * (k + 1)], one_bf[0:1, 0:1],
            start=True, stop=True),
            waits=[(sV, sv_sabf), (sS, ss_sgf)] if k == 0 else [], sem=sT)
    st_saf = sT.n
    p0.add("scalar", lambda e: e.copy(safeat[:], psC[:]), waits=[(sT, st_saf)], sem=sS)
    ss_saf = sS.n

    p0.add("vector", lambda e: e.tensor_scalar(msl(1), msl(1), 1.0, None, ALU.add),
           waits=mod_ready, sem=sV)
    sv_opca = sV.n
    for k in range(NKC):
        p0.add("tensor", lambda e, k=k: e.matmul(
            psC[:, k:k + 1], mod_acc[0:1, D + 128 * k:D + 128 * (k + 1)], one_f32[0:1, 0:1],
            start=True, stop=True),
            waits=[(sV, sv_opca), (sG, g_onef32), (sS, ss_saf)] if k == 0 else [], sem=sT)
    st_opca = sT.n
    p0.add("scalar", lambda e: e.copy(opca_feat[:], psC[:]), waits=[(sT, st_opca)], sem=sS)
    ss_opca = sS.n

    p0.add("vector", lambda e: e.tensor_scalar(msl(4), msl(4), 1.0, None, ALU.add),
           waits=mod_ready, sem=sV)
    for idx in (2, 5):
        p0.add("vector", lambda e, idx=idx: e.tensor_scalar(
            t1r[:], msl(idx), 0.0, None, ALU.is_ge),
            waits=mod_ready + [(sV, sV.n), (sS, sS.n)], sem=sV)
        p0.add("vector", lambda e: e.tensor_scalar(
            t1r[:], t1r[:], 2.0, -1.0, ALU.mult, ALU.add), waits=[(sV, sV.n)], sem=sV)
        p0.add("scalar", lambda e, idx=idx: e.activation(mod_acc[0:1, 0:D], msl(idx), AF.Abs),
               waits=mod_ready + [(sS, sS.n), (sV, sV.n)], sem=sS)
        ss_abs = sS.n
        p0.add("vector", lambda e: e.tensor_scalar(
            mod_acc[0:1, 0:D], mod_acc[0:1, 0:D], 0.25, None, ALU.max), waits=[(sS, ss_abs)], sem=sV)
        p0.add("vector", lambda e, idx=idx: e.tensor_mul(msl(idx), t1r[:], mod_acc[0:1, 0:D]),
               waits=[(sV, sV.n)], sem=sV)
    sv_gagb = sV.n

    for s_ in range(2):
        for k in range(NKC):
            w_ = [(sS, ss_saf), (sV, sv_Wbf)] if (s_ == 0 and k == 0) else []
            p0.add("tensor", lambda e, s_=s_, k=k: e.matmul(
                psR[s_][:], safeat[:, k:k + 1],
                W_bf[0:128, D * k + 512 * s_:D * k + 512 * s_ + 512],
                start=(k == 0), stop=(k == NKC - 1)), waits=w_, sem=sT)
        mmv = sT.n
        p0.add("vector", lambda e, s_=s_: e.tensor_add(
            t1r[0:1, 512 * s_:512 * (s_ + 1)], psR[s_][0:1, :],
            b_sb[0:1, 512 * s_:512 * (s_ + 1)]),
            waits=[(sT, mmv), (dC, dC_all), (sV, sv_gagb)], sem=sV)
    sv_bp = sV.n

    for src, slot, mark in ((msl(2), 0, sv_gagb), (msl(4), 1, sv_gagb),
                            (msl(3), 2, sv_gagb), (msl(5), 3, sv_gagb),
                            (t1r[:], 4, sv_bp)):
        p0.add("gpsimd", lambda e, src=src, slot=slot: e.dma_start(
            out=mod_hbm[slot:slot + 1, :], in_=src), waits=[(sV, mark)], sem=dMW, inc=16)
    dMW_all = dMW.n
    dMW_all0 = dMW_all

    p0.add("sync", lambda e: e.dma_start(
        out=bp_tile[:], in_=mod_hbm[4:5, :].broadcast_to((128, D))),
        waits=[(dMW, dMW_all)], sem=dBC, inc=16)
    dBC_bp = dBC.n

    p0b.close()
    p1sb = lambda n, sh, dt: p0a.enter_context(nc.sbuf_tensor(n, sh, dt))
    xb = [p1sb(f"xb{j}", [128, D], F32) for j in range(2)]
    h_bf = [p1sb(f"hbf{j}", [128, D], BF16) for j in range(2)]
    hT_bf = [p1sb(f"hTbf{j}", [128, D], BF16) for j in range(2)]
    ftk = [p1sb(f"ftk{j}", [128, D], BF16) for j in range(2)]
    stats = [p1sb(f"stats{j}", [128, 16], F32) for j in range(2)]
    p0early.close()
    p1ps = lambda n, sh: p0a.enter_context(nc.psum_tensor(n, sh, F32))
    ps_tr = [p1ps(f"ps_tr{j}", [128, 128]) for j in range(2)]
    ps_f = [p1ps(f"ps_f{j}", [128, D]) for j in range(2)]

    m_xfree = [None] * (n_xt + 4)
    m_trfree = [0] * (n_xt + 4)
    m_psffree = [0] * (n_xt + 4)
    m_hTdone = [0] * (n_xt + 4)
    m_ftokst = [None] * (n_xt + 4)

    for i in range(n_xt):
        pb = i % 2
        if i >= 2:
            w_ = [m_xfree[i - 2]]
        elif i == 0:
            w_ = [(dMW, dMW_all0), (sV, sv_bp), (sS, ss_opca), (sG, sg_mod),
                  (sT, sT.n)]
        else:
            w_ = []
        p0.add("sync", lambda e, i=i, pb=pb: e.dma_start(
            out=xb[pb][:], in_=x_hbm[128 * i:128 * (i + 1), :]),
            waits=w_, sem=dX[pb], inc=16)
        dxv = dX[pb].n

        st = stats[pb]
        w_ = [(dX[pb], dxv)]
        if i >= 2:
            w_.append(m_xfree[i - 2])
        p0.add("vector", lambda e, pb=pb, st=st: e.bn_stats(st[:, 0:6], xb[pb][:, 0:512]),
               waits=w_, sem=sV)
        p0.add("vector", lambda e, pb=pb, st=st: e.bn_stats(st[:, 6:12], xb[pb][:, 512:1024]),
               waits=[(sV, sV.n)], sem=sV)
        p0.add("vector", lambda e, st=st: e.bn_aggr(st[:, 12:14], st[:, 0:12]),
               waits=[(sV, sV.n)], sem=sV)
        p0.add("vector", lambda e, st=st: e.tensor_scalar(
            st[:, 14:15], st[:, 13:14], EPS, None, ALU.add), waits=[(sV, sV.n)], sem=sV)
        p0.add("vector", lambda e, st=st: e.reciprocal(st[:, 15:16], st[:, 14:15]),
               waits=[(sV, sV.n)], sem=sV)
        sv_rec = sV.n
        p0.add("scalar", lambda e, st=st: e.activation(st[:, 0:1], st[:, 15:16], AF.Sqrt),
               waits=[(sV, sv_rec)], sem=sS)
        ss_sq = sS.n
        p0.add("vector", lambda e, st=st: e.tensor_mul(st[:, 1:2], st[:, 0:1], st[:, 0:1]),
               waits=[(sS, ss_sq)], sem=sV)
        p0.add("vector", lambda e, st=st: e.tensor_mul(st[:, 2:3], st[:, 1:2], st[:, 14:15]),
               waits=[(sV, sV.n)], sem=sV)
        p0.add("vector", lambda e, st=st: e.tensor_scalar(
            st[:, 3:4], st[:, 2:3], -0.5, 1.5, ALU.mult, ALU.add), waits=[(sV, sV.n)], sem=sV)
        p0.add("vector", lambda e, st=st: e.tensor_mul(st[:, 4:5], st[:, 0:1], st[:, 3:4]),
               waits=[(sV, sV.n)], sem=sV)
        p0.add("vector", lambda e, st=st: e.tensor_mul(st[:, 5:6], st[:, 12:13], st[:, 4:5]),
               waits=[(sV, sV.n)], sem=sV)
        p0.add("vector", lambda e, st=st: e.tensor_scalar(
            st[:, 6:7], st[:, 5:6], -1.0, None, ALU.mult), waits=[(sV, sV.n)], sem=sV)
        sv_ln = sV.n

        w_ = [(sV, sv_ln)]
        if i >= 2:
            w_.append((sT, m_hTdone[i - 2]))
        p0.add("vector", lambda e, pb=pb, st=st: e.tensor_scalar(
            h_bf[pb][:], xb[pb][:], st[:, 4:5], st[:, 6:7], ALU.mult, ALU.add),
            waits=w_, sem=sV)
        m_xfree[i] = (sV, sV.n)
        ss_h = sV.n

        ss_evac = []
        for j in range(NKC):
            w_ = [(sV, ss_h)]
            if j < 2:
                w_.append((sG, g_identbf))
                if i > 0:
                    w_.append((sS, m_trfree[i - 1]))
                else:
                    w_.append((sS, ss_opca))
                    w_.append((sV, sv_bp))
                    w_.append((sG, sg_mod))
            else:
                w_.append((sS, ss_evac[j - 2]))
            p0.add("tensor", lambda e, pb=pb, j=j: e.matmul(
                ps_tr[j % 2][:], h_bf[pb][:, 128 * j:128 * (j + 1)], ident_bf[:],
                start=True, stop=True), waits=w_, sem=sT)
            mmv = sT.n
            p0.add("scalar", lambda e, pb=pb, j=j: e.activation(
                hT_bf[pb][:, 128 * j:128 * (j + 1)], ps_tr[j % 2][:], AF.Copy,
                scale=opca_feat[:, j:j + 1]),
                waits=[(sT, mmv)], sem=sS)
            ss_evac.append(sS.n)
        ss_hT = sS.n

        for s_ in range(2):
            for k in range(NKC):
                w_ = []
                if s_ == 0 and k == 0:
                    w_ = [(sS, ss_hT), (sV, sv_Wbf)]
                    if i >= 2:
                        w_.append((sV, m_psffree[i - 2]))
                p0.add("tensor", lambda e, pb=pb, s_=s_, k=k: e.matmul(
                    ps_f[pb][:, 512 * s_:512 * (s_ + 1)],
                    hT_bf[pb][:, 128 * k:128 * (k + 1)],
                    W_bf[0:128, D * k + 512 * s_:D * k + 512 * s_ + 512],
                    start=(k == 0), stop=(k == NKC - 1)), waits=w_, sem=sT)
        m_hTdone[i] = sT.n
        st_f = sT.n

        for s_ in range(2):
            w_ = [(sT, st_f), (dBC, dBC_bp)]
            if i >= 2:
                w_.append(m_ftokst[i - 2])
                w_.append((sT, m_hTdone[i - 1]))
            p0.add("vector", lambda e, pb=pb, s_=s_: e.tensor_add(
                ftk[pb][:, 512 * s_:512 * (s_ + 1)], ps_f[pb][:, 512 * s_:512 * (s_ + 1)],
                bp_tile[:, 512 * s_:512 * (s_ + 1)]),
                waits=w_ if s_ == 0 else [], sem=sV)
        m_psffree[i] = sV.n
        sv_ftk = sV.n

        p0.add("gpsimd", lambda e, pb=pb, i=i: e.dma_start(
            out=ftok_hbm[128 * i:128 * (i + 1), :], in_=ftk[pb][:]),
            waits=[(sV, sv_ftk)], sem=dSF[pb], inc=16)
        m_ftokst[i] = (dSF[pb], dSF[pb].n)

        ss_evac2 = []
        for j in range(NKC):
            w_ = [(sV, sv_ftk)]
            if j < 2:
                w_.append((sS, ss_hT))
            else:
                w_.append((sS, ss_evac2[j - 2]))
            p0.add("tensor", lambda e, pb=pb, j=j: e.matmul(
                ps_tr[j % 2][:], ftk[pb][:, 128 * j:128 * (j + 1)], ident_bf[:],
                start=True, stop=True), waits=w_, sem=sT)
            mmv = sT.n
            p0.add("scalar", lambda e, j=j, i=i: e.copy(
                f_feat[j][:, 128 * i:128 * (i + 1)], ps_tr[j % 2][:]),
                waits=[(sT, mmv)], sem=sS)
            ss_evac2.append(sS.n)
        m_trfree[i] = sS.n
        m_ftrdone = sT.n
        m_hTdone[i] = max(m_hTdone[i], m_ftrdone)

    marks["p1_sS"] = sS.n
    marks["p1_sV"] = sV.n
    marks["p1_sT"] = sT.n

    if dev & 1:
        dDV = sem("dDV")
        for j in range(NKC):
            p0.add("gpsimd", lambda e, j=j: e.dma_start(
                out=dev_f[128 * j:128 * (j + 1), :], in_=f_feat[j][:]),
                waits=[(sS, marks["p1_sS"])], sem=dDV, inc=16)
        p0.add("gpsimd", lambda e: e.nop(), waits=[(dDV, dDV.n)])

    p0.add("scalar", lambda e: e.activation(band[:], band[:], AF.Exp),
           waits=[(sV, sV.n)], sem=sS)
    p0.add("gpsimd", lambda e: e.memset(amax1[:], 0.0), sem=sG)
    p0.add("gpsimd", lambda e: e.memset(amax2[:], 0.0), sem=sG)

    for eng, s_ in (("sync", None), ("scalar", sS), ("vector", sV),
                    ("tensor", sT), ("gpsimd", sG)):
        if s_ is not None:
            p0.add(eng, lambda e: e.nop(), sem=s_)
    marks["b1"] = {"sS": sS.n, "sV": sV.n, "sT": sT.n, "sG": sG.n}
    marks["dMW_all"] = dMW_all
    marks["dSF_final"] = [(dSF[0], dSF[0].n), (dSF[1], dSF[1].n)]

    p0.emit(nc)
    p0a.close()

    env = build_p2(nc, ctx, locals())
    env = build_p4(nc, ctx, env)
    env = build_p5(nc, ctx, env)
    stack.close()
    return nc


def build_p2(nc, ctx, env):
    S = env["S"]; Q = env["Q"]; n_qt = env["n_qt"]; dev = env["dev"]
    f_feat = env["f_feat"]; ident_bf = env["ident_bf"]; band = env["band"]
    amax1 = env["amax1"]
    x_hbm = env["x_hbm"]; ftok_hbm = env["ftok_hbm"]; mod_hbm = env["mod_hbm"]
    wg_hbm = env["wg_hbm"]; wu_hbm = env["wu_hbm"]; wd_hbm = env["wd_hbm"]
    sS = env["sS"]; sV = env["sV"]; sT = env["sT"]; sG = env["sG"]
    sem = ctx.sem
    marks = env["marks"]
    b1 = marks["b1"]
    PAD = 16
    n_tc = S // 512
    n_sc = S // 128
    n_pairs = n_qt // 2

    xp_hbm = nc.dram_tensor("xp_hbm", [Q, D], F32).ap()
    h2_hbm = nc.dram_tensor("h2_hbm", [Q, D], F32).ap()
    qg_hbm = nc.dram_tensor("qg_hbm", [22, 128, D], BF16).ap()
    qu_hbm = nc.dram_tensor("qu_hbm", [22, 128, D], BF16).ap()
    qd_hbm = nc.dram_tensor("qd_hbm", [22, 128, D], BF16).ap()
    bf1_hbm = nc.dram_tensor("bandfix1", [128, 16], F32, kind="ExternalInput").ap()
    bf2_hbm = nc.dram_tensor("bandfix2", [128, 16], F32, kind="ExternalInput").ap()

    scl_g = env["scl_g"]; scl_u = env["scl_u"]; scl_d = env["scl_d"]

    p2a = ExitStack()
    sb2 = lambda n, sh, dt: p2a.enter_context(nc.sbuf_tensor(n, sh, dt))
    ps2 = lambda n, sh: p2a.enter_context(nc.psum_tensor(n, sh, F32))

    p_sb = [sb2(f"p_sb{q}", [128, PAD + S], BF16) for q in range(2)]
    NMC = [sb2(f"NMC{q}", [128, n_tc], F32) for q in range(2)]
    Erow = [sb2(f"Erow{q}", [128, n_tc], F32) for q in range(2)]
    rsrow = [sb2(f"rs{q}", [128, n_tc], F32) for q in range(2)]
    smallq = [sb2(f"smallq{q}", [128, 8], F32) for q in range(2)]
    D_sb = [[sb2(f"Dsb{q}{j}", [128, 128], BF16) for j in range(4)] for q in range(2)]
    pT_bf = [[sb2(f"pTbf{q}{j}", [128, 128], BF16) for j in range(4)] for q in range(2)]
    fstr = [sb2(f"fstr{j}", [128, D], BF16) for j in range(2)]
    fstr2 = [sb2(f"fstr2{j}", [128, D], BF16) for j in range(2)]
    ga_tile = sb2("ga_tile", [128, D], BF16)
    onepcb_tile = sb2("onepcb_tile", [128, D], F32)
    sbb_tile = sb2("sbb_tile", [128, D], BF16)
    bfix1 = sb2("bfix1", [128, 16], F32)
    bfix2 = sb2("bfix2", [128, 16], F32)
    xe = sb2("xe", [128, D], F32)
    _xp1 = sb2("xp_sb", [128, D], F32)
    _h21 = sb2("h2_sb", [128, D], F32)
    xp_sb = [_xp1, _xp1]
    h2_sb = [_h21, _h21]
    st2 = [sb2(f"st2{q}", [128, 16], F32) for q in range(2)]
    wv = [sb2(f"wv{j}", [128, 512], F32) for j in range(2)]
    yq = [sb2(f"yq{j}", [128, 512], I32) for j in range(2)]
    NSTG = 3
    stg = [sb2(f"stg{j}", [128, 512], BF16) for j in range(NSTG)]
    mcols = sb2("mcols", [128, 64], F32)
    amx_buf = sb2("amx_buf", [128, 256], F32)
    amx_c = sb2("amx_c", [128, 2], F32)

    e_ps = [ps2(f"e_ps{j}", [128, 512]) for j in range(4)]
    pT_ps = [e_ps[0], e_ps[1]]
    o_ps = [ps2(f"o_ps{q}", [128, D]) for q in range(2)]

    p2 = Phase("p2")
    dBC2 = sem("dBC2")
    dFS = [sem("dFS0"), sem("dFS1"), sem("dFS2")]
    dXE = sem("dXE")
    dXP = [sem("dXP0"), sem("dXP1")]
    dXH = [sem("dXH0"), sem("dXH1")]
    dPW = [sem("dPW0"), sem("dPW1")]
    dPA = sem("dPA")
    dQSs = [sem(f"dQSs{i}") for i in range(NSTG)]

    bar = [(sS, b1["sS"]), (sV, b1["sV"]), (sT, b1["sT"]), (sG, b1["sG"])]
    p2.add("gpsimd", lambda e: e.nop(), waits=bar + marks["dSF_final"])
    for t_ in (smallq[0], smallq[1], st2[0], st2[1], mcols, scl_g, scl_u,
               scl_d, amx_c, NMC[0], NMC[1], Erow[0], Erow[1]):
        p2.add("gpsimd", lambda e, t_=t_: e.memset(t_[:], 0.0), sem=sG)
    g_init = sG.n
    bar2 = bar + [(sG, g_init)]
    for eng in ("scalar", "vector", "tensor"):
        p2.add(eng, lambda e: e.nop(), waits=bar2)
    p2.add("sync", lambda e: e.nop(), waits=bar2 + marks["dSF_final"])

    dMW_all = marks["dMW_all"]
    dMW = env["dMW"]
    p2.add("sync", lambda e: e.dma_start(
        out=onepcb_tile[:], in_=mod_hbm[1:2, :].broadcast_to((128, D))),
        waits=[(dMW, dMW_all)], sem=dBC2, inc=16)
    p2.add("sync", lambda e: e.dma_start(out=bfix1[:], in_=bf1_hbm), sem=dBC2, inc=16)
    p2.add("sync", lambda e: e.dma_start(out=bfix2[:], in_=bf2_hbm), sem=dBC2, inc=16)
    dBC2g = sem("dBC2g")
    dBC2g2 = sem("dBC2g2")
    p2.add("gpsimd", lambda e: e.dma_start(
        out=sbb_tile[:], in_=mod_hbm[2:3, :].broadcast_to((128, D))),
        waits=[(dMW, dMW_all)], sem=dBC2g, inc=16)
    p2.add("gpsimd", lambda e: e.dma_start(
        out=ga_tile[:], in_=mod_hbm[0:1, :].broadcast_to((128, D))),
        waits=[(dMW, dMW_all)], sem=dBC2g2, inc=16)
    p2.add("sync", lambda e: e.nop(),
           waits=[(dBC2g, dBC2g.n), (dBC2g2, dBC2g2.n)], sem=dBC2, inc=1)
    dBC2_all = dBC2.n

    WKIND = {"g": (wg_hbm, qg_hbm, scl_g, 0, D), "u": (wu_hbm, qu_hbm, scl_u, 22, D),
             "d": (wd_hbm, qd_hbm, scl_d, 44, DI)}
    amax_jobs = [("g", t) for t in range(22)] + [("u", t) for t in range(22)] + \
                [("d", t) for t in range(8)]
    strip_jobs = []
    for kind, t in amax_jobs:
        Wd = WKIND[kind][4]
        for c0 in range(0, Wd, 512):
            strip_jobs.append((kind, t, c0, min(512, Wd - c0)))
    amax_done = {}
    aj = [0]
    sq_i = [0]
    free_slots = list(range(NSTG))
    pending = []
    quant_gdone = {}

    def prep_amax(budget):
        for _ in range(budget):
            if aj[0] >= len(amax_jobs):
                return
            kind, t = amax_jobs[aj[0]]
            aj[0] += 1
            src, dst, scl, mbase, Wd = WKIND[kind]
            first = True
            for c0 in range(0, Wd, 256):
                wdt = min(256, Wd - c0)
                p2.add("sync", lambda e, src=src, t=t, c0=c0, wdt=wdt: e.dma_start(
                    out=amx_buf[:, 0:wdt], in_=src[128 * t:128 * (t + 1), c0:c0 + wdt]),
                    waits=[(sV, sV.n)], sem=dPA, inc=16)
                dv = dPA.n
                if first:
                    p2.add("vector", lambda e, wdt=wdt: e.tensor_reduce(
                        amx_c[:, 0:1], amx_buf[:, 0:wdt], AX, ALU.max,
                        apply_absolute_value=True), waits=[(dPA, dv), (sV, sV.n)], sem=sV)
                else:
                    p2.add("vector", lambda e, wdt=wdt: e.tensor_reduce(
                        amx_c[:, 1:2], amx_buf[:, 0:wdt], AX, ALU.max,
                        apply_absolute_value=True), waits=[(dPA, dv), (sV, sV.n)], sem=sV)
                    p2.add("vector", lambda e: e.tensor_max(
                        amx_c[:, 0:1], amx_c[:, 0:1], amx_c[:, 1:2]),
                        waits=[(sV, sV.n)], sem=sV)
                first = False
            mi = mbase + t
            p2.add("vector", lambda e, scl=scl, t=t: e.tensor_scalar(
                scl[:, t:t + 1], amx_c[:, 0:1], 0.007874015748031496, None, ALU.mult),
                waits=[(sV, sV.n)], sem=sV)
            p2.add("vector", lambda e, mi=mi, scl=scl, t=t: e.reciprocal(
                mcols[:, mi:mi + 1], scl[:, t:t + 1]), waits=[(sV, sV.n)], sem=sV)
            amax_done[(kind, t)] = (sV, sV.n)

    def prep_quant(budget):
        for _ in range(budget):
            sj = sq_i[0]
            if sj >= len(strip_jobs) or not free_slots:
                return
            kind, t, c0, wdt = strip_jobs[sj]
            if (kind, t) not in amax_done:
                return
            sq_i[0] += 1
            src, dst, scl, mbase, Wd = WKIND[kind]
            slot = free_slots.pop(0)
            bslot = sj % 2
            mi = mbase + t
            w_ = []
            if (sj - 2) in quant_gdone:
                w_.append(quant_gdone[sj - 2])
            p2.add("sync", lambda e, src=src, t=t, c0=c0, wdt=wdt, bslot=bslot: e.dma_start(
                out=wv[bslot][:, 0:wdt], in_=src[128 * t:128 * (t + 1), c0:c0 + wdt]),
                waits=w_, sem=dPW[bslot], inc=16)
            dv = dPW[bslot].n
            p2.add("vector", lambda e, bslot=bslot, mi=mi, wdt=wdt: e.tensor_scalar(
                wv[bslot][:, 0:wdt], wv[bslot][:, 0:wdt], mcols[:, mi:mi + 1], None,
                ALU.mult), waits=[(dPW[bslot], dv), amax_done[(kind, t)], (sV, sV.n)], sem=sV)
            p2.add("vector", lambda e, bslot=bslot, wdt=wdt: e.tensor_copy(
                yq[bslot][:, 0:wdt], wv[bslot][:, 0:wdt]), waits=[(sV, sV.n)], sem=sV)
            sv_cast = sV.n
            p2.add("gpsimd", lambda e, bslot=bslot, wdt=wdt: e.tensor_sub(
                wv[bslot][:, 0:wdt], wv[bslot][:, 0:wdt], yq[bslot][:, 0:wdt]),
                waits=[(sV, sv_cast)], sem=sG)
            sg_d = sG.n
            w_c2 = [(sG, sg_d), marks.get(("stgfree", slot), (sG, 0))]
            p2.add("vector", lambda e, bslot=bslot, slot=slot, wdt=wdt: e.tensor_scalar(
                stg[slot][:, 0:wdt], wv[bslot][:, 0:wdt], -0.5, None, ALU.is_lt),
                waits=w_c2, sem=sV)
            p2.add("vector", lambda e, bslot=bslot, slot=slot, wdt=wdt: e.scalar_tensor_tensor(
                wv[bslot][:, 0:wdt], wv[bslot][:, 0:wdt], 0.5, stg[slot][:, 0:wdt],
                ALU.is_gt, ALU.subtract), waits=[(sV, sV.n)], sem=sV)
            sv_fix = sV.n
            p2.add("gpsimd", lambda e, bslot=bslot, slot=slot, wdt=wdt: e.tensor_add(
                stg[slot][:, 0:wdt], wv[bslot][:, 0:wdt], yq[bslot][:, 0:wdt]),
                waits=[(sV, sv_fix), marks.get(("stgfree", slot), (sG, 0))], sem=sG)
            quant_gdone[sj] = (sG, sG.n)
            pending.append((sj, slot, (sG, sG.n)))

    def prep_transpose(budget, pt_gate):
        last_evac = None
        for _ in range(budget):
            if not pending:
                break
            sj, slot, qmark = pending.pop(0)
            kind, t, c0, wdt = strip_jobs[sj]
            src, dst, scl, mbase, Wd = WKIND[kind]
            nch = wdt // 128
            ev = []
            for j in range(nch):
                w_ = [qmark]
                if pt_gate is not None and len(ev) == 0:
                    w_.append(pt_gate)
                bm = marks.get(("obank", j % 2))
                if bm is not None:
                    w_.append(bm)
                p2.add("tensor", lambda e, slot=slot, j=j: e.matmul(
                    o_ps[j % 2][:, 0:128], stg[slot][:, 128 * j:128 * (j + 1)],
                    ident_bf[:], start=True, stop=True), waits=w_, sem=sT)
                mmv = sT.n
                p2.add("scalar", lambda e, slot=slot, j=j: e.copy(
                    stg[slot][:, 128 * j:128 * (j + 1)], o_ps[j % 2][:, 0:128]),
                    waits=[(sT, mmv)], sem=sS)
                marks[("obank", j % 2)] = (sS, sS.n)
                ev.append(sS.n)
            if kind in ("g", "u"):
                p2.add("gpsimd", lambda e, dst=dst, t=t, c0=c0, wdt=wdt, slot=slot: e.dma_start(
                    out=dst[t, :, c0:c0 + wdt], in_=stg[slot][:, 0:wdt]),
                    waits=[(sS, ev[-1])], sem=dQSs[slot], inc=16)
            else:
                for j in range(nch):
                    ic = (c0 + 128 * j) // 128
                    p2.add("gpsimd", lambda e, dst=dst, ic=ic, t=t, j=j, slot=slot: e.dma_start(
                        out=dst[ic, :, 128 * t:128 * (t + 1)],
                        in_=stg[slot][:, 128 * j:128 * (j + 1)]),
                        waits=[(sS, ev[-1])], sem=dQSs[slot], inc=16)
            marks[("stgfree", slot)] = (dQSs[slot], dQSs[slot].n)
            free_slots.append(slot)
            last_evac = (sS, ev[-1])
        return last_evac

    m_expdone = [[0] * n_tc for _ in range(2)]
    m_exphist = []
    m_pvlast = None
    m_epidone = [None, None]
    marks["pv_prev"] = 0

    for pr in range(n_pairs):
        qts = (2 * pr, 2 * pr + 1)
        prep_amax(9 if pr < 6 else 0)
        prep_quant(4)

        ss_exp_last = [None, None]
        sv_p1done = [None, None]
        tr_mark = None
        for qi, qt in enumerate(qts):
            for tc in range(n_tc):
                bank = len(m_exphist) % 4
                if len(m_exphist) >= 4:
                    w_ = [(sS, m_exphist[-4])]
                else:
                    w_ = [(dBC2, dBC2_all)]
                for k in range(NKC):
                    p2.add("tensor", lambda e, k=k, qt=qt, tc=tc, bank=bank: e.matmul(
                        e_ps[bank][:], f_feat[k][:, 128 * qt:128 * (qt + 1)],
                        f_feat[k][:, 512 * tc:512 * (tc + 1)],
                        start=(k == 0), stop=(k == NKC - 1)),
                        waits=w_ if k == 0 else [], sem=sT)
                st_e = sT.n
                p2.add("vector", lambda e, qi=qi, tc=tc, bank=bank: e.reduce_max(
                    NMC[qi][:, tc:tc + 1], e_ps[bank][:], AX, negate=True),
                    waits=[(sT, st_e)], sem=sV)
                sv_m = sV.n
                w_ = [(sV, sv_m)]
                if tc == 0 and pr > 0:
                    w_.append((sT, marks["pv_prev"]))
                p2.add("scalar", lambda e, qi=qi, tc=tc, bank=bank: e.activation(
                    p_sb[qi][:, PAD + 512 * tc:PAD + 512 * (tc + 1)], e_ps[bank][:],
                    AF.Exp, bias=NMC[qi][:, tc:tc + 1]),
                    waits=w_, sem=sS)
                m_expdone[qi][tc] = sS.n
                m_exphist.append(sS.n)
                marks[("ptbank", bank)] = (sS, sS.n)
            ss_exp_last[qi] = sS.n

            sq = smallq[qi]
            p2.add("vector", lambda e, qi=qi, sq=sq: e.tensor_reduce(
                sq[:, 0:1], NMC[qi][:], AX, ALU.min), waits=[(sV, sV.n)], sem=sV)
            p2.add("vector", lambda e, qi=qi, sq=sq: e.tensor_scalar(
                Erow[qi][:], NMC[qi][:], -1.0, sq[:, 0:1], ALU.mult, ALU.add),
                waits=[(sV, sV.n)], sem=sV)
            sv_er = sV.n
            p2.add("scalar", lambda e, qi=qi: e.activation(
                Erow[qi][:], Erow[qi][:], AF.Exp),
                waits=[(sV, sv_er), (sS, ss_exp_last[qi])], sem=sS)
            ss_ebf = sS.n
            W0 = 128 * qt - 9
            lo = max(W0, 0)
            hi = W0 + 146
            tpl_lo = lo - W0
            p2.add("vector", lambda e, qi=qi, lo=lo, hi=hi, tpl_lo=tpl_lo: e.tensor_mul(
                p_sb[qi][:, PAD + lo:PAD + hi], p_sb[qi][:, PAD + lo:PAD + hi],
                band[:, tpl_lo:tpl_lo + (hi - lo)]),
                waits=[(sS, ss_exp_last[qi])], sem=sV)
            if qt == n_qt - 1:
                p2.add("vector", lambda e, qi=qi: e.tensor_mul(
                    p_sb[qi][:, PAD + Q:PAD + Q + 9], p_sb[qi][:, PAD + Q:PAD + Q + 9],
                    bfix1[:, 0:9]), waits=[(sV, sV.n)], sem=sV)
            if qt == 0:
                p2.add("vector", lambda e, qi=qi: e.tensor_mul(
                    p_sb[qi][:, PAD + S - 9:PAD + S], p_sb[qi][:, PAD + S - 9:PAD + S],
                    bfix2[:, 0:9]), waits=[(sV, sV.n), (dBC2, dBC2_all)], sem=sV)
            sv_band = sV.n
            for tc in range(n_tc):
                p2.add("vector", lambda e, qi=qi, tc=tc: e.reduce_sum(
                    rsrow[qi][:, tc:tc + 1],
                    p_sb[qi][:, PAD + 512 * tc:PAD + 512 * (tc + 1)], AX),
                    waits=[(sV, sv_band)] if tc == 0 else [], sem=sV)
            p2.add("vector", lambda e, qi=qi: e.tensor_mul(
                rsrow[qi][:], rsrow[qi][:], Erow[qi][:]),
                waits=[(sV, sV.n), (sS, ss_ebf)], sem=sV)
            p2.add("vector", lambda e, qi=qi, sq=sq: e.reduce_sum(
                sq[:, 1:2], rsrow[qi][:], AX), waits=[(sV, sV.n)], sem=sV)
            p2.add("vector", lambda e, qi=qi, sq=sq: e.reciprocal(
                sq[:, 2:3], sq[:, 1:2]), waits=[(sV, sV.n)], sem=sV)
            sv_p1done[qi] = sV.n

            gate = m_pvlast if qi == 0 else None
            tr = prep_transpose(4, gate if tr_mark is None else None)
            tr_mark = tr if tr is not None else tr_mark
            if qi == 0:
                prep_quant(4)

        sv_D = {}
        ss_pT = {}
        mm_pT = {}
        st_pvcons = [0] * (n_sc + 8)
        dfs_mark = {}

        def emit_D(sc):
            tc = sc // 4
            for qi in range(2):
                w_ = [(sS, ss_ebf), (sV, sv_p1done[qi])]
                if tc >= 4 and (4 * (tc - 4) + 3) in mm_pT:
                    w_.append((sT, mm_pT[4 * (tc - 4) + 3]))
                p2.add("vector", lambda e, qi=qi, tc=tc: e.tensor_scalar(
                    D_sb[qi][tc % 4][:], ident_bf[:], Erow[qi][:, tc:tc + 1],
                    None, ALU.mult), waits=w_, sem=sV)
                sv_D[(qi, tc)] = sV.n

        def emit_pT(sc):
            tc = sc // 4
            par = sc % 4
            for qi in range(2):
                w_ = [(sV, sv_D[(qi, tc)]), (sV, sv_p1done[qi])]
                bm = marks.get(("ptbank", par))
                if bm is not None:
                    w_.append(bm)
                if sc >= 4:
                    w_.append((sS, ss_pT[(qi, sc - 4)]))
                p2.add("tensor", lambda e, qi=qi, sc=sc, par=par, tc=tc: e.matmul(
                    e_ps[par][:, 0:128], p_sb[qi][:, PAD + 128 * sc:PAD + 128 * (sc + 1)],
                    D_sb[qi][tc % 4][:], start=True, stop=True),
                    waits=w_, sem=sT)
                mmv = sT.n
                mm_pT[sc] = mmv
                p2.add("scalar", lambda e, qi=qi, par=par: e.copy(
                    pT_bf[qi][par][:], e_ps[par][:, 0:128]), waits=[(sT, mmv)], sem=sS)
                marks[("ptbank", par)] = (sS, sS.n)
                ss_pT[(qi, sc)] = sS.n

        emit_D(0)
        if n_sc >= 4:
            emit_D(4)
        for _s in range(min(3, n_sc)):
            emit_pT(_s)
        for sc in range(n_sc):
            par = sc % 4
            buf = (sc // 2) % 2
            if sc % 2 == 0:
                blk = sc // 2
                w_ = []
                if blk >= 2:
                    w_.append((sT, st_pvcons[2 * (blk - 2) + 1]))
                elif pr > 0:
                    w_.append((sT, marks["pv_prev"]))
                for h, fb in ((0, fstr), (1, fstr2)):
                    p2.add("sync", lambda e, blk=blk, h=h, fb=fb, buf=buf: e.dma_start(
                        out=fb[buf][:],
                        in_=ftok_hbm[128 * (2 * blk + h):128 * (2 * blk + h + 1), :]),
                        waits=w_ if h == 0 else [], sem=dFS[buf], inc=16)
                dfs_mark[buf] = dFS[buf].n
            if (sc + 5) % 4 == 0 and sc + 5 < n_sc:
                emit_D(sc + 5)
            if sc + 3 < n_sc:
                emit_pT(sc + 3)
            fs = fstr if sc % 2 == 0 else fstr2
            for qi in range(2):
                w_ = [(sS, ss_pT[(qi, sc)]), (dFS[buf], dfs_mark[buf])]
                if sc == 0 and m_epidone[qi] is not None:
                    w_.append(m_epidone[qi])
                if sc == 0 and tr_mark is not None:
                    w_.append(tr_mark)
                for h in range(2):
                    p2.add("tensor", lambda e, qi=qi, h=h, par=par, buf=buf, fs=fs,
                           sc=sc: e.matmul(
                        o_ps[qi][:, 512 * h:512 * (h + 1)], pT_bf[qi][par][:],
                        fs[buf][:, 512 * h:512 * (h + 1)],
                        start=(sc == 0), stop=(sc == n_sc - 1)),
                        waits=w_ if h == 0 else [], sem=sT)
            st_pvcons[sc] = sT.n
        marks["pv_prev"] = sT.n
        m_pvlast = (sT, sT.n)
        st_pv = sT.n

        for qi, qt in enumerate(qts):
            sq = smallq[qi]
            w_ = [marks["xe_free"]] if "xe_free" in marks else []
            p2.add("sync", lambda e, qt=qt: e.dma_start(
                out=xe[:], in_=x_hbm[128 * qt:128 * (qt + 1), :]),
                waits=w_, sem=dXE, inc=16)
            dxe = dXE.n
            w_ = [(sT, st_pv), (dBC2, dBC2_all)]
            for m in m_epidone:
                if m is not None:
                    w_.append(m)
            if marks.get("xpstore") is not None:
                w_.append(marks["xpstore"])
            p2.add("vector", lambda e, qi=qi, sq=sq: e.scalar_tensor_tensor(
                xp_sb[qi][:], o_ps[qi][:], sq[:, 2:3], ga_tile[:], ALU.mult, ALU.mult),
                waits=w_, sem=sV)
            marks[("obank", qi)] = (sV, sV.n)
            p2.add("vector", lambda e, qi=qi: e.tensor_add(
                xp_sb[qi][:], xp_sb[qi][:], xe[:]), waits=[(sV, sV.n), (dXE, dxe)], sem=sV)
            sv_xp = sV.n
            marks["xe_free"] = (sV, sv_xp)
            p2.add("gpsimd", lambda e, qi=qi, qt=qt: e.dma_start(
                out=xp_hbm[128 * qt:128 * (qt + 1), :], in_=xp_sb[qi][:]),
                waits=[(sV, sv_xp)], sem=dXP[qi], inc=16)
            marks["xpstore"] = (dXP[qi], dXP[qi].n)
            st_ = st2[qi]
            p2.add("vector", lambda e, qi=qi, st_=st_: e.bn_stats(
                st_[:, 0:6], xp_sb[qi][:, 0:512]), waits=[(sV, sv_xp)], sem=sV)
            p2.add("vector", lambda e, qi=qi, st_=st_: e.bn_stats(
                st_[:, 6:12], xp_sb[qi][:, 512:1024]), waits=[(sV, sV.n)], sem=sV)
            p2.add("vector", lambda e, st_=st_: e.bn_aggr(
                st_[:, 12:14], st_[:, 0:12]), waits=[(sV, sV.n)], sem=sV)
            p2.add("vector", lambda e, st_=st_: e.tensor_scalar(
                st_[:, 14:15], st_[:, 13:14], EPS, None, ALU.add), waits=[(sV, sV.n)], sem=sV)
            p2.add("vector", lambda e, st_=st_: e.reciprocal(
                st_[:, 15:16], st_[:, 14:15]), waits=[(sV, sV.n)], sem=sV)
            sv_rec = sV.n
            p2.add("scalar", lambda e, st_=st_: e.activation(
                st_[:, 0:1], st_[:, 15:16], AF.Sqrt), waits=[(sV, sv_rec)], sem=sS)
            ss_sq2 = sS.n
            p2.add("vector", lambda e, st_=st_: e.tensor_mul(
                st_[:, 1:2], st_[:, 0:1], st_[:, 0:1]), waits=[(sS, ss_sq2)], sem=sV)
            p2.add("vector", lambda e, st_=st_: e.tensor_mul(
                st_[:, 2:3], st_[:, 1:2], st_[:, 14:15]), waits=[(sV, sV.n)], sem=sV)
            p2.add("vector", lambda e, st_=st_: e.tensor_scalar(
                st_[:, 3:4], st_[:, 2:3], -0.5, 1.5, ALU.mult, ALU.add),
                waits=[(sV, sV.n)], sem=sV)
            p2.add("vector", lambda e, st_=st_: e.tensor_mul(
                st_[:, 4:5], st_[:, 0:1], st_[:, 3:4]), waits=[(sV, sV.n)], sem=sV)
            p2.add("vector", lambda e, st_=st_: e.tensor_mul(
                st_[:, 5:6], st_[:, 12:13], st_[:, 4:5]), waits=[(sV, sV.n)], sem=sV)
            p2.add("vector", lambda e, st_=st_: e.tensor_scalar(
                st_[:, 6:7], st_[:, 5:6], -1.0, None, ALU.mult), waits=[(sV, sV.n)], sem=sV)
            w_ = [(sV, sV.n)]
            if m_epidone[qi] is not None:
                w_.append(m_epidone[qi])
            p2.add("vector", lambda e, qi=qi, st_=st_: e.tensor_scalar(
                h2_sb[qi][:], xp_sb[qi][:], st_[:, 4:5], st_[:, 6:7], ALU.mult, ALU.add),
                waits=w_, sem=sV)
            p2.add("vector", lambda e, qi=qi: e.tensor_mul(
                h2_sb[qi][:], h2_sb[qi][:], onepcb_tile[:]), waits=[(sV, sV.n)], sem=sV)
            p2.add("vector", lambda e, qi=qi: e.tensor_add(
                h2_sb[qi][:], h2_sb[qi][:], sbb_tile[:]), waits=[(sV, sV.n)], sem=sV)
            sv_h2 = sV.n
            p2.add("vector", lambda e, qi=qi, st_=st_: e.reduce_max(
                st_[:, 7:8], h2_sb[qi][:], AX, apply_absolute_value=True),
                waits=[(sV, sv_h2)], sem=sV)
            p2.add("vector", lambda e, st_=st_: e.tensor_max(
                amax1[:], amax1[:], st_[:, 7:8]), waits=[(sV, sV.n)], sem=sV)
            p2.add("gpsimd", lambda e, qi=qi, qt=qt: e.dma_start(
                out=h2_hbm[128 * qt:128 * (qt + 1), :], in_=h2_sb[qi][:]),
                waits=[(sV, sv_h2)], sem=dXH[qi], inc=16)
            m_epidone[qi] = (dXH[qi], dXH[qi].n)

    while aj[0] < len(amax_jobs) or sq_i[0] < len(strip_jobs) or pending:
        before = (aj[0], sq_i[0], len(pending))
        prep_amax(4)
        prep_quant(4)
        prep_transpose(8, None)
        if (aj[0], sq_i[0], len(pending)) == before:
            raise RuntimeError("weight prep drain stuck")

    for eng, s_ in (("scalar", sS), ("vector", sV), ("tensor", sT), ("gpsimd", sG)):
        p2.add(eng, lambda e: e.nop(), sem=s_)
    marks["b2"] = {"sS": sS.n, "sV": sV.n, "sT": sT.n, "sG": sG.n,
                   "h2st": list(m_epidone), "dQS": [(s_, s_.n) for s_ in dQSs],
                   "xpst": [(dXP[0], dXP[0].n), (dXP[1], dXP[1].n), (dXH[0], dXH[0].n), (dXH[1], dXH[1].n)]}

    p2.emit(nc)
    p2a.close()
    env.update(dict(xp_hbm=xp_hbm, h2_hbm=h2_hbm, qg_hbm=qg_hbm, qu_hbm=qu_hbm,
                    qd_hbm=qd_hbm, scl_g=scl_g, scl_u=scl_u, scl_d=scl_d))
    return env


def build_p4(nc, ctx, env):
    S = env["S"]; Q = env["Q"]; n_qt = env["n_qt"]
    ident_bf = env["ident_bf"]; ident_f32 = env["ident_f32"]
    amax1 = env["amax1"]; amax2 = env["amax2"]; clip_sb = env["clip_sb"]
    h2_hbm = env["h2_hbm"]; qg_hbm = env["qg_hbm"]; qu_hbm = env["qu_hbm"]
    scl_g = env["scl_g"]; scl_u = env["scl_u"]
    sS = env["sS"]; sV = env["sV"]; sT = env["sT"]; sG = env["sG"]
    sem = ctx.sem
    marks = env["marks"]
    b2 = marks["b2"]
    n_tt = Q // 128
    n_tb = Q // 512

    hmid_hbm = nc.dram_tensor("hmid_hbm", [DI, Q], F32).ap()
    scal_hbm = nc.dram_tensor("scal_hbm", [4, 8], F32).ap()
    cc1_in = nc.dram_tensor("cc1_in", [1, 8], F32)
    cc1_out = nc.dram_tensor("cc1_out", [1, 8], F32)

    env["ffstack"].close()
    p4a = ExitStack()
    sb4 = lambda n, sh, dt: p4a.enter_context(nc.sbuf_tensor(n, sh, dt))
    ps4 = lambda n, sh: p4a.enter_context(nc.psum_tensor(n, sh, F32))

    qg_sb = [sb4(f"qg_sb{t}", [128, D], BF16) for t in range(22)]
    qu_sb = [sb4(f"qu_sb{t}", [128, D], BF16) for t in range(22)]
    xqT = [sb4(f"xqT{k}", [128, Q], BF16) for k in range(NKC)]
    h2b = [sb4(f"h2b{j}", [128, D], F32) for j in range(2)]
    qbuf = [sb4(f"qbuf{j}", [128, D], F32) for j in range(2)]
    ybuf = [sb4(f"ybuf{j}", [128, D], I32) for j in range(2)]
    cbuf = [sb4(f"cbuf{j}", [128, D], F32) for j in range(2)]
    xq_bf = [sb4(f"xq_bf{j}", [128, D], BF16) for j in range(2)]
    gate_f = [sb4(f"gate_f{j}", [128, 512], F32) for j in range(2)]
    hm_sb = [sb4(f"hm_sb{j}", [128, 512], F32) for j in range(2)]
    gsc = sb4("gsc", [1, 8], F32)
    mxcol = sb4("mxcol", [128, 1], F32)
    sxcol = sb4("sxcol", [128, 1], F32)
    am4 = sb4("am4", [128, 2], F32)
    sgx = sb4("sgx", [128, 22], F32)
    sux = sb4("sux", [128, 22], F32)

    psA = ps4("psA", [1, 128])
    ps_tr2 = [ps4(f"ps_tr2{j}", [128, 512]) for j in range(2)]
    psG = [ps4(f"psG{j}", [128, 512]) for j in range(2)]
    psU = [ps4(f"psU{j}", [128, 512]) for j in range(2)]

    p4 = Phase("p4")
    dH2 = [sem("dH2_0"), sem("dH2_1")]
    dQR = sem("dQR")
    dCC1 = sem("dCC1")
    cc1s = sem("cc1s")
    dSC = sem("dSC")
    dSCB = sem("dSCB")
    dHM = [sem("dHM0"), sem("dHM1")]

    bar = [(sS, b2["sS"]), (sV, b2["sV"]), (sT, b2["sT"]), (sG, b2["sG"])]
    p4.add("gpsimd", lambda e: e.nop(), waits=bar)
    for t_ in (sgx, sux, gsc, mxcol, sxcol):
        p4.add("gpsimd", lambda e, t_=t_: e.memset(t_[:], 0.0), sem=sG)
    g_init = sG.n
    bar2 = bar + [(sG, g_init)]
    for eng in ("scalar", "vector", "tensor"):
        p4.add(eng, lambda e: e.nop(), waits=bar2)
    p4.add("sync", lambda e: e.nop(),
           waits=bar2 + b2["h2st"] + b2["xpst"] + b2["dQS"])

    p4.add("tensor", lambda e: e.matmul(
        psA[:], amax1[:], ident_f32[:], start=True, stop=True), sem=sT)
    st_a = sT.n
    p4.add("vector", lambda e: e.reduce_max(gsc[0:1, 0:1], psA[0:1, :], AX),
           waits=[(sT, st_a)], sem=sV)
    sv_g1 = sV.n
    p4.add("gpsimd", lambda e: e.dma_start(out=cc1_in[0:1, :], in_=gsc[0:1, :]),
           waits=[(sV, sv_g1)], sem=dCC1, inc=16)
    p4.add("gpsimd", lambda e: e.collective_compute(
        "AllReduce", ALU.max, replica_groups=[list(range(env["n_cores"]))],
        ins=[cc1_in.ap().opt()], outs=[cc1_out.ap().opt()]),
        waits=[(dCC1, dCC1.n)], sem=cc1s)
    p4.add("gpsimd", lambda e: e.dma_start(out=gsc[0:1, 1:2], in_=cc1_out[0:1, 0:1]),
           waits=[(cc1s, cc1s.n)], sem=dCC1, inc=16)
    d_gmax = dCC1.n
    p4.add("scalar", lambda e: e.activation(gsc[0:1, 2:3], clip_sb[0:1, 0:1], AF.Sigmoid),
           waits=[(dCC1, d_gmax)], sem=sS)
    ss_sig = sS.n
    p4.add("vector", lambda e: e.tensor_mul(gsc[0:1, 3:4], gsc[0:1, 1:2], gsc[0:1, 2:3]),
           waits=[(sS, ss_sig), (dCC1, d_gmax)], sem=sV)
    p4.add("vector", lambda e: e.tensor_scalar(
        gsc[0:1, 3:4], gsc[0:1, 3:4], 0.007874015748031496, None, ALU.mult), waits=[(sV, sV.n)], sem=sV)
    p4.add("vector", lambda e: e.reciprocal(gsc[0:1, 4:5], gsc[0:1, 3:4]),
           waits=[(sV, sV.n)], sem=sV)
    sv_mx = sV.n
    p4.add("gpsimd", lambda e: e.dma_start(out=scal_hbm[0:1, :], in_=gsc[0:1, :]),
           waits=[(sV, sv_mx)], sem=dSC, inc=16)
    p4.add("sync", lambda e: e.dma_start(
        out=mxcol[:], in_=scal_hbm[0:1, 4:5].broadcast_to((128, 1))),
        waits=[(dSC, dSC.n)], sem=dSCB, inc=16)
    p4.add("sync", lambda e: e.dma_start(
        out=sxcol[:], in_=scal_hbm[0:1, 3:4].broadcast_to((128, 1))),
        sem=dSCB, inc=16)
    d_bc = dSCB.n
    p4.add("vector", lambda e: e.tensor_scalar(
        sgx[:], scl_g[:], sxcol[:, 0:1], None, ALU.mult), waits=[(dSCB, d_bc)], sem=sV)
    p4.add("vector", lambda e: e.tensor_scalar(
        sux[:], scl_u[:], sxcol[:, 0:1], None, ALU.mult), waits=[(sV, sV.n)], sem=sV)
    sv_scales = sV.n

    dqr_marks = []
    for t in range(22):
        p4.add("sync", lambda e, t=t: e.dma_start(out=qg_sb[t][:], in_=qg_hbm[t]),
               sem=dQR, inc=16)
        p4.add("sync", lambda e, t=t: e.dma_start(out=qu_sb[t][:], in_=qu_hbm[t]),
               sem=dQR, inc=16)
    dQR_all = dQR.n

    m_quant = [None] * (n_tt + 4)
    m_h2free = [None] * (n_tt + 4)
    m_trbank = [None, None]
    quant_done = [0]

    def quant_tile(j):
        pb = j % 2
        w_ = [m_h2free[j - 2]] if j >= 2 else []
        p4.add("sync", lambda e, j=j, pb=pb: e.dma_start(
            out=h2b[pb][:], in_=h2_hbm[128 * j:128 * (j + 1), :]),
            waits=w_, sem=dH2[pb], inc=16)
        dv = dH2[pb].n
        w_ = [(dH2[pb], dv), (sV, sv_scales)]
        if j >= 2:
            w_.append(m_quant[j - 2])
        p4.add("vector", lambda e, pb=pb: e.tensor_scalar(
            qbuf[pb][:], h2b[pb][:], mxcol[:, 0:1], None, ALU.mult), waits=w_, sem=sV)
        m_h2free[j] = (sV, sV.n)
        p4.add("vector", lambda e, pb=pb: e.tensor_copy(ybuf[pb][:], qbuf[pb][:]),
               waits=[(sV, sV.n)], sem=sV)
        sv_c = sV.n
        p4.add("gpsimd", lambda e, pb=pb: e.tensor_sub(
            qbuf[pb][:], qbuf[pb][:], ybuf[pb][:]), waits=[(sV, sv_c)], sem=sG)
        sg_d = sG.n
        p4.add("vector", lambda e, pb=pb: e.tensor_scalar(
            cbuf[pb][:], qbuf[pb][:], -0.5, None, ALU.is_lt), waits=[(sG, sg_d)], sem=sV)
        p4.add("vector", lambda e, pb=pb: e.scalar_tensor_tensor(
            qbuf[pb][:], qbuf[pb][:], 0.5, cbuf[pb][:], ALU.is_gt, ALU.subtract),
            waits=[(sV, sV.n)], sem=sV)
        sv_f = sV.n
        p4.add("gpsimd", lambda e, pb=pb: e.tensor_add(
            cbuf[pb][:], qbuf[pb][:], ybuf[pb][:]), waits=[(sV, sv_f)], sem=sG)
        sg_y = sG.n
        p4.add("vector", lambda e, pb=pb: e.tensor_scalar(
            xq_bf[pb][:], cbuf[pb][:], 127.0, -128.0, ALU.min, ALU.max),
            waits=[(sG, sg_y)], sem=sV)
        sv_xq = sV.n
        for k in range(NKC):
            w_ = [(sV, sv_xq)]
            if m_trbank[k % 2] is not None:
                w_.append(m_trbank[k % 2])
            p4.add("tensor", lambda e, pb=pb, k=k: e.matmul(
                ps_tr2[k % 2][:, 0:128], xq_bf[pb][:, 128 * k:128 * (k + 1)],
                ident_bf[:], start=True, stop=True), waits=w_, sem=sT)
            mmv = sT.n
            p4.add("scalar", lambda e, k=k, j=j: e.copy(
                xqT[k][:, 128 * j:128 * (j + 1)], ps_tr2[k % 2][:, 0:128]),
                waits=[(sT, mmv)], sem=sS)
            m_trbank[k % 2] = (sS, sS.n)
        m_quant[j] = (sS, sS.n)

    mGU = {}
    m_gevac = {}
    m_hmst = {}
    for tb in range(n_tb):
        while quant_done[0] < min(n_tt, 4 * (tb + 1)):
            quant_tile(quant_done[0])
            quant_done[0] += 1
        for t in range(22):
            pb = t % 2
            w_ = [m_quant[4 * tb + 3] if 4 * tb + 3 < n_tt else m_quant[n_tt - 1],
                  (dQR, dQR_all)]
            if ("ge", pb) in m_gevac:
                w_.append(m_gevac[("ge", pb)])
            for k in range(NKC):
                p4.add("tensor", lambda e, t=t, tb=tb, k=k, pb=pb: e.matmul(
                    psG[pb][:], qg_sb[t][:, 128 * k:128 * (k + 1)],
                    xqT[k][:, 512 * tb:512 * (tb + 1)],
                    start=(k == 0), stop=(k == NKC - 1)),
                    waits=w_ if k == 0 else [], sem=sT)
            mGU[(tb, t, "g")] = (sT, sT.n)
            st_g = sT.n
            w_ = []
            if (("ue", pb)) in m_gevac:
                w_.append(m_gevac[("ue", pb)])
            for k in range(NKC):
                p4.add("tensor", lambda e, t=t, tb=tb, k=k, pb=pb: e.matmul(
                    psU[pb][:], qu_sb[t][:, 128 * k:128 * (k + 1)],
                    xqT[k][:, 512 * tb:512 * (tb + 1)],
                    start=(k == 0), stop=(k == NKC - 1)),
                    waits=w_ if k == 0 else [], sem=sT)
            st_u = sT.n
            w_ = [(sT, st_g)]
            if ("gf", pb) in m_gevac:
                w_.append(m_gevac[("gf", pb)])
            p4.add("scalar", lambda e, t=t, pb=pb: e.activation(
                gate_f[pb][:], psG[pb][:], AF.Silu, scale=sgx[:, t:t + 1]),
                waits=w_, sem=sS)
            m_gevac[("ge", pb)] = (sS, sS.n)
            ss_gf = sS.n
            w_ = [(sT, st_u), (sS, ss_gf)]
            if ("hm", pb) in m_gevac:
                w_.append(m_gevac[("hm", pb)])
            if ("hmst", pb) in m_gevac:
                w_.append(m_gevac[("hmst", pb)])
            p4.add("vector", lambda e, t=t, pb=pb: e.scalar_tensor_tensor(
                hm_sb[pb][:], psU[pb][:], sux[:, t:t + 1], gate_f[pb][:],
                ALU.mult, ALU.mult), waits=w_, sem=sV)
            m_gevac[("ue", pb)] = (sV, sV.n)
            m_gevac[("gf", pb)] = (sV, sV.n)
            sv_hm = sV.n
            p4.add("vector", lambda e, pb=pb: e.reduce_max(
                am4[:, 0:1], hm_sb[pb][:], AX, apply_absolute_value=True),
                waits=[(sV, sv_hm)], sem=sV)
            p4.add("vector", lambda e: e.tensor_max(
                amax2[:], amax2[:], am4[:, 0:1]), waits=[(sV, sV.n)], sem=sV)
            p4.add("gpsimd", lambda e, t=t, tb=tb, pb=pb: e.dma_start(
                out=hmid_hbm[128 * t:128 * (t + 1), 512 * tb:512 * (tb + 1)],
                in_=hm_sb[pb][:]), waits=[(sV, sv_hm)], sem=dHM[pb], inc=16)
            m_gevac[("hmst", pb)] = (dHM[pb], dHM[pb].n)

    for eng, s_ in (("scalar", sS), ("vector", sV), ("tensor", sT), ("gpsimd", sG)):
        p4.add(eng, lambda e: e.nop(), sem=s_)
    marks["b4"] = {"sS": sS.n, "sV": sV.n, "sT": sT.n, "sG": sG.n,
                   "hmst": [m_gevac.get(("hmst", 0)), m_gevac.get(("hmst", 1))],
                   "xpst": b2["xpst"]}

    p4.emit(nc)
    p4a.close()
    env.update(dict(hmid_hbm=hmid_hbm, scal_hbm=scal_hbm))
    return env


def build_p5(nc, ctx, env):
    S = env["S"]; Q = env["Q"]
    ident_f32 = env["ident_f32"]
    amax2 = env["amax2"]; clip_sb = env["clip_sb"]
    hmid_hbm = env["hmid_hbm"]; scal_hbm = env["scal_hbm"]
    qd_hbm = env["qd_hbm"]; xp_hbm = env["xp_hbm"]; mod_hbm = env["mod_hbm"]
    out_hbm = env["out_hbm"]
    scl_d = env["scl_d"]
    sS = env["sS"]; sV = env["sV"]; sT = env["sT"]; sG = env["sG"]
    sem = ctx.sem
    marks = env["marks"]
    b4 = marks["b4"]
    n_tt = Q // 128
    nhalf = Q // 2

    cc2_in = nc.dram_tensor("cc2_in", [1, 8], F32)
    cc2_out = nc.dram_tensor("cc2_out", [1, 8], F32)
    sd_hbm = nc.dram_tensor("sd_hbm", [8, 128], F32).ap()

    p5a = ExitStack()
    sb5 = lambda n, sh, dt: p5a.enter_context(nc.sbuf_tensor(n, sh, dt))
    ps5 = lambda n, sh: p5a.enter_context(nc.psum_tensor(n, sh, F32))

    qd_sb = [sb5(f"qd_sb{t}", [128, D], BF16) for t in range(22)]
    hq = [sb5(f"hq{t}", [128, nhalf], BF16) for t in range(22)]
    hmb = [sb5(f"hmb{j}", [128, nhalf], F32) for j in range(2)]
    y5 = sb5("y5", [128, nhalf], I32)
    c5 = sb5("c5", [128, nhalf], F32)
    xpb = [sb5(f"xpb{j}", [128, D], F32) for j in range(2)]
    outb = [sb5(f"outb{j}", [128, D], F32) for j in range(2)]
    gb_tile = sb5("gb_tile", [128, D], F32)
    sdb_tile = sb5("sdb_tile", [128, D], F32)
    sd_sb = sb5("sd_sb", [8, 128], F32)
    g5 = sb5("g5", [1, 8], F32)
    mdcol = sb5("mdcol", [128, 1], F32)
    sdcol = sb5("sdcol", [128, 1], F32)

    psA2 = ps5("psA2", [8, 128])
    psB2 = ps5("psB2", [1, 128])
    psD5 = [ps5(f"psD5{j}", [128, D]) for j in range(2)]

    p5 = Phase("p5")
    dHM5 = [sem("dHM5_0"), sem("dHM5_1")]
    dQD = sem("dQD")
    dCC2 = sem("dCC2")
    cc2s = sem("cc2s")
    dSD = sem("dSD")
    dSD2 = sem("dSD2")
    dSDB = sem("dSDB")
    dXPL = [sem("dXPL0"), sem("dXPL1")]
    dOUT = [sem("dOUT0"), sem("dOUT1")]

    bar = [(sS, b4["sS"]), (sV, b4["sV"]), (sT, b4["sT"]), (sG, b4["sG"])]
    p5.add("gpsimd", lambda e: e.nop(), waits=bar)
    for t_ in (g5, mdcol, sdcol, sd_sb):
        p5.add("gpsimd", lambda e, t_=t_: e.memset(t_[:], 0.0), sem=sG)
    g_init = sG.n
    bar2 = bar + [(sG, g_init)]
    for eng in ("scalar", "vector", "tensor"):
        p5.add(eng, lambda e: e.nop(), waits=bar2)
    p5.add("sync", lambda e: e.nop(),
           waits=bar2 + [m for m in b4["hmst"] if m] + b4["xpst"])

    p5.add("tensor", lambda e: e.matmul(
        psB2[:], amax2[:], ident_f32[:], start=True, stop=True), sem=sT)
    st_a = sT.n
    p5.add("vector", lambda e: e.reduce_max(g5[0:1, 0:1], psB2[0:1, :], AX),
           waits=[(sT, st_a)], sem=sV)
    sv_g1 = sV.n
    p5.add("gpsimd", lambda e: e.dma_start(out=cc2_in[0:1, :], in_=g5[0:1, :]),
           waits=[(sV, sv_g1)], sem=dCC2, inc=16)
    p5.add("gpsimd", lambda e: e.collective_compute(
        "AllReduce", ALU.max, replica_groups=[list(range(env["n_cores"]))],
        ins=[cc2_in.ap().opt()], outs=[cc2_out.ap().opt()]),
        waits=[(dCC2, dCC2.n)], sem=cc2s)
    p5.add("gpsimd", lambda e: e.dma_start(out=g5[0:1, 1:2], in_=cc2_out[0:1, 0:1]),
           waits=[(cc2s, cc2s.n)], sem=dCC2, inc=16)
    d_gmax = dCC2.n
    p5.add("scalar", lambda e: e.activation(g5[0:1, 2:3], clip_sb[0:1, 1:2], AF.Sigmoid),
           waits=[(dCC2, d_gmax)], sem=sS)
    ss_sig = sS.n
    p5.add("vector", lambda e: e.tensor_mul(g5[0:1, 3:4], g5[0:1, 1:2], g5[0:1, 2:3]),
           waits=[(sS, ss_sig), (dCC2, d_gmax)], sem=sV)
    p5.add("vector", lambda e: e.tensor_scalar(
        g5[0:1, 3:4], g5[0:1, 3:4], 0.007874015748031496, None, ALU.mult), waits=[(sV, sV.n)], sem=sV)
    p5.add("vector", lambda e: e.reciprocal(g5[0:1, 4:5], g5[0:1, 3:4]),
           waits=[(sV, sV.n)], sem=sV)
    sv_md = sV.n
    p5.add("gpsimd", lambda e: e.dma_start(out=scal_hbm[1:2, :], in_=g5[0:1, :]),
           waits=[(sV, sv_md)], sem=dSD, inc=16)
    dsd1 = dSD.n
    p5.add("sync", lambda e: e.dma_start(
        out=mdcol[:], in_=scal_hbm[1:2, 4:5].broadcast_to((128, 1))),
        waits=[(dSD, dsd1)], sem=dSDB, inc=16)
    p5.add("sync", lambda e: e.dma_start(
        out=sdcol[:], in_=scal_hbm[1:2, 3:4].broadcast_to((128, 1))),
        sem=dSDB, inc=16)
    d_bc = dSDB.n

    p5.add("tensor", lambda e: e.matmul(
        psA2[:], scl_d[:], ident_f32[:], start=True, stop=True), sem=sT)
    st_sd = sT.n
    p5.add("scalar", lambda e: e.copy(sd_sb[:], psA2[:]), waits=[(sT, st_sd)], sem=sS)
    ss_sd = sS.n
    p5.add("gpsimd", lambda e: e.dma_start(out=sd_hbm, in_=sd_sb[:]),
           waits=[(sS, ss_sd)], sem=dSD2, inc=16)
    dsd2 = dSD2.n
    p5.add("sync", lambda e: e.dma_start(
        out=sdb_tile[:],
        in_=sd_hbm.rearrange("a b -> (a b)").rearrange("(o d) -> o d", o=1).broadcast_to((128, D))),
        waits=[(dSD2, dsd2)], sem=dSDB, inc=16)
    p5.add("sync", lambda e: e.dma_start(
        out=gb_tile[:], in_=mod_hbm[3:4, :].broadcast_to((128, D))),
        sem=dSDB, inc=16)
    d_bc2 = dSDB.n
    p5.add("vector", lambda e: e.tensor_mul(gb_tile[:], gb_tile[:], sdb_tile[:]),
           waits=[(dSDB, d_bc2)], sem=sV)
    p5.add("vector", lambda e: e.tensor_scalar(
        gb_tile[:], gb_tile[:], sdcol[:, 0:1], None, ALU.mult),
        waits=[(sV, sV.n)], sem=sV)
    sv_gp = sV.n

    for t in range(22):
        p5.add("sync", lambda e, t=t: e.dma_start(out=qd_sb[t][:], in_=qd_hbm[t]),
               sem=dQD, inc=16)
    dQD_all = dQD.n

    m_outst = {}
    m_psd = {}
    for half in range(2):
        hq_marks = []
        for ic in range(22):
            pb = ic % 2
            w_ = []
            if ic >= 2:
                w_.append(hq_marks[ic - 2])
            elif half == 1:
                w_.append(marks["p5_half0_done"])
            p5.add("sync", lambda e, ic=ic, half=half, pb=pb: e.dma_start(
                out=hmb[pb][:], in_=hmid_hbm[128 * ic:128 * (ic + 1),
                                            nhalf * half:nhalf * (half + 1)]),
                waits=w_, sem=dHM5[pb], inc=16)
            dv = dHM5[pb].n
            w_ = [(dHM5[pb], dv), (sV, sV.n)]
            if ic == 0 and half == 0:
                w_.append((dSDB, d_bc2))
            p5.add("vector", lambda e, pb=pb: e.tensor_scalar(
                hmb[pb][:], hmb[pb][:], mdcol[:, 0:1], None, ALU.mult), waits=w_, sem=sV)
            p5.add("vector", lambda e, pb=pb: e.tensor_copy(y5[:], hmb[pb][:]),
                   waits=[(sV, sV.n)], sem=sV)
            sv_c = sV.n
            p5.add("gpsimd", lambda e, pb=pb: e.tensor_sub(
                hmb[pb][:], hmb[pb][:], y5[:]), waits=[(sV, sv_c)], sem=sG)
            sg_d = sG.n
            p5.add("vector", lambda e, pb=pb: e.tensor_scalar(
                c5[:], hmb[pb][:], -0.5, None, ALU.is_lt), waits=[(sG, sg_d)], sem=sV)
            p5.add("vector", lambda e, pb=pb: e.scalar_tensor_tensor(
                hmb[pb][:], hmb[pb][:], 0.5, c5[:], ALU.is_gt, ALU.subtract),
                waits=[(sV, sV.n)], sem=sV)
            sv_f = sV.n
            p5.add("gpsimd", lambda e, pb=pb: e.tensor_add(
                c5[:], hmb[pb][:], y5[:]), waits=[(sV, sv_f)], sem=sG)
            sg_y = sG.n
            p5.add("vector", lambda e, ic=ic, pb=pb: e.tensor_scalar(
                hq[ic][:], c5[:], 127.0, -128.0, ALU.min, ALU.max),
                waits=[(sG, sg_y)], sem=sV)
            hq_marks.append((sV, sV.n))
        hq_all = (sV, sV.n)

        for jj in range(n_tt // 2):
            j = (n_tt // 2) * half + jj
            pb = jj % 2
            w_ = [m_outst[pb]] if pb in m_outst else []
            p5.add("sync", lambda e, j=j, pb=pb: e.dma_start(
                out=xpb[pb][:], in_=xp_hbm[128 * j:128 * (j + 1), :]),
                waits=w_, sem=dXPL[pb], inc=16)
            dxp = dXPL[pb].n
            for hc in range(2):
                w_ = [hq_all, (dQD, dQD_all)]
                if (pb, hc) in m_psd:
                    w_.append(m_psd[(pb, hc)])
                for ic in range(22):
                    p5.add("tensor", lambda e, ic=ic, jj=jj, hc=hc, pb=pb: e.matmul(
                        psD5[pb][:, 512 * hc:512 * (hc + 1)],
                        hq[ic][:, 128 * jj:128 * (jj + 1)],
                        qd_sb[ic][:, 512 * hc:512 * (hc + 1)],
                        start=(ic == 0), stop=(ic == 21)),
                        waits=w_ if ic == 0 else [], sem=sT)
            st_d = sT.n
            w_ = [(sT, st_d), (sV, sv_gp), (dXPL[pb], dxp)]
            if pb in m_outst:
                w_.append(m_outst[pb])
            p5.add("vector", lambda e, pb=pb: e.tensor_mul(
                outb[pb][:], psD5[pb][:], gb_tile[:]), waits=w_, sem=sV)
            m_psd[(pb, 0)] = (sV, sV.n)
            m_psd[(pb, 1)] = (sV, sV.n)
            p5.add("vector", lambda e, pb=pb: e.tensor_add(
                outb[pb][:], outb[pb][:], xpb[pb][:]), waits=[(sV, sV.n)], sem=sV)
            sv_o = sV.n
            p5.add("gpsimd", lambda e, j=j, pb=pb: e.dma_start(
                out=out_hbm[128 * j:128 * (j + 1), :], in_=outb[pb][:]),
                waits=[(sV, sv_o)], sem=dOUT[pb], inc=16)
            m_outst[pb] = (dOUT[pb], dOUT[pb].n)
        marks["p5_half0_done"] = (sT, sT.n)

    p5.add("gpsimd", lambda e: e.nop(),
           waits=[(dOUT[0], dOUT[0].n), (dOUT[1], dOUT[1].n)])
    p5.emit(nc)
    p5a.close()
    return env



from concourse.bass_utils import run_bass_kernel_spmd

_S = 8192
_Q = _S // 2


def _bandfix_tiles(qh):
    bf1 = np.ones((128, 16), np.float32)
    bf2 = np.ones((128, 16), np.float32)
    if qh == 1:
        for r in range(128):
            for u in range(9):
                d = u + 128 - r
                bf1[r, u] = np.exp(0.01 * max(0, 10 - d), dtype=np.float64).astype(np.float32)
                dg = r + 9 - u
                bf2[r, u] = np.exp(-0.01 * max(0, 10 - dg), dtype=np.float64).astype(np.float32)
    return bf1, bf2


_NC_CACHE = {}


def kernel(L0_attn_x, L0_attn_W, L0_attn_b, L0_ffn_clip_x, L0_ffn_clip_down,
           L0_ffn_w_gate, L0_ffn_w_up, L0_ffn_w_down, L0_c, L0_wmod):
    x = np.ascontiguousarray(np.asarray(L0_attn_x, dtype=np.float32))
    W = np.ascontiguousarray(np.asarray(L0_attn_W, dtype=np.float32))
    battn = np.ascontiguousarray(np.asarray(L0_attn_b, dtype=np.float32)).reshape(1, D)
    cx = np.ascontiguousarray(np.asarray(L0_ffn_clip_x, dtype=np.float32)).reshape(1, 1)
    cd = np.ascontiguousarray(np.asarray(L0_ffn_clip_down, dtype=np.float32)).reshape(1, 1)
    wg = np.ascontiguousarray(np.asarray(L0_ffn_w_gate, dtype=np.float32))
    wu = np.ascontiguousarray(np.asarray(L0_ffn_w_up, dtype=np.float32))
    wd = np.ascontiguousarray(np.asarray(L0_ffn_w_down, dtype=np.float32))
    cmat = np.ascontiguousarray(np.asarray(L0_c, dtype=np.float32))
    wmod = np.ascontiguousarray(np.asarray(L0_wmod, dtype=np.float32))

    if "nc" not in _NC_CACHE:
        _NC_CACHE["nc"] = build_nc(S=_S, dev=0, n_cores=8)
    nc = _NC_CACHE["nc"]

    in_maps = []
    for corei in range(8):
        b_, qh = corei // 2, corei % 2
        xb = np.ascontiguousarray(np.roll(x[b_], -_Q * qh, axis=0))
        bf1, bf2 = _bandfix_tiles(qh)
        in_maps.append(dict(
            x_full=xb, W=W, attn_b=battn, c_row=cmat[b_:b_ + 1],
            wmod=wmod, clip_x=cx, clip_down=cd,
            w_gate=wg, w_up=wu, w_down=wd,
            bandfix1=bf1, bandfix2=bf2))
    res = run_bass_kernel_spmd(nc, in_maps, core_ids=list(range(8)))
    out = np.empty((4, _S, D), np.float32)
    for corei in range(8):
        b_, qh = corei // 2, corei % 2
        out[b_, _Q * qh:_Q * (qh + 1)] = res.results[corei]["out"]
    return out


# version 09
# speedup vs baseline: 1.0016x; 1.0016x over previous
from contextlib import ExitStack

import numpy as np
import concourse.bass as bass
import concourse.mybir as mybir

F32 = mybir.dt.float32
BF16 = mybir.dt.bfloat16
I32 = mybir.dt.int32
ALU = mybir.AluOpType
AF = mybir.ActivationFunctionType
AX = mybir.AxisListType.X


class Sem:
    def __init__(self, nc, stack, name):
        self.h = stack.enter_context(nc.semaphore(name))
        self.n = 0

    def inc(self, k=1):
        self.n += k
        return self.n


class Item:
    __slots__ = ("eng", "emit", "waits", "inc_sem", "inc_amt")

    def __init__(self, eng, emit, waits, inc_sem, inc_amt):
        self.eng = eng
        self.emit = emit
        self.waits = waits
        self.inc_sem = inc_sem
        self.inc_amt = inc_amt


class Phase:
    ENGINES = ("sync", "scalar", "vector", "tensor", "gpsimd")

    def __init__(self, name):
        self.name = name
        self.items = []

    def add(self, eng, emit, waits=(), sem=None, inc=1):
        assert eng in Phase.ENGINES, eng
        waits = list(waits)
        if sem is not None and not waits:
            waits = [(sem, sem.n)]
        if sem is not None:
            sem.inc(inc)
        self.items.append(Item(eng, emit, waits, sem, inc))

    def emit(self, nc):
        with nc.Block() as blk:
            for eng_name in Phase.ENGINES:
                items = [it for it in self.items if it.eng == eng_name]
                if not items:
                    continue

                def body(e, items=items):
                    for it in items:
                        for (s, v) in it.waits:
                            e.wait_ge(s.h, v)
                    raise RuntimeError("unused")

                def body2(e, items=items):
                    for it in items:
                        for (s, v) in it.waits:
                            e.wait_ge(s.h, v)
                        inst = it.emit(e)
                        if it.inc_sem is not None:
                            inst.then_inc(it.inc_sem.h, it.inc_amt)

                getattr(blk, eng_name)(body2)


class Ctx:
    def __init__(self, nc, stack):
        self.nc = nc
        self.stack = stack
        self.sems = {}

    def sem(self, name):
        if name not in self.sems:
            self.sems[name] = Sem(self.nc, self.stack, name)
        return self.sems[name]

    def sbuf(self, name, shape, dtype):
        return self.stack.enter_context(self.nc.sbuf_tensor(name, shape, dtype))

    def psum(self, name, shape, dtype=F32):
        return self.stack.enter_context(self.nc.psum_tensor(name, shape, dtype))


"""adaln megakernel builder. Phases:
P0: constants, modulation (mod = silu(c)@wmod), b' = b + sa@W, transforms
P1: f = (LN(x)*(1+ca))@W + b' for all S tokens -> f_feat (SBUF) + f_tok (HBM)
P2: flash attention + residual + LN2 -> x', h2 spilled; h2 absmax
P3: weight quant+transpose (overlapped later), collective max #1
P4: gate/up + silu*up -> hmid spilled; absmax; collective max #2
P5: down + residual -> out
"""
import numpy as np
import concourse.bass as bass
import concourse.mybir as mybir

D = 1024
DI = 2816
NKC = 8
EPS = 1e-5


def build_nc(S=8192, dev=0, n_cores=8):
    nc = bass.Bass("TRN2", target_bir_lowering=False)
    Q = S // 2
    n_xt = S // 128
    n_qt = Q // 128
    stack = ExitStack()
    ctx = Ctx(nc, stack)
    sb, ps, sem = ctx.sbuf, ctx.psum, ctx.sem

    x_hbm = nc.dram_tensor("x_full", [S, D], F32, kind="ExternalInput").ap()
    W_hbm = nc.dram_tensor("W", [D, D], F32, kind="ExternalInput").ap()
    b_hbm = nc.dram_tensor("attn_b", [1, D], F32, kind="ExternalInput").ap()
    c_hbm = nc.dram_tensor("c_row", [1, D], F32, kind="ExternalInput").ap()
    wmod_hbm = nc.dram_tensor("wmod", [D, 6 * D], F32, kind="ExternalInput").ap()
    clipx_hbm = nc.dram_tensor("clip_x", [1, 1], F32, kind="ExternalInput").ap()
    clipd_hbm = nc.dram_tensor("clip_down", [1, 1], F32, kind="ExternalInput").ap()
    wg_hbm = nc.dram_tensor("w_gate", [DI, D], F32, kind="ExternalInput").ap()
    wu_hbm = nc.dram_tensor("w_up", [DI, D], F32, kind="ExternalInput").ap()
    wd_hbm = nc.dram_tensor("w_down", [D, DI], F32, kind="ExternalInput").ap()
    out_hbm = nc.dram_tensor("out", [Q, D], F32, kind="ExternalOutput").ap()

    ftok_hbm = nc.dram_tensor("ftok_hbm", [S, D], BF16).ap()
    mod_hbm = nc.dram_tensor("mod_hbm", [8, D], F32).ap()

    dev_f = dev_ftk = None
    if dev & 1:
        dev_f = nc.dram_tensor("dev_ffeat", [NKC * 128, S], BF16, kind="ExternalOutput").ap()

    ident_bf = sb("ident_bf", [128, 128], BF16)
    ident_f32 = sb("ident_f32", [128, 128], F32)
    band = sb("band", [128, 160], F32)
    amax1 = sb("amax1", [128, 1], F32)
    amax2 = sb("amax2", [128, 1], F32)
    one_bf = sb("one_bf", [1, 1], BF16)
    one_f32 = sb("one_f32", [1, 1], F32)
    clip_sb = sb("clip_sb", [1, 2], F32)
    scl_g = sb("scl_g", [128, 22], F32)
    scl_u = sb("scl_u", [128, 22], F32)
    scl_d = sb("scl_d", [128, 8], F32)
    ffstack = ExitStack()
    f_feat = [ffstack.enter_context(nc.sbuf_tensor(f"f_feat{k}", [128, S], BF16))
              for k in range(NKC)]

    dC = sem("dC")
    dW = [sem("dW0"), sem("dW1")]
    dWM = [sem("dWM0"), sem("dWM1")]
    dBC = sem("dBC")
    dX = [sem("dX0"), sem("dX1")]
    dSF = [sem("dSF0"), sem("dSF1")]
    dMW = sem("dMW")
    sS = sem("sS")
    sV = sem("sV")
    sT = sem("sT")
    sG = sem("sG")

    p0 = Phase("p0p1")
    marks = {}

    p0a = ExitStack()
    p0b = ExitStack()
    p0early = ExitStack()
    p0sb = lambda n, sh, dt: p0b.enter_context(nc.sbuf_tensor(n, sh, dt))
    p0ps = lambda n, sh: p0early.enter_context(nc.psum_tensor(n, sh, F32))
    W_bf = p0a.enter_context(nc.sbuf_tensor("W_bf", [128, NKC * D], BF16))
    opca_feat = p0a.enter_context(nc.sbuf_tensor("opca_feat", [128, NKC], F32))
    bp_tile = p0a.enter_context(nc.sbuf_tensor("bp_tile", [128, D], F32))

    c_sb = p0sb("c_sb", [1, D], F32)
    b_sb = p0sb("b_sb", [1, D], F32)
    sgfeat = p0sb("sgfeat", [128, NKC], F32)
    iota_t = p0sb("iota_t", [128, 160], F32)
    wmbuf = [p0sb(f"wmbuf{j}", [128, 2 * D], F32) for j in range(2)]
    wbuf = wmbuf
    mod_acc = p0sb("mod_acc", [1, 6 * D], F32)
    t1r = p0sb("t1r", [1, D], F32)
    sa_bf = p0sb("sa_bf", [1, D], BF16)
    safeat = p0sb("safeat", [128, NKC], BF16)

    psC = p0ps("psC", [128, NKC])
    psM = [p0ps(f"psM{j}", [1, 512]) for j in range(2)]
    psR = [p0ps(f"psR{j}", [1, 512]) for j in range(2)]

    p0.add("sync", lambda e: e.dma_start(out=c_sb[:], in_=c_hbm), sem=dC, inc=16)
    p0.add("sync", lambda e: e.dma_start(out=b_sb[:], in_=b_hbm), sem=dC, inc=16)
    p0.add("sync", lambda e: e.dma_start(out=clip_sb[:, 0:1], in_=clipx_hbm), sem=dC, inc=16)
    p0.add("sync", lambda e: e.dma_start(out=clip_sb[:, 1:2], in_=clipd_hbm), sem=dC, inc=16)
    dC_all = dC.n

    p0.add("gpsimd", lambda e: e.memset(ident_bf[:], 0.0), sem=sG)
    p0.add("gpsimd", lambda e: e.affine_select(
        out=ident_bf[:], in_=ident_bf[:], compare_op=ALU.not_equal,
        fill=1.0, base=0, pattern=[[-1, 128]], channel_multiplier=1),
        waits=[(sG, sG.n)], sem=sG)
    g_identbf = sG.n
    p0.add("gpsimd", lambda e: e.memset(ident_f32[:], 0.0), sem=sG)
    p0.add("gpsimd", lambda e: e.affine_select(
        out=ident_f32[:], in_=ident_f32[:], compare_op=ALU.not_equal,
        fill=1.0, base=0, pattern=[[-1, 128]], channel_multiplier=1),
        waits=[(sG, sG.n)], sem=sG)
    p0.add("gpsimd", lambda e: e.memset(one_bf[:], 1.0), sem=sG)
    g_onebf = sG.n
    p0.add("gpsimd", lambda e: e.memset(one_f32[:], 1.0), sem=sG)
    g_onef32 = sG.n
    p0.add("gpsimd", lambda e: e.iota(
        iota_t[:], [[1, 160]], base=-9, channel_multiplier=-1,
        allow_small_or_imprecise_dtypes=True), sem=sG)
    g_iota = sG.n
    p0.add("scalar", lambda e: e.activation(band[:], iota_t[:], AF.Abs),
           waits=[(sG, g_iota)], sem=sS)
    ss_b = sS.n
    p0.add("vector", lambda e: e.tensor_scalar(
        band[:], band[:], -1.0, 10.0, ALU.mult, ALU.add), waits=[(sS, ss_b)], sem=sV)
    p0.add("vector", lambda e: e.tensor_scalar(
        band[:], band[:], 0.0, -0.01, ALU.max, ALU.mult), waits=[(sV, sV.n)], sem=sV)

    p0.add("scalar", lambda e: e.activation(c_sb[:], c_sb[:], AF.Silu),
           waits=[(dC, dC_all)], sem=sS)
    ss_siluc = sS.n
    for k in range(NKC):
        p0.add("tensor", lambda e, k=k: e.matmul(
            psC[:, k:k + 1], c_sb[0:1, 128 * k:128 * (k + 1)], one_f32[0:1, 0:1],
            start=True, stop=True),
            waits=[(sS, ss_siluc), (sG, g_onef32)] if k == 0 else [], sem=sT)
    st_sgf = sT.n
    p0.add("scalar", lambda e: e.copy(sgfeat[:], psC[:]), waits=[(sT, st_sgf)], sem=sS)
    ss_sgf = sS.n

    sv_wcast = []
    for k in range(NKC):
        w_ = [(sV, sv_wcast[k - 2])] if k >= 2 else []
        p0.add("sync", lambda e, k=k: e.dma_start(
            out=wbuf[k % 2][:, 0:D], in_=W_hbm[128 * k:128 * (k + 1), :]),
            waits=w_, sem=dW[k % 2], inc=16)
        dval = dW[k % 2].n
        p0.add("vector", lambda e, k=k: e.tensor_copy(
            W_bf[:, D * k:D * (k + 1)], wbuf[k % 2][:, 0:D]),
            waits=[(dW[k % 2], dval)], sem=sV)
        sv_wcast.append(sV.n)
    sv_Wbf = sV.n

    wmm_cons = {}
    addmark = {}
    nsub = 3
    subs = [(k, c4) for k in range(NKC) for c4 in range(nsub)]
    for idx, (k, c4) in enumerate(subs):
        pbuf = idx % 2
        w_ = [wmm_cons[idx - 2]] if idx >= 2 else [(sV, sv_wcast[6 + idx])]
        p0.add("sync", lambda e, k=k, c4=c4, pbuf=pbuf: e.dma_start(
            out=wmbuf[pbuf][:], in_=wmod_hbm[128 * k:128 * (k + 1),
                                             2 * D * c4:2 * D * (c4 + 1)]),
            waits=w_, sem=dWM[pbuf], inc=16)
        dval = dWM[pbuf].n
        for h in range(4):
            s_ = 4 * c4 + h
            bank = s_ % 2
            w_mm = [(dWM[pbuf], dval), (sS, ss_sgf)]
            if bank in addmark:
                w_mm.append(addmark[bank])
            p0.add("tensor", lambda e, k=k, h=h, pbuf=pbuf, bank=bank: e.matmul(
                psM[bank][:], sgfeat[:, k:k + 1], wmbuf[pbuf][0:128, 512 * h:512 * (h + 1)],
                start=True, stop=True), waits=w_mm, sem=sT)
            mmv = sT.n
            wmm_cons[idx] = (sT, mmv)
            eng = "vector"
            esem = sV
            if k == 0:
                p0.add(eng, lambda e, s_=s_, bank=bank: e.tensor_copy(
                    mod_acc[0:1, 512 * s_:512 * (s_ + 1)], psM[bank][0:1, :]),
                    waits=[(sT, mmv)], sem=esem)
            else:
                p0.add(eng, lambda e, s_=s_, bank=bank: e.tensor_add(
                    mod_acc[0:1, 512 * s_:512 * (s_ + 1)],
                    mod_acc[0:1, 512 * s_:512 * (s_ + 1)], psM[bank][0:1, :]),
                    waits=[(sT, mmv)], sem=esem)
            addmark[bank] = (esem, esem.n)
    sv_mod = sV.n
    sg_mod = sG.n

    def msl(i):
        return mod_acc[0:1, D * i:D * (i + 1)]
    mod_ready = [(sV, sv_mod), (sG, sg_mod)]

    p0.add("vector", lambda e: e.tensor_copy(sa_bf[:], msl(0)), waits=mod_ready, sem=sV)
    sv_sabf = sV.n
    for k in range(NKC):
        p0.add("tensor", lambda e, k=k: e.matmul(
            psC[:, k:k + 1], sa_bf[0:1, 128 * k:128 * (k + 1)], one_bf[0:1, 0:1],
            start=True, stop=True),
            waits=[(sV, sv_sabf), (sS, ss_sgf)] if k == 0 else [], sem=sT)
    st_saf = sT.n
    p0.add("scalar", lambda e: e.copy(safeat[:], psC[:]), waits=[(sT, st_saf)], sem=sS)
    ss_saf = sS.n

    p0.add("vector", lambda e: e.tensor_scalar(msl(1), msl(1), 1.0, None, ALU.add),
           waits=mod_ready, sem=sV)
    sv_opca = sV.n
    for k in range(NKC):
        p0.add("tensor", lambda e, k=k: e.matmul(
            psC[:, k:k + 1], mod_acc[0:1, D + 128 * k:D + 128 * (k + 1)], one_f32[0:1, 0:1],
            start=True, stop=True),
            waits=[(sV, sv_opca), (sG, g_onef32), (sS, ss_saf)] if k == 0 else [], sem=sT)
    st_opca = sT.n
    p0.add("scalar", lambda e: e.copy(opca_feat[:], psC[:]), waits=[(sT, st_opca)], sem=sS)
    ss_opca = sS.n

    p0.add("vector", lambda e: e.tensor_scalar(msl(4), msl(4), 1.0, None, ALU.add),
           waits=mod_ready, sem=sV)
    for idx in (2, 5):
        p0.add("vector", lambda e, idx=idx: e.tensor_scalar(
            t1r[:], msl(idx), 0.0, None, ALU.is_ge),
            waits=mod_ready + [(sV, sV.n), (sS, sS.n)], sem=sV)
        p0.add("vector", lambda e: e.tensor_scalar(
            t1r[:], t1r[:], 2.0, -1.0, ALU.mult, ALU.add), waits=[(sV, sV.n)], sem=sV)
        p0.add("scalar", lambda e, idx=idx: e.activation(mod_acc[0:1, 0:D], msl(idx), AF.Abs),
               waits=mod_ready + [(sS, sS.n), (sV, sV.n)], sem=sS)
        ss_abs = sS.n
        p0.add("vector", lambda e: e.tensor_scalar(
            mod_acc[0:1, 0:D], mod_acc[0:1, 0:D], 0.25, None, ALU.max), waits=[(sS, ss_abs)], sem=sV)
        p0.add("vector", lambda e, idx=idx: e.tensor_mul(msl(idx), t1r[:], mod_acc[0:1, 0:D]),
               waits=[(sV, sV.n)], sem=sV)
    sv_gagb = sV.n

    for s_ in range(2):
        for k in range(NKC):
            w_ = [(sS, ss_saf), (sV, sv_Wbf)] if (s_ == 0 and k == 0) else []
            p0.add("tensor", lambda e, s_=s_, k=k: e.matmul(
                psR[s_][:], safeat[:, k:k + 1],
                W_bf[0:128, D * k + 512 * s_:D * k + 512 * s_ + 512],
                start=(k == 0), stop=(k == NKC - 1)), waits=w_, sem=sT)
        mmv = sT.n
        p0.add("vector", lambda e, s_=s_: e.tensor_add(
            t1r[0:1, 512 * s_:512 * (s_ + 1)], psR[s_][0:1, :],
            b_sb[0:1, 512 * s_:512 * (s_ + 1)]),
            waits=[(sT, mmv), (dC, dC_all), (sV, sv_gagb)], sem=sV)
    sv_bp = sV.n

    for src, slot, mark in ((msl(2), 0, sv_gagb), (msl(4), 1, sv_gagb),
                            (msl(3), 2, sv_gagb), (msl(5), 3, sv_gagb),
                            (t1r[:], 4, sv_bp)):
        p0.add("gpsimd", lambda e, src=src, slot=slot: e.dma_start(
            out=mod_hbm[slot:slot + 1, :], in_=src), waits=[(sV, mark)], sem=dMW, inc=16)
    dMW_all = dMW.n
    dMW_all0 = dMW_all

    p0.add("sync", lambda e: e.dma_start(
        out=bp_tile[:], in_=mod_hbm[4:5, :].broadcast_to((128, D))),
        waits=[(dMW, dMW_all)], sem=dBC, inc=16)
    dBC_bp = dBC.n

    p0b.close()
    p1sb = lambda n, sh, dt: p0a.enter_context(nc.sbuf_tensor(n, sh, dt))
    xb = [p1sb(f"xb{j}", [128, D], F32) for j in range(2)]
    h_bf = [p1sb(f"hbf{j}", [128, D], BF16) for j in range(2)]
    hT_bf = [p1sb(f"hTbf{j}", [128, D], BF16) for j in range(2)]
    ftk = [p1sb(f"ftk{j}", [128, D], BF16) for j in range(2)]
    stats = [p1sb(f"stats{j}", [128, 16], F32) for j in range(2)]
    p0early.close()
    p1ps = lambda n, sh: p0a.enter_context(nc.psum_tensor(n, sh, F32))
    ps_tr = [p1ps(f"ps_tr{j}", [128, 128]) for j in range(2)]
    ps_f = [p1ps(f"ps_f{j}", [128, D]) for j in range(2)]

    m_xfree = [None] * (n_xt + 4)
    m_trfree = [0] * (n_xt + 4)
    m_psffree = [0] * (n_xt + 4)
    m_hTdone = [0] * (n_xt + 4)
    m_ftokst = [None] * (n_xt + 4)

    for i in range(n_xt):
        pb = i % 2
        if i >= 2:
            w_ = [m_xfree[i - 2]]
        elif i == 0:
            w_ = [(dMW, dMW_all0), (sV, sv_bp), (sS, ss_opca), (sG, sg_mod),
                  (sT, sT.n)]
        else:
            w_ = []
        p0.add("sync", lambda e, i=i, pb=pb: e.dma_start(
            out=xb[pb][:], in_=x_hbm[128 * i:128 * (i + 1), :]),
            waits=w_, sem=dX[pb], inc=16)
        dxv = dX[pb].n

        st = stats[pb]
        w_ = [(dX[pb], dxv)]
        if i >= 2:
            w_.append(m_xfree[i - 2])
        p0.add("vector", lambda e, pb=pb, st=st: e.bn_stats(st[:, 0:6], xb[pb][:, 0:512]),
               waits=w_, sem=sV)
        p0.add("vector", lambda e, pb=pb, st=st: e.bn_stats(st[:, 6:12], xb[pb][:, 512:1024]),
               waits=[(sV, sV.n)], sem=sV)
        p0.add("vector", lambda e, st=st: e.bn_aggr(st[:, 12:14], st[:, 0:12]),
               waits=[(sV, sV.n)], sem=sV)
        p0.add("vector", lambda e, st=st: e.tensor_scalar(
            st[:, 14:15], st[:, 13:14], EPS, None, ALU.add), waits=[(sV, sV.n)], sem=sV)
        p0.add("vector", lambda e, st=st: e.reciprocal(st[:, 15:16], st[:, 14:15]),
               waits=[(sV, sV.n)], sem=sV)
        sv_rec = sV.n
        p0.add("scalar", lambda e, st=st: e.activation(st[:, 0:1], st[:, 15:16], AF.Sqrt),
               waits=[(sV, sv_rec)], sem=sS)
        ss_sq = sS.n
        p0.add("vector", lambda e, st=st: e.tensor_mul(st[:, 1:2], st[:, 0:1], st[:, 0:1]),
               waits=[(sS, ss_sq)], sem=sV)
        p0.add("vector", lambda e, st=st: e.tensor_mul(st[:, 2:3], st[:, 1:2], st[:, 14:15]),
               waits=[(sV, sV.n)], sem=sV)
        p0.add("vector", lambda e, st=st: e.tensor_scalar(
            st[:, 3:4], st[:, 2:3], -0.5, 1.5, ALU.mult, ALU.add), waits=[(sV, sV.n)], sem=sV)
        p0.add("vector", lambda e, st=st: e.tensor_mul(st[:, 4:5], st[:, 0:1], st[:, 3:4]),
               waits=[(sV, sV.n)], sem=sV)
        p0.add("vector", lambda e, st=st: e.tensor_mul(st[:, 5:6], st[:, 12:13], st[:, 4:5]),
               waits=[(sV, sV.n)], sem=sV)
        p0.add("vector", lambda e, st=st: e.tensor_scalar(
            st[:, 6:7], st[:, 5:6], -1.0, None, ALU.mult), waits=[(sV, sV.n)], sem=sV)
        sv_ln = sV.n

        w_ = [(sV, sv_ln)]
        if i >= 2:
            w_.append((sT, m_hTdone[i - 2]))
        p0.add("vector", lambda e, pb=pb, st=st: e.tensor_scalar(
            h_bf[pb][:], xb[pb][:], st[:, 4:5], st[:, 6:7], ALU.mult, ALU.add),
            waits=w_, sem=sV)
        m_xfree[i] = (sV, sV.n)
        ss_h = sV.n

        ss_evac = []
        for j in range(NKC):
            w_ = [(sV, ss_h)]
            if j < 2:
                w_.append((sG, g_identbf))
                if i > 0:
                    w_.append((sS, m_trfree[i - 1]))
                else:
                    w_.append((sS, ss_opca))
                    w_.append((sV, sv_bp))
                    w_.append((sG, sg_mod))
            else:
                w_.append((sS, ss_evac[j - 2]))
            p0.add("tensor", lambda e, pb=pb, j=j: e.matmul(
                ps_tr[j % 2][:], h_bf[pb][:, 128 * j:128 * (j + 1)], ident_bf[:],
                start=True, stop=True), waits=w_, sem=sT)
            mmv = sT.n
            p0.add("scalar", lambda e, pb=pb, j=j: e.activation(
                hT_bf[pb][:, 128 * j:128 * (j + 1)], ps_tr[j % 2][:], AF.Copy,
                scale=opca_feat[:, j:j + 1]),
                waits=[(sT, mmv)], sem=sS)
            ss_evac.append(sS.n)
        ss_hT = sS.n

        for s_ in range(2):
            for k in range(NKC):
                w_ = []
                if s_ == 0 and k == 0:
                    w_ = [(sS, ss_hT), (sV, sv_Wbf)]
                    if i >= 2:
                        w_.append((sV, m_psffree[i - 2]))
                p0.add("tensor", lambda e, pb=pb, s_=s_, k=k: e.matmul(
                    ps_f[pb][:, 512 * s_:512 * (s_ + 1)],
                    hT_bf[pb][:, 128 * k:128 * (k + 1)],
                    W_bf[0:128, D * k + 512 * s_:D * k + 512 * s_ + 512],
                    start=(k == 0), stop=(k == NKC - 1)), waits=w_, sem=sT)
        m_hTdone[i] = sT.n
        st_f = sT.n

        for s_ in range(2):
            w_ = [(sT, st_f), (dBC, dBC_bp)]
            if i >= 2:
                w_.append(m_ftokst[i - 2])
                w_.append((sT, m_hTdone[i - 1]))
            p0.add("vector", lambda e, pb=pb, s_=s_: e.tensor_add(
                ftk[pb][:, 512 * s_:512 * (s_ + 1)], ps_f[pb][:, 512 * s_:512 * (s_ + 1)],
                bp_tile[:, 512 * s_:512 * (s_ + 1)]),
                waits=w_ if s_ == 0 else [], sem=sV)
        m_psffree[i] = sV.n
        sv_ftk = sV.n

        p0.add("gpsimd", lambda e, pb=pb, i=i: e.dma_start(
            out=ftok_hbm[128 * i:128 * (i + 1), :], in_=ftk[pb][:]),
            waits=[(sV, sv_ftk)], sem=dSF[pb], inc=16)
        m_ftokst[i] = (dSF[pb], dSF[pb].n)

        ss_evac2 = []
        for j in range(NKC):
            w_ = [(sV, sv_ftk)]
            if j < 2:
                w_.append((sS, ss_hT))
            else:
                w_.append((sS, ss_evac2[j - 2]))
            p0.add("tensor", lambda e, pb=pb, j=j: e.matmul(
                ps_tr[j % 2][:], ftk[pb][:, 128 * j:128 * (j + 1)], ident_bf[:],
                start=True, stop=True), waits=w_, sem=sT)
            mmv = sT.n
            p0.add("scalar", lambda e, j=j, i=i: e.copy(
                f_feat[j][:, 128 * i:128 * (i + 1)], ps_tr[j % 2][:]),
                waits=[(sT, mmv)], sem=sS)
            ss_evac2.append(sS.n)
        m_trfree[i] = sS.n
        m_ftrdone = sT.n
        m_hTdone[i] = max(m_hTdone[i], m_ftrdone)

    marks["p1_sS"] = sS.n
    marks["p1_sV"] = sV.n
    marks["p1_sT"] = sT.n

    if dev & 1:
        dDV = sem("dDV")
        for j in range(NKC):
            p0.add("gpsimd", lambda e, j=j: e.dma_start(
                out=dev_f[128 * j:128 * (j + 1), :], in_=f_feat[j][:]),
                waits=[(sS, marks["p1_sS"])], sem=dDV, inc=16)
        p0.add("gpsimd", lambda e: e.nop(), waits=[(dDV, dDV.n)])

    p0.add("scalar", lambda e: e.activation(band[:], band[:], AF.Exp),
           waits=[(sV, sV.n)], sem=sS)
    p0.add("gpsimd", lambda e: e.memset(amax1[:], 0.0), sem=sG)
    p0.add("gpsimd", lambda e: e.memset(amax2[:], 0.0), sem=sG)

    for eng, s_ in (("sync", None), ("scalar", sS), ("vector", sV),
                    ("tensor", sT), ("gpsimd", sG)):
        if s_ is not None:
            p0.add(eng, lambda e: e.nop(), sem=s_)
    marks["b1"] = {"sS": sS.n, "sV": sV.n, "sT": sT.n, "sG": sG.n}
    marks["dMW_all"] = dMW_all
    marks["dSF_final"] = [(dSF[0], dSF[0].n), (dSF[1], dSF[1].n)]

    p0.emit(nc)
    p0a.close()

    env = build_p2(nc, ctx, locals())
    env = build_p4(nc, ctx, env)
    env = build_p5(nc, ctx, env)
    stack.close()
    return nc


def build_p2(nc, ctx, env):
    S = env["S"]; Q = env["Q"]; n_qt = env["n_qt"]; dev = env["dev"]
    f_feat = env["f_feat"]; ident_bf = env["ident_bf"]; band = env["band"]
    amax1 = env["amax1"]
    x_hbm = env["x_hbm"]; ftok_hbm = env["ftok_hbm"]; mod_hbm = env["mod_hbm"]
    wg_hbm = env["wg_hbm"]; wu_hbm = env["wu_hbm"]; wd_hbm = env["wd_hbm"]
    sS = env["sS"]; sV = env["sV"]; sT = env["sT"]; sG = env["sG"]
    sem = ctx.sem
    marks = env["marks"]
    b1 = marks["b1"]
    PAD = 16
    n_tc = S // 512
    n_sc = S // 128
    n_pairs = n_qt // 2

    xp_hbm = nc.dram_tensor("xp_hbm", [Q, D], F32).ap()
    h2_hbm = nc.dram_tensor("h2_hbm", [Q, D], F32).ap()
    qg_hbm = nc.dram_tensor("qg_hbm", [22, 128, D], BF16).ap()
    qu_hbm = nc.dram_tensor("qu_hbm", [22, 128, D], BF16).ap()
    qd_hbm = nc.dram_tensor("qd_hbm", [22, 128, D], BF16).ap()
    bf1_hbm = nc.dram_tensor("bandfix1", [128, 16], F32, kind="ExternalInput").ap()
    bf2_hbm = nc.dram_tensor("bandfix2", [128, 16], F32, kind="ExternalInput").ap()

    scl_g = env["scl_g"]; scl_u = env["scl_u"]; scl_d = env["scl_d"]

    p2a = ExitStack()
    sb2 = lambda n, sh, dt: p2a.enter_context(nc.sbuf_tensor(n, sh, dt))
    ps2 = lambda n, sh: p2a.enter_context(nc.psum_tensor(n, sh, F32))

    p_sb = [sb2(f"p_sb{q}", [128, PAD + S], BF16) for q in range(2)]
    NMC = [sb2(f"NMC{q}", [128, n_tc], F32) for q in range(2)]
    Erow = [sb2(f"Erow{q}", [128, n_tc], F32) for q in range(2)]
    rsrow = [sb2(f"rs{q}", [128, n_tc], F32) for q in range(2)]
    smallq = [sb2(f"smallq{q}", [128, 8], F32) for q in range(2)]
    D_sb = [[sb2(f"Dsb{q}{j}", [128, 128], BF16) for j in range(2)] for q in range(2)]
    pT_bf = [[sb2(f"pTbf{q}{j}", [128, 128], BF16) for j in range(4)] for q in range(2)]
    fstr = [sb2(f"fstr{j}", [128, D], BF16) for j in range(2)]
    fstr2 = [sb2(f"fstr2{j}", [128, D], BF16) for j in range(2)]
    ga_tile = sb2("ga_tile", [128, D], BF16)
    onepcb_tile = sb2("onepcb_tile", [128, D], F32)
    sbb_tile = sb2("sbb_tile", [128, D], BF16)
    bfix1 = sb2("bfix1", [128, 16], F32)
    bfix2 = sb2("bfix2", [128, 16], F32)
    xe = sb2("xe", [128, D], F32)
    _xp1 = sb2("xp_sb", [128, D], F32)
    _h21 = sb2("h2_sb", [128, D], F32)
    xp_sb = [_xp1, _xp1]
    h2_sb = [_h21, _h21]
    st2 = [sb2(f"st2{q}", [128, 16], F32) for q in range(2)]
    wv = [sb2(f"wv{j}", [128, 512], F32) for j in range(2)]
    yq = [sb2(f"yq{j}", [128, 512], I32) for j in range(2)]
    NSTG = 3
    stg = [sb2(f"stg{j}", [128, 512], BF16) for j in range(NSTG)]
    mcols = sb2("mcols", [128, 64], F32)
    amx_buf = sb2("amx_buf", [128, 256], F32)
    amx_c = sb2("amx_c", [128, 2], F32)

    e_ps = [ps2(f"e_ps{j}", [128, 512]) for j in range(4)]
    pT_ps = [e_ps[0], e_ps[1]]
    o_ps = [ps2(f"o_ps{q}", [128, D]) for q in range(2)]

    p2 = Phase("p2")
    dBC2 = sem("dBC2")
    dFS = [sem("dFS0"), sem("dFS1"), sem("dFS2")]
    dXE = sem("dXE")
    dXP = [sem("dXP0"), sem("dXP1")]
    dXH = [sem("dXH0"), sem("dXH1")]
    dPW = [sem("dPW0"), sem("dPW1")]
    dPA = sem("dPA")
    dQSs = [sem(f"dQSs{i}") for i in range(NSTG)]

    bar = [(sS, b1["sS"]), (sV, b1["sV"]), (sT, b1["sT"]), (sG, b1["sG"])]
    p2.add("gpsimd", lambda e: e.nop(), waits=bar + marks["dSF_final"])
    for t_ in (smallq[0], smallq[1], st2[0], st2[1], mcols, scl_g, scl_u,
               scl_d, amx_c, NMC[0], NMC[1], Erow[0], Erow[1]):
        p2.add("gpsimd", lambda e, t_=t_: e.memset(t_[:], 0.0), sem=sG)
    g_init = sG.n
    bar2 = bar + [(sG, g_init)]
    for eng in ("scalar", "vector", "tensor"):
        p2.add(eng, lambda e: e.nop(), waits=bar2)
    p2.add("sync", lambda e: e.nop(), waits=bar2 + marks["dSF_final"])

    dMW_all = marks["dMW_all"]
    dMW = env["dMW"]
    p2.add("sync", lambda e: e.dma_start(
        out=onepcb_tile[:], in_=mod_hbm[1:2, :].broadcast_to((128, D))),
        waits=[(dMW, dMW_all)], sem=dBC2, inc=16)
    p2.add("sync", lambda e: e.dma_start(out=bfix1[:], in_=bf1_hbm), sem=dBC2, inc=16)
    p2.add("sync", lambda e: e.dma_start(out=bfix2[:], in_=bf2_hbm), sem=dBC2, inc=16)
    dBC2g = sem("dBC2g")
    dBC2g2 = sem("dBC2g2")
    p2.add("gpsimd", lambda e: e.dma_start(
        out=sbb_tile[:], in_=mod_hbm[2:3, :].broadcast_to((128, D))),
        waits=[(dMW, dMW_all)], sem=dBC2g, inc=16)
    p2.add("gpsimd", lambda e: e.dma_start(
        out=ga_tile[:], in_=mod_hbm[0:1, :].broadcast_to((128, D))),
        waits=[(dMW, dMW_all)], sem=dBC2g2, inc=16)
    p2.add("sync", lambda e: e.nop(),
           waits=[(dBC2g, dBC2g.n), (dBC2g2, dBC2g2.n)], sem=dBC2, inc=1)
    dBC2_all = dBC2.n

    WKIND = {"g": (wg_hbm, qg_hbm, scl_g, 0, D), "u": (wu_hbm, qu_hbm, scl_u, 22, D),
             "d": (wd_hbm, qd_hbm, scl_d, 44, DI)}
    amax_jobs = [("g", t) for t in range(22)] + [("u", t) for t in range(22)] + \
                [("d", t) for t in range(8)]
    strip_jobs = []
    for kind, t in amax_jobs:
        Wd = WKIND[kind][4]
        for c0 in range(0, Wd, 512):
            strip_jobs.append((kind, t, c0, min(512, Wd - c0)))
    amax_done = {}
    aj = [0]
    sq_i = [0]
    free_slots = list(range(NSTG))
    pending = []
    quant_gdone = {}

    def prep_amax(budget):
        for _ in range(budget):
            if aj[0] >= len(amax_jobs):
                return
            kind, t = amax_jobs[aj[0]]
            aj[0] += 1
            src, dst, scl, mbase, Wd = WKIND[kind]
            first = True
            for c0 in range(0, Wd, 256):
                wdt = min(256, Wd - c0)
                p2.add("sync", lambda e, src=src, t=t, c0=c0, wdt=wdt: e.dma_start(
                    out=amx_buf[:, 0:wdt], in_=src[128 * t:128 * (t + 1), c0:c0 + wdt]),
                    waits=[(sV, sV.n)], sem=dPA, inc=16)
                dv = dPA.n
                if first:
                    p2.add("vector", lambda e, wdt=wdt: e.tensor_reduce(
                        amx_c[:, 0:1], amx_buf[:, 0:wdt], AX, ALU.max,
                        apply_absolute_value=True), waits=[(dPA, dv), (sV, sV.n)], sem=sV)
                else:
                    p2.add("vector", lambda e, wdt=wdt: e.tensor_reduce(
                        amx_c[:, 1:2], amx_buf[:, 0:wdt], AX, ALU.max,
                        apply_absolute_value=True), waits=[(dPA, dv), (sV, sV.n)], sem=sV)
                    p2.add("vector", lambda e: e.tensor_max(
                        amx_c[:, 0:1], amx_c[:, 0:1], amx_c[:, 1:2]),
                        waits=[(sV, sV.n)], sem=sV)
                first = False
            mi = mbase + t
            p2.add("vector", lambda e, scl=scl, t=t: e.tensor_scalar(
                scl[:, t:t + 1], amx_c[:, 0:1], 0.007874015748031496, None, ALU.mult),
                waits=[(sV, sV.n)], sem=sV)
            p2.add("vector", lambda e, mi=mi, scl=scl, t=t: e.reciprocal(
                mcols[:, mi:mi + 1], scl[:, t:t + 1]), waits=[(sV, sV.n)], sem=sV)
            amax_done[(kind, t)] = (sV, sV.n)

    def prep_quant(budget):
        for _ in range(budget):
            sj = sq_i[0]
            if sj >= len(strip_jobs) or not free_slots:
                return
            kind, t, c0, wdt = strip_jobs[sj]
            if (kind, t) not in amax_done:
                return
            sq_i[0] += 1
            src, dst, scl, mbase, Wd = WKIND[kind]
            slot = free_slots.pop(0)
            bslot = sj % 2
            mi = mbase + t
            w_ = []
            if (sj - 2) in quant_gdone:
                w_.append(quant_gdone[sj - 2])
            p2.add("sync", lambda e, src=src, t=t, c0=c0, wdt=wdt, bslot=bslot: e.dma_start(
                out=wv[bslot][:, 0:wdt], in_=src[128 * t:128 * (t + 1), c0:c0 + wdt]),
                waits=w_, sem=dPW[bslot], inc=16)
            dv = dPW[bslot].n
            p2.add("vector", lambda e, bslot=bslot, mi=mi, wdt=wdt: e.tensor_scalar(
                wv[bslot][:, 0:wdt], wv[bslot][:, 0:wdt], mcols[:, mi:mi + 1], None,
                ALU.mult), waits=[(dPW[bslot], dv), amax_done[(kind, t)], (sV, sV.n)], sem=sV)
            p2.add("vector", lambda e, bslot=bslot, wdt=wdt: e.tensor_copy(
                yq[bslot][:, 0:wdt], wv[bslot][:, 0:wdt]), waits=[(sV, sV.n)], sem=sV)
            sv_cast = sV.n
            p2.add("gpsimd", lambda e, bslot=bslot, wdt=wdt: e.tensor_sub(
                wv[bslot][:, 0:wdt], wv[bslot][:, 0:wdt], yq[bslot][:, 0:wdt]),
                waits=[(sV, sv_cast)], sem=sG)
            sg_d = sG.n
            w_c2 = [(sG, sg_d), marks.get(("stgfree", slot), (sG, 0))]
            p2.add("vector", lambda e, bslot=bslot, slot=slot, wdt=wdt: e.tensor_scalar(
                stg[slot][:, 0:wdt], wv[bslot][:, 0:wdt], -0.5, None, ALU.is_lt),
                waits=w_c2, sem=sV)
            p2.add("vector", lambda e, bslot=bslot, slot=slot, wdt=wdt: e.scalar_tensor_tensor(
                wv[bslot][:, 0:wdt], wv[bslot][:, 0:wdt], 0.5, stg[slot][:, 0:wdt],
                ALU.is_gt, ALU.subtract), waits=[(sV, sV.n)], sem=sV)
            sv_fix = sV.n
            p2.add("gpsimd", lambda e, bslot=bslot, slot=slot, wdt=wdt: e.tensor_add(
                stg[slot][:, 0:wdt], wv[bslot][:, 0:wdt], yq[bslot][:, 0:wdt]),
                waits=[(sV, sv_fix), marks.get(("stgfree", slot), (sG, 0))], sem=sG)
            quant_gdone[sj] = (sG, sG.n)
            pending.append((sj, slot, (sG, sG.n)))

    def prep_transpose(budget, pt_gate):
        last_evac = None
        for _ in range(budget):
            if not pending:
                break
            sj, slot, qmark = pending.pop(0)
            kind, t, c0, wdt = strip_jobs[sj]
            src, dst, scl, mbase, Wd = WKIND[kind]
            nch = wdt // 128
            ev = []
            for j in range(nch):
                w_ = [qmark]
                if pt_gate is not None and len(ev) == 0:
                    w_.append(pt_gate)
                bm = marks.get(("obank", j % 2))
                if bm is not None:
                    w_.append(bm)
                p2.add("tensor", lambda e, slot=slot, j=j: e.matmul(
                    o_ps[j % 2][:, 0:128], stg[slot][:, 128 * j:128 * (j + 1)],
                    ident_bf[:], start=True, stop=True), waits=w_, sem=sT)
                mmv = sT.n
                p2.add("scalar", lambda e, slot=slot, j=j: e.copy(
                    stg[slot][:, 128 * j:128 * (j + 1)], o_ps[j % 2][:, 0:128]),
                    waits=[(sT, mmv)], sem=sS)
                marks[("obank", j % 2)] = (sS, sS.n)
                ev.append(sS.n)
            if kind in ("g", "u"):
                p2.add("gpsimd", lambda e, dst=dst, t=t, c0=c0, wdt=wdt, slot=slot: e.dma_start(
                    out=dst[t, :, c0:c0 + wdt], in_=stg[slot][:, 0:wdt]),
                    waits=[(sS, ev[-1])], sem=dQSs[slot], inc=16)
            else:
                for j in range(nch):
                    ic = (c0 + 128 * j) // 128
                    p2.add("gpsimd", lambda e, dst=dst, ic=ic, t=t, j=j, slot=slot: e.dma_start(
                        out=dst[ic, :, 128 * t:128 * (t + 1)],
                        in_=stg[slot][:, 128 * j:128 * (j + 1)]),
                        waits=[(sS, ev[-1])], sem=dQSs[slot], inc=16)
            marks[("stgfree", slot)] = (dQSs[slot], dQSs[slot].n)
            free_slots.append(slot)
            last_evac = (sS, ev[-1])
        return last_evac

    m_expdone = [[0] * n_tc for _ in range(2)]
    m_exphist = []
    m_pvlast = None
    m_epidone = [None, None]
    marks["pv_prev"] = 0

    for pr in range(n_pairs):
        qts = (2 * pr, 2 * pr + 1)
        prep_amax(9 if pr < 6 else 0)
        prep_quant(4)

        ss_exp_last = [None, None]
        sv_p1done = [None, None]
        tr_mark = None
        for qi, qt in enumerate(qts):
            for tc in range(n_tc):
                bank = len(m_exphist) % 4
                if len(m_exphist) >= 4:
                    w_ = [(sS, m_exphist[-4])]
                else:
                    w_ = [(dBC2, dBC2_all)]
                for k in range(NKC):
                    p2.add("tensor", lambda e, k=k, qt=qt, tc=tc, bank=bank: e.matmul(
                        e_ps[bank][:], f_feat[k][:, 128 * qt:128 * (qt + 1)],
                        f_feat[k][:, 512 * tc:512 * (tc + 1)],
                        start=(k == 0), stop=(k == NKC - 1)),
                        waits=w_ if k == 0 else [], sem=sT)
                st_e = sT.n
                p2.add("vector", lambda e, qi=qi, tc=tc, bank=bank: e.reduce_max(
                    NMC[qi][:, tc:tc + 1], e_ps[bank][:], AX, negate=True),
                    waits=[(sT, st_e)], sem=sV)
                sv_m = sV.n
                w_ = [(sV, sv_m)]
                if tc == 0 and pr > 0:
                    w_.append((sT, marks["pv_prev"]))
                p2.add("scalar", lambda e, qi=qi, tc=tc, bank=bank: e.activation(
                    p_sb[qi][:, PAD + 512 * tc:PAD + 512 * (tc + 1)], e_ps[bank][:],
                    AF.Exp, bias=NMC[qi][:, tc:tc + 1]),
                    waits=w_, sem=sS)
                m_expdone[qi][tc] = sS.n
                m_exphist.append(sS.n)
                marks[("ptbank", bank)] = (sS, sS.n)
            ss_exp_last[qi] = sS.n

            sq = smallq[qi]
            p2.add("vector", lambda e, qi=qi, sq=sq: e.tensor_reduce(
                sq[:, 0:1], NMC[qi][:], AX, ALU.min), waits=[(sV, sV.n)], sem=sV)
            p2.add("vector", lambda e, qi=qi, sq=sq: e.tensor_scalar(
                Erow[qi][:], NMC[qi][:], -1.0, sq[:, 0:1], ALU.mult, ALU.add),
                waits=[(sV, sV.n)], sem=sV)
            sv_er = sV.n
            p2.add("scalar", lambda e, qi=qi: e.activation(
                Erow[qi][:], Erow[qi][:], AF.Exp),
                waits=[(sV, sv_er), (sS, ss_exp_last[qi])], sem=sS)
            ss_ebf = sS.n
            W0 = 128 * qt - 9
            lo = max(W0, 0)
            hi = W0 + 146
            tpl_lo = lo - W0
            p2.add("vector", lambda e, qi=qi, lo=lo, hi=hi, tpl_lo=tpl_lo: e.tensor_mul(
                p_sb[qi][:, PAD + lo:PAD + hi], p_sb[qi][:, PAD + lo:PAD + hi],
                band[:, tpl_lo:tpl_lo + (hi - lo)]),
                waits=[(sS, ss_exp_last[qi])], sem=sV)
            if qt == n_qt - 1:
                p2.add("vector", lambda e, qi=qi: e.tensor_mul(
                    p_sb[qi][:, PAD + Q:PAD + Q + 9], p_sb[qi][:, PAD + Q:PAD + Q + 9],
                    bfix1[:, 0:9]), waits=[(sV, sV.n)], sem=sV)
            if qt == 0:
                p2.add("vector", lambda e, qi=qi: e.tensor_mul(
                    p_sb[qi][:, PAD + S - 9:PAD + S], p_sb[qi][:, PAD + S - 9:PAD + S],
                    bfix2[:, 0:9]), waits=[(sV, sV.n), (dBC2, dBC2_all)], sem=sV)
            sv_band = sV.n
            for tc in range(n_tc):
                p2.add("vector", lambda e, qi=qi, tc=tc: e.reduce_sum(
                    rsrow[qi][:, tc:tc + 1],
                    p_sb[qi][:, PAD + 512 * tc:PAD + 512 * (tc + 1)], AX),
                    waits=[(sV, sv_band)] if tc == 0 else [], sem=sV)
            p2.add("vector", lambda e, qi=qi: e.tensor_mul(
                rsrow[qi][:], rsrow[qi][:], Erow[qi][:]),
                waits=[(sV, sV.n), (sS, ss_ebf)], sem=sV)
            p2.add("vector", lambda e, qi=qi, sq=sq: e.reduce_sum(
                sq[:, 1:2], rsrow[qi][:], AX), waits=[(sV, sV.n)], sem=sV)
            p2.add("vector", lambda e, qi=qi, sq=sq: e.reciprocal(
                sq[:, 2:3], sq[:, 1:2]), waits=[(sV, sV.n)], sem=sV)
            sv_p1done[qi] = sV.n

            gate = m_pvlast if qi == 0 else None
            tr = prep_transpose(4, gate if tr_mark is None else None)
            tr_mark = tr if tr is not None else tr_mark
            if qi == 0:
                prep_quant(4)

        sv_D = {}
        ss_pT = {}
        mm_pT = {}
        st_pvcons = [0] * (n_sc + 8)
        dfs_mark = {}

        def emit_D(sc):
            tc = sc // 4
            for qi in range(2):
                w_ = [(sS, ss_ebf), (sV, sv_p1done[qi])]
                if tc >= 2 and (4 * (tc - 2) + 3) in mm_pT:
                    w_.append((sT, mm_pT[4 * (tc - 2) + 3]))
                p2.add("vector", lambda e, qi=qi, tc=tc: e.tensor_scalar(
                    D_sb[qi][tc % 2][:], ident_bf[:], Erow[qi][:, tc:tc + 1],
                    None, ALU.mult), waits=w_, sem=sV)
                sv_D[(qi, tc)] = sV.n

        def emit_pT(sc):
            tc = sc // 4
            par = sc % 4
            for qi in range(2):
                w_ = [(sV, sv_D[(qi, tc)]), (sV, sv_p1done[qi])]
                bm = marks.get(("ptbank", par))
                if bm is not None:
                    w_.append(bm)
                if sc >= 4:
                    w_.append((sS, ss_pT[(qi, sc - 4)]))
                p2.add("tensor", lambda e, qi=qi, sc=sc, par=par, tc=tc: e.matmul(
                    e_ps[par][:, 0:128], p_sb[qi][:, PAD + 128 * sc:PAD + 128 * (sc + 1)],
                    D_sb[qi][tc % 2][:], start=True, stop=True),
                    waits=w_, sem=sT)
                mmv = sT.n
                mm_pT[sc] = mmv
                p2.add("scalar", lambda e, qi=qi, par=par: e.copy(
                    pT_bf[qi][par][:], e_ps[par][:, 0:128]), waits=[(sT, mmv)], sem=sS)
                marks[("ptbank", par)] = (sS, sS.n)
                ss_pT[(qi, sc)] = sS.n

        emit_D(0)
        for _s in range(min(2, n_sc)):
            emit_pT(_s)

        dfs_val = {}

        def emit_fload(blk):
            buf = blk % 2
            w_ = []
            if blk >= 2:
                w_.append((sT, st_pvcons[2 * (blk - 2) + 1]))
            elif pr > 0:
                w_.append((sT, marks["pv_prev"]))
            for h, fb in ((0, fstr), (1, fstr2)):
                p2.add("sync", lambda e, blk=blk, h=h, fb=fb, buf=buf: e.dma_start(
                    out=fb[buf][:],
                    in_=ftok_hbm[128 * (2 * blk + h):128 * (2 * blk + h + 1), :]),
                    waits=w_ if h == 0 else [], sem=dFS[buf], inc=16)
            dfs_val[blk] = dFS[buf].n

        emit_fload(0)
        for sc in range(n_sc):
            par = sc % 4
            buf = (sc // 2) % 2
            if sc % 2 == 0:
                if sc // 2 + 1 < n_sc // 2:
                    emit_fload(sc // 2 + 1)
                dfs_mark[buf] = dfs_val[sc // 2]
            if (sc + 3) % 4 == 0 and sc + 3 < n_sc:
                emit_D(sc + 3)
            if sc + 2 < n_sc:
                emit_pT(sc + 2)
            fs = fstr if sc % 2 == 0 else fstr2
            for qi in range(2):
                w_ = [(sS, ss_pT[(qi, sc)]), (dFS[buf], dfs_mark[buf])]
                if sc == 0 and m_epidone[qi] is not None:
                    w_.append(m_epidone[qi])
                if sc == 0 and tr_mark is not None:
                    w_.append(tr_mark)
                for h in range(2):
                    p2.add("tensor", lambda e, qi=qi, h=h, par=par, buf=buf, fs=fs,
                           sc=sc: e.matmul(
                        o_ps[qi][:, 512 * h:512 * (h + 1)], pT_bf[qi][par][:],
                        fs[buf][:, 512 * h:512 * (h + 1)],
                        start=(sc == 0), stop=(sc == n_sc - 1)),
                        waits=w_ if h == 0 else [], sem=sT)
            st_pvcons[sc] = sT.n
        marks["pv_prev"] = sT.n
        m_pvlast = (sT, sT.n)
        st_pv = sT.n

        for qi, qt in enumerate(qts):
            sq = smallq[qi]
            w_ = [marks["xe_free"]] if "xe_free" in marks else []
            p2.add("sync", lambda e, qt=qt: e.dma_start(
                out=xe[:], in_=x_hbm[128 * qt:128 * (qt + 1), :]),
                waits=w_, sem=dXE, inc=16)
            dxe = dXE.n
            w_ = [(sT, st_pv), (dBC2, dBC2_all)]
            for m in m_epidone:
                if m is not None:
                    w_.append(m)
            if marks.get("xpstore") is not None:
                w_.append(marks["xpstore"])
            p2.add("vector", lambda e, qi=qi, sq=sq: e.scalar_tensor_tensor(
                xp_sb[qi][:], o_ps[qi][:], sq[:, 2:3], ga_tile[:], ALU.mult, ALU.mult),
                waits=w_, sem=sV)
            marks[("obank", qi)] = (sV, sV.n)
            p2.add("vector", lambda e, qi=qi: e.tensor_add(
                xp_sb[qi][:], xp_sb[qi][:], xe[:]), waits=[(sV, sV.n), (dXE, dxe)], sem=sV)
            sv_xp = sV.n
            marks["xe_free"] = (sV, sv_xp)
            p2.add("gpsimd", lambda e, qi=qi, qt=qt: e.dma_start(
                out=xp_hbm[128 * qt:128 * (qt + 1), :], in_=xp_sb[qi][:]),
                waits=[(sV, sv_xp)], sem=dXP[qi], inc=16)
            marks["xpstore"] = (dXP[qi], dXP[qi].n)
            st_ = st2[qi]
            p2.add("vector", lambda e, qi=qi, st_=st_: e.bn_stats(
                st_[:, 0:6], xp_sb[qi][:, 0:512]), waits=[(sV, sv_xp)], sem=sV)
            p2.add("vector", lambda e, qi=qi, st_=st_: e.bn_stats(
                st_[:, 6:12], xp_sb[qi][:, 512:1024]), waits=[(sV, sV.n)], sem=sV)
            p2.add("vector", lambda e, st_=st_: e.bn_aggr(
                st_[:, 12:14], st_[:, 0:12]), waits=[(sV, sV.n)], sem=sV)
            p2.add("vector", lambda e, st_=st_: e.tensor_scalar(
                st_[:, 14:15], st_[:, 13:14], EPS, None, ALU.add), waits=[(sV, sV.n)], sem=sV)
            p2.add("vector", lambda e, st_=st_: e.reciprocal(
                st_[:, 15:16], st_[:, 14:15]), waits=[(sV, sV.n)], sem=sV)
            sv_rec = sV.n
            p2.add("scalar", lambda e, st_=st_: e.activation(
                st_[:, 0:1], st_[:, 15:16], AF.Sqrt), waits=[(sV, sv_rec)], sem=sS)
            ss_sq2 = sS.n
            p2.add("vector", lambda e, st_=st_: e.tensor_mul(
                st_[:, 1:2], st_[:, 0:1], st_[:, 0:1]), waits=[(sS, ss_sq2)], sem=sV)
            p2.add("vector", lambda e, st_=st_: e.tensor_mul(
                st_[:, 2:3], st_[:, 1:2], st_[:, 14:15]), waits=[(sV, sV.n)], sem=sV)
            p2.add("vector", lambda e, st_=st_: e.tensor_scalar(
                st_[:, 3:4], st_[:, 2:3], -0.5, 1.5, ALU.mult, ALU.add),
                waits=[(sV, sV.n)], sem=sV)
            p2.add("vector", lambda e, st_=st_: e.tensor_mul(
                st_[:, 4:5], st_[:, 0:1], st_[:, 3:4]), waits=[(sV, sV.n)], sem=sV)
            p2.add("vector", lambda e, st_=st_: e.tensor_mul(
                st_[:, 5:6], st_[:, 12:13], st_[:, 4:5]), waits=[(sV, sV.n)], sem=sV)
            p2.add("vector", lambda e, st_=st_: e.tensor_scalar(
                st_[:, 6:7], st_[:, 5:6], -1.0, None, ALU.mult), waits=[(sV, sV.n)], sem=sV)
            w_ = [(sV, sV.n)]
            if m_epidone[qi] is not None:
                w_.append(m_epidone[qi])
            p2.add("vector", lambda e, qi=qi, st_=st_: e.tensor_scalar(
                h2_sb[qi][:], xp_sb[qi][:], st_[:, 4:5], st_[:, 6:7], ALU.mult, ALU.add),
                waits=w_, sem=sV)
            p2.add("vector", lambda e, qi=qi: e.tensor_mul(
                h2_sb[qi][:], h2_sb[qi][:], onepcb_tile[:]), waits=[(sV, sV.n)], sem=sV)
            p2.add("vector", lambda e, qi=qi: e.tensor_add(
                h2_sb[qi][:], h2_sb[qi][:], sbb_tile[:]), waits=[(sV, sV.n)], sem=sV)
            sv_h2 = sV.n
            p2.add("vector", lambda e, qi=qi, st_=st_: e.reduce_max(
                st_[:, 7:8], h2_sb[qi][:], AX, apply_absolute_value=True),
                waits=[(sV, sv_h2)], sem=sV)
            p2.add("vector", lambda e, st_=st_: e.tensor_max(
                amax1[:], amax1[:], st_[:, 7:8]), waits=[(sV, sV.n)], sem=sV)
            p2.add("gpsimd", lambda e, qi=qi, qt=qt: e.dma_start(
                out=h2_hbm[128 * qt:128 * (qt + 1), :], in_=h2_sb[qi][:]),
                waits=[(sV, sv_h2)], sem=dXH[qi], inc=16)
            m_epidone[qi] = (dXH[qi], dXH[qi].n)

    while aj[0] < len(amax_jobs) or sq_i[0] < len(strip_jobs) or pending:
        before = (aj[0], sq_i[0], len(pending))
        prep_amax(4)
        prep_quant(4)
        prep_transpose(8, None)
        if (aj[0], sq_i[0], len(pending)) == before:
            raise RuntimeError("weight prep drain stuck")

    for eng, s_ in (("scalar", sS), ("vector", sV), ("tensor", sT), ("gpsimd", sG)):
        p2.add(eng, lambda e: e.nop(), sem=s_)
    marks["b2"] = {"sS": sS.n, "sV": sV.n, "sT": sT.n, "sG": sG.n,
                   "h2st": list(m_epidone), "dQS": [(s_, s_.n) for s_ in dQSs],
                   "xpst": [(dXP[0], dXP[0].n), (dXP[1], dXP[1].n), (dXH[0], dXH[0].n), (dXH[1], dXH[1].n)]}

    p2.emit(nc)
    p2a.close()
    env.update(dict(xp_hbm=xp_hbm, h2_hbm=h2_hbm, qg_hbm=qg_hbm, qu_hbm=qu_hbm,
                    qd_hbm=qd_hbm, scl_g=scl_g, scl_u=scl_u, scl_d=scl_d))
    return env


def build_p4(nc, ctx, env):
    S = env["S"]; Q = env["Q"]; n_qt = env["n_qt"]
    ident_bf = env["ident_bf"]; ident_f32 = env["ident_f32"]
    amax1 = env["amax1"]; amax2 = env["amax2"]; clip_sb = env["clip_sb"]
    h2_hbm = env["h2_hbm"]; qg_hbm = env["qg_hbm"]; qu_hbm = env["qu_hbm"]
    scl_g = env["scl_g"]; scl_u = env["scl_u"]
    sS = env["sS"]; sV = env["sV"]; sT = env["sT"]; sG = env["sG"]
    sem = ctx.sem
    marks = env["marks"]
    b2 = marks["b2"]
    n_tt = Q // 128
    n_tb = Q // 512

    hmid_hbm = nc.dram_tensor("hmid_hbm", [DI, Q], F32).ap()
    scal_hbm = nc.dram_tensor("scal_hbm", [4, 8], F32).ap()
    cc1_in = nc.dram_tensor("cc1_in", [1, 8], F32)
    cc1_out = nc.dram_tensor("cc1_out", [1, 8], F32)

    env["ffstack"].close()
    p4a = ExitStack()
    sb4 = lambda n, sh, dt: p4a.enter_context(nc.sbuf_tensor(n, sh, dt))
    ps4 = lambda n, sh: p4a.enter_context(nc.psum_tensor(n, sh, F32))

    qg_sb = [sb4(f"qg_sb{t}", [128, D], BF16) for t in range(22)]
    qu_sb = [sb4(f"qu_sb{t}", [128, D], BF16) for t in range(22)]
    xqT = [sb4(f"xqT{k}", [128, Q], BF16) for k in range(NKC)]
    h2b = [sb4(f"h2b{j}", [128, D], F32) for j in range(2)]
    qbuf = [sb4(f"qbuf{j}", [128, D], F32) for j in range(2)]
    ybuf = [sb4(f"ybuf{j}", [128, D], I32) for j in range(2)]
    cbuf = [sb4(f"cbuf{j}", [128, D], F32) for j in range(2)]
    xq_bf = [sb4(f"xq_bf{j}", [128, D], BF16) for j in range(2)]
    gate_f = [sb4(f"gate_f{j}", [128, 512], F32) for j in range(2)]
    hm_sb = [sb4(f"hm_sb{j}", [128, 512], F32) for j in range(2)]
    gsc = sb4("gsc", [1, 8], F32)
    mxcol = sb4("mxcol", [128, 1], F32)
    sxcol = sb4("sxcol", [128, 1], F32)
    am4 = sb4("am4", [128, 2], F32)
    sgx = sb4("sgx", [128, 22], F32)
    sux = sb4("sux", [128, 22], F32)

    psA = ps4("psA", [1, 128])
    ps_tr2 = [ps4(f"ps_tr2{j}", [128, 512]) for j in range(2)]
    psG = [ps4(f"psG{j}", [128, 512]) for j in range(2)]
    psU = [ps4(f"psU{j}", [128, 512]) for j in range(2)]

    p4 = Phase("p4")
    dH2 = [sem("dH2_0"), sem("dH2_1")]
    dQR = sem("dQR")
    dCC1 = sem("dCC1")
    cc1s = sem("cc1s")
    dSC = sem("dSC")
    dSCB = sem("dSCB")
    dHM = [sem("dHM0"), sem("dHM1")]

    bar = [(sS, b2["sS"]), (sV, b2["sV"]), (sT, b2["sT"]), (sG, b2["sG"])]
    p4.add("gpsimd", lambda e: e.nop(), waits=bar)
    for t_ in (sgx, sux, gsc, mxcol, sxcol):
        p4.add("gpsimd", lambda e, t_=t_: e.memset(t_[:], 0.0), sem=sG)
    g_init = sG.n
    bar2 = bar + [(sG, g_init)]
    for eng in ("scalar", "vector", "tensor"):
        p4.add(eng, lambda e: e.nop(), waits=bar2)
    p4.add("sync", lambda e: e.nop(),
           waits=bar2 + b2["h2st"] + b2["xpst"] + b2["dQS"])

    p4.add("tensor", lambda e: e.matmul(
        psA[:], amax1[:], ident_f32[:], start=True, stop=True), sem=sT)
    st_a = sT.n
    p4.add("vector", lambda e: e.reduce_max(gsc[0:1, 0:1], psA[0:1, :], AX),
           waits=[(sT, st_a)], sem=sV)
    sv_g1 = sV.n
    p4.add("gpsimd", lambda e: e.dma_start(out=cc1_in[0:1, :], in_=gsc[0:1, :]),
           waits=[(sV, sv_g1)], sem=dCC1, inc=16)
    p4.add("gpsimd", lambda e: e.collective_compute(
        "AllReduce", ALU.max, replica_groups=[list(range(env["n_cores"]))],
        ins=[cc1_in.ap().opt()], outs=[cc1_out.ap().opt()]),
        waits=[(dCC1, dCC1.n)], sem=cc1s)
    p4.add("gpsimd", lambda e: e.dma_start(out=gsc[0:1, 1:2], in_=cc1_out[0:1, 0:1]),
           waits=[(cc1s, cc1s.n)], sem=dCC1, inc=16)
    d_gmax = dCC1.n
    p4.add("scalar", lambda e: e.activation(gsc[0:1, 2:3], clip_sb[0:1, 0:1], AF.Sigmoid),
           waits=[(dCC1, d_gmax)], sem=sS)
    ss_sig = sS.n
    p4.add("vector", lambda e: e.tensor_mul(gsc[0:1, 3:4], gsc[0:1, 1:2], gsc[0:1, 2:3]),
           waits=[(sS, ss_sig), (dCC1, d_gmax)], sem=sV)
    p4.add("vector", lambda e: e.tensor_scalar(
        gsc[0:1, 3:4], gsc[0:1, 3:4], 0.007874015748031496, None, ALU.mult), waits=[(sV, sV.n)], sem=sV)
    p4.add("vector", lambda e: e.reciprocal(gsc[0:1, 4:5], gsc[0:1, 3:4]),
           waits=[(sV, sV.n)], sem=sV)
    sv_mx = sV.n
    p4.add("gpsimd", lambda e: e.dma_start(out=scal_hbm[0:1, :], in_=gsc[0:1, :]),
           waits=[(sV, sv_mx)], sem=dSC, inc=16)
    p4.add("sync", lambda e: e.dma_start(
        out=mxcol[:], in_=scal_hbm[0:1, 4:5].broadcast_to((128, 1))),
        waits=[(dSC, dSC.n)], sem=dSCB, inc=16)
    p4.add("sync", lambda e: e.dma_start(
        out=sxcol[:], in_=scal_hbm[0:1, 3:4].broadcast_to((128, 1))),
        sem=dSCB, inc=16)
    d_bc = dSCB.n
    p4.add("vector", lambda e: e.tensor_scalar(
        sgx[:], scl_g[:], sxcol[:, 0:1], None, ALU.mult), waits=[(dSCB, d_bc)], sem=sV)
    p4.add("vector", lambda e: e.tensor_scalar(
        sux[:], scl_u[:], sxcol[:, 0:1], None, ALU.mult), waits=[(sV, sV.n)], sem=sV)
    sv_scales = sV.n

    dqr_marks = []
    for t in range(22):
        p4.add("sync", lambda e, t=t: e.dma_start(out=qg_sb[t][:], in_=qg_hbm[t]),
               sem=dQR, inc=16)
        p4.add("sync", lambda e, t=t: e.dma_start(out=qu_sb[t][:], in_=qu_hbm[t]),
               sem=dQR, inc=16)
    dQR_all = dQR.n

    m_quant = [None] * (n_tt + 4)
    m_h2free = [None] * (n_tt + 4)
    m_trbank = [None, None]
    quant_done = [0]

    def quant_tile(j):
        pb = j % 2
        w_ = [m_h2free[j - 2]] if j >= 2 else []
        p4.add("sync", lambda e, j=j, pb=pb: e.dma_start(
            out=h2b[pb][:], in_=h2_hbm[128 * j:128 * (j + 1), :]),
            waits=w_, sem=dH2[pb], inc=16)
        dv = dH2[pb].n
        w_ = [(dH2[pb], dv), (sV, sv_scales)]
        if j >= 2:
            w_.append(m_quant[j - 2])
        p4.add("vector", lambda e, pb=pb: e.tensor_scalar(
            qbuf[pb][:], h2b[pb][:], mxcol[:, 0:1], None, ALU.mult), waits=w_, sem=sV)
        m_h2free[j] = (sV, sV.n)
        p4.add("vector", lambda e, pb=pb: e.tensor_copy(ybuf[pb][:], qbuf[pb][:]),
               waits=[(sV, sV.n)], sem=sV)
        sv_c = sV.n
        p4.add("gpsimd", lambda e, pb=pb: e.tensor_sub(
            qbuf[pb][:], qbuf[pb][:], ybuf[pb][:]), waits=[(sV, sv_c)], sem=sG)
        sg_d = sG.n
        p4.add("vector", lambda e, pb=pb: e.tensor_scalar(
            cbuf[pb][:], qbuf[pb][:], -0.5, None, ALU.is_lt), waits=[(sG, sg_d)], sem=sV)
        p4.add("vector", lambda e, pb=pb: e.scalar_tensor_tensor(
            qbuf[pb][:], qbuf[pb][:], 0.5, cbuf[pb][:], ALU.is_gt, ALU.subtract),
            waits=[(sV, sV.n)], sem=sV)
        sv_f = sV.n
        p4.add("gpsimd", lambda e, pb=pb: e.tensor_add(
            cbuf[pb][:], qbuf[pb][:], ybuf[pb][:]), waits=[(sV, sv_f)], sem=sG)
        sg_y = sG.n
        p4.add("vector", lambda e, pb=pb: e.tensor_scalar(
            xq_bf[pb][:], cbuf[pb][:], 127.0, -128.0, ALU.min, ALU.max),
            waits=[(sG, sg_y)], sem=sV)
        sv_xq = sV.n
        for k in range(NKC):
            w_ = [(sV, sv_xq)]
            if m_trbank[k % 2] is not None:
                w_.append(m_trbank[k % 2])
            p4.add("tensor", lambda e, pb=pb, k=k: e.matmul(
                ps_tr2[k % 2][:, 0:128], xq_bf[pb][:, 128 * k:128 * (k + 1)],
                ident_bf[:], start=True, stop=True), waits=w_, sem=sT)
            mmv = sT.n
            p4.add("scalar", lambda e, k=k, j=j: e.copy(
                xqT[k][:, 128 * j:128 * (j + 1)], ps_tr2[k % 2][:, 0:128]),
                waits=[(sT, mmv)], sem=sS)
            m_trbank[k % 2] = (sS, sS.n)
        m_quant[j] = (sS, sS.n)

    mGU = {}
    m_gevac = {}
    m_hmst = {}
    for tb in range(n_tb):
        while quant_done[0] < min(n_tt, 4 * (tb + 1)):
            quant_tile(quant_done[0])
            quant_done[0] += 1
        for t in range(22):
            pb = t % 2
            w_ = [m_quant[4 * tb + 3] if 4 * tb + 3 < n_tt else m_quant[n_tt - 1],
                  (dQR, dQR_all)]
            if ("ge", pb) in m_gevac:
                w_.append(m_gevac[("ge", pb)])
            for k in range(NKC):
                p4.add("tensor", lambda e, t=t, tb=tb, k=k, pb=pb: e.matmul(
                    psG[pb][:], qg_sb[t][:, 128 * k:128 * (k + 1)],
                    xqT[k][:, 512 * tb:512 * (tb + 1)],
                    start=(k == 0), stop=(k == NKC - 1)),
                    waits=w_ if k == 0 else [], sem=sT)
            mGU[(tb, t, "g")] = (sT, sT.n)
            st_g = sT.n
            w_ = []
            if (("ue", pb)) in m_gevac:
                w_.append(m_gevac[("ue", pb)])
            for k in range(NKC):
                p4.add("tensor", lambda e, t=t, tb=tb, k=k, pb=pb: e.matmul(
                    psU[pb][:], qu_sb[t][:, 128 * k:128 * (k + 1)],
                    xqT[k][:, 512 * tb:512 * (tb + 1)],
                    start=(k == 0), stop=(k == NKC - 1)),
                    waits=w_ if k == 0 else [], sem=sT)
            st_u = sT.n
            w_ = [(sT, st_g)]
            if ("gf", pb) in m_gevac:
                w_.append(m_gevac[("gf", pb)])
            p4.add("scalar", lambda e, t=t, pb=pb: e.activation(
                gate_f[pb][:], psG[pb][:], AF.Silu, scale=sgx[:, t:t + 1]),
                waits=w_, sem=sS)
            m_gevac[("ge", pb)] = (sS, sS.n)
            ss_gf = sS.n
            w_ = [(sT, st_u), (sS, ss_gf)]
            if ("hm", pb) in m_gevac:
                w_.append(m_gevac[("hm", pb)])
            if ("hmst", pb) in m_gevac:
                w_.append(m_gevac[("hmst", pb)])
            p4.add("vector", lambda e, t=t, pb=pb: e.scalar_tensor_tensor(
                hm_sb[pb][:], psU[pb][:], sux[:, t:t + 1], gate_f[pb][:],
                ALU.mult, ALU.mult), waits=w_, sem=sV)
            m_gevac[("ue", pb)] = (sV, sV.n)
            m_gevac[("gf", pb)] = (sV, sV.n)
            sv_hm = sV.n
            p4.add("vector", lambda e, pb=pb: e.reduce_max(
                am4[:, 0:1], hm_sb[pb][:], AX, apply_absolute_value=True),
                waits=[(sV, sv_hm)], sem=sV)
            p4.add("vector", lambda e: e.tensor_max(
                amax2[:], amax2[:], am4[:, 0:1]), waits=[(sV, sV.n)], sem=sV)
            p4.add("gpsimd", lambda e, t=t, tb=tb, pb=pb: e.dma_start(
                out=hmid_hbm[128 * t:128 * (t + 1), 512 * tb:512 * (tb + 1)],
                in_=hm_sb[pb][:]), waits=[(sV, sv_hm)], sem=dHM[pb], inc=16)
            m_gevac[("hmst", pb)] = (dHM[pb], dHM[pb].n)

    for eng, s_ in (("scalar", sS), ("vector", sV), ("tensor", sT), ("gpsimd", sG)):
        p4.add(eng, lambda e: e.nop(), sem=s_)
    marks["b4"] = {"sS": sS.n, "sV": sV.n, "sT": sT.n, "sG": sG.n,
                   "hmst": [m_gevac.get(("hmst", 0)), m_gevac.get(("hmst", 1))],
                   "xpst": b2["xpst"]}

    p4.emit(nc)
    p4a.close()
    env.update(dict(hmid_hbm=hmid_hbm, scal_hbm=scal_hbm))
    return env


def build_p5(nc, ctx, env):
    S = env["S"]; Q = env["Q"]
    ident_f32 = env["ident_f32"]
    amax2 = env["amax2"]; clip_sb = env["clip_sb"]
    hmid_hbm = env["hmid_hbm"]; scal_hbm = env["scal_hbm"]
    qd_hbm = env["qd_hbm"]; xp_hbm = env["xp_hbm"]; mod_hbm = env["mod_hbm"]
    out_hbm = env["out_hbm"]
    scl_d = env["scl_d"]
    sS = env["sS"]; sV = env["sV"]; sT = env["sT"]; sG = env["sG"]
    sem = ctx.sem
    marks = env["marks"]
    b4 = marks["b4"]
    n_tt = Q // 128
    nhalf = Q // 2

    cc2_in = nc.dram_tensor("cc2_in", [1, 8], F32)
    cc2_out = nc.dram_tensor("cc2_out", [1, 8], F32)
    sd_hbm = nc.dram_tensor("sd_hbm", [8, 128], F32).ap()

    p5a = ExitStack()
    sb5 = lambda n, sh, dt: p5a.enter_context(nc.sbuf_tensor(n, sh, dt))
    ps5 = lambda n, sh: p5a.enter_context(nc.psum_tensor(n, sh, F32))

    qd_sb = [sb5(f"qd_sb{t}", [128, D], BF16) for t in range(22)]
    hq = [sb5(f"hq{t}", [128, nhalf], BF16) for t in range(22)]
    hmb = [sb5(f"hmb{j}", [128, nhalf], F32) for j in range(2)]
    y5 = sb5("y5", [128, nhalf], I32)
    c5 = sb5("c5", [128, nhalf], F32)
    xpb = [sb5(f"xpb{j}", [128, D], F32) for j in range(2)]
    outb = [sb5(f"outb{j}", [128, D], F32) for j in range(2)]
    gb_tile = sb5("gb_tile", [128, D], F32)
    sdb_tile = sb5("sdb_tile", [128, D], F32)
    sd_sb = sb5("sd_sb", [8, 128], F32)
    g5 = sb5("g5", [1, 8], F32)
    mdcol = sb5("mdcol", [128, 1], F32)
    sdcol = sb5("sdcol", [128, 1], F32)

    psA2 = ps5("psA2", [8, 128])
    psB2 = ps5("psB2", [1, 128])
    psD5 = [ps5(f"psD5{j}", [128, D]) for j in range(2)]

    p5 = Phase("p5")
    dHM5 = [sem("dHM5_0"), sem("dHM5_1")]
    dQD = sem("dQD")
    dCC2 = sem("dCC2")
    cc2s = sem("cc2s")
    dSD = sem("dSD")
    dSD2 = sem("dSD2")
    dSDB = sem("dSDB")
    dXPL = [sem("dXPL0"), sem("dXPL1")]
    dOUT = [sem("dOUT0"), sem("dOUT1")]

    bar = [(sS, b4["sS"]), (sV, b4["sV"]), (sT, b4["sT"]), (sG, b4["sG"])]
    p5.add("gpsimd", lambda e: e.nop(), waits=bar)
    for t_ in (g5, mdcol, sdcol, sd_sb):
        p5.add("gpsimd", lambda e, t_=t_: e.memset(t_[:], 0.0), sem=sG)
    g_init = sG.n
    bar2 = bar + [(sG, g_init)]
    for eng in ("scalar", "vector", "tensor"):
        p5.add(eng, lambda e: e.nop(), waits=bar2)
    p5.add("sync", lambda e: e.nop(),
           waits=bar2 + [m for m in b4["hmst"] if m] + b4["xpst"])

    p5.add("tensor", lambda e: e.matmul(
        psB2[:], amax2[:], ident_f32[:], start=True, stop=True), sem=sT)
    st_a = sT.n
    p5.add("vector", lambda e: e.reduce_max(g5[0:1, 0:1], psB2[0:1, :], AX),
           waits=[(sT, st_a)], sem=sV)
    sv_g1 = sV.n
    p5.add("gpsimd", lambda e: e.dma_start(out=cc2_in[0:1, :], in_=g5[0:1, :]),
           waits=[(sV, sv_g1)], sem=dCC2, inc=16)
    p5.add("gpsimd", lambda e: e.collective_compute(
        "AllReduce", ALU.max, replica_groups=[list(range(env["n_cores"]))],
        ins=[cc2_in.ap().opt()], outs=[cc2_out.ap().opt()]),
        waits=[(dCC2, dCC2.n)], sem=cc2s)
    p5.add("gpsimd", lambda e: e.dma_start(out=g5[0:1, 1:2], in_=cc2_out[0:1, 0:1]),
           waits=[(cc2s, cc2s.n)], sem=dCC2, inc=16)
    d_gmax = dCC2.n
    p5.add("scalar", lambda e: e.activation(g5[0:1, 2:3], clip_sb[0:1, 1:2], AF.Sigmoid),
           waits=[(dCC2, d_gmax)], sem=sS)
    ss_sig = sS.n
    p5.add("vector", lambda e: e.tensor_mul(g5[0:1, 3:4], g5[0:1, 1:2], g5[0:1, 2:3]),
           waits=[(sS, ss_sig), (dCC2, d_gmax)], sem=sV)
    p5.add("vector", lambda e: e.tensor_scalar(
        g5[0:1, 3:4], g5[0:1, 3:4], 0.007874015748031496, None, ALU.mult), waits=[(sV, sV.n)], sem=sV)
    p5.add("vector", lambda e: e.reciprocal(g5[0:1, 4:5], g5[0:1, 3:4]),
           waits=[(sV, sV.n)], sem=sV)
    sv_md = sV.n
    p5.add("gpsimd", lambda e: e.dma_start(out=scal_hbm[1:2, :], in_=g5[0:1, :]),
           waits=[(sV, sv_md)], sem=dSD, inc=16)
    dsd1 = dSD.n
    p5.add("sync", lambda e: e.dma_start(
        out=mdcol[:], in_=scal_hbm[1:2, 4:5].broadcast_to((128, 1))),
        waits=[(dSD, dsd1)], sem=dSDB, inc=16)
    p5.add("sync", lambda e: e.dma_start(
        out=sdcol[:], in_=scal_hbm[1:2, 3:4].broadcast_to((128, 1))),
        sem=dSDB, inc=16)
    d_bc = dSDB.n

    p5.add("tensor", lambda e: e.matmul(
        psA2[:], scl_d[:], ident_f32[:], start=True, stop=True), sem=sT)
    st_sd = sT.n
    p5.add("scalar", lambda e: e.copy(sd_sb[:], psA2[:]), waits=[(sT, st_sd)], sem=sS)
    ss_sd = sS.n
    p5.add("gpsimd", lambda e: e.dma_start(out=sd_hbm, in_=sd_sb[:]),
           waits=[(sS, ss_sd)], sem=dSD2, inc=16)
    dsd2 = dSD2.n
    p5.add("sync", lambda e: e.dma_start(
        out=sdb_tile[:],
        in_=sd_hbm.rearrange("a b -> (a b)").rearrange("(o d) -> o d", o=1).broadcast_to((128, D))),
        waits=[(dSD2, dsd2)], sem=dSDB, inc=16)
    p5.add("sync", lambda e: e.dma_start(
        out=gb_tile[:], in_=mod_hbm[3:4, :].broadcast_to((128, D))),
        sem=dSDB, inc=16)
    d_bc2 = dSDB.n
    p5.add("vector", lambda e: e.tensor_mul(gb_tile[:], gb_tile[:], sdb_tile[:]),
           waits=[(dSDB, d_bc2)], sem=sV)
    p5.add("vector", lambda e: e.tensor_scalar(
        gb_tile[:], gb_tile[:], sdcol[:, 0:1], None, ALU.mult),
        waits=[(sV, sV.n)], sem=sV)
    sv_gp = sV.n

    for t in range(22):
        p5.add("sync", lambda e, t=t: e.dma_start(out=qd_sb[t][:], in_=qd_hbm[t]),
               sem=dQD, inc=16)
    dQD_all = dQD.n

    m_outst = {}
    m_psd = {}
    for half in range(2):
        hq_marks = []
        for ic in range(22):
            pb = ic % 2
            w_ = []
            if ic >= 2:
                w_.append(hq_marks[ic - 2])
            elif half == 1:
                w_.append(marks["p5_half0_done"])
            p5.add("sync", lambda e, ic=ic, half=half, pb=pb: e.dma_start(
                out=hmb[pb][:], in_=hmid_hbm[128 * ic:128 * (ic + 1),
                                            nhalf * half:nhalf * (half + 1)]),
                waits=w_, sem=dHM5[pb], inc=16)
            dv = dHM5[pb].n
            w_ = [(dHM5[pb], dv), (sV, sV.n)]
            if ic == 0 and half == 0:
                w_.append((dSDB, d_bc2))
            p5.add("vector", lambda e, pb=pb: e.tensor_scalar(
                hmb[pb][:], hmb[pb][:], mdcol[:, 0:1], None, ALU.mult), waits=w_, sem=sV)
            p5.add("vector", lambda e, pb=pb: e.tensor_copy(y5[:], hmb[pb][:]),
                   waits=[(sV, sV.n)], sem=sV)
            sv_c = sV.n
            p5.add("gpsimd", lambda e, pb=pb: e.tensor_sub(
                hmb[pb][:], hmb[pb][:], y5[:]), waits=[(sV, sv_c)], sem=sG)
            sg_d = sG.n
            p5.add("vector", lambda e, pb=pb: e.tensor_scalar(
                c5[:], hmb[pb][:], -0.5, None, ALU.is_lt), waits=[(sG, sg_d)], sem=sV)
            p5.add("vector", lambda e, pb=pb: e.scalar_tensor_tensor(
                hmb[pb][:], hmb[pb][:], 0.5, c5[:], ALU.is_gt, ALU.subtract),
                waits=[(sV, sV.n)], sem=sV)
            sv_f = sV.n
            p5.add("gpsimd", lambda e, pb=pb: e.tensor_add(
                c5[:], hmb[pb][:], y5[:]), waits=[(sV, sv_f)], sem=sG)
            sg_y = sG.n
            p5.add("vector", lambda e, ic=ic, pb=pb: e.tensor_scalar(
                hq[ic][:], c5[:], 127.0, -128.0, ALU.min, ALU.max),
                waits=[(sG, sg_y)], sem=sV)
            hq_marks.append((sV, sV.n))
        hq_all = (sV, sV.n)

        for jj in range(n_tt // 2):
            j = (n_tt // 2) * half + jj
            pb = jj % 2
            w_ = [m_outst[pb]] if pb in m_outst else []
            p5.add("sync", lambda e, j=j, pb=pb: e.dma_start(
                out=xpb[pb][:], in_=xp_hbm[128 * j:128 * (j + 1), :]),
                waits=w_, sem=dXPL[pb], inc=16)
            dxp = dXPL[pb].n
            for hc in range(2):
                w_ = [hq_all, (dQD, dQD_all)]
                if (pb, hc) in m_psd:
                    w_.append(m_psd[(pb, hc)])
                for ic in range(22):
                    p5.add("tensor", lambda e, ic=ic, jj=jj, hc=hc, pb=pb: e.matmul(
                        psD5[pb][:, 512 * hc:512 * (hc + 1)],
                        hq[ic][:, 128 * jj:128 * (jj + 1)],
                        qd_sb[ic][:, 512 * hc:512 * (hc + 1)],
                        start=(ic == 0), stop=(ic == 21)),
                        waits=w_ if ic == 0 else [], sem=sT)
            st_d = sT.n
            w_ = [(sT, st_d), (sV, sv_gp), (dXPL[pb], dxp)]
            if pb in m_outst:
                w_.append(m_outst[pb])
            p5.add("vector", lambda e, pb=pb: e.tensor_mul(
                outb[pb][:], psD5[pb][:], gb_tile[:]), waits=w_, sem=sV)
            m_psd[(pb, 0)] = (sV, sV.n)
            m_psd[(pb, 1)] = (sV, sV.n)
            p5.add("vector", lambda e, pb=pb: e.tensor_add(
                outb[pb][:], outb[pb][:], xpb[pb][:]), waits=[(sV, sV.n)], sem=sV)
            sv_o = sV.n
            p5.add("gpsimd", lambda e, j=j, pb=pb: e.dma_start(
                out=out_hbm[128 * j:128 * (j + 1), :], in_=outb[pb][:]),
                waits=[(sV, sv_o)], sem=dOUT[pb], inc=16)
            m_outst[pb] = (dOUT[pb], dOUT[pb].n)
        marks["p5_half0_done"] = (sT, sT.n)

    p5.add("gpsimd", lambda e: e.nop(),
           waits=[(dOUT[0], dOUT[0].n), (dOUT[1], dOUT[1].n)])
    p5.emit(nc)
    p5a.close()
    return env



from concourse.bass_utils import run_bass_kernel_spmd

_S = 8192
_Q = _S // 2


def _bandfix_tiles(qh):
    bf1 = np.ones((128, 16), np.float32)
    bf2 = np.ones((128, 16), np.float32)
    if qh == 1:
        for r in range(128):
            for u in range(9):
                d = u + 128 - r
                bf1[r, u] = np.exp(0.01 * max(0, 10 - d), dtype=np.float64).astype(np.float32)
                dg = r + 9 - u
                bf2[r, u] = np.exp(-0.01 * max(0, 10 - dg), dtype=np.float64).astype(np.float32)
    return bf1, bf2


_NC_CACHE = {}


def kernel(L0_attn_x, L0_attn_W, L0_attn_b, L0_ffn_clip_x, L0_ffn_clip_down,
           L0_ffn_w_gate, L0_ffn_w_up, L0_ffn_w_down, L0_c, L0_wmod):
    x = np.ascontiguousarray(np.asarray(L0_attn_x, dtype=np.float32))
    W = np.ascontiguousarray(np.asarray(L0_attn_W, dtype=np.float32))
    battn = np.ascontiguousarray(np.asarray(L0_attn_b, dtype=np.float32)).reshape(1, D)
    cx = np.ascontiguousarray(np.asarray(L0_ffn_clip_x, dtype=np.float32)).reshape(1, 1)
    cd = np.ascontiguousarray(np.asarray(L0_ffn_clip_down, dtype=np.float32)).reshape(1, 1)
    wg = np.ascontiguousarray(np.asarray(L0_ffn_w_gate, dtype=np.float32))
    wu = np.ascontiguousarray(np.asarray(L0_ffn_w_up, dtype=np.float32))
    wd = np.ascontiguousarray(np.asarray(L0_ffn_w_down, dtype=np.float32))
    cmat = np.ascontiguousarray(np.asarray(L0_c, dtype=np.float32))
    wmod = np.ascontiguousarray(np.asarray(L0_wmod, dtype=np.float32))

    if "nc" not in _NC_CACHE:
        _NC_CACHE["nc"] = build_nc(S=_S, dev=0, n_cores=8)
    nc = _NC_CACHE["nc"]

    in_maps = []
    for corei in range(8):
        b_, qh = corei // 2, corei % 2
        xb = np.ascontiguousarray(np.roll(x[b_], -_Q * qh, axis=0))
        bf1, bf2 = _bandfix_tiles(qh)
        in_maps.append(dict(
            x_full=xb, W=W, attn_b=battn, c_row=cmat[b_:b_ + 1],
            wmod=wmod, clip_x=cx, clip_down=cd,
            w_gate=wg, w_up=wu, w_down=wd,
            bandfix1=bf1, bandfix2=bf2))
    res = run_bass_kernel_spmd(nc, in_maps, core_ids=list(range(8)))
    out = np.empty((4, _S, D), np.float32)
    for corei in range(8):
        b_, qh = corei // 2, corei % 2
        out[b_, _Q * qh:_Q * (qh + 1)] = res.results[corei]["out"]
    return out


# version 11
# speedup vs baseline: 1.0417x; 1.0401x over previous
from contextlib import ExitStack

import numpy as np
import concourse.bass as bass
import concourse.mybir as mybir

F32 = mybir.dt.float32
BF16 = mybir.dt.bfloat16
I32 = mybir.dt.int32
ALU = mybir.AluOpType
AF = mybir.ActivationFunctionType
AX = mybir.AxisListType.X


class Sem:
    def __init__(self, nc, stack, name):
        self.h = stack.enter_context(nc.semaphore(name))
        self.n = 0

    def inc(self, k=1):
        self.n += k
        return self.n


class Item:
    __slots__ = ("eng", "emit", "waits", "inc_sem", "inc_amt")

    def __init__(self, eng, emit, waits, inc_sem, inc_amt):
        self.eng = eng
        self.emit = emit
        self.waits = waits
        self.inc_sem = inc_sem
        self.inc_amt = inc_amt


class Phase:
    ENGINES = ("sync", "scalar", "vector", "tensor", "gpsimd")

    def __init__(self, name):
        self.name = name
        self.items = []

    def add(self, eng, emit, waits=(), sem=None, inc=1):
        assert eng in Phase.ENGINES, eng
        waits = list(waits)
        if sem is not None and not waits:
            waits = [(sem, sem.n)]
        if sem is not None:
            sem.inc(inc)
        self.items.append(Item(eng, emit, waits, sem, inc))

    def emit(self, nc):
        with nc.Block() as blk:
            for eng_name in Phase.ENGINES:
                items = [it for it in self.items if it.eng == eng_name]
                if not items:
                    continue

                def body(e, items=items):
                    for it in items:
                        for (s, v) in it.waits:
                            e.wait_ge(s.h, v)
                    raise RuntimeError("unused")

                def body2(e, items=items):
                    for it in items:
                        for (s, v) in it.waits:
                            e.wait_ge(s.h, v)
                        inst = it.emit(e)
                        if it.inc_sem is not None:
                            inst.then_inc(it.inc_sem.h, it.inc_amt)

                getattr(blk, eng_name)(body2)


class Ctx:
    def __init__(self, nc, stack):
        self.nc = nc
        self.stack = stack
        self.sems = {}

    def sem(self, name):
        if name not in self.sems:
            self.sems[name] = Sem(self.nc, self.stack, name)
        return self.sems[name]

    def sbuf(self, name, shape, dtype):
        return self.stack.enter_context(self.nc.sbuf_tensor(name, shape, dtype))

    def psum(self, name, shape, dtype=F32):
        return self.stack.enter_context(self.nc.psum_tensor(name, shape, dtype))


"""adaln megakernel builder. Phases:
P0: constants, modulation (mod = silu(c)@wmod), b' = b + sa@W, transforms
P1: f = (LN(x)*(1+ca))@W + b' for all S tokens -> f_feat (SBUF) + f_tok (HBM)
P2: flash attention + residual + LN2 -> x', h2 spilled; h2 absmax
P3: weight quant+transpose (overlapped later), collective max #1
P4: gate/up + silu*up -> hmid spilled; absmax; collective max #2
P5: down + residual -> out
"""
import numpy as np
import concourse.bass as bass
import concourse.mybir as mybir

D = 1024
DI = 2816
NKC = 8
EPS = 1e-5


def build_nc(S=8192, dev=0, n_cores=8):
    nc = bass.Bass("TRN2", target_bir_lowering=False)
    Q = S // 2
    n_xt = S // 128
    n_qt = Q // 128
    stack = ExitStack()
    ctx = Ctx(nc, stack)
    sb, ps, sem = ctx.sbuf, ctx.psum, ctx.sem

    x_hbm = nc.dram_tensor("x_full", [S, D], F32, kind="ExternalInput").ap()
    W_hbm = nc.dram_tensor("W", [D, D], F32, kind="ExternalInput").ap()
    b_hbm = nc.dram_tensor("attn_b", [1, D], F32, kind="ExternalInput").ap()
    c_hbm = nc.dram_tensor("c_row", [1, D], F32, kind="ExternalInput").ap()
    wmod_hbm = nc.dram_tensor("wmod", [D, 6 * D], F32, kind="ExternalInput").ap()
    clipx_hbm = nc.dram_tensor("clip_x", [1, 1], F32, kind="ExternalInput").ap()
    clipd_hbm = nc.dram_tensor("clip_down", [1, 1], F32, kind="ExternalInput").ap()
    wg_hbm = nc.dram_tensor("w_gate", [DI, D], F32, kind="ExternalInput").ap()
    wu_hbm = nc.dram_tensor("w_up", [DI, D], F32, kind="ExternalInput").ap()
    wd_hbm = nc.dram_tensor("w_down", [D, DI], F32, kind="ExternalInput").ap()
    out_hbm = nc.dram_tensor("out", [Q, D], F32, kind="ExternalOutput").ap()

    ftok_hbm = nc.dram_tensor("ftok_hbm", [S, D], BF16).ap()
    mod_hbm = nc.dram_tensor("mod_hbm", [8, D], F32).ap()

    dev_f = dev_ftk = None
    if dev & 1:
        dev_f = nc.dram_tensor("dev_ffeat", [NKC * 128, S], BF16, kind="ExternalOutput").ap()

    ident_bf = sb("ident_bf", [128, 128], BF16)
    ident_f32 = sb("ident_f32", [128, 128], F32)
    band = sb("band", [128, 160], F32)
    amax1 = sb("amax1", [128, 1], F32)
    amax2 = sb("amax2", [128, 1], F32)
    one_bf = sb("one_bf", [1, 1], BF16)
    one_f32 = sb("one_f32", [1, 1], F32)
    clip_sb = sb("clip_sb", [1, 2], F32)
    scl_g = sb("scl_g", [128, 22], F32)
    scl_u = sb("scl_u", [128, 22], F32)
    scl_d = sb("scl_d", [128, 8], F32)
    mcols = sb("mcols", [128, 52], F32)
    ffstack = ExitStack()
    f_feat = [ffstack.enter_context(nc.sbuf_tensor(f"f_feat{k}", [128, S], BF16))
              for k in range(NKC)]

    dC = sem("dC")
    dW = [sem("dW0"), sem("dW1")]
    dWM = [sem("dWM0"), sem("dWM1"), sem("dWM2")]
    dBC = sem("dBC")
    dX = [sem("dX0"), sem("dX1")]
    dSF = [sem("dSF0"), sem("dSF1")]
    dMW = sem("dMW")
    sS = sem("sS")
    sV = sem("sV")
    sT = sem("sT")
    sG = sem("sG")

    p0 = Phase("p0p1")
    marks = {}

    p0a = ExitStack()
    p0b = ExitStack()
    p0early = ExitStack()
    p0sb = lambda n, sh, dt: p0b.enter_context(nc.sbuf_tensor(n, sh, dt))
    p0ps = lambda n, sh: p0early.enter_context(nc.psum_tensor(n, sh, F32))
    W_bf = p0a.enter_context(nc.sbuf_tensor("W_bf", [128, NKC * D], BF16))
    opca_feat = p0a.enter_context(nc.sbuf_tensor("opca_feat", [128, NKC], F32))
    bp_tile = p0a.enter_context(nc.sbuf_tensor("bp_tile", [128, D], F32))
    abuf = [p0a.enter_context(nc.sbuf_tensor(f"abuf{j}", [128, 704], F32))
            for j in range(2)]
    amx_c = p0a.enter_context(nc.sbuf_tensor("amx_c", [128, 4], F32))

    c_sb = p0sb("c_sb", [1, D], F32)
    b_sb = p0sb("b_sb", [1, D], F32)
    sgfeat = p0sb("sgfeat", [128, NKC], F32)
    iota_t = p0sb("iota_t", [128, 160], F32)
    wmbuf = [p0sb(f"wmbuf{j}", [128, D], F32) for j in range(3)]
    wbuf = wmbuf
    mod_acc = p0sb("mod_acc", [1, 6 * D], F32)
    t1r = p0sb("t1r", [1, D], F32)
    sa_bf = p0sb("sa_bf", [1, D], BF16)
    safeat = p0sb("safeat", [128, NKC], BF16)

    psC = p0ps("psC", [128, NKC])
    psM = [p0ps(f"psM{j}", [1, 512]) for j in range(2)]
    psR = [p0ps(f"psR{j}", [1, 512]) for j in range(2)]

    p0.add("sync", lambda e: e.dma_start(out=c_sb[:], in_=c_hbm), sem=dC, inc=16)
    p0.add("sync", lambda e: e.dma_start(out=b_sb[:], in_=b_hbm), sem=dC, inc=16)
    p0.add("sync", lambda e: e.dma_start(out=clip_sb[:, 0:1], in_=clipx_hbm), sem=dC, inc=16)
    p0.add("sync", lambda e: e.dma_start(out=clip_sb[:, 1:2], in_=clipd_hbm), sem=dC, inc=16)
    dC_all = dC.n

    p0.add("gpsimd", lambda e: e.memset(ident_bf[:], 0.0), sem=sG)
    p0.add("gpsimd", lambda e: e.affine_select(
        out=ident_bf[:], in_=ident_bf[:], compare_op=ALU.not_equal,
        fill=1.0, base=0, pattern=[[-1, 128]], channel_multiplier=1),
        waits=[(sG, sG.n)], sem=sG)
    g_identbf = sG.n
    p0.add("gpsimd", lambda e: e.memset(ident_f32[:], 0.0), sem=sG)
    p0.add("gpsimd", lambda e: e.affine_select(
        out=ident_f32[:], in_=ident_f32[:], compare_op=ALU.not_equal,
        fill=1.0, base=0, pattern=[[-1, 128]], channel_multiplier=1),
        waits=[(sG, sG.n)], sem=sG)
    p0.add("gpsimd", lambda e: e.memset(one_bf[:], 1.0), sem=sG)
    g_onebf = sG.n
    p0.add("gpsimd", lambda e: e.memset(one_f32[:], 1.0), sem=sG)
    g_onef32 = sG.n
    p0.add("gpsimd", lambda e: e.iota(
        iota_t[:], [[1, 160]], base=-9, channel_multiplier=-1,
        allow_small_or_imprecise_dtypes=True), sem=sG)
    g_iota = sG.n
    p0.add("scalar", lambda e: e.activation(band[:], iota_t[:], AF.Abs),
           waits=[(sG, g_iota)], sem=sS)
    ss_b = sS.n
    p0.add("vector", lambda e: e.tensor_scalar(
        band[:], band[:], -1.0, 10.0, ALU.mult, ALU.add), waits=[(sS, ss_b)], sem=sV)
    p0.add("vector", lambda e: e.tensor_scalar(
        band[:], band[:], 0.0, -0.01, ALU.max, ALU.mult), waits=[(sV, sV.n)], sem=sV)

    p0.add("scalar", lambda e: e.activation(c_sb[:], c_sb[:], AF.Silu),
           waits=[(dC, dC_all)], sem=sS)
    ss_siluc = sS.n
    for k in range(NKC):
        p0.add("tensor", lambda e, k=k: e.matmul(
            psC[:, k:k + 1], c_sb[0:1, 128 * k:128 * (k + 1)], one_f32[0:1, 0:1],
            start=True, stop=True),
            waits=[(sS, ss_siluc), (sG, g_onef32)] if k == 0 else [], sem=sT)
    st_sgf = sT.n
    p0.add("scalar", lambda e: e.copy(sgfeat[:], psC[:]), waits=[(sT, st_sgf)], sem=sS)
    ss_sgf = sS.n

    sv_wcast = []
    for k in range(NKC):
        w_ = [(sV, sv_wcast[k - 2])] if k >= 2 else []
        p0.add("sync", lambda e, k=k: e.dma_start(
            out=wbuf[k % 2][:, 0:D], in_=W_hbm[128 * k:128 * (k + 1), :]),
            waits=w_, sem=dW[k % 2], inc=16)
        dval = dW[k % 2].n
        p0.add("vector", lambda e, k=k: e.tensor_copy(
            W_bf[:, D * k:D * (k + 1)], wbuf[k % 2][:, 0:D]),
            waits=[(dW[k % 2], dval)], sem=sV)
        sv_wcast.append(sV.n)
    sv_Wbf = sV.n

    wmm_cons = {}
    addmark = {}
    nsub = 6
    subs = [(k, c4) for k in range(NKC) for c4 in range(nsub)]
    for idx, (k, c4) in enumerate(subs):
        pbuf = idx % 3
        if idx >= 3:
            w_ = [wmm_cons[idx - 3]]
        elif idx < 2:
            w_ = [(sV, sv_wcast[6 + idx])]
        else:
            w_ = []
        p0.add("sync", lambda e, k=k, c4=c4, pbuf=pbuf: e.dma_start(
            out=wmbuf[pbuf][:], in_=wmod_hbm[128 * k:128 * (k + 1),
                                             D * c4:D * (c4 + 1)]),
            waits=w_, sem=dWM[pbuf], inc=16)
        dval = dWM[pbuf].n
        for h in range(2):
            s_ = 2 * c4 + h
            bank = s_ % 2
            w_mm = [(dWM[pbuf], dval), (sS, ss_sgf)]
            if bank in addmark:
                w_mm.append(addmark[bank])
            p0.add("tensor", lambda e, k=k, h=h, pbuf=pbuf, bank=bank: e.matmul(
                psM[bank][:], sgfeat[:, k:k + 1], wmbuf[pbuf][0:128, 512 * h:512 * (h + 1)],
                start=True, stop=True), waits=w_mm, sem=sT)
            mmv = sT.n
            wmm_cons[idx] = (sT, mmv)
            eng = "vector"
            esem = sV
            if k == 0:
                p0.add(eng, lambda e, s_=s_, bank=bank: e.tensor_copy(
                    mod_acc[0:1, 512 * s_:512 * (s_ + 1)], psM[bank][0:1, :]),
                    waits=[(sT, mmv)], sem=esem)
            else:
                p0.add(eng, lambda e, s_=s_, bank=bank: e.tensor_add(
                    mod_acc[0:1, 512 * s_:512 * (s_ + 1)],
                    mod_acc[0:1, 512 * s_:512 * (s_ + 1)], psM[bank][0:1, :]),
                    waits=[(sT, mmv)], sem=esem)
            addmark[bank] = (esem, esem.n)
    sv_mod = sV.n
    sg_mod = sG.n

    def msl(i):
        return mod_acc[0:1, D * i:D * (i + 1)]
    mod_ready = [(sV, sv_mod), (sG, sg_mod)]

    p0.add("vector", lambda e: e.tensor_copy(sa_bf[:], msl(0)), waits=mod_ready, sem=sV)
    sv_sabf = sV.n
    for k in range(NKC):
        p0.add("tensor", lambda e, k=k: e.matmul(
            psC[:, k:k + 1], sa_bf[0:1, 128 * k:128 * (k + 1)], one_bf[0:1, 0:1],
            start=True, stop=True),
            waits=[(sV, sv_sabf), (sS, ss_sgf)] if k == 0 else [], sem=sT)
    st_saf = sT.n
    p0.add("scalar", lambda e: e.copy(safeat[:], psC[:]), waits=[(sT, st_saf)], sem=sS)
    ss_saf = sS.n

    p0.add("vector", lambda e: e.tensor_scalar(msl(1), msl(1), 1.0, None, ALU.add),
           waits=mod_ready, sem=sV)
    sv_opca = sV.n
    for k in range(NKC):
        p0.add("tensor", lambda e, k=k: e.matmul(
            psC[:, k:k + 1], mod_acc[0:1, D + 128 * k:D + 128 * (k + 1)], one_f32[0:1, 0:1],
            start=True, stop=True),
            waits=[(sV, sv_opca), (sG, g_onef32), (sS, ss_saf)] if k == 0 else [], sem=sT)
    st_opca = sT.n
    p0.add("scalar", lambda e: e.copy(opca_feat[:], psC[:]), waits=[(sT, st_opca)], sem=sS)
    ss_opca = sS.n

    p0.add("vector", lambda e: e.tensor_scalar(msl(4), msl(4), 1.0, None, ALU.add),
           waits=mod_ready, sem=sV)
    for idx in (2, 5):
        p0.add("vector", lambda e, idx=idx: e.tensor_scalar(
            t1r[:], msl(idx), 0.0, None, ALU.is_ge),
            waits=mod_ready + [(sV, sV.n), (sS, sS.n)], sem=sV)
        p0.add("vector", lambda e: e.tensor_scalar(
            t1r[:], t1r[:], 2.0, -1.0, ALU.mult, ALU.add), waits=[(sV, sV.n)], sem=sV)
        p0.add("scalar", lambda e, idx=idx: e.activation(mod_acc[0:1, 0:D], msl(idx), AF.Abs),
               waits=mod_ready + [(sS, sS.n), (sV, sV.n)], sem=sS)
        ss_abs = sS.n
        p0.add("vector", lambda e: e.tensor_scalar(
            mod_acc[0:1, 0:D], mod_acc[0:1, 0:D], 0.25, None, ALU.max), waits=[(sS, ss_abs)], sem=sV)
        p0.add("vector", lambda e, idx=idx: e.tensor_mul(msl(idx), t1r[:], mod_acc[0:1, 0:D]),
               waits=[(sV, sV.n)], sem=sV)
    sv_gagb = sV.n

    for s_ in range(2):
        for k in range(NKC):
            w_ = [(sS, ss_saf), (sV, sv_Wbf)] if (s_ == 0 and k == 0) else []
            p0.add("tensor", lambda e, s_=s_, k=k: e.matmul(
                psR[s_][:], safeat[:, k:k + 1],
                W_bf[0:128, D * k + 512 * s_:D * k + 512 * s_ + 512],
                start=(k == 0), stop=(k == NKC - 1)), waits=w_, sem=sT)
        mmv = sT.n
        p0.add("vector", lambda e, s_=s_: e.tensor_add(
            t1r[0:1, 512 * s_:512 * (s_ + 1)], psR[s_][0:1, :],
            b_sb[0:1, 512 * s_:512 * (s_ + 1)]),
            waits=[(sT, mmv), (dC, dC_all), (sV, sv_gagb)], sem=sV)
    sv_bp = sV.n

    for src, slot, mark in ((msl(2), 0, sv_gagb), (msl(4), 1, sv_gagb),
                            (msl(3), 2, sv_gagb), (msl(5), 3, sv_gagb),
                            (t1r[:], 4, sv_bp)):
        p0.add("gpsimd", lambda e, src=src, slot=slot: e.dma_start(
            out=mod_hbm[slot:slot + 1, :], in_=src), waits=[(sV, mark)], sem=dMW, inc=16)
    dMW_all = dMW.n
    dMW_all0 = dMW_all

    p0.add("sync", lambda e: e.dma_start(
        out=bp_tile[:], in_=mod_hbm[4:5, :].broadcast_to((128, D))),
        waits=[(dMW, dMW_all)], sem=dBC, inc=16)
    dBC_bp = dBC.n

    p0b.close()
    p1sb = lambda n, sh, dt: p0a.enter_context(nc.sbuf_tensor(n, sh, dt))
    xb = [p1sb(f"xb{j}", [128, D], F32) for j in range(2)]
    h_bf = [p1sb(f"hbf{j}", [128, D], BF16) for j in range(2)]
    hT_bf = [p1sb(f"hTbf{j}", [128, D], BF16) for j in range(2)]
    ftk = [p1sb(f"ftk{j}", [128, D], BF16) for j in range(2)]
    stats = [p1sb(f"stats{j}", [128, 16], F32) for j in range(2)]
    p0early.close()
    p1ps = lambda n, sh: p0a.enter_context(nc.psum_tensor(n, sh, F32))
    ps_tr = [p1ps(f"ps_tr{j}", [128, 128]) for j in range(2)]
    ps_f = [p1ps(f"ps_f{j}", [128, D]) for j in range(2)]

    amax_done = {}
    WK1 = {"g": (wg_hbm, scl_g, 0, D), "u": (wu_hbm, scl_u, 22, D),
           "d": (wd_hbm, scl_d, 44, DI)}
    amax_jobs1 = ([("g", t) for t in range(22)] + [("u", t) for t in range(22)]
                  + [("d", t) for t in range(8)])
    dAM = [sem("dAM0"), sem("dAM1")]
    am_state = {"i": 0, "pend": []}

    def _amax_issue(ph):
        if am_state["i"] >= len(amax_jobs1):
            return
        j = am_state["i"]
        am_state["i"] += 1
        kind, t = amax_jobs1[j]
        srcw, scl, mbase, Wd = WK1[kind]
        ab = j % 2
        w_ = [am_state[("free", ab)]] if ("free", ab) in am_state else []
        wdt = min(704, Wd)
        ph.add("sync", lambda e, srcw=srcw, t=t, wdt=wdt, ab=ab: e.dma_start(
            out=abuf[ab][:, 0:wdt], in_=srcw[128 * t:128 * (t + 1), 0:wdt]),
            waits=w_, sem=dAM[ab], inc=16)
        am_state["pend"].append((j, kind, t, ab, dAM[ab].n))

    def _amax_reduce(ph):
        if not am_state["pend"]:
            return
        j, kind, t, ab, dv = am_state["pend"].pop(0)
        srcw, scl, mbase, Wd = WK1[kind]
        ph.add("vector", lambda e, ab=ab, Wd=Wd: e.tensor_reduce(
            amx_c[:, 0:1], abuf[ab][:, 0:min(704, Wd)], AX, ALU.max,
            apply_absolute_value=True), waits=[(dAM[ab], dv), (sV, sV.n)], sem=sV)
        for c0 in range(704, Wd, 704):
            wdt = min(704, Wd - c0)
            ph.add("sync", lambda e, srcw=srcw, t=t, c0=c0, wdt=wdt, ab=ab: e.dma_start(
                out=abuf[ab][:, 0:wdt], in_=srcw[128 * t:128 * (t + 1), c0:c0 + wdt]),
                waits=[(sV, sV.n)], sem=dAM[ab], inc=16)
            dv2 = dAM[ab].n
            ph.add("vector", lambda e, ab=ab, wdt=wdt: e.tensor_reduce(
                amx_c[:, 1:2], abuf[ab][:, 0:wdt], AX, ALU.max,
                apply_absolute_value=True), waits=[(dAM[ab], dv2), (sV, sV.n)], sem=sV)
            ph.add("vector", lambda e: e.tensor_max(
                amx_c[:, 0:1], amx_c[:, 0:1], amx_c[:, 1:2]), waits=[(sV, sV.n)], sem=sV)
        am_state[("free", ab)] = (sV, sV.n)
        ph.add("vector", lambda e, scl=scl, t=t: e.tensor_scalar(
            scl[:, t:t + 1], amx_c[:, 0:1], 0.007874015748031496, None, ALU.mult),
            waits=[(sV, sV.n)], sem=sV)
        mi = mbase + t
        ph.add("vector", lambda e, mi=mi, scl=scl, t=t: e.reciprocal(
            mcols[:, mi:mi + 1], scl[:, t:t + 1]), waits=[(sV, sV.n)], sem=sV)
        amax_done[(kind, t)] = (sV, sV.n)

    def amax_step(ph):
        if len(am_state["pend"]) >= 2:
            _amax_reduce(ph)
        _amax_issue(ph)

    m_xfree = [None] * (n_xt + 4)
    m_trfree = [0] * (n_xt + 4)
    m_psffree = [0] * (n_xt + 4)
    m_hTdone = [0] * (n_xt + 4)
    m_ftokst = [None] * (n_xt + 4)

    for i in range(n_xt):
        amax_step(p0)
        pb = i % 2
        if i >= 2:
            w_ = [m_xfree[i - 2]]
        elif i == 0:
            w_ = [(dMW, dMW_all0), (sV, sv_bp), (sS, ss_opca), (sG, sg_mod),
                  (sT, sT.n)]
        else:
            w_ = []
        p0.add("sync", lambda e, i=i, pb=pb: e.dma_start(
            out=xb[pb][:], in_=x_hbm[128 * i:128 * (i + 1), :]),
            waits=w_, sem=dX[pb], inc=16)
        dxv = dX[pb].n

        st = stats[pb]
        w_ = [(dX[pb], dxv)]
        if i >= 2:
            w_.append(m_xfree[i - 2])
        p0.add("vector", lambda e, pb=pb, st=st: e.bn_stats(st[:, 0:6], xb[pb][:, 0:512]),
               waits=w_, sem=sV)
        p0.add("vector", lambda e, pb=pb, st=st: e.bn_stats(st[:, 6:12], xb[pb][:, 512:1024]),
               waits=[(sV, sV.n)], sem=sV)
        p0.add("vector", lambda e, st=st: e.bn_aggr(st[:, 12:14], st[:, 0:12]),
               waits=[(sV, sV.n)], sem=sV)
        p0.add("vector", lambda e, st=st: e.tensor_scalar(
            st[:, 14:15], st[:, 13:14], EPS, None, ALU.add), waits=[(sV, sV.n)], sem=sV)
        p0.add("vector", lambda e, st=st: e.reciprocal(st[:, 15:16], st[:, 14:15]),
               waits=[(sV, sV.n)], sem=sV)
        sv_rec = sV.n
        p0.add("scalar", lambda e, st=st: e.activation(st[:, 0:1], st[:, 15:16], AF.Sqrt),
               waits=[(sV, sv_rec)], sem=sS)
        ss_sq = sS.n
        p0.add("vector", lambda e, st=st: e.tensor_mul(st[:, 1:2], st[:, 0:1], st[:, 0:1]),
               waits=[(sS, ss_sq)], sem=sV)
        p0.add("vector", lambda e, st=st: e.tensor_mul(st[:, 2:3], st[:, 1:2], st[:, 14:15]),
               waits=[(sV, sV.n)], sem=sV)
        p0.add("vector", lambda e, st=st: e.tensor_scalar(
            st[:, 3:4], st[:, 2:3], -0.5, 1.5, ALU.mult, ALU.add), waits=[(sV, sV.n)], sem=sV)
        p0.add("vector", lambda e, st=st: e.tensor_mul(st[:, 4:5], st[:, 0:1], st[:, 3:4]),
               waits=[(sV, sV.n)], sem=sV)
        p0.add("vector", lambda e, st=st: e.tensor_mul(st[:, 5:6], st[:, 12:13], st[:, 4:5]),
               waits=[(sV, sV.n)], sem=sV)
        p0.add("vector", lambda e, st=st: e.tensor_scalar(
            st[:, 6:7], st[:, 5:6], -1.0, None, ALU.mult), waits=[(sV, sV.n)], sem=sV)
        sv_ln = sV.n

        w_ = [(sV, sv_ln)]
        if i >= 2:
            w_.append((sT, m_hTdone[i - 2]))
        p0.add("vector", lambda e, pb=pb, st=st: e.tensor_scalar(
            h_bf[pb][:], xb[pb][:], st[:, 4:5], st[:, 6:7], ALU.mult, ALU.add),
            waits=w_, sem=sV)
        m_xfree[i] = (sV, sV.n)
        ss_h = sV.n

        ss_evac = []
        for j in range(NKC):
            w_ = [(sV, ss_h)]
            if j < 2:
                w_.append((sG, g_identbf))
                if i > 0:
                    w_.append((sS, m_trfree[i - 1]))
                else:
                    w_.append((sS, ss_opca))
                    w_.append((sV, sv_bp))
                    w_.append((sG, sg_mod))
            else:
                w_.append((sS, ss_evac[j - 2]))
            p0.add("tensor", lambda e, pb=pb, j=j: e.matmul(
                ps_tr[j % 2][:], h_bf[pb][:, 128 * j:128 * (j + 1)], ident_bf[:],
                start=True, stop=True), waits=w_, sem=sT)
            mmv = sT.n
            p0.add("scalar", lambda e, pb=pb, j=j: e.activation(
                hT_bf[pb][:, 128 * j:128 * (j + 1)], ps_tr[j % 2][:], AF.Copy,
                scale=opca_feat[:, j:j + 1]),
                waits=[(sT, mmv)], sem=sS)
            ss_evac.append(sS.n)
        ss_hT = sS.n

        for s_ in range(2):
            for k in range(NKC):
                w_ = []
                if s_ == 0 and k == 0:
                    w_ = [(sS, ss_hT), (sV, sv_Wbf)]
                    if i >= 2:
                        w_.append((sV, m_psffree[i - 2]))
                p0.add("tensor", lambda e, pb=pb, s_=s_, k=k: e.matmul(
                    ps_f[pb][:, 512 * s_:512 * (s_ + 1)],
                    hT_bf[pb][:, 128 * k:128 * (k + 1)],
                    W_bf[0:128, D * k + 512 * s_:D * k + 512 * s_ + 512],
                    start=(k == 0), stop=(k == NKC - 1)), waits=w_, sem=sT)
        m_hTdone[i] = sT.n
        st_f = sT.n

        for s_ in range(2):
            w_ = [(sT, st_f), (dBC, dBC_bp)]
            if i >= 2:
                w_.append(m_ftokst[i - 2])
                w_.append((sT, m_hTdone[i - 1]))
            p0.add("vector", lambda e, pb=pb, s_=s_: e.tensor_add(
                ftk[pb][:, 512 * s_:512 * (s_ + 1)], ps_f[pb][:, 512 * s_:512 * (s_ + 1)],
                bp_tile[:, 512 * s_:512 * (s_ + 1)]),
                waits=w_ if s_ == 0 else [], sem=sV)
        m_psffree[i] = sV.n
        sv_ftk = sV.n

        p0.add("gpsimd", lambda e, pb=pb, i=i: e.dma_start(
            out=ftok_hbm[128 * i:128 * (i + 1), :], in_=ftk[pb][:]),
            waits=[(sV, sv_ftk)], sem=dSF[pb], inc=16)
        m_ftokst[i] = (dSF[pb], dSF[pb].n)

        ss_evac2 = []
        for j in range(NKC):
            w_ = [(sV, sv_ftk)]
            if j < 2:
                w_.append((sS, ss_hT))
            else:
                w_.append((sS, ss_evac2[j - 2]))
            p0.add("tensor", lambda e, pb=pb, j=j: e.matmul(
                ps_tr[j % 2][:], ftk[pb][:, 128 * j:128 * (j + 1)], ident_bf[:],
                start=True, stop=True), waits=w_, sem=sT)
            mmv = sT.n
            p0.add("scalar", lambda e, j=j, i=i: e.copy(
                f_feat[j][:, 128 * i:128 * (i + 1)], ps_tr[j % 2][:]),
                waits=[(sT, mmv)], sem=sS)
            ss_evac2.append(sS.n)
        m_trfree[i] = sS.n
        m_ftrdone = sT.n
        m_hTdone[i] = max(m_hTdone[i], m_ftrdone)

    while am_state["pend"] or am_state["i"] < len(amax_jobs1):
        _amax_reduce(p0)
        _amax_issue(p0)

    marks["p1_sS"] = sS.n
    marks["p1_sV"] = sV.n
    marks["p1_sT"] = sT.n

    if dev & 1:
        dDV = sem("dDV")
        for j in range(NKC):
            p0.add("gpsimd", lambda e, j=j: e.dma_start(
                out=dev_f[128 * j:128 * (j + 1), :], in_=f_feat[j][:]),
                waits=[(sS, marks["p1_sS"])], sem=dDV, inc=16)
        p0.add("gpsimd", lambda e: e.nop(), waits=[(dDV, dDV.n)])

    p0.add("scalar", lambda e: e.activation(band[:], band[:], AF.Exp),
           waits=[(sV, sV.n)], sem=sS)
    p0.add("gpsimd", lambda e: e.memset(amax1[:], 0.0), sem=sG)
    p0.add("gpsimd", lambda e: e.memset(amax2[:], 0.0), sem=sG)

    for eng, s_ in (("sync", None), ("scalar", sS), ("vector", sV),
                    ("tensor", sT), ("gpsimd", sG)):
        if s_ is not None:
            p0.add(eng, lambda e: e.nop(), sem=s_)
    marks["b1"] = {"sS": sS.n, "sV": sV.n, "sT": sT.n, "sG": sG.n}
    marks["dMW_all"] = dMW_all
    marks["dSF_final"] = [(dSF[0], dSF[0].n), (dSF[1], dSF[1].n)]

    p0.emit(nc)
    p0a.close()

    env = build_p2(nc, ctx, locals())
    env = build_p4(nc, ctx, env)
    env = build_p5(nc, ctx, env)
    stack.close()
    return nc


def build_p2(nc, ctx, env):
    S = env["S"]; Q = env["Q"]; n_qt = env["n_qt"]; dev = env["dev"]
    f_feat = env["f_feat"]; ident_bf = env["ident_bf"]; band = env["band"]
    amax1 = env["amax1"]
    x_hbm = env["x_hbm"]; ftok_hbm = env["ftok_hbm"]; mod_hbm = env["mod_hbm"]
    wg_hbm = env["wg_hbm"]; wu_hbm = env["wu_hbm"]; wd_hbm = env["wd_hbm"]
    sS = env["sS"]; sV = env["sV"]; sT = env["sT"]; sG = env["sG"]
    sem = ctx.sem
    marks = env["marks"]
    b1 = marks["b1"]
    PAD = 16
    n_tc = S // 512
    n_sc = S // 128
    n_pairs = n_qt // 2

    xp_hbm = nc.dram_tensor("xp_hbm", [Q, D], F32).ap()
    h2_hbm = nc.dram_tensor("h2_hbm", [Q, D], F32).ap()
    qg_hbm = nc.dram_tensor("qg_hbm", [22, 128, D], BF16).ap()
    qu_hbm = nc.dram_tensor("qu_hbm", [22, 128, D], BF16).ap()
    qd_hbm = nc.dram_tensor("qd_hbm", [22, 128, D], BF16).ap()
    bf1_hbm = nc.dram_tensor("bandfix1", [128, 16], F32, kind="ExternalInput").ap()
    bf2_hbm = nc.dram_tensor("bandfix2", [128, 16], F32, kind="ExternalInput").ap()

    scl_g = env["scl_g"]; scl_u = env["scl_u"]; scl_d = env["scl_d"]

    p2a = ExitStack()
    sb2 = lambda n, sh, dt: p2a.enter_context(nc.sbuf_tensor(n, sh, dt))
    ps2 = lambda n, sh: p2a.enter_context(nc.psum_tensor(n, sh, F32))

    p_sb = [sb2(f"p_sb{q}", [128, PAD + S], BF16) for q in range(2)]
    NMC = [sb2(f"NMC{q}", [128, n_tc], F32) for q in range(2)]
    Erow = [sb2(f"Erow{q}", [128, n_tc], F32) for q in range(2)]
    rsrow = [sb2(f"rs{q}", [128, n_tc], F32) for q in range(2)]
    smallq = [sb2(f"smallq{q}", [128, 8], F32) for q in range(2)]
    D_sb = [[sb2(f"Dsb{q}{j}", [128, 128], BF16) for j in range(2)] for q in range(2)]
    pT_bf = [[sb2(f"pTbf{q}{j}", [128, 128], BF16) for j in range(4)] for q in range(2)]
    fstr = [sb2(f"fstr{j}", [128, D], BF16) for j in range(2)]
    fstr2 = [sb2(f"fstr2{j}", [128, D], BF16) for j in range(2)]
    ga_tile = sb2("ga_tile", [128, D], BF16)
    onepcb_tile = sb2("onepcb_tile", [128, D], F32)
    sbb_tile = sb2("sbb_tile", [128, D], BF16)
    bfix1 = sb2("bfix1", [128, 16], F32)
    bfix2 = sb2("bfix2", [128, 16], F32)
    xe = sb2("xe", [128, D], F32)
    _xp1 = sb2("xp_sb", [128, D], F32)
    _h21 = sb2("h2_sb", [128, D], F32)
    xp_sb = [_xp1, _xp1]
    h2_sb = [_h21, _h21]
    st2 = [sb2(f"st2{q}", [128, 16], F32) for q in range(2)]
    wv = [sb2(f"wv{j}", [128, 512], F32) for j in range(2)]
    yq = [sb2(f"yq{j}", [128, 512], I32) for j in range(2)]
    NSTG = 3
    stg = [sb2(f"stg{j}", [128, 512], BF16) for j in range(NSTG)]
    mcols = env["mcols"]

    e_ps = [ps2(f"e_ps{j}", [128, 512]) for j in range(4)]
    pT_ps = [e_ps[0], e_ps[1]]
    o_ps = [ps2(f"o_ps{q}", [128, D]) for q in range(2)]

    p2 = Phase("p2")
    dBC2 = sem("dBC2")
    dFS = [sem("dFS0"), sem("dFS1"), sem("dFS2")]
    dXE = sem("dXE")
    dXP = [sem("dXP0"), sem("dXP1")]
    dXH = [sem("dXH0"), sem("dXH1")]
    dPW = [sem("dPW0"), sem("dPW1")]
    dQSs = [sem(f"dQSs{i}") for i in range(NSTG)]

    bar = [(sS, b1["sS"]), (sV, b1["sV"]), (sT, b1["sT"]), (sG, b1["sG"])]
    p2.add("gpsimd", lambda e: e.nop(), waits=bar + marks["dSF_final"])
    for t_ in (smallq[0], smallq[1], st2[0], st2[1], NMC[0], NMC[1],
               Erow[0], Erow[1]):
        p2.add("gpsimd", lambda e, t_=t_: e.memset(t_[:], 0.0), sem=sG)
    g_init = sG.n
    bar2 = bar + [(sG, g_init)]
    for eng in ("scalar", "vector", "tensor"):
        p2.add(eng, lambda e: e.nop(), waits=bar2)
    p2.add("sync", lambda e: e.nop(), waits=bar2 + marks["dSF_final"])

    dMW_all = marks["dMW_all"]
    dMW = env["dMW"]
    p2.add("sync", lambda e: e.dma_start(
        out=onepcb_tile[:], in_=mod_hbm[1:2, :].broadcast_to((128, D))),
        waits=[(dMW, dMW_all)], sem=dBC2, inc=16)
    p2.add("sync", lambda e: e.dma_start(out=bfix1[:], in_=bf1_hbm), sem=dBC2, inc=16)
    p2.add("sync", lambda e: e.dma_start(out=bfix2[:], in_=bf2_hbm), sem=dBC2, inc=16)
    dBC2g = sem("dBC2g")
    dBC2g2 = sem("dBC2g2")
    p2.add("gpsimd", lambda e: e.dma_start(
        out=sbb_tile[:], in_=mod_hbm[2:3, :].broadcast_to((128, D))),
        waits=[(dMW, dMW_all)], sem=dBC2g, inc=16)
    p2.add("gpsimd", lambda e: e.dma_start(
        out=ga_tile[:], in_=mod_hbm[0:1, :].broadcast_to((128, D))),
        waits=[(dMW, dMW_all)], sem=dBC2g2, inc=16)
    p2.add("sync", lambda e: e.nop(),
           waits=[(dBC2g, dBC2g.n), (dBC2g2, dBC2g2.n)], sem=dBC2, inc=1)
    dBC2_all = dBC2.n

    WKIND = {"g": (wg_hbm, qg_hbm, scl_g, 0, D), "u": (wu_hbm, qu_hbm, scl_u, 22, D),
             "d": (wd_hbm, qd_hbm, scl_d, 44, DI)}
    amax_jobs = [("g", t) for t in range(22)] + [("u", t) for t in range(22)] + \
                [("d", t) for t in range(8)]
    strip_jobs = []
    for kind, t in amax_jobs:
        Wd = WKIND[kind][4]
        for c0 in range(0, Wd, 512):
            strip_jobs.append((kind, t, c0, min(512, Wd - c0)))
    amax_done = env["amax_done"]
    aj = [len(amax_jobs)]
    sq_i = [0]
    free_slots = list(range(NSTG))
    pending = []
    quant_gdone = {}

    def prep_quant(budget):
        for _ in range(budget):
            sj = sq_i[0]
            if sj >= len(strip_jobs) or not free_slots:
                return
            kind, t, c0, wdt = strip_jobs[sj]
            if (kind, t) not in amax_done:
                return
            sq_i[0] += 1
            src, dst, scl, mbase, Wd = WKIND[kind]
            slot = free_slots.pop(0)
            bslot = sj % 2
            mi = mbase + t
            w_ = []
            if (sj - 2) in quant_gdone:
                w_.append(quant_gdone[sj - 2])
            if (kind, t) not in amax_done:
                pass
            p2.add("sync", lambda e, src=src, t=t, c0=c0, wdt=wdt, bslot=bslot: e.dma_start(
                out=wv[bslot][:, 0:wdt], in_=src[128 * t:128 * (t + 1), c0:c0 + wdt]),
                waits=w_, sem=dPW[bslot], inc=16)
            dv = dPW[bslot].n
            p2.add("vector", lambda e, bslot=bslot, mi=mi, wdt=wdt: e.tensor_scalar(
                wv[bslot][:, 0:wdt], wv[bslot][:, 0:wdt], mcols[:, mi:mi + 1], None,
                ALU.mult), waits=[(dPW[bslot], dv), amax_done[(kind, t)], (sV, sV.n)], sem=sV)
            p2.add("vector", lambda e, bslot=bslot, wdt=wdt: e.tensor_copy(
                yq[bslot][:, 0:wdt], wv[bslot][:, 0:wdt]), waits=[(sV, sV.n)], sem=sV)
            p2.add("vector", lambda e, bslot=bslot, wdt=wdt: e.tensor_sub(
                wv[bslot][:, 0:wdt], wv[bslot][:, 0:wdt], yq[bslot][:, 0:wdt]),
                waits=[(sV, sV.n)], sem=sV)
            w_c2 = [(sV, sV.n), marks.get(("stgfree", slot), (sV, 0))]
            p2.add("vector", lambda e, bslot=bslot, slot=slot, wdt=wdt: e.tensor_scalar(
                stg[slot][:, 0:wdt], wv[bslot][:, 0:wdt], -0.5, None, ALU.is_lt),
                waits=w_c2, sem=sV)
            p2.add("vector", lambda e, bslot=bslot, slot=slot, wdt=wdt: e.scalar_tensor_tensor(
                wv[bslot][:, 0:wdt], wv[bslot][:, 0:wdt], 0.5, stg[slot][:, 0:wdt],
                ALU.is_gt, ALU.subtract), waits=[(sV, sV.n)], sem=sV)
            p2.add("vector", lambda e, bslot=bslot, slot=slot, wdt=wdt: e.tensor_add(
                stg[slot][:, 0:wdt], wv[bslot][:, 0:wdt], yq[bslot][:, 0:wdt]),
                waits=[(sV, sV.n)], sem=sV)
            quant_gdone[sj] = (sV, sV.n)
            pending.append((sj, slot, (sV, sV.n)))

    def prep_transpose(budget, pt_gate):
        last_evac = None
        for _ in range(budget):
            if not pending:
                break
            sj, slot, qmark = pending.pop(0)
            kind, t, c0, wdt = strip_jobs[sj]
            src, dst, scl, mbase, Wd = WKIND[kind]
            nch = wdt // 128
            ev = []
            for j in range(nch):
                w_ = [qmark]
                if pt_gate is not None and len(ev) == 0:
                    w_.append(pt_gate)
                bm = marks.get(("obank", j % 2))
                if bm is not None:
                    w_.append(bm)
                p2.add("tensor", lambda e, slot=slot, j=j: e.matmul(
                    o_ps[j % 2][:, 0:128], stg[slot][:, 128 * j:128 * (j + 1)],
                    ident_bf[:], start=True, stop=True), waits=w_, sem=sT)
                mmv = sT.n
                p2.add("scalar", lambda e, slot=slot, j=j: e.copy(
                    stg[slot][:, 128 * j:128 * (j + 1)], o_ps[j % 2][:, 0:128]),
                    waits=[(sT, mmv)], sem=sS)
                marks[("obank", j % 2)] = (sS, sS.n)
                ev.append(sS.n)
            if kind in ("g", "u"):
                p2.add("gpsimd", lambda e, dst=dst, t=t, c0=c0, wdt=wdt, slot=slot: e.dma_start(
                    out=dst[t, :, c0:c0 + wdt], in_=stg[slot][:, 0:wdt]),
                    waits=[(sS, ev[-1])], sem=dQSs[slot], inc=16)
            else:
                for j in range(nch):
                    ic = (c0 + 128 * j) // 128
                    p2.add("gpsimd", lambda e, dst=dst, ic=ic, t=t, j=j, slot=slot: e.dma_start(
                        out=dst[ic, :, 128 * t:128 * (t + 1)],
                        in_=stg[slot][:, 128 * j:128 * (j + 1)]),
                        waits=[(sS, ev[-1])], sem=dQSs[slot], inc=16)
            marks[("stgfree", slot)] = (dQSs[slot], dQSs[slot].n)
            free_slots.append(slot)
            last_evac = (sS, ev[-1])
        return last_evac

    m_expdone = [[0] * n_tc for _ in range(2)]
    m_exphist = []
    m_pvlast = None
    m_epidone = [None, None]
    marks["pv_prev"] = 0

    for pr in range(n_pairs):
        qts = (2 * pr, 2 * pr + 1)
        prep_quant(2)

        ss_exp_last = [None, None]
        sv_p1done = [None, None]
        tr_mark = None
        for qi, qt in enumerate(qts):
            for tc in range(n_tc):
                bank = len(m_exphist) % 4
                if len(m_exphist) >= 4:
                    w_ = [(sS, m_exphist[-4])]
                else:
                    w_ = [(dBC2, dBC2_all)]
                for k in range(NKC):
                    p2.add("tensor", lambda e, k=k, qt=qt, tc=tc, bank=bank: e.matmul(
                        e_ps[bank][:], f_feat[k][:, 128 * qt:128 * (qt + 1)],
                        f_feat[k][:, 512 * tc:512 * (tc + 1)],
                        start=(k == 0), stop=(k == NKC - 1)),
                        waits=w_ if k == 0 else [], sem=sT)
                st_e = sT.n
                p2.add("vector", lambda e, qi=qi, tc=tc, bank=bank: e.reduce_max(
                    NMC[qi][:, tc:tc + 1], e_ps[bank][:], AX, negate=True),
                    waits=[(sT, st_e)], sem=sV)
                sv_m = sV.n
                w_ = [(sV, sv_m)]
                if tc == 0 and pr > 0:
                    w_.append((sT, marks["pv_prev"]))
                p2.add("scalar", lambda e, qi=qi, tc=tc, bank=bank: e.activation(
                    p_sb[qi][:, PAD + 512 * tc:PAD + 512 * (tc + 1)], e_ps[bank][:],
                    AF.Exp, bias=NMC[qi][:, tc:tc + 1]),
                    waits=w_, sem=sS)
                m_expdone[qi][tc] = sS.n
                m_exphist.append(sS.n)
                marks[("ptbank", bank)] = (sS, sS.n)
            ss_exp_last[qi] = sS.n

            sq = smallq[qi]
            p2.add("vector", lambda e, qi=qi, sq=sq: e.tensor_reduce(
                sq[:, 0:1], NMC[qi][:], AX, ALU.min), waits=[(sV, sV.n)], sem=sV)
            p2.add("vector", lambda e, qi=qi, sq=sq: e.tensor_scalar(
                Erow[qi][:], NMC[qi][:], -1.0, sq[:, 0:1], ALU.mult, ALU.add),
                waits=[(sV, sV.n)], sem=sV)
            sv_er = sV.n
            p2.add("scalar", lambda e, qi=qi: e.activation(
                Erow[qi][:], Erow[qi][:], AF.Exp),
                waits=[(sV, sv_er), (sS, ss_exp_last[qi])], sem=sS)
            ss_ebf = sS.n
            W0 = 128 * qt - 9
            lo = max(W0, 0)
            hi = W0 + 146
            tpl_lo = lo - W0
            p2.add("vector", lambda e, qi=qi, lo=lo, hi=hi, tpl_lo=tpl_lo: e.tensor_mul(
                p_sb[qi][:, PAD + lo:PAD + hi], p_sb[qi][:, PAD + lo:PAD + hi],
                band[:, tpl_lo:tpl_lo + (hi - lo)]),
                waits=[(sS, ss_exp_last[qi])], sem=sV)
            if qt == n_qt - 1:
                p2.add("vector", lambda e, qi=qi: e.tensor_mul(
                    p_sb[qi][:, PAD + Q:PAD + Q + 9], p_sb[qi][:, PAD + Q:PAD + Q + 9],
                    bfix1[:, 0:9]), waits=[(sV, sV.n)], sem=sV)
            if qt == 0:
                p2.add("vector", lambda e, qi=qi: e.tensor_mul(
                    p_sb[qi][:, PAD + S - 9:PAD + S], p_sb[qi][:, PAD + S - 9:PAD + S],
                    bfix2[:, 0:9]), waits=[(sV, sV.n), (dBC2, dBC2_all)], sem=sV)
            sv_band = sV.n
            for tc in range(n_tc):
                p2.add("vector", lambda e, qi=qi, tc=tc: e.reduce_sum(
                    rsrow[qi][:, tc:tc + 1],
                    p_sb[qi][:, PAD + 512 * tc:PAD + 512 * (tc + 1)], AX),
                    waits=[(sV, sv_band)] if tc == 0 else [], sem=sV)
            p2.add("vector", lambda e, qi=qi: e.tensor_mul(
                rsrow[qi][:], rsrow[qi][:], Erow[qi][:]),
                waits=[(sV, sV.n), (sS, ss_ebf)], sem=sV)
            p2.add("vector", lambda e, qi=qi, sq=sq: e.reduce_sum(
                sq[:, 1:2], rsrow[qi][:], AX), waits=[(sV, sV.n)], sem=sV)
            p2.add("vector", lambda e, qi=qi, sq=sq: e.reciprocal(
                sq[:, 2:3], sq[:, 1:2]), waits=[(sV, sV.n)], sem=sV)
            sv_p1done[qi] = sV.n

            gate = m_pvlast if qi == 0 else None
            tr = prep_transpose(3, gate if tr_mark is None else None)
            tr_mark = tr if tr is not None else tr_mark
            prep_quant(3)

        sv_D = {}
        ss_pT = {}
        mm_pT = {}
        st_pvcons = [0] * (n_sc + 8)
        dfs_mark = {}

        def emit_D(sc):
            tc = sc // 4
            for qi in range(2):
                w_ = [(sS, ss_ebf), (sV, sv_p1done[qi])]
                if tc >= 2 and (4 * (tc - 2) + 3) in mm_pT:
                    w_.append((sT, mm_pT[4 * (tc - 2) + 3]))
                p2.add("vector", lambda e, qi=qi, tc=tc: e.tensor_scalar(
                    D_sb[qi][tc % 2][:], ident_bf[:], Erow[qi][:, tc:tc + 1],
                    None, ALU.mult), waits=w_, sem=sV)
                sv_D[(qi, tc)] = sV.n

        def emit_pT(sc):
            tc = sc // 4
            par = sc % 4
            for qi in range(2):
                w_ = [(sV, sv_D[(qi, tc)]), (sV, sv_p1done[qi])]
                bm = marks.get(("ptbank", par))
                if bm is not None:
                    w_.append(bm)
                if sc >= 4:
                    w_.append((sS, ss_pT[(qi, sc - 4)]))
                p2.add("tensor", lambda e, qi=qi, sc=sc, par=par, tc=tc: e.matmul(
                    e_ps[par][:, 0:128], p_sb[qi][:, PAD + 128 * sc:PAD + 128 * (sc + 1)],
                    D_sb[qi][tc % 2][:], start=True, stop=True),
                    waits=w_, sem=sT)
                mmv = sT.n
                mm_pT[sc] = mmv
                p2.add("scalar", lambda e, qi=qi, par=par: e.copy(
                    pT_bf[qi][par][:], e_ps[par][:, 0:128]), waits=[(sT, mmv)], sem=sS)
                marks[("ptbank", par)] = (sS, sS.n)
                ss_pT[(qi, sc)] = sS.n

        emit_D(0)
        for _s in range(min(2, n_sc)):
            emit_pT(_s)

        dfs_val = {}

        def emit_fload(blk):
            buf = blk % 2
            w_ = []
            if blk >= 2:
                w_.append((sT, st_pvcons[2 * (blk - 2) + 1]))
            elif pr > 0:
                w_.append((sT, marks["pv_prev"]))
            for h, fb in ((0, fstr), (1, fstr2)):
                p2.add("sync", lambda e, blk=blk, h=h, fb=fb, buf=buf: e.dma_start(
                    out=fb[buf][:],
                    in_=ftok_hbm[128 * (2 * blk + h):128 * (2 * blk + h + 1), :]),
                    waits=w_ if h == 0 else [], sem=dFS[buf], inc=16)
            dfs_val[blk] = dFS[buf].n

        emit_fload(0)
        for sc in range(n_sc):
            par = sc % 4
            buf = (sc // 2) % 2
            if sc % 2 == 0:
                if sc // 2 + 1 < n_sc // 2:
                    emit_fload(sc // 2 + 1)
                dfs_mark[buf] = dfs_val[sc // 2]
            if (sc + 3) % 4 == 0 and sc + 3 < n_sc:
                emit_D(sc + 3)
            if sc + 2 < n_sc:
                emit_pT(sc + 2)
            fs = fstr if sc % 2 == 0 else fstr2
            for qi in range(2):
                w_ = [(sS, ss_pT[(qi, sc)]), (dFS[buf], dfs_mark[buf])]
                if sc == 0 and m_epidone[qi] is not None:
                    w_.append(m_epidone[qi])
                if sc == 0 and tr_mark is not None:
                    w_.append(tr_mark)
                for h in range(2):
                    p2.add("tensor", lambda e, qi=qi, h=h, par=par, buf=buf, fs=fs,
                           sc=sc: e.matmul(
                        o_ps[qi][:, 512 * h:512 * (h + 1)], pT_bf[qi][par][:],
                        fs[buf][:, 512 * h:512 * (h + 1)],
                        start=(sc == 0), stop=(sc == n_sc - 1)),
                        waits=w_ if h == 0 else [], sem=sT)
            st_pvcons[sc] = sT.n
        marks["pv_prev"] = sT.n
        m_pvlast = (sT, sT.n)
        st_pv = sT.n

        for qi, qt in enumerate(qts):
            sq = smallq[qi]
            w_ = [marks["xe_free"]] if "xe_free" in marks else []
            p2.add("sync", lambda e, qt=qt: e.dma_start(
                out=xe[:], in_=x_hbm[128 * qt:128 * (qt + 1), :]),
                waits=w_, sem=dXE, inc=16)
            dxe = dXE.n
            w_ = [(sT, st_pv), (dBC2, dBC2_all)]
            for m in m_epidone:
                if m is not None:
                    w_.append(m)
            if marks.get("xpstore") is not None:
                w_.append(marks["xpstore"])
            p2.add("vector", lambda e, qi=qi, sq=sq: e.scalar_tensor_tensor(
                xp_sb[qi][:], o_ps[qi][:], sq[:, 2:3], ga_tile[:], ALU.mult, ALU.mult),
                waits=w_, sem=sV)
            marks[("obank", qi)] = (sV, sV.n)
            p2.add("vector", lambda e, qi=qi: e.tensor_add(
                xp_sb[qi][:], xp_sb[qi][:], xe[:]), waits=[(sV, sV.n), (dXE, dxe)], sem=sV)
            sv_xp = sV.n
            marks["xe_free"] = (sV, sv_xp)
            p2.add("gpsimd", lambda e, qi=qi, qt=qt: e.dma_start(
                out=xp_hbm[128 * qt:128 * (qt + 1), :], in_=xp_sb[qi][:]),
                waits=[(sV, sv_xp)], sem=dXP[qi], inc=16)
            marks["xpstore"] = (dXP[qi], dXP[qi].n)
            st_ = st2[qi]
            p2.add("vector", lambda e, qi=qi, st_=st_: e.bn_stats(
                st_[:, 0:6], xp_sb[qi][:, 0:512]), waits=[(sV, sv_xp)], sem=sV)
            p2.add("vector", lambda e, qi=qi, st_=st_: e.bn_stats(
                st_[:, 6:12], xp_sb[qi][:, 512:1024]), waits=[(sV, sV.n)], sem=sV)
            p2.add("vector", lambda e, st_=st_: e.bn_aggr(
                st_[:, 12:14], st_[:, 0:12]), waits=[(sV, sV.n)], sem=sV)
            p2.add("vector", lambda e, st_=st_: e.tensor_scalar(
                st_[:, 14:15], st_[:, 13:14], EPS, None, ALU.add), waits=[(sV, sV.n)], sem=sV)
            p2.add("vector", lambda e, st_=st_: e.reciprocal(
                st_[:, 15:16], st_[:, 14:15]), waits=[(sV, sV.n)], sem=sV)
            sv_rec = sV.n
            p2.add("scalar", lambda e, st_=st_: e.activation(
                st_[:, 0:1], st_[:, 15:16], AF.Sqrt), waits=[(sV, sv_rec)], sem=sS)
            ss_sq2 = sS.n
            p2.add("vector", lambda e, st_=st_: e.tensor_mul(
                st_[:, 1:2], st_[:, 0:1], st_[:, 0:1]), waits=[(sS, ss_sq2)], sem=sV)
            p2.add("vector", lambda e, st_=st_: e.tensor_mul(
                st_[:, 2:3], st_[:, 1:2], st_[:, 14:15]), waits=[(sV, sV.n)], sem=sV)
            p2.add("vector", lambda e, st_=st_: e.tensor_scalar(
                st_[:, 3:4], st_[:, 2:3], -0.5, 1.5, ALU.mult, ALU.add),
                waits=[(sV, sV.n)], sem=sV)
            p2.add("vector", lambda e, st_=st_: e.tensor_mul(
                st_[:, 4:5], st_[:, 0:1], st_[:, 3:4]), waits=[(sV, sV.n)], sem=sV)
            p2.add("vector", lambda e, st_=st_: e.tensor_mul(
                st_[:, 5:6], st_[:, 12:13], st_[:, 4:5]), waits=[(sV, sV.n)], sem=sV)
            p2.add("vector", lambda e, st_=st_: e.tensor_scalar(
                st_[:, 6:7], st_[:, 5:6], -1.0, None, ALU.mult), waits=[(sV, sV.n)], sem=sV)
            w_ = [(sV, sV.n)]
            if m_epidone[qi] is not None:
                w_.append(m_epidone[qi])
            p2.add("vector", lambda e, qi=qi, st_=st_: e.tensor_scalar(
                h2_sb[qi][:], xp_sb[qi][:], st_[:, 4:5], st_[:, 6:7], ALU.mult, ALU.add),
                waits=w_, sem=sV)
            p2.add("vector", lambda e, qi=qi: e.tensor_mul(
                h2_sb[qi][:], h2_sb[qi][:], onepcb_tile[:]), waits=[(sV, sV.n)], sem=sV)
            p2.add("vector", lambda e, qi=qi: e.tensor_add(
                h2_sb[qi][:], h2_sb[qi][:], sbb_tile[:]), waits=[(sV, sV.n)], sem=sV)
            sv_h2 = sV.n
            p2.add("vector", lambda e, qi=qi, st_=st_: e.reduce_max(
                st_[:, 7:8], h2_sb[qi][:], AX, apply_absolute_value=True),
                waits=[(sV, sv_h2)], sem=sV)
            p2.add("vector", lambda e, st_=st_: e.tensor_max(
                amax1[:], amax1[:], st_[:, 7:8]), waits=[(sV, sV.n)], sem=sV)
            p2.add("gpsimd", lambda e, qi=qi, qt=qt: e.dma_start(
                out=h2_hbm[128 * qt:128 * (qt + 1), :], in_=h2_sb[qi][:]),
                waits=[(sV, sv_h2)], sem=dXH[qi], inc=16)
            m_epidone[qi] = (dXH[qi], dXH[qi].n)

    while sq_i[0] < len(strip_jobs) or pending:
        before = (sq_i[0], len(pending))
        prep_quant(3)
        prep_transpose(6, None)
        if (sq_i[0], len(pending)) == before:
            raise RuntimeError("weight prep drain stuck")

    for eng, s_ in (("scalar", sS), ("vector", sV), ("tensor", sT), ("gpsimd", sG)):
        p2.add(eng, lambda e: e.nop(), sem=s_)
    marks["b2"] = {"sS": sS.n, "sV": sV.n, "sT": sT.n, "sG": sG.n,
                   "h2st": list(m_epidone), "dQS": [(s_, s_.n) for s_ in dQSs],
                   "xpst": [(dXP[0], dXP[0].n), (dXP[1], dXP[1].n), (dXH[0], dXH[0].n), (dXH[1], dXH[1].n)]}

    p2.emit(nc)
    p2a.close()
    env.update(dict(xp_hbm=xp_hbm, h2_hbm=h2_hbm, qg_hbm=qg_hbm, qu_hbm=qu_hbm,
                    qd_hbm=qd_hbm, scl_g=scl_g, scl_u=scl_u, scl_d=scl_d))
    return env


def build_p4(nc, ctx, env):
    S = env["S"]; Q = env["Q"]; n_qt = env["n_qt"]
    ident_bf = env["ident_bf"]; ident_f32 = env["ident_f32"]
    amax1 = env["amax1"]; amax2 = env["amax2"]; clip_sb = env["clip_sb"]
    h2_hbm = env["h2_hbm"]; qg_hbm = env["qg_hbm"]; qu_hbm = env["qu_hbm"]
    scl_g = env["scl_g"]; scl_u = env["scl_u"]
    sS = env["sS"]; sV = env["sV"]; sT = env["sT"]; sG = env["sG"]
    sem = ctx.sem
    marks = env["marks"]
    b2 = marks["b2"]
    n_tt = Q // 128
    n_tb = Q // 512

    hmid_hbm = nc.dram_tensor("hmid_hbm", [DI, Q], F32).ap()
    scal_hbm = nc.dram_tensor("scal_hbm", [4, 8], F32).ap()
    cc1_in = nc.dram_tensor("cc1_in", [1, 8], F32)
    cc1_out = nc.dram_tensor("cc1_out", [1, 8], F32)

    env["ffstack"].close()
    p4a = ExitStack()
    sb4 = lambda n, sh, dt: p4a.enter_context(nc.sbuf_tensor(n, sh, dt))
    ps4 = lambda n, sh: p4a.enter_context(nc.psum_tensor(n, sh, F32))

    qg_sb = [sb4(f"qg_sb{t}", [128, D], BF16) for t in range(22)]
    qu_sb = [sb4(f"qu_sb{t}", [128, D], BF16) for t in range(22)]
    xqT = [sb4(f"xqT{k}", [128, Q], BF16) for k in range(NKC)]
    h2b = [sb4(f"h2b{j}", [128, D], F32) for j in range(2)]
    qbuf = [sb4(f"qbuf{j}", [128, D], F32) for j in range(2)]
    ybuf = [sb4(f"ybuf{j}", [128, D], I32) for j in range(2)]
    cbuf = [sb4(f"cbuf{j}", [128, D], F32) for j in range(2)]
    xq_bf = [sb4(f"xq_bf{j}", [128, D], BF16) for j in range(2)]
    gate_f = [sb4(f"gate_f{j}", [128, 512], F32) for j in range(2)]
    hm_sb = [sb4(f"hm_sb{j}", [128, 512], F32) for j in range(2)]
    gsc = sb4("gsc", [1, 8], F32)
    mxcol = sb4("mxcol", [128, 1], F32)
    sxcol = sb4("sxcol", [128, 1], F32)
    am4 = sb4("am4", [128, 2], F32)
    sgx = sb4("sgx", [128, 22], F32)
    sux = sb4("sux", [128, 22], F32)

    psA = ps4("psA", [1, 128])
    ps_tr2 = [ps4(f"ps_tr2{j}", [128, 512]) for j in range(2)]
    psG = [ps4(f"psG{j}", [128, 512]) for j in range(2)]
    psU = [ps4(f"psU{j}", [128, 512]) for j in range(2)]

    p4 = Phase("p4")
    dH2 = [sem("dH2_0"), sem("dH2_1")]
    dQR = sem("dQR")
    dCC1 = sem("dCC1")
    cc1s = sem("cc1s")
    dSC = sem("dSC")
    dSCB = sem("dSCB")
    dHM = [sem("dHM0"), sem("dHM1")]

    bar = [(sS, b2["sS"]), (sV, b2["sV"]), (sT, b2["sT"]), (sG, b2["sG"])]
    p4.add("gpsimd", lambda e: e.nop(), waits=bar)
    for t_ in (sgx, sux, gsc, mxcol, sxcol):
        p4.add("gpsimd", lambda e, t_=t_: e.memset(t_[:], 0.0), sem=sG)
    g_init = sG.n
    bar2 = bar + [(sG, g_init)]
    for eng in ("scalar", "vector", "tensor"):
        p4.add(eng, lambda e: e.nop(), waits=bar2)
    p4.add("sync", lambda e: e.nop(),
           waits=bar2 + b2["h2st"] + b2["xpst"] + b2["dQS"])

    p4.add("tensor", lambda e: e.matmul(
        psA[:], amax1[:], ident_f32[:], start=True, stop=True), sem=sT)
    st_a = sT.n
    p4.add("vector", lambda e: e.reduce_max(gsc[0:1, 0:1], psA[0:1, :], AX),
           waits=[(sT, st_a)], sem=sV)
    sv_g1 = sV.n
    p4.add("gpsimd", lambda e: e.dma_start(out=cc1_in[0:1, :], in_=gsc[0:1, :]),
           waits=[(sV, sv_g1)], sem=dCC1, inc=16)
    p4.add("gpsimd", lambda e: e.collective_compute(
        "AllReduce", ALU.max, replica_groups=[list(range(env["n_cores"]))],
        ins=[cc1_in.ap().opt()], outs=[cc1_out.ap().opt()]),
        waits=[(dCC1, dCC1.n)], sem=cc1s)
    p4.add("gpsimd", lambda e: e.dma_start(out=gsc[0:1, 1:2], in_=cc1_out[0:1, 0:1]),
           waits=[(cc1s, cc1s.n)], sem=dCC1, inc=16)
    d_gmax = dCC1.n
    p4.add("scalar", lambda e: e.activation(gsc[0:1, 2:3], clip_sb[0:1, 0:1], AF.Sigmoid),
           waits=[(dCC1, d_gmax)], sem=sS)
    ss_sig = sS.n
    p4.add("vector", lambda e: e.tensor_mul(gsc[0:1, 3:4], gsc[0:1, 1:2], gsc[0:1, 2:3]),
           waits=[(sS, ss_sig), (dCC1, d_gmax)], sem=sV)
    p4.add("vector", lambda e: e.tensor_scalar(
        gsc[0:1, 3:4], gsc[0:1, 3:4], 0.007874015748031496, None, ALU.mult), waits=[(sV, sV.n)], sem=sV)
    p4.add("vector", lambda e: e.reciprocal(gsc[0:1, 4:5], gsc[0:1, 3:4]),
           waits=[(sV, sV.n)], sem=sV)
    sv_mx = sV.n
    p4.add("gpsimd", lambda e: e.dma_start(out=scal_hbm[0:1, :], in_=gsc[0:1, :]),
           waits=[(sV, sv_mx)], sem=dSC, inc=16)
    p4.add("sync", lambda e: e.dma_start(
        out=mxcol[:], in_=scal_hbm[0:1, 4:5].broadcast_to((128, 1))),
        waits=[(dSC, dSC.n)], sem=dSCB, inc=16)
    p4.add("sync", lambda e: e.dma_start(
        out=sxcol[:], in_=scal_hbm[0:1, 3:4].broadcast_to((128, 1))),
        sem=dSCB, inc=16)
    d_bc = dSCB.n
    p4.add("vector", lambda e: e.tensor_scalar(
        sgx[:], scl_g[:], sxcol[:, 0:1], None, ALU.mult), waits=[(dSCB, d_bc)], sem=sV)
    p4.add("vector", lambda e: e.tensor_scalar(
        sux[:], scl_u[:], sxcol[:, 0:1], None, ALU.mult), waits=[(sV, sV.n)], sem=sV)
    sv_scales = sV.n

    dqr_marks = []
    for t in range(22):
        p4.add("sync", lambda e, t=t: e.dma_start(out=qg_sb[t][:], in_=qg_hbm[t]),
               sem=dQR, inc=16)
        p4.add("sync", lambda e, t=t: e.dma_start(out=qu_sb[t][:], in_=qu_hbm[t]),
               sem=dQR, inc=16)
    dQR_all = dQR.n

    m_quant = [None] * (n_tt + 4)
    m_h2free = [None] * (n_tt + 4)
    m_trbank = [None, None]
    quant_done = [0]

    def quant_tile(j):
        pb = j % 2
        w_ = [m_h2free[j - 2]] if j >= 2 else []
        p4.add("sync", lambda e, j=j, pb=pb: e.dma_start(
            out=h2b[pb][:], in_=h2_hbm[128 * j:128 * (j + 1), :]),
            waits=w_, sem=dH2[pb], inc=16)
        dv = dH2[pb].n
        w_ = [(dH2[pb], dv), (sV, sv_scales)]
        if j >= 2:
            w_.append(m_quant[j - 2])
        p4.add("vector", lambda e, pb=pb: e.tensor_scalar(
            qbuf[pb][:], h2b[pb][:], mxcol[:, 0:1], None, ALU.mult), waits=w_, sem=sV)
        m_h2free[j] = (sV, sV.n)
        p4.add("vector", lambda e, pb=pb: e.tensor_copy(ybuf[pb][:], qbuf[pb][:]),
               waits=[(sV, sV.n)], sem=sV)
        sv_c = sV.n
        p4.add("gpsimd", lambda e, pb=pb: e.tensor_sub(
            qbuf[pb][:], qbuf[pb][:], ybuf[pb][:]), waits=[(sV, sv_c)], sem=sG)
        sg_d = sG.n
        p4.add("vector", lambda e, pb=pb: e.tensor_scalar(
            cbuf[pb][:], qbuf[pb][:], -0.5, None, ALU.is_lt), waits=[(sG, sg_d)], sem=sV)
        p4.add("vector", lambda e, pb=pb: e.scalar_tensor_tensor(
            qbuf[pb][:], qbuf[pb][:], 0.5, cbuf[pb][:], ALU.is_gt, ALU.subtract),
            waits=[(sV, sV.n)], sem=sV)
        sv_f = sV.n
        p4.add("gpsimd", lambda e, pb=pb: e.tensor_add(
            cbuf[pb][:], qbuf[pb][:], ybuf[pb][:]), waits=[(sV, sv_f)], sem=sG)
        sg_y = sG.n
        p4.add("vector", lambda e, pb=pb: e.tensor_scalar(
            xq_bf[pb][:], cbuf[pb][:], 127.0, -128.0, ALU.min, ALU.max),
            waits=[(sG, sg_y)], sem=sV)
        sv_xq = sV.n
        for k in range(NKC):
            w_ = [(sV, sv_xq)]
            if m_trbank[k % 2] is not None:
                w_.append(m_trbank[k % 2])
            p4.add("tensor", lambda e, pb=pb, k=k: e.matmul(
                ps_tr2[k % 2][:, 0:128], xq_bf[pb][:, 128 * k:128 * (k + 1)],
                ident_bf[:], start=True, stop=True), waits=w_, sem=sT)
            mmv = sT.n
            p4.add("scalar", lambda e, k=k, j=j: e.copy(
                xqT[k][:, 128 * j:128 * (j + 1)], ps_tr2[k % 2][:, 0:128]),
                waits=[(sT, mmv)], sem=sS)
            m_trbank[k % 2] = (sS, sS.n)
        m_quant[j] = (sS, sS.n)

    mGU = {}
    m_gevac = {}
    m_hmst = {}
    for tb in range(n_tb):
        while quant_done[0] < min(n_tt, 4 * (tb + 1)):
            quant_tile(quant_done[0])
            quant_done[0] += 1
        for t in range(22):
            pb = t % 2
            w_ = [m_quant[4 * tb + 3] if 4 * tb + 3 < n_tt else m_quant[n_tt - 1],
                  (dQR, dQR_all)]
            if ("ge", pb) in m_gevac:
                w_.append(m_gevac[("ge", pb)])
            for k in range(NKC):
                p4.add("tensor", lambda e, t=t, tb=tb, k=k, pb=pb: e.matmul(
                    psG[pb][:], qg_sb[t][:, 128 * k:128 * (k + 1)],
                    xqT[k][:, 512 * tb:512 * (tb + 1)],
                    start=(k == 0), stop=(k == NKC - 1)),
                    waits=w_ if k == 0 else [], sem=sT)
            mGU[(tb, t, "g")] = (sT, sT.n)
            st_g = sT.n
            w_ = []
            if (("ue", pb)) in m_gevac:
                w_.append(m_gevac[("ue", pb)])
            for k in range(NKC):
                p4.add("tensor", lambda e, t=t, tb=tb, k=k, pb=pb: e.matmul(
                    psU[pb][:], qu_sb[t][:, 128 * k:128 * (k + 1)],
                    xqT[k][:, 512 * tb:512 * (tb + 1)],
                    start=(k == 0), stop=(k == NKC - 1)),
                    waits=w_ if k == 0 else [], sem=sT)
            st_u = sT.n
            w_ = [(sT, st_g)]
            if ("gf", pb) in m_gevac:
                w_.append(m_gevac[("gf", pb)])
            p4.add("scalar", lambda e, t=t, pb=pb: e.activation(
                gate_f[pb][:], psG[pb][:], AF.Silu, scale=sgx[:, t:t + 1]),
                waits=w_, sem=sS)
            m_gevac[("ge", pb)] = (sS, sS.n)
            ss_gf = sS.n
            w_ = [(sT, st_u), (sS, ss_gf)]
            if ("hm", pb) in m_gevac:
                w_.append(m_gevac[("hm", pb)])
            if ("hmst", pb) in m_gevac:
                w_.append(m_gevac[("hmst", pb)])
            p4.add("vector", lambda e, t=t, pb=pb: e.scalar_tensor_tensor(
                hm_sb[pb][:], psU[pb][:], sux[:, t:t + 1], gate_f[pb][:],
                ALU.mult, ALU.mult), waits=w_, sem=sV)
            m_gevac[("ue", pb)] = (sV, sV.n)
            m_gevac[("gf", pb)] = (sV, sV.n)
            sv_hm = sV.n
            p4.add("vector", lambda e, pb=pb: e.reduce_max(
                am4[:, 0:1], hm_sb[pb][:], AX, apply_absolute_value=True),
                waits=[(sV, sv_hm)], sem=sV)
            p4.add("vector", lambda e: e.tensor_max(
                amax2[:], amax2[:], am4[:, 0:1]), waits=[(sV, sV.n)], sem=sV)
            p4.add("gpsimd", lambda e, t=t, tb=tb, pb=pb: e.dma_start(
                out=hmid_hbm[128 * t:128 * (t + 1), 512 * tb:512 * (tb + 1)],
                in_=hm_sb[pb][:]), waits=[(sV, sv_hm)], sem=dHM[pb], inc=16)
            m_gevac[("hmst", pb)] = (dHM[pb], dHM[pb].n)

    for eng, s_ in (("scalar", sS), ("vector", sV), ("tensor", sT), ("gpsimd", sG)):
        p4.add(eng, lambda e: e.nop(), sem=s_)
    marks["b4"] = {"sS": sS.n, "sV": sV.n, "sT": sT.n, "sG": sG.n,
                   "hmst": [m_gevac.get(("hmst", 0)), m_gevac.get(("hmst", 1))],
                   "xpst": b2["xpst"]}

    p4.emit(nc)
    p4a.close()
    env.update(dict(hmid_hbm=hmid_hbm, scal_hbm=scal_hbm))
    return env


def build_p5(nc, ctx, env):
    S = env["S"]; Q = env["Q"]
    ident_f32 = env["ident_f32"]
    amax2 = env["amax2"]; clip_sb = env["clip_sb"]
    hmid_hbm = env["hmid_hbm"]; scal_hbm = env["scal_hbm"]
    qd_hbm = env["qd_hbm"]; xp_hbm = env["xp_hbm"]; mod_hbm = env["mod_hbm"]
    out_hbm = env["out_hbm"]
    scl_d = env["scl_d"]
    sS = env["sS"]; sV = env["sV"]; sT = env["sT"]; sG = env["sG"]
    sem = ctx.sem
    marks = env["marks"]
    b4 = marks["b4"]
    n_tt = Q // 128
    nhalf = Q // 2

    cc2_in = nc.dram_tensor("cc2_in", [1, 8], F32)
    cc2_out = nc.dram_tensor("cc2_out", [1, 8], F32)
    sd_hbm = nc.dram_tensor("sd_hbm", [8, 128], F32).ap()

    p5a = ExitStack()
    sb5 = lambda n, sh, dt: p5a.enter_context(nc.sbuf_tensor(n, sh, dt))
    ps5 = lambda n, sh: p5a.enter_context(nc.psum_tensor(n, sh, F32))

    qd_sb = [sb5(f"qd_sb{t}", [128, D], BF16) for t in range(22)]
    hq = [sb5(f"hq{t}", [128, nhalf], BF16) for t in range(22)]
    hmb = [sb5(f"hmb{j}", [128, nhalf], F32) for j in range(2)]
    y5 = sb5("y5", [128, nhalf], I32)
    c5 = sb5("c5", [128, nhalf], F32)
    xpb = [sb5(f"xpb{j}", [128, D], F32) for j in range(2)]
    outb = [sb5(f"outb{j}", [128, D], F32) for j in range(2)]
    gb_tile = sb5("gb_tile", [128, D], F32)
    sdb_tile = sb5("sdb_tile", [128, D], F32)
    sd_sb = sb5("sd_sb", [8, 128], F32)
    g5 = sb5("g5", [1, 8], F32)
    mdcol = sb5("mdcol", [128, 1], F32)
    sdcol = sb5("sdcol", [128, 1], F32)

    psA2 = ps5("psA2", [8, 128])
    psB2 = ps5("psB2", [1, 128])
    psD5 = [ps5(f"psD5{j}", [128, D]) for j in range(2)]

    p5 = Phase("p5")
    dHM5 = [sem("dHM5_0"), sem("dHM5_1")]
    dQD = sem("dQD")
    dCC2 = sem("dCC2")
    cc2s = sem("cc2s")
    dSD = sem("dSD")
    dSD2 = sem("dSD2")
    dSDB = sem("dSDB")
    dXPL = [sem("dXPL0"), sem("dXPL1")]
    dOUT = [sem("dOUT0"), sem("dOUT1")]

    bar = [(sS, b4["sS"]), (sV, b4["sV"]), (sT, b4["sT"]), (sG, b4["sG"])]
    p5.add("gpsimd", lambda e: e.nop(), waits=bar)
    for t_ in (g5, mdcol, sdcol, sd_sb):
        p5.add("gpsimd", lambda e, t_=t_: e.memset(t_[:], 0.0), sem=sG)
    g_init = sG.n
    bar2 = bar + [(sG, g_init)]
    for eng in ("scalar", "vector", "tensor"):
        p5.add(eng, lambda e: e.nop(), waits=bar2)
    p5.add("sync", lambda e: e.nop(),
           waits=bar2 + [m for m in b4["hmst"] if m] + b4["xpst"])

    p5.add("tensor", lambda e: e.matmul(
        psB2[:], amax2[:], ident_f32[:], start=True, stop=True), sem=sT)
    st_a = sT.n
    p5.add("vector", lambda e: e.reduce_max(g5[0:1, 0:1], psB2[0:1, :], AX),
           waits=[(sT, st_a)], sem=sV)
    sv_g1 = sV.n
    p5.add("gpsimd", lambda e: e.dma_start(out=cc2_in[0:1, :], in_=g5[0:1, :]),
           waits=[(sV, sv_g1)], sem=dCC2, inc=16)
    p5.add("gpsimd", lambda e: e.collective_compute(
        "AllReduce", ALU.max, replica_groups=[list(range(env["n_cores"]))],
        ins=[cc2_in.ap().opt()], outs=[cc2_out.ap().opt()]),
        waits=[(dCC2, dCC2.n)], sem=cc2s)
    p5.add("gpsimd", lambda e: e.dma_start(out=g5[0:1, 1:2], in_=cc2_out[0:1, 0:1]),
           waits=[(cc2s, cc2s.n)], sem=dCC2, inc=16)
    d_gmax = dCC2.n
    p5.add("scalar", lambda e: e.activation(g5[0:1, 2:3], clip_sb[0:1, 1:2], AF.Sigmoid),
           waits=[(dCC2, d_gmax)], sem=sS)
    ss_sig = sS.n
    p5.add("vector", lambda e: e.tensor_mul(g5[0:1, 3:4], g5[0:1, 1:2], g5[0:1, 2:3]),
           waits=[(sS, ss_sig), (dCC2, d_gmax)], sem=sV)
    p5.add("vector", lambda e: e.tensor_scalar(
        g5[0:1, 3:4], g5[0:1, 3:4], 0.007874015748031496, None, ALU.mult), waits=[(sV, sV.n)], sem=sV)
    p5.add("vector", lambda e: e.reciprocal(g5[0:1, 4:5], g5[0:1, 3:4]),
           waits=[(sV, sV.n)], sem=sV)
    sv_md = sV.n
    p5.add("gpsimd", lambda e: e.dma_start(out=scal_hbm[1:2, :], in_=g5[0:1, :]),
           waits=[(sV, sv_md)], sem=dSD, inc=16)
    dsd1 = dSD.n
    p5.add("sync", lambda e: e.dma_start(
        out=mdcol[:], in_=scal_hbm[1:2, 4:5].broadcast_to((128, 1))),
        waits=[(dSD, dsd1)], sem=dSDB, inc=16)
    p5.add("sync", lambda e: e.dma_start(
        out=sdcol[:], in_=scal_hbm[1:2, 3:4].broadcast_to((128, 1))),
        sem=dSDB, inc=16)
    d_bc = dSDB.n

    p5.add("tensor", lambda e: e.matmul(
        psA2[:], scl_d[:], ident_f32[:], start=True, stop=True), sem=sT)
    st_sd = sT.n
    p5.add("scalar", lambda e: e.copy(sd_sb[:], psA2[:]), waits=[(sT, st_sd)], sem=sS)
    ss_sd = sS.n
    p5.add("gpsimd", lambda e: e.dma_start(out=sd_hbm, in_=sd_sb[:]),
           waits=[(sS, ss_sd)], sem=dSD2, inc=16)
    dsd2 = dSD2.n
    p5.add("sync", lambda e: e.dma_start(
        out=sdb_tile[:],
        in_=sd_hbm.rearrange("a b -> (a b)").rearrange("(o d) -> o d", o=1).broadcast_to((128, D))),
        waits=[(dSD2, dsd2)], sem=dSDB, inc=16)
    p5.add("sync", lambda e: e.dma_start(
        out=gb_tile[:], in_=mod_hbm[3:4, :].broadcast_to((128, D))),
        sem=dSDB, inc=16)
    d_bc2 = dSDB.n
    p5.add("vector", lambda e: e.tensor_mul(gb_tile[:], gb_tile[:], sdb_tile[:]),
           waits=[(dSDB, d_bc2)], sem=sV)
    p5.add("vector", lambda e: e.tensor_scalar(
        gb_tile[:], gb_tile[:], sdcol[:, 0:1], None, ALU.mult),
        waits=[(sV, sV.n)], sem=sV)
    sv_gp = sV.n

    for t in range(22):
        p5.add("sync", lambda e, t=t: e.dma_start(out=qd_sb[t][:], in_=qd_hbm[t]),
               sem=dQD, inc=16)
    dQD_all = dQD.n

    m_outst = {}
    m_psd = {}
    for half in range(2):
        hq_marks = []
        for ic in range(22):
            pb = ic % 2
            w_ = []
            if ic >= 2:
                w_.append(hq_marks[ic - 2])
            elif half == 1:
                w_.append(marks["p5_half0_done"])
            p5.add("sync", lambda e, ic=ic, half=half, pb=pb: e.dma_start(
                out=hmb[pb][:], in_=hmid_hbm[128 * ic:128 * (ic + 1),
                                            nhalf * half:nhalf * (half + 1)]),
                waits=w_, sem=dHM5[pb], inc=16)
            dv = dHM5[pb].n
            w_ = [(dHM5[pb], dv), (sV, sV.n)]
            if ic == 0 and half == 0:
                w_.append((dSDB, d_bc2))
            p5.add("vector", lambda e, pb=pb: e.tensor_scalar(
                hmb[pb][:], hmb[pb][:], mdcol[:, 0:1], None, ALU.mult), waits=w_, sem=sV)
            p5.add("vector", lambda e, pb=pb: e.tensor_copy(y5[:], hmb[pb][:]),
                   waits=[(sV, sV.n)], sem=sV)
            sv_c = sV.n
            p5.add("gpsimd", lambda e, pb=pb: e.tensor_sub(
                hmb[pb][:], hmb[pb][:], y5[:]), waits=[(sV, sv_c)], sem=sG)
            sg_d = sG.n
            p5.add("vector", lambda e, pb=pb: e.tensor_scalar(
                c5[:], hmb[pb][:], -0.5, None, ALU.is_lt), waits=[(sG, sg_d)], sem=sV)
            p5.add("vector", lambda e, pb=pb: e.scalar_tensor_tensor(
                hmb[pb][:], hmb[pb][:], 0.5, c5[:], ALU.is_gt, ALU.subtract),
                waits=[(sV, sV.n)], sem=sV)
            sv_f = sV.n
            p5.add("gpsimd", lambda e, pb=pb: e.tensor_add(
                c5[:], hmb[pb][:], y5[:]), waits=[(sV, sv_f)], sem=sG)
            sg_y = sG.n
            p5.add("vector", lambda e, ic=ic, pb=pb: e.tensor_scalar(
                hq[ic][:], c5[:], 127.0, -128.0, ALU.min, ALU.max),
                waits=[(sG, sg_y)], sem=sV)
            hq_marks.append((sV, sV.n))
        hq_all = (sV, sV.n)

        for jj in range(n_tt // 2):
            j = (n_tt // 2) * half + jj
            pb = jj % 2
            w_ = [m_outst[pb]] if pb in m_outst else []
            p5.add("sync", lambda e, j=j, pb=pb: e.dma_start(
                out=xpb[pb][:], in_=xp_hbm[128 * j:128 * (j + 1), :]),
                waits=w_, sem=dXPL[pb], inc=16)
            dxp = dXPL[pb].n
            for hc in range(2):
                w_ = [hq_all, (dQD, dQD_all)]
                if (pb, hc) in m_psd:
                    w_.append(m_psd[(pb, hc)])
                for ic in range(22):
                    p5.add("tensor", lambda e, ic=ic, jj=jj, hc=hc, pb=pb: e.matmul(
                        psD5[pb][:, 512 * hc:512 * (hc + 1)],
                        hq[ic][:, 128 * jj:128 * (jj + 1)],
                        qd_sb[ic][:, 512 * hc:512 * (hc + 1)],
                        start=(ic == 0), stop=(ic == 21)),
                        waits=w_ if ic == 0 else [], sem=sT)
            st_d = sT.n
            w_ = [(sT, st_d), (sV, sv_gp), (dXPL[pb], dxp)]
            if pb in m_outst:
                w_.append(m_outst[pb])
            p5.add("vector", lambda e, pb=pb: e.tensor_mul(
                outb[pb][:], psD5[pb][:], gb_tile[:]), waits=w_, sem=sV)
            m_psd[(pb, 0)] = (sV, sV.n)
            m_psd[(pb, 1)] = (sV, sV.n)
            p5.add("vector", lambda e, pb=pb: e.tensor_add(
                outb[pb][:], outb[pb][:], xpb[pb][:]), waits=[(sV, sV.n)], sem=sV)
            sv_o = sV.n
            p5.add("gpsimd", lambda e, j=j, pb=pb: e.dma_start(
                out=out_hbm[128 * j:128 * (j + 1), :], in_=outb[pb][:]),
                waits=[(sV, sv_o)], sem=dOUT[pb], inc=16)
            m_outst[pb] = (dOUT[pb], dOUT[pb].n)
        marks["p5_half0_done"] = (sT, sT.n)

    p5.add("gpsimd", lambda e: e.nop(),
           waits=[(dOUT[0], dOUT[0].n), (dOUT[1], dOUT[1].n)])
    p5.emit(nc)
    p5a.close()
    return env



from concourse.bass_utils import run_bass_kernel_spmd

_S = 8192
_Q = _S // 2


def _bandfix_tiles(qh):
    bf1 = np.ones((128, 16), np.float32)
    bf2 = np.ones((128, 16), np.float32)
    if qh == 1:
        for r in range(128):
            for u in range(9):
                d = u + 128 - r
                bf1[r, u] = np.exp(0.01 * max(0, 10 - d), dtype=np.float64).astype(np.float32)
                dg = r + 9 - u
                bf2[r, u] = np.exp(-0.01 * max(0, 10 - dg), dtype=np.float64).astype(np.float32)
    return bf1, bf2


_NC_CACHE = {}


def kernel(L0_attn_x, L0_attn_W, L0_attn_b, L0_ffn_clip_x, L0_ffn_clip_down,
           L0_ffn_w_gate, L0_ffn_w_up, L0_ffn_w_down, L0_c, L0_wmod):
    x = np.ascontiguousarray(np.asarray(L0_attn_x, dtype=np.float32))
    W = np.ascontiguousarray(np.asarray(L0_attn_W, dtype=np.float32))
    battn = np.ascontiguousarray(np.asarray(L0_attn_b, dtype=np.float32)).reshape(1, D)
    cx = np.ascontiguousarray(np.asarray(L0_ffn_clip_x, dtype=np.float32)).reshape(1, 1)
    cd = np.ascontiguousarray(np.asarray(L0_ffn_clip_down, dtype=np.float32)).reshape(1, 1)
    wg = np.ascontiguousarray(np.asarray(L0_ffn_w_gate, dtype=np.float32))
    wu = np.ascontiguousarray(np.asarray(L0_ffn_w_up, dtype=np.float32))
    wd = np.ascontiguousarray(np.asarray(L0_ffn_w_down, dtype=np.float32))
    cmat = np.ascontiguousarray(np.asarray(L0_c, dtype=np.float32))
    wmod = np.ascontiguousarray(np.asarray(L0_wmod, dtype=np.float32))

    if "nc" not in _NC_CACHE:
        _NC_CACHE["nc"] = build_nc(S=_S, dev=0, n_cores=8)
    nc = _NC_CACHE["nc"]

    in_maps = []
    for corei in range(8):
        b_, qh = corei // 2, corei % 2
        xb = np.ascontiguousarray(np.roll(x[b_], -_Q * qh, axis=0))
        bf1, bf2 = _bandfix_tiles(qh)
        in_maps.append(dict(
            x_full=xb, W=W, attn_b=battn, c_row=cmat[b_:b_ + 1],
            wmod=wmod, clip_x=cx, clip_down=cd,
            w_gate=wg, w_up=wu, w_down=wd,
            bandfix1=bf1, bandfix2=bf2))
    res = run_bass_kernel_spmd(nc, in_maps, core_ids=list(range(8)))
    out = np.empty((4, _S, D), np.float32)
    for corei in range(8):
        b_, qh = corei // 2, corei % 2
        out[b_, _Q * qh:_Q * (qh + 1)] = res.results[corei]["out"]
    return out


# version 12
# speedup vs baseline: 1.0686x; 1.0257x over previous
from contextlib import ExitStack

import numpy as np
import concourse.bass as bass
import concourse.mybir as mybir

F32 = mybir.dt.float32
BF16 = mybir.dt.bfloat16
I32 = mybir.dt.int32
ALU = mybir.AluOpType
AF = mybir.ActivationFunctionType
AX = mybir.AxisListType.X


class Sem:
    def __init__(self, nc, stack, name):
        self.h = stack.enter_context(nc.semaphore(name))
        self.n = 0

    def inc(self, k=1):
        self.n += k
        return self.n


class Item:
    __slots__ = ("eng", "emit", "waits", "inc_sem", "inc_amt")

    def __init__(self, eng, emit, waits, inc_sem, inc_amt):
        self.eng = eng
        self.emit = emit
        self.waits = waits
        self.inc_sem = inc_sem
        self.inc_amt = inc_amt


class Phase:
    ENGINES = ("sync", "scalar", "vector", "tensor", "gpsimd")

    def __init__(self, name):
        self.name = name
        self.items = []

    def add(self, eng, emit, waits=(), sem=None, inc=1):
        assert eng in Phase.ENGINES, eng
        waits = list(waits)
        if sem is not None and not waits:
            waits = [(sem, sem.n)]
        if sem is not None:
            sem.inc(inc)
        self.items.append(Item(eng, emit, waits, sem, inc))

    def emit(self, nc):
        with nc.Block() as blk:
            for eng_name in Phase.ENGINES:
                items = [it for it in self.items if it.eng == eng_name]
                if not items:
                    continue

                def body(e, items=items):
                    for it in items:
                        for (s, v) in it.waits:
                            e.wait_ge(s.h, v)
                    raise RuntimeError("unused")

                def body2(e, items=items):
                    for it in items:
                        for (s, v) in it.waits:
                            e.wait_ge(s.h, v)
                        inst = it.emit(e)
                        if it.inc_sem is not None:
                            inst.then_inc(it.inc_sem.h, it.inc_amt)

                getattr(blk, eng_name)(body2)


class Ctx:
    def __init__(self, nc, stack):
        self.nc = nc
        self.stack = stack
        self.sems = {}

    def sem(self, name):
        if name not in self.sems:
            self.sems[name] = Sem(self.nc, self.stack, name)
        return self.sems[name]

    def sbuf(self, name, shape, dtype):
        return self.stack.enter_context(self.nc.sbuf_tensor(name, shape, dtype))

    def psum(self, name, shape, dtype=F32):
        return self.stack.enter_context(self.nc.psum_tensor(name, shape, dtype))


"""adaln megakernel builder. Phases:
P0: constants, modulation (mod = silu(c)@wmod), b' = b + sa@W, transforms
P1: f = (LN(x)*(1+ca))@W + b' for all S tokens -> f_feat (SBUF) + f_tok (HBM)
P2: flash attention + residual + LN2 -> x', h2 spilled; h2 absmax
P3: weight quant+transpose (overlapped later), collective max #1
P4: gate/up + silu*up -> hmid spilled; absmax; collective max #2
P5: down + residual -> out
"""
import numpy as np
import concourse.bass as bass
import concourse.mybir as mybir

D = 1024
DI = 2816
NKC = 8
EPS = 1e-5


def build_nc(S=8192, dev=0, n_cores=8):
    nc = bass.Bass("TRN2", target_bir_lowering=False)
    Q = S // 2
    n_xt = S // 128
    n_qt = Q // 128
    stack = ExitStack()
    ctx = Ctx(nc, stack)
    sb, ps, sem = ctx.sbuf, ctx.psum, ctx.sem

    x_hbm = nc.dram_tensor("x_full", [S, D], F32, kind="ExternalInput").ap()
    W_hbm = nc.dram_tensor("W", [D, D], F32, kind="ExternalInput").ap()
    b_hbm = nc.dram_tensor("attn_b", [1, D], F32, kind="ExternalInput").ap()
    c_hbm = nc.dram_tensor("c_row", [1, D], F32, kind="ExternalInput").ap()
    wmod_hbm = nc.dram_tensor("wmod", [D, 6 * D], F32, kind="ExternalInput").ap()
    clipx_hbm = nc.dram_tensor("clip_x", [1, 1], F32, kind="ExternalInput").ap()
    clipd_hbm = nc.dram_tensor("clip_down", [1, 1], F32, kind="ExternalInput").ap()
    wg_hbm = nc.dram_tensor("w_gate", [DI, D], F32, kind="ExternalInput").ap()
    wu_hbm = nc.dram_tensor("w_up", [DI, D], F32, kind="ExternalInput").ap()
    wd_hbm = nc.dram_tensor("w_down", [D, DI], F32, kind="ExternalInput").ap()
    out_hbm = nc.dram_tensor("out", [Q, D], F32, kind="ExternalOutput").ap()

    ftok_hbm = nc.dram_tensor("ftok_hbm", [S, D], BF16).ap()
    mod_hbm = nc.dram_tensor("mod_hbm", [8, D], F32).ap()

    dev_f = dev_ftk = None
    if dev & 1:
        dev_f = nc.dram_tensor("dev_ffeat", [NKC * 128, S], BF16, kind="ExternalOutput").ap()

    ident_bf = sb("ident_bf", [128, 128], BF16)
    ident_f32 = sb("ident_f32", [128, 128], F32)
    band = sb("band", [128, 160], F32)
    amax1 = sb("amax1", [128, 1], F32)
    amax2 = sb("amax2", [128, 1], F32)
    one_bf = sb("one_bf", [1, 1], BF16)
    one_f32 = sb("one_f32", [1, 1], F32)
    clip_sb = sb("clip_sb", [1, 2], F32)
    scl_g = sb("scl_g", [128, 22], F32)
    scl_u = sb("scl_u", [128, 22], F32)
    scl_d = sb("scl_d", [128, 8], F32)
    mcols = sb("mcols", [128, 52], F32)
    ffstack = ExitStack()
    f_feat = [ffstack.enter_context(nc.sbuf_tensor(f"f_feat{k}", [128, S], BF16))
              for k in range(NKC)]

    dC = sem("dC")
    dW = [sem("dW0"), sem("dW1")]
    dWM = [sem("dWM0"), sem("dWM1"), sem("dWM2")]
    dBC = sem("dBC")
    dX = [sem("dX0"), sem("dX1")]
    dSF = [sem("dSF0"), sem("dSF1")]
    dMW = sem("dMW")
    sS = sem("sS")
    sV = sem("sV")
    sT = sem("sT")
    sG = sem("sG")

    p0 = Phase("p0p1")
    marks = {}

    p0a = ExitStack()
    p0b = ExitStack()
    p0early = ExitStack()
    p0sb = lambda n, sh, dt: p0b.enter_context(nc.sbuf_tensor(n, sh, dt))
    p0ps = lambda n, sh: p0early.enter_context(nc.psum_tensor(n, sh, F32))
    W_bf = p0a.enter_context(nc.sbuf_tensor("W_bf", [128, NKC * D], BF16))
    opca_feat = p0a.enter_context(nc.sbuf_tensor("opca_feat", [128, NKC], F32))
    bp_tile = p0a.enter_context(nc.sbuf_tensor("bp_tile", [128, D], F32))
    abuf = [p0a.enter_context(nc.sbuf_tensor(f"abuf{j}", [128, 704], F32))
            for j in range(2)]
    amx_c = p0a.enter_context(nc.sbuf_tensor("amx_c", [128, 4], F32))

    c_sb = p0sb("c_sb", [1, D], F32)
    b_sb = p0sb("b_sb", [1, D], F32)
    sgfeat = p0sb("sgfeat", [128, NKC], F32)
    iota_t = p0sb("iota_t", [128, 160], F32)
    wmbuf = [p0sb(f"wmbuf{j}", [128, D], F32) for j in range(3)]
    wbuf = wmbuf
    mod_acc = p0sb("mod_acc", [1, 6 * D], F32)
    t1r = p0sb("t1r", [1, D], F32)
    sa_bf = p0sb("sa_bf", [1, D], BF16)
    safeat = p0sb("safeat", [128, NKC], BF16)

    psC = p0ps("psC", [128, NKC])
    psM = [p0ps(f"psM{j}", [1, 512]) for j in range(2)]
    psR = [p0ps(f"psR{j}", [1, 512]) for j in range(2)]

    p0.add("sync", lambda e: e.dma_start(out=c_sb[:], in_=c_hbm), sem=dC, inc=16)
    p0.add("sync", lambda e: e.dma_start(out=b_sb[:], in_=b_hbm), sem=dC, inc=16)
    p0.add("sync", lambda e: e.dma_start(out=clip_sb[:, 0:1], in_=clipx_hbm), sem=dC, inc=16)
    p0.add("sync", lambda e: e.dma_start(out=clip_sb[:, 1:2], in_=clipd_hbm), sem=dC, inc=16)
    dC_all = dC.n

    p0.add("gpsimd", lambda e: e.memset(ident_bf[:], 0.0), sem=sG)
    p0.add("gpsimd", lambda e: e.affine_select(
        out=ident_bf[:], in_=ident_bf[:], compare_op=ALU.not_equal,
        fill=1.0, base=0, pattern=[[-1, 128]], channel_multiplier=1),
        waits=[(sG, sG.n)], sem=sG)
    g_identbf = sG.n
    p0.add("gpsimd", lambda e: e.memset(ident_f32[:], 0.0), sem=sG)
    p0.add("gpsimd", lambda e: e.affine_select(
        out=ident_f32[:], in_=ident_f32[:], compare_op=ALU.not_equal,
        fill=1.0, base=0, pattern=[[-1, 128]], channel_multiplier=1),
        waits=[(sG, sG.n)], sem=sG)
    p0.add("gpsimd", lambda e: e.memset(one_bf[:], 1.0), sem=sG)
    g_onebf = sG.n
    p0.add("gpsimd", lambda e: e.memset(one_f32[:], 1.0), sem=sG)
    g_onef32 = sG.n
    p0.add("gpsimd", lambda e: e.iota(
        iota_t[:], [[1, 160]], base=-9, channel_multiplier=-1,
        allow_small_or_imprecise_dtypes=True), sem=sG)
    g_iota = sG.n
    p0.add("scalar", lambda e: e.activation(band[:], iota_t[:], AF.Abs),
           waits=[(sG, g_iota)], sem=sS)
    ss_b = sS.n
    p0.add("vector", lambda e: e.tensor_scalar(
        band[:], band[:], -1.0, 10.0, ALU.mult, ALU.add), waits=[(sS, ss_b)], sem=sV)
    p0.add("vector", lambda e: e.tensor_scalar(
        band[:], band[:], 0.0, -0.01, ALU.max, ALU.mult), waits=[(sV, sV.n)], sem=sV)

    p0.add("scalar", lambda e: e.activation(c_sb[:], c_sb[:], AF.Silu),
           waits=[(dC, dC_all)], sem=sS)
    ss_siluc = sS.n
    for k in range(NKC):
        p0.add("tensor", lambda e, k=k: e.matmul(
            psC[:, k:k + 1], c_sb[0:1, 128 * k:128 * (k + 1)], one_f32[0:1, 0:1],
            start=True, stop=True),
            waits=[(sS, ss_siluc), (sG, g_onef32)] if k == 0 else [], sem=sT)
    st_sgf = sT.n
    p0.add("scalar", lambda e: e.copy(sgfeat[:], psC[:]), waits=[(sT, st_sgf)], sem=sS)
    ss_sgf = sS.n

    sv_wcast = []
    for k in range(NKC):
        w_ = [(sV, sv_wcast[k - 2])] if k >= 2 else []
        p0.add("sync", lambda e, k=k: e.dma_start(
            out=wbuf[k % 2][:, 0:D], in_=W_hbm[128 * k:128 * (k + 1), :]),
            waits=w_, sem=dW[k % 2], inc=16)
        dval = dW[k % 2].n
        p0.add("vector", lambda e, k=k: e.tensor_copy(
            W_bf[:, D * k:D * (k + 1)], wbuf[k % 2][:, 0:D]),
            waits=[(dW[k % 2], dval)], sem=sV)
        sv_wcast.append(sV.n)
    sv_Wbf = sV.n

    wmm_cons = {}
    addmark = {}
    nsub = 6
    subs = [(k, c4) for k in range(NKC) for c4 in range(nsub)]
    for idx, (k, c4) in enumerate(subs):
        pbuf = idx % 3
        if idx >= 3:
            w_ = [wmm_cons[idx - 3]]
        elif idx < 2:
            w_ = [(sV, sv_wcast[6 + idx])]
        else:
            w_ = []
        p0.add("sync", lambda e, k=k, c4=c4, pbuf=pbuf: e.dma_start(
            out=wmbuf[pbuf][:], in_=wmod_hbm[128 * k:128 * (k + 1),
                                             D * c4:D * (c4 + 1)]),
            waits=w_, sem=dWM[pbuf], inc=16)
        dval = dWM[pbuf].n
        for h in range(2):
            s_ = 2 * c4 + h
            bank = s_ % 2
            w_mm = [(dWM[pbuf], dval), (sS, ss_sgf)]
            if bank in addmark:
                w_mm.append(addmark[bank])
            p0.add("tensor", lambda e, k=k, h=h, pbuf=pbuf, bank=bank: e.matmul(
                psM[bank][:], sgfeat[:, k:k + 1], wmbuf[pbuf][0:128, 512 * h:512 * (h + 1)],
                start=True, stop=True), waits=w_mm, sem=sT)
            mmv = sT.n
            wmm_cons[idx] = (sT, mmv)
            eng = "vector"
            esem = sV
            if k == 0:
                p0.add(eng, lambda e, s_=s_, bank=bank: e.tensor_copy(
                    mod_acc[0:1, 512 * s_:512 * (s_ + 1)], psM[bank][0:1, :]),
                    waits=[(sT, mmv)], sem=esem)
            else:
                p0.add(eng, lambda e, s_=s_, bank=bank: e.tensor_add(
                    mod_acc[0:1, 512 * s_:512 * (s_ + 1)],
                    mod_acc[0:1, 512 * s_:512 * (s_ + 1)], psM[bank][0:1, :]),
                    waits=[(sT, mmv)], sem=esem)
            addmark[bank] = (esem, esem.n)
    sv_mod = sV.n
    sg_mod = sG.n

    def msl(i):
        return mod_acc[0:1, D * i:D * (i + 1)]
    mod_ready = [(sV, sv_mod), (sG, sg_mod)]

    p0.add("vector", lambda e: e.tensor_copy(sa_bf[:], msl(0)), waits=mod_ready, sem=sV)
    sv_sabf = sV.n
    for k in range(NKC):
        p0.add("tensor", lambda e, k=k: e.matmul(
            psC[:, k:k + 1], sa_bf[0:1, 128 * k:128 * (k + 1)], one_bf[0:1, 0:1],
            start=True, stop=True),
            waits=[(sV, sv_sabf), (sS, ss_sgf)] if k == 0 else [], sem=sT)
    st_saf = sT.n
    p0.add("scalar", lambda e: e.copy(safeat[:], psC[:]), waits=[(sT, st_saf)], sem=sS)
    ss_saf = sS.n

    p0.add("vector", lambda e: e.tensor_scalar(msl(1), msl(1), 1.0, None, ALU.add),
           waits=mod_ready, sem=sV)
    sv_opca = sV.n
    for k in range(NKC):
        p0.add("tensor", lambda e, k=k: e.matmul(
            psC[:, k:k + 1], mod_acc[0:1, D + 128 * k:D + 128 * (k + 1)], one_f32[0:1, 0:1],
            start=True, stop=True),
            waits=[(sV, sv_opca), (sG, g_onef32), (sS, ss_saf)] if k == 0 else [], sem=sT)
    st_opca = sT.n
    p0.add("scalar", lambda e: e.copy(opca_feat[:], psC[:]), waits=[(sT, st_opca)], sem=sS)
    ss_opca = sS.n

    p0.add("vector", lambda e: e.tensor_scalar(msl(4), msl(4), 1.0, None, ALU.add),
           waits=mod_ready, sem=sV)
    for idx in (2, 5):
        p0.add("vector", lambda e, idx=idx: e.tensor_scalar(
            t1r[:], msl(idx), 0.0, None, ALU.is_ge),
            waits=mod_ready + [(sV, sV.n), (sS, sS.n)], sem=sV)
        p0.add("vector", lambda e: e.tensor_scalar(
            t1r[:], t1r[:], 2.0, -1.0, ALU.mult, ALU.add), waits=[(sV, sV.n)], sem=sV)
        p0.add("scalar", lambda e, idx=idx: e.activation(mod_acc[0:1, 0:D], msl(idx), AF.Abs),
               waits=mod_ready + [(sS, sS.n), (sV, sV.n)], sem=sS)
        ss_abs = sS.n
        p0.add("vector", lambda e: e.tensor_scalar(
            mod_acc[0:1, 0:D], mod_acc[0:1, 0:D], 0.25, None, ALU.max), waits=[(sS, ss_abs)], sem=sV)
        p0.add("vector", lambda e, idx=idx: e.tensor_mul(msl(idx), t1r[:], mod_acc[0:1, 0:D]),
               waits=[(sV, sV.n)], sem=sV)
    sv_gagb = sV.n

    for s_ in range(2):
        for k in range(NKC):
            w_ = [(sS, ss_saf), (sV, sv_Wbf)] if (s_ == 0 and k == 0) else []
            p0.add("tensor", lambda e, s_=s_, k=k: e.matmul(
                psR[s_][:], safeat[:, k:k + 1],
                W_bf[0:128, D * k + 512 * s_:D * k + 512 * s_ + 512],
                start=(k == 0), stop=(k == NKC - 1)), waits=w_, sem=sT)
        mmv = sT.n
        p0.add("vector", lambda e, s_=s_: e.tensor_add(
            t1r[0:1, 512 * s_:512 * (s_ + 1)], psR[s_][0:1, :],
            b_sb[0:1, 512 * s_:512 * (s_ + 1)]),
            waits=[(sT, mmv), (dC, dC_all), (sV, sv_gagb)], sem=sV)
    sv_bp = sV.n

    for src, slot, mark in ((msl(2), 0, sv_gagb), (msl(4), 1, sv_gagb),
                            (msl(3), 2, sv_gagb), (msl(5), 3, sv_gagb),
                            (t1r[:], 4, sv_bp)):
        p0.add("gpsimd", lambda e, src=src, slot=slot: e.dma_start(
            out=mod_hbm[slot:slot + 1, :], in_=src), waits=[(sV, mark)], sem=dMW, inc=16)
    dMW_all = dMW.n
    dMW_all0 = dMW_all

    p0.add("sync", lambda e: e.dma_start(
        out=bp_tile[:], in_=mod_hbm[4:5, :].broadcast_to((128, D))),
        waits=[(dMW, dMW_all)], sem=dBC, inc=16)
    dBC_bp = dBC.n

    p0b.close()
    p1sb = lambda n, sh, dt: p0a.enter_context(nc.sbuf_tensor(n, sh, dt))
    xb = [p1sb(f"xb{j}", [128, D], F32) for j in range(2)]
    h_bf = [p1sb(f"hbf{j}", [128, D], BF16) for j in range(2)]
    hT_bf = [p1sb(f"hTbf{j}", [128, D], BF16) for j in range(2)]
    ftk = [p1sb(f"ftk{j}", [128, D], BF16) for j in range(2)]
    stats = [p1sb(f"stats{j}", [128, 16], F32) for j in range(2)]
    p0early.close()
    p1ps = lambda n, sh: p0a.enter_context(nc.psum_tensor(n, sh, F32))
    ps_tr = [p1ps(f"ps_tr{j}", [128, 128]) for j in range(2)]
    ps_f = [p1ps(f"ps_f{j}", [128, D]) for j in range(2)]

    amax_done = {}
    WK1 = {"g": (wg_hbm, scl_g, 0, D), "u": (wu_hbm, scl_u, 22, D),
           "d": (wd_hbm, scl_d, 44, DI)}
    amax_jobs1 = ([("g", t) for t in range(22)] + [("u", t) for t in range(22)]
                  + [("d", t) for t in range(8)])
    dAM = [sem("dAM0"), sem("dAM1")]
    am_state = {"i": 0, "pend": []}

    def _amax_issue(ph):
        if am_state["i"] >= len(amax_jobs1):
            return
        j = am_state["i"]
        am_state["i"] += 1
        kind, t = amax_jobs1[j]
        srcw, scl, mbase, Wd = WK1[kind]
        ab = j % 2
        w_ = [am_state[("free", ab)]] if ("free", ab) in am_state else []
        wdt = min(704, Wd)
        ph.add("sync", lambda e, srcw=srcw, t=t, wdt=wdt, ab=ab: e.dma_start(
            out=abuf[ab][:, 0:wdt], in_=srcw[128 * t:128 * (t + 1), 0:wdt]),
            waits=w_, sem=dAM[ab], inc=16)
        am_state["pend"].append((j, kind, t, ab, dAM[ab].n))

    def _amax_reduce(ph):
        if not am_state["pend"]:
            return
        j, kind, t, ab, dv = am_state["pend"].pop(0)
        srcw, scl, mbase, Wd = WK1[kind]
        ph.add("vector", lambda e, ab=ab, Wd=Wd: e.tensor_reduce(
            amx_c[:, 0:1], abuf[ab][:, 0:min(704, Wd)], AX, ALU.max,
            apply_absolute_value=True), waits=[(dAM[ab], dv), (sV, sV.n)], sem=sV)
        for c0 in range(704, Wd, 704):
            wdt = min(704, Wd - c0)
            ph.add("sync", lambda e, srcw=srcw, t=t, c0=c0, wdt=wdt, ab=ab: e.dma_start(
                out=abuf[ab][:, 0:wdt], in_=srcw[128 * t:128 * (t + 1), c0:c0 + wdt]),
                waits=[(sV, sV.n)], sem=dAM[ab], inc=16)
            dv2 = dAM[ab].n
            ph.add("vector", lambda e, ab=ab, wdt=wdt: e.tensor_reduce(
                amx_c[:, 1:2], abuf[ab][:, 0:wdt], AX, ALU.max,
                apply_absolute_value=True), waits=[(dAM[ab], dv2), (sV, sV.n)], sem=sV)
            ph.add("vector", lambda e: e.tensor_max(
                amx_c[:, 0:1], amx_c[:, 0:1], amx_c[:, 1:2]), waits=[(sV, sV.n)], sem=sV)
        am_state[("free", ab)] = (sV, sV.n)
        ph.add("vector", lambda e, scl=scl, t=t: e.tensor_scalar(
            scl[:, t:t + 1], amx_c[:, 0:1], 0.007874015748031496, None, ALU.mult),
            waits=[(sV, sV.n)], sem=sV)
        mi = mbase + t
        ph.add("vector", lambda e, mi=mi, scl=scl, t=t: e.reciprocal(
            mcols[:, mi:mi + 1], scl[:, t:t + 1]), waits=[(sV, sV.n)], sem=sV)
        amax_done[(kind, t)] = (sV, sV.n)

    def amax_step(ph):
        if len(am_state["pend"]) >= 2:
            _amax_reduce(ph)
        _amax_issue(ph)

    m_xfree = [None] * (n_xt + 4)
    m_trfree = [0] * (n_xt + 4)
    m_psffree = [0] * (n_xt + 4)
    m_hTdone = [0] * (n_xt + 4)
    m_ftokst = [None] * (n_xt + 4)

    for i in range(n_xt):
        amax_step(p0)
        pb = i % 2
        if i >= 2:
            w_ = [m_xfree[i - 2]]
        elif i == 0:
            w_ = [(dMW, dMW_all0), (sV, sv_bp), (sS, ss_opca), (sG, sg_mod),
                  (sT, sT.n)]
        else:
            w_ = []
        p0.add("sync", lambda e, i=i, pb=pb: e.dma_start(
            out=xb[pb][:], in_=x_hbm[128 * i:128 * (i + 1), :]),
            waits=w_, sem=dX[pb], inc=16)
        dxv = dX[pb].n

        st = stats[pb]
        w_ = [(dX[pb], dxv)]
        if i >= 2:
            w_.append(m_xfree[i - 2])
        p0.add("vector", lambda e, pb=pb, st=st: e.bn_stats(st[:, 0:6], xb[pb][:, 0:512]),
               waits=w_, sem=sV)
        p0.add("vector", lambda e, pb=pb, st=st: e.bn_stats(st[:, 6:12], xb[pb][:, 512:1024]),
               waits=[(sV, sV.n)], sem=sV)
        p0.add("vector", lambda e, st=st: e.bn_aggr(st[:, 12:14], st[:, 0:12]),
               waits=[(sV, sV.n)], sem=sV)
        p0.add("vector", lambda e, st=st: e.tensor_scalar(
            st[:, 14:15], st[:, 13:14], EPS, None, ALU.add), waits=[(sV, sV.n)], sem=sV)
        p0.add("vector", lambda e, st=st: e.reciprocal(st[:, 15:16], st[:, 14:15]),
               waits=[(sV, sV.n)], sem=sV)
        sv_rec = sV.n
        p0.add("scalar", lambda e, st=st: e.activation(st[:, 0:1], st[:, 15:16], AF.Sqrt),
               waits=[(sV, sv_rec)], sem=sS)
        ss_sq = sS.n
        p0.add("vector", lambda e, st=st: e.tensor_mul(st[:, 1:2], st[:, 0:1], st[:, 0:1]),
               waits=[(sS, ss_sq)], sem=sV)
        p0.add("vector", lambda e, st=st: e.tensor_mul(st[:, 2:3], st[:, 1:2], st[:, 14:15]),
               waits=[(sV, sV.n)], sem=sV)
        p0.add("vector", lambda e, st=st: e.tensor_scalar(
            st[:, 3:4], st[:, 2:3], -0.5, 1.5, ALU.mult, ALU.add), waits=[(sV, sV.n)], sem=sV)
        p0.add("vector", lambda e, st=st: e.tensor_mul(st[:, 4:5], st[:, 0:1], st[:, 3:4]),
               waits=[(sV, sV.n)], sem=sV)
        p0.add("vector", lambda e, st=st: e.tensor_mul(st[:, 5:6], st[:, 12:13], st[:, 4:5]),
               waits=[(sV, sV.n)], sem=sV)
        p0.add("vector", lambda e, st=st: e.tensor_scalar(
            st[:, 6:7], st[:, 5:6], -1.0, None, ALU.mult), waits=[(sV, sV.n)], sem=sV)
        sv_ln = sV.n

        w_ = [(sV, sv_ln)]
        if i >= 2:
            w_.append((sT, m_hTdone[i - 2]))
        p0.add("vector", lambda e, pb=pb, st=st: e.tensor_scalar(
            h_bf[pb][:], xb[pb][:], st[:, 4:5], st[:, 6:7], ALU.mult, ALU.add),
            waits=w_, sem=sV)
        m_xfree[i] = (sV, sV.n)
        ss_h = sV.n

        ss_evac = []
        for j in range(NKC):
            w_ = [(sV, ss_h)]
            if j < 2:
                w_.append((sG, g_identbf))
                if i > 0:
                    w_.append((sS, m_trfree[i - 1]))
                else:
                    w_.append((sS, ss_opca))
                    w_.append((sV, sv_bp))
                    w_.append((sG, sg_mod))
            else:
                w_.append((sS, ss_evac[j - 2]))
            p0.add("tensor", lambda e, pb=pb, j=j: e.matmul(
                ps_tr[j % 2][:], h_bf[pb][:, 128 * j:128 * (j + 1)], ident_bf[:],
                start=True, stop=True), waits=w_, sem=sT)
            mmv = sT.n
            p0.add("scalar", lambda e, pb=pb, j=j: e.activation(
                hT_bf[pb][:, 128 * j:128 * (j + 1)], ps_tr[j % 2][:], AF.Copy,
                scale=opca_feat[:, j:j + 1]),
                waits=[(sT, mmv)], sem=sS)
            ss_evac.append(sS.n)
        ss_hT = sS.n

        for s_ in range(2):
            for k in range(NKC):
                w_ = []
                if s_ == 0 and k == 0:
                    w_ = [(sS, ss_hT), (sV, sv_Wbf)]
                    if i >= 2:
                        w_.append((sV, m_psffree[i - 2]))
                p0.add("tensor", lambda e, pb=pb, s_=s_, k=k: e.matmul(
                    ps_f[pb][:, 512 * s_:512 * (s_ + 1)],
                    hT_bf[pb][:, 128 * k:128 * (k + 1)],
                    W_bf[0:128, D * k + 512 * s_:D * k + 512 * s_ + 512],
                    start=(k == 0), stop=(k == NKC - 1)), waits=w_, sem=sT)
        m_hTdone[i] = sT.n
        st_f = sT.n

        for s_ in range(2):
            w_ = [(sT, st_f), (dBC, dBC_bp)]
            if i >= 2:
                w_.append(m_ftokst[i - 2])
                w_.append((sT, m_hTdone[i - 1]))
            p0.add("vector", lambda e, pb=pb, s_=s_: e.tensor_add(
                ftk[pb][:, 512 * s_:512 * (s_ + 1)], ps_f[pb][:, 512 * s_:512 * (s_ + 1)],
                bp_tile[:, 512 * s_:512 * (s_ + 1)]),
                waits=w_ if s_ == 0 else [], sem=sV)
        m_psffree[i] = sV.n
        sv_ftk = sV.n

        p0.add("gpsimd", lambda e, pb=pb, i=i: e.dma_start(
            out=ftok_hbm[128 * i:128 * (i + 1), :], in_=ftk[pb][:]),
            waits=[(sV, sv_ftk)], sem=dSF[pb], inc=16)
        m_ftokst[i] = (dSF[pb], dSF[pb].n)

        ss_evac2 = []
        for j in range(NKC):
            w_ = [(sV, sv_ftk)]
            if j < 2:
                w_.append((sS, ss_hT))
            else:
                w_.append((sS, ss_evac2[j - 2]))
            p0.add("tensor", lambda e, pb=pb, j=j: e.matmul(
                ps_tr[j % 2][:], ftk[pb][:, 128 * j:128 * (j + 1)], ident_bf[:],
                start=True, stop=True), waits=w_, sem=sT)
            mmv = sT.n
            p0.add("scalar", lambda e, j=j, i=i: e.copy(
                f_feat[j][:, 128 * i:128 * (i + 1)], ps_tr[j % 2][:]),
                waits=[(sT, mmv)], sem=sS)
            ss_evac2.append(sS.n)
        m_trfree[i] = sS.n
        m_ftrdone = sT.n
        m_hTdone[i] = max(m_hTdone[i], m_ftrdone)

    while am_state["pend"] or am_state["i"] < len(amax_jobs1):
        _amax_reduce(p0)
        _amax_issue(p0)

    marks["p1_sS"] = sS.n
    marks["p1_sV"] = sV.n
    marks["p1_sT"] = sT.n

    if dev & 1:
        dDV = sem("dDV")
        for j in range(NKC):
            p0.add("gpsimd", lambda e, j=j: e.dma_start(
                out=dev_f[128 * j:128 * (j + 1), :], in_=f_feat[j][:]),
                waits=[(sS, marks["p1_sS"])], sem=dDV, inc=16)
        p0.add("gpsimd", lambda e: e.nop(), waits=[(dDV, dDV.n)])

    p0.add("scalar", lambda e: e.activation(band[:], band[:], AF.Exp),
           waits=[(sV, sV.n)], sem=sS)
    p0.add("gpsimd", lambda e: e.memset(amax1[:], 0.0), sem=sG)
    p0.add("gpsimd", lambda e: e.memset(amax2[:], 0.0), sem=sG)

    for eng, s_ in (("sync", None), ("scalar", sS), ("vector", sV),
                    ("tensor", sT), ("gpsimd", sG)):
        if s_ is not None:
            p0.add(eng, lambda e: e.nop(), sem=s_)
    marks["b1"] = {"sS": sS.n, "sV": sV.n, "sT": sT.n, "sG": sG.n}
    marks["dMW_all"] = dMW_all
    marks["dSF_final"] = [(dSF[0], dSF[0].n), (dSF[1], dSF[1].n)]

    p0.emit(nc)
    p0a.close()

    env = build_p2(nc, ctx, locals())
    env = build_p4(nc, ctx, env)
    env = build_p5(nc, ctx, env)
    stack.close()
    return nc


def build_p2(nc, ctx, env):
    S = env["S"]; Q = env["Q"]; n_qt = env["n_qt"]; dev = env["dev"]
    f_feat = env["f_feat"]; ident_bf = env["ident_bf"]; band = env["band"]
    amax1 = env["amax1"]
    x_hbm = env["x_hbm"]; ftok_hbm = env["ftok_hbm"]; mod_hbm = env["mod_hbm"]
    wg_hbm = env["wg_hbm"]; wu_hbm = env["wu_hbm"]; wd_hbm = env["wd_hbm"]
    sS = env["sS"]; sV = env["sV"]; sT = env["sT"]; sG = env["sG"]
    sem = ctx.sem
    marks = env["marks"]
    b1 = marks["b1"]
    PAD = 16
    n_tc = S // 512
    n_sc = S // 128
    n_pairs = n_qt // 2

    xp_hbm = nc.dram_tensor("xp_hbm", [Q, D], F32).ap()
    h2_hbm = nc.dram_tensor("h2_hbm", [Q, D], F32).ap()
    qg_hbm = nc.dram_tensor("qg_hbm", [22, 128, D], BF16).ap()
    qu_hbm = nc.dram_tensor("qu_hbm", [22, 128, D], BF16).ap()
    qd_hbm = nc.dram_tensor("qd_hbm", [22, 128, D], BF16).ap()
    bf1_hbm = nc.dram_tensor("bandfix1", [128, 16], F32, kind="ExternalInput").ap()
    bf2_hbm = nc.dram_tensor("bandfix2", [128, 16], F32, kind="ExternalInput").ap()

    scl_g = env["scl_g"]; scl_u = env["scl_u"]; scl_d = env["scl_d"]

    p2a = ExitStack()
    sb2 = lambda n, sh, dt: p2a.enter_context(nc.sbuf_tensor(n, sh, dt))
    ps2 = lambda n, sh: p2a.enter_context(nc.psum_tensor(n, sh, F32))

    p_sb = [sb2(f"p_sb{q}", [128, PAD + S], BF16) for q in range(2)]
    NMC = [sb2(f"NMC{q}", [128, n_tc], F32) for q in range(2)]
    Erow = [sb2(f"Erow{q}", [128, n_tc], F32) for q in range(2)]
    rsrow = [sb2(f"rs{q}", [128, n_tc], F32) for q in range(2)]
    smallq = [sb2(f"smallq{q}", [128, 8], F32) for q in range(2)]
    D_sb = [[sb2(f"Dsb{q}{j}", [128, 128], BF16) for j in range(2)] for q in range(2)]
    pT_bf = [[sb2(f"pTbf{q}{j}", [128, 128], BF16) for j in range(4)] for q in range(2)]
    fstr = [sb2(f"fstr{j}", [128, D], BF16) for j in range(2)]
    fstr2 = [sb2(f"fstr2{j}", [128, D], BF16) for j in range(2)]
    ga_tile = sb2("ga_tile", [128, D], BF16)
    onepcb_tile = sb2("onepcb_tile", [128, D], F32)
    sbb_tile = sb2("sbb_tile", [128, D], BF16)
    bfix1 = sb2("bfix1", [128, 16], F32)
    bfix2 = sb2("bfix2", [128, 16], F32)
    xe = sb2("xe", [128, D], F32)
    _xp1 = sb2("xp_sb", [128, D], F32)
    _h21 = sb2("h2_sb", [128, D], F32)
    xp_sb = [_xp1, _xp1]
    h2_sb = [_h21, _h21]
    st2 = [sb2(f"st2{q}", [128, 16], F32) for q in range(2)]
    wv = [sb2(f"wv{j}", [128, 512], F32) for j in range(2)]
    yq = [sb2(f"yq{j}", [128, 512], I32) for j in range(2)]
    NSTG = 3
    stg = [sb2(f"stg{j}", [128, 512], BF16) for j in range(NSTG)]
    mcols = env["mcols"]

    e_ps = [ps2(f"e_ps{j}", [128, 512]) for j in range(4)]
    pT_ps = [e_ps[0], e_ps[1]]
    o_ps = [ps2(f"o_ps{q}", [128, D]) for q in range(2)]

    p2 = Phase("p2")
    dBC2 = sem("dBC2")
    dFS = [sem("dFS0"), sem("dFS1"), sem("dFS2")]
    dXE = sem("dXE")
    dXP = [sem("dXP0"), sem("dXP1")]
    dXH = [sem("dXH0"), sem("dXH1")]
    dPW = [sem("dPW0"), sem("dPW1")]
    dQSs = [sem(f"dQSs{i}") for i in range(NSTG)]

    bar = [(sS, b1["sS"]), (sV, b1["sV"]), (sT, b1["sT"]), (sG, b1["sG"])]
    p2.add("gpsimd", lambda e: e.nop(), waits=bar + marks["dSF_final"])
    for t_ in (smallq[0], smallq[1], st2[0], st2[1], NMC[0], NMC[1],
               Erow[0], Erow[1]):
        p2.add("gpsimd", lambda e, t_=t_: e.memset(t_[:], 0.0), sem=sG)
    g_init = sG.n
    bar2 = bar + [(sG, g_init)]
    for eng in ("scalar", "vector", "tensor"):
        p2.add(eng, lambda e: e.nop(), waits=bar2)
    p2.add("sync", lambda e: e.nop(), waits=bar2 + marks["dSF_final"])

    dMW_all = marks["dMW_all"]
    dMW = env["dMW"]
    p2.add("sync", lambda e: e.dma_start(
        out=onepcb_tile[:], in_=mod_hbm[1:2, :].broadcast_to((128, D))),
        waits=[(dMW, dMW_all)], sem=dBC2, inc=16)
    p2.add("sync", lambda e: e.dma_start(out=bfix1[:], in_=bf1_hbm), sem=dBC2, inc=16)
    p2.add("sync", lambda e: e.dma_start(out=bfix2[:], in_=bf2_hbm), sem=dBC2, inc=16)
    dBC2g = sem("dBC2g")
    dBC2g2 = sem("dBC2g2")
    p2.add("gpsimd", lambda e: e.dma_start(
        out=sbb_tile[:], in_=mod_hbm[2:3, :].broadcast_to((128, D))),
        waits=[(dMW, dMW_all)], sem=dBC2g, inc=16)
    p2.add("gpsimd", lambda e: e.dma_start(
        out=ga_tile[:], in_=mod_hbm[0:1, :].broadcast_to((128, D))),
        waits=[(dMW, dMW_all)], sem=dBC2g2, inc=16)
    p2.add("sync", lambda e: e.nop(),
           waits=[(dBC2g, dBC2g.n), (dBC2g2, dBC2g2.n)], sem=dBC2, inc=1)
    dBC2_all = dBC2.n

    WKIND = {"g": (wg_hbm, qg_hbm, scl_g, 0, D), "u": (wu_hbm, qu_hbm, scl_u, 22, D),
             "d": (wd_hbm, qd_hbm, scl_d, 44, DI)}
    amax_jobs = [("g", t) for t in range(22)] + [("u", t) for t in range(22)] + \
                [("d", t) for t in range(8)]
    strip_jobs = []
    for kind, t in amax_jobs:
        Wd = WKIND[kind][4]
        for c0 in range(0, Wd, 512):
            strip_jobs.append((kind, t, c0, min(512, Wd - c0)))
    amax_done = env["amax_done"]
    aj = [len(amax_jobs)]
    sq_i = [0]
    free_slots = list(range(NSTG))
    pending = []
    quant_gdone = {}

    def prep_quant(budget):
        for _ in range(budget):
            sj = sq_i[0]
            if sj >= len(strip_jobs) or not free_slots:
                return
            kind, t, c0, wdt = strip_jobs[sj]
            if (kind, t) not in amax_done:
                return
            sq_i[0] += 1
            src, dst, scl, mbase, Wd = WKIND[kind]
            slot = free_slots.pop(0)
            bslot = sj % 2
            mi = mbase + t
            w_ = []
            if (sj - 2) in quant_gdone:
                w_.append(quant_gdone[sj - 2])
            if (kind, t) not in amax_done:
                pass
            p2.add("sync", lambda e, src=src, t=t, c0=c0, wdt=wdt, bslot=bslot: e.dma_start(
                out=wv[bslot][:, 0:wdt], in_=src[128 * t:128 * (t + 1), c0:c0 + wdt]),
                waits=w_, sem=dPW[bslot], inc=16)
            dv = dPW[bslot].n
            p2.add("vector", lambda e, bslot=bslot, mi=mi, wdt=wdt: e.tensor_scalar(
                wv[bslot][:, 0:wdt], wv[bslot][:, 0:wdt], mcols[:, mi:mi + 1], None,
                ALU.mult), waits=[(dPW[bslot], dv), amax_done[(kind, t)], (sV, sV.n)], sem=sV)
            p2.add("vector", lambda e, bslot=bslot, wdt=wdt: e.tensor_copy(
                yq[bslot][:, 0:wdt], wv[bslot][:, 0:wdt]), waits=[(sV, sV.n)], sem=sV)
            p2.add("vector", lambda e, bslot=bslot, wdt=wdt: e.tensor_sub(
                wv[bslot][:, 0:wdt], wv[bslot][:, 0:wdt], yq[bslot][:, 0:wdt]),
                waits=[(sV, sV.n)], sem=sV)
            w_c2 = [(sV, sV.n), marks.get(("stgfree", slot), (sV, 0))]
            p2.add("vector", lambda e, bslot=bslot, slot=slot, wdt=wdt: e.tensor_scalar(
                stg[slot][:, 0:wdt], wv[bslot][:, 0:wdt], -0.5, None, ALU.is_lt),
                waits=w_c2, sem=sV)
            p2.add("vector", lambda e, bslot=bslot, slot=slot, wdt=wdt: e.scalar_tensor_tensor(
                wv[bslot][:, 0:wdt], wv[bslot][:, 0:wdt], 0.5, stg[slot][:, 0:wdt],
                ALU.is_gt, ALU.subtract), waits=[(sV, sV.n)], sem=sV)
            p2.add("vector", lambda e, bslot=bslot, slot=slot, wdt=wdt: e.tensor_add(
                stg[slot][:, 0:wdt], wv[bslot][:, 0:wdt], yq[bslot][:, 0:wdt]),
                waits=[(sV, sV.n)], sem=sV)
            quant_gdone[sj] = (sV, sV.n)
            pending.append((sj, slot, (sV, sV.n)))

    def prep_transpose(budget, pt_gate):
        last_evac = None
        for _ in range(budget):
            if not pending:
                break
            sj, slot, qmark = pending.pop(0)
            kind, t, c0, wdt = strip_jobs[sj]
            src, dst, scl, mbase, Wd = WKIND[kind]
            nch = wdt // 128
            ev = []
            for j in range(nch):
                w_ = [qmark]
                if pt_gate is not None and len(ev) == 0:
                    w_.append(pt_gate)
                bm = marks.get(("obank", j % 2))
                if bm is not None:
                    w_.append(bm)
                p2.add("tensor", lambda e, slot=slot, j=j: e.matmul(
                    o_ps[j % 2][:, 0:128], stg[slot][:, 128 * j:128 * (j + 1)],
                    ident_bf[:], start=True, stop=True), waits=w_, sem=sT)
                mmv = sT.n
                p2.add("scalar", lambda e, slot=slot, j=j: e.copy(
                    stg[slot][:, 128 * j:128 * (j + 1)], o_ps[j % 2][:, 0:128]),
                    waits=[(sT, mmv)], sem=sS)
                marks[("obank", j % 2)] = (sS, sS.n)
                ev.append(sS.n)
            if kind in ("g", "u"):
                p2.add("gpsimd", lambda e, dst=dst, t=t, c0=c0, wdt=wdt, slot=slot: e.dma_start(
                    out=dst[t, :, c0:c0 + wdt], in_=stg[slot][:, 0:wdt]),
                    waits=[(sS, ev[-1])], sem=dQSs[slot], inc=16)
            else:
                for j in range(nch):
                    ic = (c0 + 128 * j) // 128
                    p2.add("gpsimd", lambda e, dst=dst, ic=ic, t=t, j=j, slot=slot: e.dma_start(
                        out=dst[ic, :, 128 * t:128 * (t + 1)],
                        in_=stg[slot][:, 128 * j:128 * (j + 1)]),
                        waits=[(sS, ev[-1])], sem=dQSs[slot], inc=16)
            marks[("stgfree", slot)] = (dQSs[slot], dQSs[slot].n)
            free_slots.append(slot)
            last_evac = (sS, ev[-1])
        return last_evac

    m_expdone = [[0] * n_tc for _ in range(2)]
    m_exphist = []
    m_pvlast = None
    m_epidone = [None, None]
    marks["pv_prev"] = 0

    for pr in range(n_pairs):
        qts = (2 * pr, 2 * pr + 1)
        prep_quant(2)

        ss_exp_last = [None, None]
        sv_p1done = [None, None]
        tr_mark = None
        for qi, qt in enumerate(qts):
            for tc in range(n_tc):
                bank = len(m_exphist) % 4
                if len(m_exphist) >= 4:
                    w_ = [(sS, m_exphist[-4])]
                else:
                    w_ = [(dBC2, dBC2_all)]
                for k in range(NKC):
                    p2.add("tensor", lambda e, k=k, qt=qt, tc=tc, bank=bank: e.matmul(
                        e_ps[bank][:], f_feat[k][:, 128 * qt:128 * (qt + 1)],
                        f_feat[k][:, 512 * tc:512 * (tc + 1)],
                        start=(k == 0), stop=(k == NKC - 1)),
                        waits=w_ if k == 0 else [], sem=sT)
                st_e = sT.n
                p2.add("vector", lambda e, qi=qi, tc=tc, bank=bank: e.reduce_max(
                    NMC[qi][:, tc:tc + 1], e_ps[bank][:], AX, negate=True),
                    waits=[(sT, st_e)], sem=sV)
                sv_m = sV.n
                w_ = [(sV, sv_m)]
                if tc == 0 and pr > 0:
                    w_.append((sT, marks["pv_prev"]))
                p2.add("scalar", lambda e, qi=qi, tc=tc, bank=bank: e.activation(
                    p_sb[qi][:, PAD + 512 * tc:PAD + 512 * (tc + 1)], e_ps[bank][:],
                    AF.Exp, bias=NMC[qi][:, tc:tc + 1]),
                    waits=w_, sem=sS)
                m_expdone[qi][tc] = sS.n
                m_exphist.append(sS.n)
                marks[("ptbank", bank)] = (sS, sS.n)
            ss_exp_last[qi] = sS.n

            sq = smallq[qi]
            p2.add("vector", lambda e, qi=qi, sq=sq: e.tensor_reduce(
                sq[:, 0:1], NMC[qi][:], AX, ALU.min), waits=[(sV, sV.n)], sem=sV)
            p2.add("vector", lambda e, qi=qi, sq=sq: e.tensor_scalar(
                Erow[qi][:], NMC[qi][:], -1.0, sq[:, 0:1], ALU.mult, ALU.add),
                waits=[(sV, sV.n)], sem=sV)
            sv_er = sV.n
            p2.add("scalar", lambda e, qi=qi: e.activation(
                Erow[qi][:], Erow[qi][:], AF.Exp),
                waits=[(sV, sv_er), (sS, ss_exp_last[qi])], sem=sS)
            ss_ebf = sS.n
            W0 = 128 * qt - 9
            lo = max(W0, 0)
            hi = W0 + 146
            tpl_lo = lo - W0
            p2.add("vector", lambda e, qi=qi, lo=lo, hi=hi, tpl_lo=tpl_lo: e.tensor_mul(
                p_sb[qi][:, PAD + lo:PAD + hi], p_sb[qi][:, PAD + lo:PAD + hi],
                band[:, tpl_lo:tpl_lo + (hi - lo)]),
                waits=[(sS, ss_exp_last[qi])], sem=sV)
            if qt == n_qt - 1:
                p2.add("vector", lambda e, qi=qi: e.tensor_mul(
                    p_sb[qi][:, PAD + Q:PAD + Q + 9], p_sb[qi][:, PAD + Q:PAD + Q + 9],
                    bfix1[:, 0:9]), waits=[(sV, sV.n)], sem=sV)
            if qt == 0:
                p2.add("vector", lambda e, qi=qi: e.tensor_mul(
                    p_sb[qi][:, PAD + S - 9:PAD + S], p_sb[qi][:, PAD + S - 9:PAD + S],
                    bfix2[:, 0:9]), waits=[(sV, sV.n), (dBC2, dBC2_all)], sem=sV)
            sv_band = sV.n
            for tc in range(n_tc):
                p2.add("vector", lambda e, qi=qi, tc=tc: e.reduce_sum(
                    rsrow[qi][:, tc:tc + 1],
                    p_sb[qi][:, PAD + 512 * tc:PAD + 512 * (tc + 1)], AX),
                    waits=[(sV, sv_band)] if tc == 0 else [], sem=sV)
            p2.add("vector", lambda e, qi=qi: e.tensor_mul(
                rsrow[qi][:], rsrow[qi][:], Erow[qi][:]),
                waits=[(sV, sV.n), (sS, ss_ebf)], sem=sV)
            p2.add("vector", lambda e, qi=qi, sq=sq: e.reduce_sum(
                sq[:, 1:2], rsrow[qi][:], AX), waits=[(sV, sV.n)], sem=sV)
            p2.add("vector", lambda e, qi=qi, sq=sq: e.reciprocal(
                sq[:, 2:3], sq[:, 1:2]), waits=[(sV, sV.n)], sem=sV)
            sv_p1done[qi] = sV.n

            gate = m_pvlast if qi == 0 else None
            tr = prep_transpose(3, gate if tr_mark is None else None)
            tr_mark = tr if tr is not None else tr_mark
            prep_quant(3)

        sv_D = {}
        ss_pT = {}
        mm_pT = {}
        st_pvcons = [0] * (n_sc + 8)
        dfs_mark = {}

        def emit_D(sc):
            tc = sc // 4
            for qi in range(2):
                w_ = [(sS, ss_ebf), (sV, sv_p1done[qi])]
                if tc >= 2 and (4 * (tc - 2) + 3) in mm_pT:
                    w_.append((sT, mm_pT[4 * (tc - 2) + 3]))
                p2.add("vector", lambda e, qi=qi, tc=tc: e.tensor_scalar(
                    D_sb[qi][tc % 2][:], ident_bf[:], Erow[qi][:, tc:tc + 1],
                    None, ALU.mult), waits=w_, sem=sV)
                sv_D[(qi, tc)] = sV.n

        def emit_pT(sc):
            tc = sc // 4
            par = sc % 4
            for qi in range(2):
                w_ = [(sV, sv_D[(qi, tc)]), (sV, sv_p1done[qi])]
                bm = marks.get(("ptbank", par))
                if bm is not None:
                    w_.append(bm)
                if sc >= 4:
                    w_.append((sS, ss_pT[(qi, sc - 4)]))
                p2.add("tensor", lambda e, qi=qi, sc=sc, par=par, tc=tc: e.matmul(
                    e_ps[par][:, 0:128], p_sb[qi][:, PAD + 128 * sc:PAD + 128 * (sc + 1)],
                    D_sb[qi][tc % 2][:], start=True, stop=True),
                    waits=w_, sem=sT)
                mmv = sT.n
                mm_pT[sc] = mmv
                p2.add("scalar", lambda e, qi=qi, par=par: e.copy(
                    pT_bf[qi][par][:], e_ps[par][:, 0:128]), waits=[(sT, mmv)], sem=sS)
                marks[("ptbank", par)] = (sS, sS.n)
                ss_pT[(qi, sc)] = sS.n

        emit_D(0)
        for _s in range(min(2, n_sc)):
            emit_pT(_s)

        dfs_val = {}

        def emit_fload(blk):
            buf = blk % 2
            w_ = []
            if blk >= 2:
                w_.append((sT, st_pvcons[2 * (blk - 2) + 1]))
            elif pr > 0:
                w_.append((sT, marks["pv_prev"]))
            for h, fb in ((0, fstr), (1, fstr2)):
                p2.add("sync", lambda e, blk=blk, h=h, fb=fb, buf=buf: e.dma_start(
                    out=fb[buf][:],
                    in_=ftok_hbm[128 * (2 * blk + h):128 * (2 * blk + h + 1), :]),
                    waits=w_ if h == 0 else [], sem=dFS[buf], inc=16)
            dfs_val[blk] = dFS[buf].n

        emit_fload(0)
        for sc in range(n_sc):
            par = sc % 4
            buf = (sc // 2) % 2
            if sc % 2 == 0:
                if sc // 2 + 1 < n_sc // 2:
                    emit_fload(sc // 2 + 1)
                dfs_mark[buf] = dfs_val[sc // 2]
            if (sc + 3) % 4 == 0 and sc + 3 < n_sc:
                emit_D(sc + 3)
            if sc + 2 < n_sc:
                emit_pT(sc + 2)
            fs = fstr if sc % 2 == 0 else fstr2
            for qi in range(2):
                w_ = [(sS, ss_pT[(qi, sc)]), (dFS[buf], dfs_mark[buf])]
                if sc == 0 and m_epidone[qi] is not None:
                    w_.append(m_epidone[qi])
                if sc == 0 and tr_mark is not None:
                    w_.append(tr_mark)
                for h in range(2):
                    p2.add("tensor", lambda e, qi=qi, h=h, par=par, buf=buf, fs=fs,
                           sc=sc: e.matmul(
                        o_ps[qi][:, 512 * h:512 * (h + 1)], pT_bf[qi][par][:],
                        fs[buf][:, 512 * h:512 * (h + 1)],
                        start=(sc == 0), stop=(sc == n_sc - 1)),
                        waits=w_ if h == 0 else [], sem=sT)
            st_pvcons[sc] = sT.n
        marks["pv_prev"] = sT.n
        m_pvlast = (sT, sT.n)
        st_pv = sT.n

        for qi, qt in enumerate(qts):
            sq = smallq[qi]
            w_ = [marks["xe_free"]] if "xe_free" in marks else []
            p2.add("sync", lambda e, qt=qt: e.dma_start(
                out=xe[:], in_=x_hbm[128 * qt:128 * (qt + 1), :]),
                waits=w_, sem=dXE, inc=16)
            dxe = dXE.n
            w_ = [(sT, st_pv), (dBC2, dBC2_all)]
            for m in m_epidone:
                if m is not None:
                    w_.append(m)
            if marks.get("xpstore") is not None:
                w_.append(marks["xpstore"])
            p2.add("vector", lambda e, qi=qi, sq=sq: e.scalar_tensor_tensor(
                xp_sb[qi][:], o_ps[qi][:], sq[:, 2:3], ga_tile[:], ALU.mult, ALU.mult),
                waits=w_, sem=sV)
            marks[("obank", qi)] = (sV, sV.n)
            p2.add("vector", lambda e, qi=qi: e.tensor_add(
                xp_sb[qi][:], xp_sb[qi][:], xe[:]), waits=[(sV, sV.n), (dXE, dxe)], sem=sV)
            sv_xp = sV.n
            marks["xe_free"] = (sV, sv_xp)
            p2.add("gpsimd", lambda e, qi=qi, qt=qt: e.dma_start(
                out=xp_hbm[128 * qt:128 * (qt + 1), :], in_=xp_sb[qi][:]),
                waits=[(sV, sv_xp)], sem=dXP[qi], inc=16)
            marks["xpstore"] = (dXP[qi], dXP[qi].n)
            st_ = st2[qi]
            p2.add("vector", lambda e, qi=qi, st_=st_: e.bn_stats(
                st_[:, 0:6], xp_sb[qi][:, 0:512]), waits=[(sV, sv_xp)], sem=sV)
            p2.add("vector", lambda e, qi=qi, st_=st_: e.bn_stats(
                st_[:, 6:12], xp_sb[qi][:, 512:1024]), waits=[(sV, sV.n)], sem=sV)
            p2.add("vector", lambda e, st_=st_: e.bn_aggr(
                st_[:, 12:14], st_[:, 0:12]), waits=[(sV, sV.n)], sem=sV)
            p2.add("vector", lambda e, st_=st_: e.tensor_scalar(
                st_[:, 14:15], st_[:, 13:14], EPS, None, ALU.add), waits=[(sV, sV.n)], sem=sV)
            p2.add("vector", lambda e, st_=st_: e.reciprocal(
                st_[:, 15:16], st_[:, 14:15]), waits=[(sV, sV.n)], sem=sV)
            sv_rec = sV.n
            p2.add("scalar", lambda e, st_=st_: e.activation(
                st_[:, 0:1], st_[:, 15:16], AF.Sqrt), waits=[(sV, sv_rec)], sem=sS)
            ss_sq2 = sS.n
            p2.add("vector", lambda e, st_=st_: e.tensor_mul(
                st_[:, 1:2], st_[:, 0:1], st_[:, 0:1]), waits=[(sS, ss_sq2)], sem=sV)
            p2.add("vector", lambda e, st_=st_: e.tensor_mul(
                st_[:, 2:3], st_[:, 1:2], st_[:, 14:15]), waits=[(sV, sV.n)], sem=sV)
            p2.add("vector", lambda e, st_=st_: e.tensor_scalar(
                st_[:, 3:4], st_[:, 2:3], -0.5, 1.5, ALU.mult, ALU.add),
                waits=[(sV, sV.n)], sem=sV)
            p2.add("vector", lambda e, st_=st_: e.tensor_mul(
                st_[:, 4:5], st_[:, 0:1], st_[:, 3:4]), waits=[(sV, sV.n)], sem=sV)
            p2.add("vector", lambda e, st_=st_: e.tensor_mul(
                st_[:, 5:6], st_[:, 12:13], st_[:, 4:5]), waits=[(sV, sV.n)], sem=sV)
            p2.add("vector", lambda e, st_=st_: e.tensor_scalar(
                st_[:, 6:7], st_[:, 5:6], -1.0, None, ALU.mult), waits=[(sV, sV.n)], sem=sV)
            w_ = [(sV, sV.n)]
            if m_epidone[qi] is not None:
                w_.append(m_epidone[qi])
            p2.add("vector", lambda e, qi=qi, st_=st_: e.tensor_scalar(
                h2_sb[qi][:], xp_sb[qi][:], st_[:, 4:5], st_[:, 6:7], ALU.mult, ALU.add),
                waits=w_, sem=sV)
            p2.add("vector", lambda e, qi=qi: e.tensor_mul(
                h2_sb[qi][:], h2_sb[qi][:], onepcb_tile[:]), waits=[(sV, sV.n)], sem=sV)
            p2.add("vector", lambda e, qi=qi: e.tensor_add(
                h2_sb[qi][:], h2_sb[qi][:], sbb_tile[:]), waits=[(sV, sV.n)], sem=sV)
            sv_h2 = sV.n
            p2.add("vector", lambda e, qi=qi, st_=st_: e.reduce_max(
                st_[:, 7:8], h2_sb[qi][:], AX, apply_absolute_value=True),
                waits=[(sV, sv_h2)], sem=sV)
            p2.add("vector", lambda e, st_=st_: e.tensor_max(
                amax1[:], amax1[:], st_[:, 7:8]), waits=[(sV, sV.n)], sem=sV)
            p2.add("gpsimd", lambda e, qi=qi, qt=qt: e.dma_start(
                out=h2_hbm[128 * qt:128 * (qt + 1), :], in_=h2_sb[qi][:]),
                waits=[(sV, sv_h2)], sem=dXH[qi], inc=16)
            m_epidone[qi] = (dXH[qi], dXH[qi].n)

    while sq_i[0] < len(strip_jobs) or pending:
        before = (sq_i[0], len(pending))
        prep_quant(3)
        prep_transpose(6, None)
        if (sq_i[0], len(pending)) == before:
            raise RuntimeError("weight prep drain stuck")

    for eng, s_ in (("scalar", sS), ("vector", sV), ("tensor", sT), ("gpsimd", sG)):
        p2.add(eng, lambda e: e.nop(), sem=s_)
    marks["b2"] = {"sS": sS.n, "sV": sV.n, "sT": sT.n, "sG": sG.n,
                   "h2st": list(m_epidone), "dQS": [(s_, s_.n) for s_ in dQSs],
                   "xpst": [(dXP[0], dXP[0].n), (dXP[1], dXP[1].n), (dXH[0], dXH[0].n), (dXH[1], dXH[1].n)]}

    p2.emit(nc)
    p2a.close()
    env.update(dict(xp_hbm=xp_hbm, h2_hbm=h2_hbm, qg_hbm=qg_hbm, qu_hbm=qu_hbm,
                    qd_hbm=qd_hbm, scl_g=scl_g, scl_u=scl_u, scl_d=scl_d))
    return env


def build_p4(nc, ctx, env):
    S = env["S"]; Q = env["Q"]; n_qt = env["n_qt"]
    ident_bf = env["ident_bf"]; ident_f32 = env["ident_f32"]
    amax1 = env["amax1"]; amax2 = env["amax2"]; clip_sb = env["clip_sb"]
    h2_hbm = env["h2_hbm"]; qg_hbm = env["qg_hbm"]; qu_hbm = env["qu_hbm"]
    scl_g = env["scl_g"]; scl_u = env["scl_u"]
    sS = env["sS"]; sV = env["sV"]; sT = env["sT"]; sG = env["sG"]
    sem = ctx.sem
    marks = env["marks"]
    b2 = marks["b2"]
    n_tt = Q // 128
    n_tb = Q // 512

    hmid_hbm = nc.dram_tensor("hmid_hbm", [DI, Q], F32).ap()
    scal_hbm = nc.dram_tensor("scal_hbm", [4, 8], F32).ap()
    cc1_in = nc.dram_tensor("cc1_in", [1, 8], F32)
    cc1_out = nc.dram_tensor("cc1_out", [1, 8], F32)

    env["ffstack"].close()
    p4a = ExitStack()
    sb4 = lambda n, sh, dt: p4a.enter_context(nc.sbuf_tensor(n, sh, dt))
    ps4 = lambda n, sh: p4a.enter_context(nc.psum_tensor(n, sh, F32))

    qg_sb = [sb4(f"qg_sb{t}", [128, D], BF16) for t in range(22)]
    qu_sb = [sb4(f"qu_sb{t}", [128, D], BF16) for t in range(22)]
    xqT = [sb4(f"xqT{k}", [128, Q], BF16) for k in range(NKC)]
    h2b = [sb4(f"h2b{j}", [128, D], F32) for j in range(2)]
    qbuf = [sb4(f"qbuf{j}", [128, D], F32) for j in range(2)]
    ybuf = [sb4(f"ybuf{j}", [128, D], I32) for j in range(2)]
    cbuf = [sb4(f"cbuf{j}", [128, D], F32) for j in range(2)]
    xq_bf = [sb4(f"xq_bf{j}", [128, D], BF16) for j in range(2)]
    gate_f = [sb4(f"gate_f{j}", [128, 512], F32) for j in range(2)]
    hm_sb = [sb4(f"hm_sb{j}", [128, 512], F32) for j in range(2)]
    gsc = sb4("gsc", [1, 8], F32)
    mxcol = sb4("mxcol", [128, 1], F32)
    sxcol = sb4("sxcol", [128, 1], F32)
    am4 = sb4("am4", [128, 2], F32)
    sgx = sb4("sgx", [128, 22], F32)
    sux = sb4("sux", [128, 22], F32)

    psA = ps4("psA", [1, 128])
    ps_tr2 = [ps4(f"ps_tr2{j}", [128, 512]) for j in range(2)]
    psG = [ps4(f"psG{j}", [128, 512]) for j in range(2)]
    psU = [ps4(f"psU{j}", [128, 512]) for j in range(2)]

    p4 = Phase("p4")
    dH2 = [sem("dH2_0"), sem("dH2_1")]
    dQR = sem("dQR")
    dCC1 = sem("dCC1")
    cc1s = sem("cc1s")
    dSC = sem("dSC")
    dSCB = sem("dSCB")
    dHM = [sem("dHM0"), sem("dHM1")]

    bar = [(sS, b2["sS"]), (sV, b2["sV"]), (sT, b2["sT"]), (sG, b2["sG"])]
    p4.add("gpsimd", lambda e: e.nop(), waits=bar)
    for t_ in (sgx, sux, gsc, mxcol, sxcol):
        p4.add("gpsimd", lambda e, t_=t_: e.memset(t_[:], 0.0), sem=sG)
    g_init = sG.n
    bar2 = bar + [(sG, g_init)]
    for eng in ("scalar", "vector", "tensor"):
        p4.add(eng, lambda e: e.nop(), waits=bar2)
    p4.add("sync", lambda e: e.nop(),
           waits=bar2 + b2["h2st"] + b2["xpst"] + b2["dQS"])

    dqr_t = []
    for t in range(22):
        p4.add("sync", lambda e, t=t: e.dma_start(out=qg_sb[t][:], in_=qg_hbm[t]),
               sem=dQR, inc=16)
        p4.add("sync", lambda e, t=t: e.dma_start(out=qu_sb[t][:], in_=qu_hbm[t]),
               sem=dQR, inc=16)
        dqr_t.append(dQR.n)

    p4.add("tensor", lambda e: e.matmul(
        psA[:], amax1[:], ident_f32[:], start=True, stop=True), sem=sT)
    st_a = sT.n
    p4.add("vector", lambda e: e.reduce_max(gsc[0:1, 0:1], psA[0:1, :], AX),
           waits=[(sT, st_a)], sem=sV)
    sv_g1 = sV.n
    p4.add("gpsimd", lambda e: e.dma_start(out=cc1_in[0:1, :], in_=gsc[0:1, :]),
           waits=[(sV, sv_g1)], sem=dCC1, inc=16)
    p4.add("gpsimd", lambda e: e.collective_compute(
        "AllReduce", ALU.max, replica_groups=[list(range(env["n_cores"]))],
        ins=[cc1_in.ap().opt()], outs=[cc1_out.ap().opt()]),
        waits=[(dCC1, dCC1.n)], sem=cc1s)
    p4.add("gpsimd", lambda e: e.dma_start(out=gsc[0:1, 1:2], in_=cc1_out[0:1, 0:1]),
           waits=[(cc1s, cc1s.n)], sem=dCC1, inc=16)
    d_gmax = dCC1.n
    p4.add("scalar", lambda e: e.activation(gsc[0:1, 2:3], clip_sb[0:1, 0:1], AF.Sigmoid),
           waits=[(dCC1, d_gmax)], sem=sS)
    ss_sig = sS.n
    p4.add("vector", lambda e: e.tensor_mul(gsc[0:1, 3:4], gsc[0:1, 1:2], gsc[0:1, 2:3]),
           waits=[(sS, ss_sig), (dCC1, d_gmax)], sem=sV)
    p4.add("vector", lambda e: e.tensor_scalar(
        gsc[0:1, 3:4], gsc[0:1, 3:4], 0.007874015748031496, None, ALU.mult), waits=[(sV, sV.n)], sem=sV)
    p4.add("vector", lambda e: e.reciprocal(gsc[0:1, 4:5], gsc[0:1, 3:4]),
           waits=[(sV, sV.n)], sem=sV)
    sv_mx = sV.n
    p4.add("gpsimd", lambda e: e.dma_start(out=scal_hbm[0:1, :], in_=gsc[0:1, :]),
           waits=[(sV, sv_mx)], sem=dSC, inc=16)
    p4.add("scalar", lambda e: e.dma_start(
        out=mxcol[:], in_=scal_hbm[0:1, 4:5].broadcast_to((128, 1))),
        waits=[(dSC, dSC.n)], sem=dSCB, inc=16)
    p4.add("scalar", lambda e: e.dma_start(
        out=sxcol[:], in_=scal_hbm[0:1, 3:4].broadcast_to((128, 1))),
        sem=dSCB, inc=16)
    d_bc = dSCB.n
    p4.add("vector", lambda e: e.tensor_scalar(
        sgx[:], scl_g[:], sxcol[:, 0:1], None, ALU.mult), waits=[(dSCB, d_bc)], sem=sV)
    p4.add("vector", lambda e: e.tensor_scalar(
        sux[:], scl_u[:], sxcol[:, 0:1], None, ALU.mult), waits=[(sV, sV.n)], sem=sV)
    sv_scales = sV.n

    m_quant = [None] * (n_tt + 4)
    m_h2free = [None] * (n_tt + 4)
    m_trbank = [None, None]
    quant_done = [0]

    def quant_tile(j):
        pb = j % 2
        w_ = [m_h2free[j - 2]] if j >= 2 else []
        p4.add("sync", lambda e, j=j, pb=pb: e.dma_start(
            out=h2b[pb][:], in_=h2_hbm[128 * j:128 * (j + 1), :]),
            waits=w_, sem=dH2[pb], inc=16)
        dv = dH2[pb].n
        w_ = [(dH2[pb], dv), (sV, sv_scales)]
        if j >= 2:
            w_.append(m_quant[j - 2])
        p4.add("vector", lambda e, pb=pb: e.tensor_scalar(
            qbuf[pb][:], h2b[pb][:], mxcol[:, 0:1], None, ALU.mult), waits=w_, sem=sV)
        m_h2free[j] = (sV, sV.n)
        p4.add("vector", lambda e, pb=pb: e.tensor_copy(ybuf[pb][:], qbuf[pb][:]),
               sem=sV)
        p4.add("vector", lambda e, pb=pb: e.tensor_sub(
            qbuf[pb][:], qbuf[pb][:], ybuf[pb][:]), sem=sV)
        p4.add("vector", lambda e, pb=pb: e.tensor_scalar(
            cbuf[pb][:], qbuf[pb][:], -0.5, None, ALU.is_lt), sem=sV)
        p4.add("vector", lambda e, pb=pb: e.scalar_tensor_tensor(
            qbuf[pb][:], qbuf[pb][:], 0.5, cbuf[pb][:], ALU.is_gt, ALU.subtract),
            sem=sV)
        p4.add("vector", lambda e, pb=pb: e.tensor_add(
            cbuf[pb][:], qbuf[pb][:], ybuf[pb][:]), sem=sV)
        p4.add("vector", lambda e, pb=pb: e.tensor_scalar(
            xq_bf[pb][:], cbuf[pb][:], 127.0, -128.0, ALU.min, ALU.max),
            sem=sV)
        sv_xq = sV.n
        for k in range(NKC):
            w_ = [(sV, sv_xq)]
            if m_trbank[k % 2] is not None:
                w_.append(m_trbank[k % 2])
            p4.add("tensor", lambda e, pb=pb, k=k: e.matmul(
                ps_tr2[k % 2][:, 0:128], xq_bf[pb][:, 128 * k:128 * (k + 1)],
                ident_bf[:], start=True, stop=True), waits=w_, sem=sT)
            mmv = sT.n
            p4.add("scalar", lambda e, k=k, j=j: e.copy(
                xqT[k][:, 128 * j:128 * (j + 1)], ps_tr2[k % 2][:, 0:128]),
                waits=[(sT, mmv)], sem=sS)
            m_trbank[k % 2] = (sS, sS.n)
        m_quant[j] = (sS, sS.n)

    mGU = {}
    m_gevac = {}
    m_hmst = {}
    for tb in range(n_tb):
        while quant_done[0] < min(n_tt, 4 * (tb + 1)):
            quant_tile(quant_done[0])
            quant_done[0] += 1
        for t in range(22):
            pb = t % 2
            w_ = [m_quant[4 * tb + 3] if 4 * tb + 3 < n_tt else m_quant[n_tt - 1],
                  (dQR, dqr_t[t])]
            if ("ge", pb) in m_gevac:
                w_.append(m_gevac[("ge", pb)])
            for k in range(NKC):
                p4.add("tensor", lambda e, t=t, tb=tb, k=k, pb=pb: e.matmul(
                    psG[pb][:], qg_sb[t][:, 128 * k:128 * (k + 1)],
                    xqT[k][:, 512 * tb:512 * (tb + 1)],
                    start=(k == 0), stop=(k == NKC - 1)),
                    waits=w_ if k == 0 else [], sem=sT)
            mGU[(tb, t, "g")] = (sT, sT.n)
            st_g = sT.n
            w_ = []
            if (("ue", pb)) in m_gevac:
                w_.append(m_gevac[("ue", pb)])
            for k in range(NKC):
                p4.add("tensor", lambda e, t=t, tb=tb, k=k, pb=pb: e.matmul(
                    psU[pb][:], qu_sb[t][:, 128 * k:128 * (k + 1)],
                    xqT[k][:, 512 * tb:512 * (tb + 1)],
                    start=(k == 0), stop=(k == NKC - 1)),
                    waits=w_ if k == 0 else [], sem=sT)
            st_u = sT.n
            w_ = [(sT, st_g)]
            if ("gf", pb) in m_gevac:
                w_.append(m_gevac[("gf", pb)])
            p4.add("scalar", lambda e, t=t, pb=pb: e.activation(
                gate_f[pb][:], psG[pb][:], AF.Silu, scale=sgx[:, t:t + 1]),
                waits=w_, sem=sS)
            m_gevac[("ge", pb)] = (sS, sS.n)
            ss_gf = sS.n
            w_ = [(sT, st_u), (sS, ss_gf)]
            if ("hm", pb) in m_gevac:
                w_.append(m_gevac[("hm", pb)])
            if ("hmst", pb) in m_gevac:
                w_.append(m_gevac[("hmst", pb)])
            p4.add("vector", lambda e, t=t, pb=pb: e.scalar_tensor_tensor(
                hm_sb[pb][:], psU[pb][:], sux[:, t:t + 1], gate_f[pb][:],
                ALU.mult, ALU.mult), waits=w_, sem=sV)
            m_gevac[("ue", pb)] = (sV, sV.n)
            m_gevac[("gf", pb)] = (sV, sV.n)
            sv_hm = sV.n
            p4.add("vector", lambda e, pb=pb: e.reduce_max(
                am4[:, 0:1], hm_sb[pb][:], AX, apply_absolute_value=True),
                waits=[(sV, sv_hm)], sem=sV)
            p4.add("vector", lambda e: e.tensor_max(
                amax2[:], amax2[:], am4[:, 0:1]), waits=[(sV, sV.n)], sem=sV)
            p4.add("gpsimd", lambda e, t=t, tb=tb, pb=pb: e.dma_start(
                out=hmid_hbm[128 * t:128 * (t + 1), 512 * tb:512 * (tb + 1)],
                in_=hm_sb[pb][:]), waits=[(sV, sv_hm)], sem=dHM[pb], inc=16)
            m_gevac[("hmst", pb)] = (dHM[pb], dHM[pb].n)

    for eng, s_ in (("scalar", sS), ("vector", sV), ("tensor", sT), ("gpsimd", sG)):
        p4.add(eng, lambda e: e.nop(), sem=s_)
    marks["b4"] = {"sS": sS.n, "sV": sV.n, "sT": sT.n, "sG": sG.n,
                   "hmst": [m_gevac.get(("hmst", 0)), m_gevac.get(("hmst", 1))],
                   "xpst": b2["xpst"]}

    p4.emit(nc)
    p4a.close()
    env.update(dict(hmid_hbm=hmid_hbm, scal_hbm=scal_hbm))
    return env


def build_p5(nc, ctx, env):
    S = env["S"]; Q = env["Q"]
    ident_f32 = env["ident_f32"]
    amax2 = env["amax2"]; clip_sb = env["clip_sb"]
    hmid_hbm = env["hmid_hbm"]; scal_hbm = env["scal_hbm"]
    qd_hbm = env["qd_hbm"]; xp_hbm = env["xp_hbm"]; mod_hbm = env["mod_hbm"]
    out_hbm = env["out_hbm"]
    scl_d = env["scl_d"]
    sS = env["sS"]; sV = env["sV"]; sT = env["sT"]; sG = env["sG"]
    sem = ctx.sem
    marks = env["marks"]
    b4 = marks["b4"]
    n_tt = Q // 128
    nhalf = Q // 2

    cc2_in = nc.dram_tensor("cc2_in", [1, 8], F32)
    cc2_out = nc.dram_tensor("cc2_out", [1, 8], F32)
    sd_hbm = nc.dram_tensor("sd_hbm", [8, 128], F32).ap()

    p5a = ExitStack()
    sb5 = lambda n, sh, dt: p5a.enter_context(nc.sbuf_tensor(n, sh, dt))
    ps5 = lambda n, sh: p5a.enter_context(nc.psum_tensor(n, sh, F32))

    qd_sb = [sb5(f"qd_sb{t}", [128, D], BF16) for t in range(22)]
    hq = [sb5(f"hq{t}", [128, nhalf], BF16) for t in range(22)]
    hmb = [sb5(f"hmb{j}", [128, nhalf], F32) for j in range(2)]
    y5 = sb5("y5", [128, nhalf], I32)
    c5 = sb5("c5", [128, nhalf], F32)
    xpb = [sb5(f"xpb{j}", [128, D], F32) for j in range(2)]
    outb = [sb5(f"outb{j}", [128, D], F32) for j in range(2)]
    gb_tile = sb5("gb_tile", [128, D], F32)
    sdb_tile = sb5("sdb_tile", [128, D], F32)
    sd_sb = sb5("sd_sb", [8, 128], F32)
    g5 = sb5("g5", [1, 8], F32)
    mdcol = sb5("mdcol", [128, 1], F32)
    sdcol = sb5("sdcol", [128, 1], F32)

    psA2 = ps5("psA2", [8, 128])
    psB2 = ps5("psB2", [1, 128])
    psD5 = [ps5(f"psD5{j}", [128, D]) for j in range(2)]

    p5 = Phase("p5")
    dHM5 = [sem("dHM5_0"), sem("dHM5_1")]
    dQD = sem("dQD")
    dCC2 = sem("dCC2")
    cc2s = sem("cc2s")
    dSD = sem("dSD")
    dSD2 = sem("dSD2")
    dSDB = sem("dSDB")
    dXPL = [sem("dXPL0"), sem("dXPL1")]
    dOUT = [sem("dOUT0"), sem("dOUT1")]

    bar = [(sS, b4["sS"]), (sV, b4["sV"]), (sT, b4["sT"]), (sG, b4["sG"])]
    p5.add("gpsimd", lambda e: e.nop(), waits=bar)
    for t_ in (g5, mdcol, sdcol, sd_sb):
        p5.add("gpsimd", lambda e, t_=t_: e.memset(t_[:], 0.0), sem=sG)
    g_init = sG.n
    bar2 = bar + [(sG, g_init)]
    for eng in ("scalar", "vector", "tensor"):
        p5.add(eng, lambda e: e.nop(), waits=bar2)
    p5.add("sync", lambda e: e.nop(),
           waits=bar2 + [m for m in b4["hmst"] if m] + b4["xpst"])

    for t in range(22):
        p5.add("sync", lambda e, t=t: e.dma_start(out=qd_sb[t][:], in_=qd_hbm[t]),
               sem=dQD, inc=16)
    dQD_all = dQD.n

    p5.add("tensor", lambda e: e.matmul(
        psB2[:], amax2[:], ident_f32[:], start=True, stop=True), sem=sT)
    st_a = sT.n
    p5.add("vector", lambda e: e.reduce_max(g5[0:1, 0:1], psB2[0:1, :], AX),
           waits=[(sT, st_a)], sem=sV)
    sv_g1 = sV.n
    p5.add("gpsimd", lambda e: e.dma_start(out=cc2_in[0:1, :], in_=g5[0:1, :]),
           waits=[(sV, sv_g1)], sem=dCC2, inc=16)
    p5.add("gpsimd", lambda e: e.collective_compute(
        "AllReduce", ALU.max, replica_groups=[list(range(env["n_cores"]))],
        ins=[cc2_in.ap().opt()], outs=[cc2_out.ap().opt()]),
        waits=[(dCC2, dCC2.n)], sem=cc2s)
    p5.add("gpsimd", lambda e: e.dma_start(out=g5[0:1, 1:2], in_=cc2_out[0:1, 0:1]),
           waits=[(cc2s, cc2s.n)], sem=dCC2, inc=16)
    d_gmax = dCC2.n
    p5.add("scalar", lambda e: e.activation(g5[0:1, 2:3], clip_sb[0:1, 1:2], AF.Sigmoid),
           waits=[(dCC2, d_gmax)], sem=sS)
    ss_sig = sS.n
    p5.add("vector", lambda e: e.tensor_mul(g5[0:1, 3:4], g5[0:1, 1:2], g5[0:1, 2:3]),
           waits=[(sS, ss_sig), (dCC2, d_gmax)], sem=sV)
    p5.add("vector", lambda e: e.tensor_scalar(
        g5[0:1, 3:4], g5[0:1, 3:4], 0.007874015748031496, None, ALU.mult), waits=[(sV, sV.n)], sem=sV)
    p5.add("vector", lambda e: e.reciprocal(g5[0:1, 4:5], g5[0:1, 3:4]),
           waits=[(sV, sV.n)], sem=sV)
    sv_md = sV.n
    p5.add("gpsimd", lambda e: e.dma_start(out=scal_hbm[1:2, :], in_=g5[0:1, :]),
           waits=[(sV, sv_md)], sem=dSD, inc=16)
    dsd1 = dSD.n
    p5.add("scalar", lambda e: e.dma_start(
        out=mdcol[:], in_=scal_hbm[1:2, 4:5].broadcast_to((128, 1))),
        waits=[(dSD, dsd1)], sem=dSDB, inc=16)
    p5.add("scalar", lambda e: e.dma_start(
        out=sdcol[:], in_=scal_hbm[1:2, 3:4].broadcast_to((128, 1))),
        sem=dSDB, inc=16)
    d_bc = dSDB.n

    p5.add("tensor", lambda e: e.matmul(
        psA2[:], scl_d[:], ident_f32[:], start=True, stop=True), sem=sT)
    st_sd = sT.n
    p5.add("scalar", lambda e: e.copy(sd_sb[:], psA2[:]), waits=[(sT, st_sd)], sem=sS)
    ss_sd = sS.n
    p5.add("gpsimd", lambda e: e.dma_start(out=sd_hbm, in_=sd_sb[:]),
           waits=[(sS, ss_sd)], sem=dSD2, inc=16)
    dsd2 = dSD2.n
    p5.add("scalar", lambda e: e.dma_start(
        out=sdb_tile[:],
        in_=sd_hbm.rearrange("a b -> (a b)").rearrange("(o d) -> o d", o=1).broadcast_to((128, D))),
        waits=[(dSD2, dsd2)], sem=dSDB, inc=16)
    p5.add("scalar", lambda e: e.dma_start(
        out=gb_tile[:], in_=mod_hbm[3:4, :].broadcast_to((128, D))),
        sem=dSDB, inc=16)
    d_bc2 = dSDB.n
    p5.add("vector", lambda e: e.tensor_mul(gb_tile[:], gb_tile[:], sdb_tile[:]),
           waits=[(dSDB, d_bc2)], sem=sV)
    p5.add("vector", lambda e: e.tensor_scalar(
        gb_tile[:], gb_tile[:], sdcol[:, 0:1], None, ALU.mult),
        waits=[(sV, sV.n)], sem=sV)
    sv_gp = sV.n

    m_outst = {}
    m_psd = {}
    for half in range(2):
        hq_marks = []
        for ic in range(22):
            pb = ic % 2
            w_ = []
            if ic >= 2:
                w_.append(hq_marks[ic - 2])
            elif half == 1:
                w_.append(marks["p5_half0_done"])
            p5.add("sync", lambda e, ic=ic, half=half, pb=pb: e.dma_start(
                out=hmb[pb][:], in_=hmid_hbm[128 * ic:128 * (ic + 1),
                                            nhalf * half:nhalf * (half + 1)]),
                waits=w_, sem=dHM5[pb], inc=16)
            dv = dHM5[pb].n
            w_ = [(dHM5[pb], dv), (sV, sV.n)]
            if ic == 0 and half == 0:
                w_.append((dSDB, d_bc2))
            p5.add("vector", lambda e, pb=pb: e.tensor_scalar(
                hmb[pb][:], hmb[pb][:], mdcol[:, 0:1], None, ALU.mult), waits=w_, sem=sV)
            p5.add("vector", lambda e, pb=pb: e.tensor_copy(y5[:], hmb[pb][:]),
                   sem=sV)
            p5.add("vector", lambda e, pb=pb: e.tensor_sub(
                hmb[pb][:], hmb[pb][:], y5[:]), sem=sV)
            p5.add("vector", lambda e, pb=pb: e.tensor_scalar(
                c5[:], hmb[pb][:], -0.5, None, ALU.is_lt), sem=sV)
            p5.add("vector", lambda e, pb=pb: e.scalar_tensor_tensor(
                hmb[pb][:], hmb[pb][:], 0.5, c5[:], ALU.is_gt, ALU.subtract),
                sem=sV)
            p5.add("vector", lambda e, pb=pb: e.tensor_add(
                c5[:], hmb[pb][:], y5[:]), sem=sV)
            p5.add("vector", lambda e, ic=ic, pb=pb: e.tensor_scalar(
                hq[ic][:], c5[:], 127.0, -128.0, ALU.min, ALU.max),
                sem=sV)
            hq_marks.append((sV, sV.n))
        hq_all = (sV, sV.n)

        for jj in range(n_tt // 2):
            j = (n_tt // 2) * half + jj
            pb = jj % 2
            w_ = [m_outst[pb]] if pb in m_outst else []
            p5.add("sync", lambda e, j=j, pb=pb: e.dma_start(
                out=xpb[pb][:], in_=xp_hbm[128 * j:128 * (j + 1), :]),
                waits=w_, sem=dXPL[pb], inc=16)
            dxp = dXPL[pb].n
            for hc in range(2):
                w_ = [(dQD, dQD_all)]
                if (pb, hc) in m_psd:
                    w_.append(m_psd[(pb, hc)])
                for ic in range(22):
                    if jj == 0 and hc == 0:
                        wic = w_ + [hq_marks[ic]] if ic == 0 else [hq_marks[ic]]
                    else:
                        wic = w_ if ic == 0 else []
                    p5.add("tensor", lambda e, ic=ic, jj=jj, hc=hc, pb=pb: e.matmul(
                        psD5[pb][:, 512 * hc:512 * (hc + 1)],
                        hq[ic][:, 128 * jj:128 * (jj + 1)],
                        qd_sb[ic][:, 512 * hc:512 * (hc + 1)],
                        start=(ic == 0), stop=(ic == 21)),
                        waits=wic, sem=sT)
            st_d = sT.n
            w_ = [(sT, st_d), (sV, sv_gp), (dXPL[pb], dxp)]
            if pb in m_outst:
                w_.append(m_outst[pb])
            p5.add("vector", lambda e, pb=pb: e.tensor_mul(
                outb[pb][:], psD5[pb][:], gb_tile[:]), waits=w_, sem=sV)
            m_psd[(pb, 0)] = (sV, sV.n)
            m_psd[(pb, 1)] = (sV, sV.n)
            p5.add("vector", lambda e, pb=pb: e.tensor_add(
                outb[pb][:], outb[pb][:], xpb[pb][:]), waits=[(sV, sV.n)], sem=sV)
            sv_o = sV.n
            p5.add("gpsimd", lambda e, j=j, pb=pb: e.dma_start(
                out=out_hbm[128 * j:128 * (j + 1), :], in_=outb[pb][:]),
                waits=[(sV, sv_o)], sem=dOUT[pb], inc=16)
            m_outst[pb] = (dOUT[pb], dOUT[pb].n)
        marks["p5_half0_done"] = (sT, sT.n)

    p5.add("gpsimd", lambda e: e.nop(),
           waits=[(dOUT[0], dOUT[0].n), (dOUT[1], dOUT[1].n)])
    p5.emit(nc)
    p5a.close()
    return env



from concourse.bass_utils import run_bass_kernel_spmd

_S = 8192
_Q = _S // 2


def _bandfix_tiles(qh):
    bf1 = np.ones((128, 16), np.float32)
    bf2 = np.ones((128, 16), np.float32)
    if qh == 1:
        for r in range(128):
            for u in range(9):
                d = u + 128 - r
                bf1[r, u] = np.exp(0.01 * max(0, 10 - d), dtype=np.float64).astype(np.float32)
                dg = r + 9 - u
                bf2[r, u] = np.exp(-0.01 * max(0, 10 - dg), dtype=np.float64).astype(np.float32)
    return bf1, bf2


_NC_CACHE = {}


def kernel(L0_attn_x, L0_attn_W, L0_attn_b, L0_ffn_clip_x, L0_ffn_clip_down,
           L0_ffn_w_gate, L0_ffn_w_up, L0_ffn_w_down, L0_c, L0_wmod):
    x = np.ascontiguousarray(np.asarray(L0_attn_x, dtype=np.float32))
    W = np.ascontiguousarray(np.asarray(L0_attn_W, dtype=np.float32))
    battn = np.ascontiguousarray(np.asarray(L0_attn_b, dtype=np.float32)).reshape(1, D)
    cx = np.ascontiguousarray(np.asarray(L0_ffn_clip_x, dtype=np.float32)).reshape(1, 1)
    cd = np.ascontiguousarray(np.asarray(L0_ffn_clip_down, dtype=np.float32)).reshape(1, 1)
    wg = np.ascontiguousarray(np.asarray(L0_ffn_w_gate, dtype=np.float32))
    wu = np.ascontiguousarray(np.asarray(L0_ffn_w_up, dtype=np.float32))
    wd = np.ascontiguousarray(np.asarray(L0_ffn_w_down, dtype=np.float32))
    cmat = np.ascontiguousarray(np.asarray(L0_c, dtype=np.float32))
    wmod = np.ascontiguousarray(np.asarray(L0_wmod, dtype=np.float32))

    if "nc" not in _NC_CACHE:
        _NC_CACHE["nc"] = build_nc(S=_S, dev=0, n_cores=8)
    nc = _NC_CACHE["nc"]

    in_maps = []
    for corei in range(8):
        b_, qh = corei // 2, corei % 2
        xb = np.ascontiguousarray(np.roll(x[b_], -_Q * qh, axis=0))
        bf1, bf2 = _bandfix_tiles(qh)
        in_maps.append(dict(
            x_full=xb, W=W, attn_b=battn, c_row=cmat[b_:b_ + 1],
            wmod=wmod, clip_x=cx, clip_down=cd,
            w_gate=wg, w_up=wu, w_down=wd,
            bandfix1=bf1, bandfix2=bf2))
    res = run_bass_kernel_spmd(nc, in_maps, core_ids=list(range(8)))
    out = np.empty((4, _S, D), np.float32)
    for corei in range(8):
        b_, qh = corei // 2, corei % 2
        out[b_, _Q * qh:_Q * (qh + 1)] = res.results[corei]["out"]
    return out


# version 13
# speedup vs baseline: 1.0857x; 1.0161x over previous
from contextlib import ExitStack

import numpy as np
import concourse.bass as bass
import concourse.mybir as mybir

F32 = mybir.dt.float32
BF16 = mybir.dt.bfloat16
I32 = mybir.dt.int32
ALU = mybir.AluOpType
AF = mybir.ActivationFunctionType
AX = mybir.AxisListType.X


class Sem:
    def __init__(self, nc, stack, name):
        self.h = stack.enter_context(nc.semaphore(name))
        self.n = 0

    def inc(self, k=1):
        self.n += k
        return self.n


class Item:
    __slots__ = ("eng", "emit", "waits", "inc_sem", "inc_amt")

    def __init__(self, eng, emit, waits, inc_sem, inc_amt):
        self.eng = eng
        self.emit = emit
        self.waits = waits
        self.inc_sem = inc_sem
        self.inc_amt = inc_amt


class Phase:
    ENGINES = ("sync", "scalar", "vector", "tensor", "gpsimd")

    def __init__(self, name):
        self.name = name
        self.items = []

    def add(self, eng, emit, waits=(), sem=None, inc=1):
        assert eng in Phase.ENGINES, eng
        waits = list(waits)
        if sem is not None and not waits:
            waits = [(sem, sem.n)]
        if sem is not None:
            sem.inc(inc)
        self.items.append(Item(eng, emit, waits, sem, inc))

    def emit(self, nc):
        with nc.Block() as blk:
            for eng_name in Phase.ENGINES:
                items = [it for it in self.items if it.eng == eng_name]
                if not items:
                    continue

                def body(e, items=items):
                    for it in items:
                        for (s, v) in it.waits:
                            e.wait_ge(s.h, v)
                    raise RuntimeError("unused")

                def body2(e, items=items):
                    for it in items:
                        for (s, v) in it.waits:
                            e.wait_ge(s.h, v)
                        inst = it.emit(e)
                        if it.inc_sem is not None:
                            inst.then_inc(it.inc_sem.h, it.inc_amt)

                getattr(blk, eng_name)(body2)


class Ctx:
    def __init__(self, nc, stack):
        self.nc = nc
        self.stack = stack
        self.sems = {}

    def sem(self, name):
        if name not in self.sems:
            self.sems[name] = Sem(self.nc, self.stack, name)
        return self.sems[name]

    def sbuf(self, name, shape, dtype):
        return self.stack.enter_context(self.nc.sbuf_tensor(name, shape, dtype))

    def psum(self, name, shape, dtype=F32):
        return self.stack.enter_context(self.nc.psum_tensor(name, shape, dtype))


"""adaln megakernel builder. Phases:
P0: constants, modulation (mod = silu(c)@wmod), b' = b + sa@W, transforms
P1: f = (LN(x)*(1+ca))@W + b' for all S tokens -> f_feat (SBUF) + f_tok (HBM)
P2: flash attention + residual + LN2 -> x', h2 spilled; h2 absmax
P3: weight quant+transpose (overlapped later), collective max #1
P4: gate/up + silu*up -> hmid spilled; absmax; collective max #2
P5: down + residual -> out
"""
import numpy as np
import concourse.bass as bass
import concourse.mybir as mybir

D = 1024
DI = 2816
NKC = 8
EPS = 1e-5


def build_nc(S=8192, dev=0, n_cores=8):
    nc = bass.Bass("TRN2", target_bir_lowering=False)
    Q = S // 2
    n_xt = S // 128
    n_qt = Q // 128
    stack = ExitStack()
    ctx = Ctx(nc, stack)
    sb, ps, sem = ctx.sbuf, ctx.psum, ctx.sem

    x_hbm = nc.dram_tensor("x_full", [S, D], F32, kind="ExternalInput").ap()
    W_hbm = nc.dram_tensor("W", [D, D], F32, kind="ExternalInput").ap()
    b_hbm = nc.dram_tensor("attn_b", [1, D], F32, kind="ExternalInput").ap()
    c_hbm = nc.dram_tensor("c_row", [1, D], F32, kind="ExternalInput").ap()
    wmod_hbm = nc.dram_tensor("wmod", [D, 6 * D], F32, kind="ExternalInput").ap()
    clipx_hbm = nc.dram_tensor("clip_x", [1, 1], F32, kind="ExternalInput").ap()
    clipd_hbm = nc.dram_tensor("clip_down", [1, 1], F32, kind="ExternalInput").ap()
    wg_hbm = nc.dram_tensor("w_gate", [DI, D], F32, kind="ExternalInput").ap()
    wu_hbm = nc.dram_tensor("w_up", [DI, D], F32, kind="ExternalInput").ap()
    wd_hbm = nc.dram_tensor("w_down", [D, DI], F32, kind="ExternalInput").ap()
    out_hbm = nc.dram_tensor("out", [Q, D], F32, kind="ExternalOutput").ap()

    ftok_hbm = nc.dram_tensor("ftok_hbm", [S, D], BF16).ap()
    mod_hbm = nc.dram_tensor("mod_hbm", [8, D], F32).ap()

    dev_f = dev_ftk = None
    if dev & 1:
        dev_f = nc.dram_tensor("dev_ffeat", [NKC * 128, S], BF16, kind="ExternalOutput").ap()

    ident_bf = sb("ident_bf", [128, 128], BF16)
    ident_f32 = sb("ident_f32", [128, 128], F32)
    band = sb("band", [128, 160], F32)
    amax1 = sb("amax1", [128, 1], F32)
    amax2 = sb("amax2", [128, 1], F32)
    one_bf = sb("one_bf", [1, 1], BF16)
    one_f32 = sb("one_f32", [1, 1], F32)
    clip_sb = sb("clip_sb", [1, 2], F32)
    scl_g = sb("scl_g", [128, 22], F32)
    scl_u = sb("scl_u", [128, 22], F32)
    scl_d = sb("scl_d", [128, 8], F32)
    mcols = sb("mcols", [128, 52], F32)
    ffstack = ExitStack()
    f_feat = [ffstack.enter_context(nc.sbuf_tensor(f"f_feat{k}", [128, S], BF16))
              for k in range(NKC)]

    dC = sem("dC")
    dW = [sem("dW0"), sem("dW1")]
    dWM = [sem("dWM0"), sem("dWM1"), sem("dWM2")]
    dBC = sem("dBC")
    dX = [sem("dX0"), sem("dX1")]
    dSF = [sem("dSF0"), sem("dSF1")]
    dMW = sem("dMW")
    sS = sem("sS")
    sV = sem("sV")
    sT = sem("sT")
    sG = sem("sG")

    p0 = Phase("p0p1")
    marks = {}

    p0a = ExitStack()
    p0b = ExitStack()
    p0early = ExitStack()
    p0sb = lambda n, sh, dt: p0b.enter_context(nc.sbuf_tensor(n, sh, dt))
    p0ps = lambda n, sh: p0early.enter_context(nc.psum_tensor(n, sh, F32))
    W_bf = p0a.enter_context(nc.sbuf_tensor("W_bf", [128, NKC * D], BF16))
    opca_feat = p0a.enter_context(nc.sbuf_tensor("opca_feat", [128, NKC], F32))
    bp_tile = p0a.enter_context(nc.sbuf_tensor("bp_tile", [128, D], F32))
    abuf = [p0a.enter_context(nc.sbuf_tensor(f"abuf{j}", [128, 704], F32))
            for j in range(2)]
    amx_c = p0a.enter_context(nc.sbuf_tensor("amx_c", [128, 4], F32))

    c_sb = p0sb("c_sb", [1, D], F32)
    b_sb = p0sb("b_sb", [1, D], F32)
    sgfeat = p0sb("sgfeat", [128, NKC], F32)
    iota_t = p0sb("iota_t", [128, 160], F32)
    wmbuf = [p0sb(f"wmbuf{j}", [128, D], F32) for j in range(3)]
    wbuf = wmbuf
    mod_acc = p0sb("mod_acc", [1, 6 * D], F32)
    t1r = p0sb("t1r", [1, D], F32)
    sa_bf = p0sb("sa_bf", [1, D], BF16)
    safeat = p0sb("safeat", [128, NKC], BF16)

    psC = p0ps("psC", [128, NKC])
    psM = [p0ps(f"psM{j}", [1, 512]) for j in range(2)]
    psR = [p0ps(f"psR{j}", [1, 512]) for j in range(2)]

    p0.add("sync", lambda e: e.dma_start(out=c_sb[:], in_=c_hbm), sem=dC, inc=16)
    p0.add("sync", lambda e: e.dma_start(out=b_sb[:], in_=b_hbm), sem=dC, inc=16)
    p0.add("sync", lambda e: e.dma_start(out=clip_sb[:, 0:1], in_=clipx_hbm), sem=dC, inc=16)
    p0.add("sync", lambda e: e.dma_start(out=clip_sb[:, 1:2], in_=clipd_hbm), sem=dC, inc=16)
    dC_all = dC.n

    p0.add("gpsimd", lambda e: e.memset(ident_bf[:], 0.0), sem=sG)
    p0.add("gpsimd", lambda e: e.affine_select(
        out=ident_bf[:], in_=ident_bf[:], compare_op=ALU.not_equal,
        fill=1.0, base=0, pattern=[[-1, 128]], channel_multiplier=1),
        waits=[(sG, sG.n)], sem=sG)
    g_identbf = sG.n
    p0.add("gpsimd", lambda e: e.memset(ident_f32[:], 0.0), sem=sG)
    p0.add("gpsimd", lambda e: e.affine_select(
        out=ident_f32[:], in_=ident_f32[:], compare_op=ALU.not_equal,
        fill=1.0, base=0, pattern=[[-1, 128]], channel_multiplier=1),
        waits=[(sG, sG.n)], sem=sG)
    p0.add("gpsimd", lambda e: e.memset(one_bf[:], 1.0), sem=sG)
    g_onebf = sG.n
    p0.add("gpsimd", lambda e: e.memset(one_f32[:], 1.0), sem=sG)
    g_onef32 = sG.n
    p0.add("gpsimd", lambda e: e.iota(
        iota_t[:], [[1, 160]], base=-9, channel_multiplier=-1,
        allow_small_or_imprecise_dtypes=True), sem=sG)
    g_iota = sG.n
    p0.add("scalar", lambda e: e.activation(band[:], iota_t[:], AF.Abs),
           waits=[(sG, g_iota)], sem=sS)
    ss_b = sS.n
    p0.add("vector", lambda e: e.tensor_scalar(
        band[:], band[:], -1.0, 10.0, ALU.mult, ALU.add), waits=[(sS, ss_b)], sem=sV)
    p0.add("vector", lambda e: e.tensor_scalar(
        band[:], band[:], 0.0, -0.01, ALU.max, ALU.mult), waits=[(sV, sV.n)], sem=sV)

    p0.add("scalar", lambda e: e.activation(c_sb[:], c_sb[:], AF.Silu),
           waits=[(dC, dC_all)], sem=sS)
    ss_siluc = sS.n
    for k in range(NKC):
        p0.add("tensor", lambda e, k=k: e.matmul(
            psC[:, k:k + 1], c_sb[0:1, 128 * k:128 * (k + 1)], one_f32[0:1, 0:1],
            start=True, stop=True),
            waits=[(sS, ss_siluc), (sG, g_onef32)] if k == 0 else [], sem=sT)
    st_sgf = sT.n
    p0.add("scalar", lambda e: e.copy(sgfeat[:], psC[:]), waits=[(sT, st_sgf)], sem=sS)
    ss_sgf = sS.n

    sv_wcast = []
    for k in range(NKC):
        w_ = [(sV, sv_wcast[k - 2])] if k >= 2 else []
        p0.add("sync", lambda e, k=k: e.dma_start(
            out=wbuf[k % 2][:, 0:D], in_=W_hbm[128 * k:128 * (k + 1), :]),
            waits=w_, sem=dW[k % 2], inc=16)
        dval = dW[k % 2].n
        p0.add("vector", lambda e, k=k: e.tensor_copy(
            W_bf[:, D * k:D * (k + 1)], wbuf[k % 2][:, 0:D]),
            waits=[(dW[k % 2], dval)], sem=sV)
        sv_wcast.append(sV.n)
    sv_Wbf = sV.n

    wmm_cons = {}
    addmark = {}
    nsub = 6
    subs = [(k, c4) for k in range(NKC) for c4 in range(nsub)]
    for idx, (k, c4) in enumerate(subs):
        pbuf = idx % 3
        if idx >= 3:
            w_ = [wmm_cons[idx - 3]]
        elif idx < 2:
            w_ = [(sV, sv_wcast[6 + idx])]
        else:
            w_ = []
        p0.add("sync", lambda e, k=k, c4=c4, pbuf=pbuf: e.dma_start(
            out=wmbuf[pbuf][:], in_=wmod_hbm[128 * k:128 * (k + 1),
                                             D * c4:D * (c4 + 1)]),
            waits=w_, sem=dWM[pbuf], inc=16)
        dval = dWM[pbuf].n
        for h in range(2):
            s_ = 2 * c4 + h
            bank = s_ % 2
            w_mm = [(dWM[pbuf], dval), (sS, ss_sgf)]
            if bank in addmark:
                w_mm.append(addmark[bank])
            p0.add("tensor", lambda e, k=k, h=h, pbuf=pbuf, bank=bank: e.matmul(
                psM[bank][:], sgfeat[:, k:k + 1], wmbuf[pbuf][0:128, 512 * h:512 * (h + 1)],
                start=True, stop=True), waits=w_mm, sem=sT)
            mmv = sT.n
            wmm_cons[idx] = (sT, mmv)
            eng = "vector"
            esem = sV
            if k == 0:
                p0.add(eng, lambda e, s_=s_, bank=bank: e.tensor_copy(
                    mod_acc[0:1, 512 * s_:512 * (s_ + 1)], psM[bank][0:1, :]),
                    waits=[(sT, mmv)], sem=esem)
            else:
                p0.add(eng, lambda e, s_=s_, bank=bank: e.tensor_add(
                    mod_acc[0:1, 512 * s_:512 * (s_ + 1)],
                    mod_acc[0:1, 512 * s_:512 * (s_ + 1)], psM[bank][0:1, :]),
                    waits=[(sT, mmv)], sem=esem)
            addmark[bank] = (esem, esem.n)
    sv_mod = sV.n
    sg_mod = sG.n

    def msl(i):
        return mod_acc[0:1, D * i:D * (i + 1)]
    mod_ready = [(sV, sv_mod), (sG, sg_mod)]

    p0.add("vector", lambda e: e.tensor_copy(sa_bf[:], msl(0)), waits=mod_ready, sem=sV)
    sv_sabf = sV.n
    for k in range(NKC):
        p0.add("tensor", lambda e, k=k: e.matmul(
            psC[:, k:k + 1], sa_bf[0:1, 128 * k:128 * (k + 1)], one_bf[0:1, 0:1],
            start=True, stop=True),
            waits=[(sV, sv_sabf), (sS, ss_sgf)] if k == 0 else [], sem=sT)
    st_saf = sT.n
    p0.add("scalar", lambda e: e.copy(safeat[:], psC[:]), waits=[(sT, st_saf)], sem=sS)
    ss_saf = sS.n

    p0.add("vector", lambda e: e.tensor_scalar(msl(1), msl(1), 1.0, None, ALU.add),
           waits=mod_ready, sem=sV)
    sv_opca = sV.n
    for k in range(NKC):
        p0.add("tensor", lambda e, k=k: e.matmul(
            psC[:, k:k + 1], mod_acc[0:1, D + 128 * k:D + 128 * (k + 1)], one_f32[0:1, 0:1],
            start=True, stop=True),
            waits=[(sV, sv_opca), (sG, g_onef32), (sS, ss_saf)] if k == 0 else [], sem=sT)
    st_opca = sT.n
    p0.add("scalar", lambda e: e.copy(opca_feat[:], psC[:]), waits=[(sT, st_opca)], sem=sS)
    ss_opca = sS.n

    p0.add("vector", lambda e: e.tensor_scalar(msl(4), msl(4), 1.0, None, ALU.add),
           waits=mod_ready, sem=sV)
    for idx in (2, 5):
        p0.add("vector", lambda e, idx=idx: e.tensor_scalar(
            t1r[:], msl(idx), 0.0, None, ALU.is_ge),
            waits=mod_ready + [(sV, sV.n), (sS, sS.n)], sem=sV)
        p0.add("vector", lambda e: e.tensor_scalar(
            t1r[:], t1r[:], 2.0, -1.0, ALU.mult, ALU.add), waits=[(sV, sV.n)], sem=sV)
        p0.add("scalar", lambda e, idx=idx: e.activation(mod_acc[0:1, 0:D], msl(idx), AF.Abs),
               waits=mod_ready + [(sS, sS.n), (sV, sV.n)], sem=sS)
        ss_abs = sS.n
        p0.add("vector", lambda e: e.tensor_scalar(
            mod_acc[0:1, 0:D], mod_acc[0:1, 0:D], 0.25, None, ALU.max), waits=[(sS, ss_abs)], sem=sV)
        p0.add("vector", lambda e, idx=idx: e.tensor_mul(msl(idx), t1r[:], mod_acc[0:1, 0:D]),
               waits=[(sV, sV.n)], sem=sV)
    sv_gagb = sV.n

    for s_ in range(2):
        for k in range(NKC):
            w_ = [(sS, ss_saf), (sV, sv_Wbf)] if (s_ == 0 and k == 0) else []
            p0.add("tensor", lambda e, s_=s_, k=k: e.matmul(
                psR[s_][:], safeat[:, k:k + 1],
                W_bf[0:128, D * k + 512 * s_:D * k + 512 * s_ + 512],
                start=(k == 0), stop=(k == NKC - 1)), waits=w_, sem=sT)
        mmv = sT.n
        p0.add("vector", lambda e, s_=s_: e.tensor_add(
            t1r[0:1, 512 * s_:512 * (s_ + 1)], psR[s_][0:1, :],
            b_sb[0:1, 512 * s_:512 * (s_ + 1)]),
            waits=[(sT, mmv), (dC, dC_all), (sV, sv_gagb)], sem=sV)
    sv_bp = sV.n

    for src, slot, mark in ((msl(2), 0, sv_gagb), (msl(4), 1, sv_gagb),
                            (msl(3), 2, sv_gagb), (msl(5), 3, sv_gagb),
                            (t1r[:], 4, sv_bp)):
        p0.add("gpsimd", lambda e, src=src, slot=slot: e.dma_start(
            out=mod_hbm[slot:slot + 1, :], in_=src), waits=[(sV, mark)], sem=dMW, inc=16)
    dMW_all = dMW.n
    dMW_all0 = dMW_all

    p0.add("sync", lambda e: e.dma_start(
        out=bp_tile[:], in_=mod_hbm[4:5, :].broadcast_to((128, D))),
        waits=[(dMW, dMW_all)], sem=dBC, inc=16)
    dBC_bp = dBC.n

    p0b.close()
    p1sb = lambda n, sh, dt: p0a.enter_context(nc.sbuf_tensor(n, sh, dt))
    xb = [p1sb(f"xb{j}", [128, D], F32) for j in range(2)]
    h_bf = [p1sb(f"hbf{j}", [128, D], BF16) for j in range(2)]
    hT_bf = [p1sb(f"hTbf{j}", [128, D], BF16) for j in range(2)]
    ftk = [p1sb(f"ftk{j}", [128, D], BF16) for j in range(2)]
    stats = [p1sb(f"stats{j}", [128, 16], F32) for j in range(2)]
    p0early.close()
    p1ps = lambda n, sh: p0a.enter_context(nc.psum_tensor(n, sh, F32))
    ps_tr = [p1ps(f"ps_tr{j}", [128, 128]) for j in range(2)]
    ps_f = [p1ps(f"ps_f{j}", [128, D]) for j in range(2)]

    amax_done = {}
    WK1 = {"g": (wg_hbm, scl_g, 0, D), "u": (wu_hbm, scl_u, 22, D),
           "d": (wd_hbm, scl_d, 44, DI)}
    amax_jobs1 = ([("g", t) for t in range(22)] + [("u", t) for t in range(22)]
                  + [("d", t) for t in range(8)])
    dAM = [sem("dAM0"), sem("dAM1")]
    am_state = {"i": 0, "pend": []}

    def _amax_issue(ph):
        if am_state["i"] >= len(amax_jobs1):
            return
        j = am_state["i"]
        am_state["i"] += 1
        kind, t = amax_jobs1[j]
        srcw, scl, mbase, Wd = WK1[kind]
        ab = j % 2
        w_ = [am_state[("free", ab)]] if ("free", ab) in am_state else []
        wdt = min(704, Wd)
        ph.add("sync", lambda e, srcw=srcw, t=t, wdt=wdt, ab=ab: e.dma_start(
            out=abuf[ab][:, 0:wdt], in_=srcw[128 * t:128 * (t + 1), 0:wdt]),
            waits=w_, sem=dAM[ab], inc=16)
        am_state["pend"].append((j, kind, t, ab, dAM[ab].n))

    def _amax_reduce(ph):
        if not am_state["pend"]:
            return
        j, kind, t, ab, dv = am_state["pend"].pop(0)
        srcw, scl, mbase, Wd = WK1[kind]
        ph.add("vector", lambda e, ab=ab, Wd=Wd: e.tensor_reduce(
            amx_c[:, 0:1], abuf[ab][:, 0:min(704, Wd)], AX, ALU.max,
            apply_absolute_value=True), waits=[(dAM[ab], dv), (sV, sV.n)], sem=sV)
        for c0 in range(704, Wd, 704):
            wdt = min(704, Wd - c0)
            ph.add("sync", lambda e, srcw=srcw, t=t, c0=c0, wdt=wdt, ab=ab: e.dma_start(
                out=abuf[ab][:, 0:wdt], in_=srcw[128 * t:128 * (t + 1), c0:c0 + wdt]),
                waits=[(sV, sV.n)], sem=dAM[ab], inc=16)
            dv2 = dAM[ab].n
            ph.add("vector", lambda e, ab=ab, wdt=wdt: e.tensor_reduce(
                amx_c[:, 1:2], abuf[ab][:, 0:wdt], AX, ALU.max,
                apply_absolute_value=True), waits=[(dAM[ab], dv2), (sV, sV.n)], sem=sV)
            ph.add("vector", lambda e: e.tensor_max(
                amx_c[:, 0:1], amx_c[:, 0:1], amx_c[:, 1:2]), waits=[(sV, sV.n)], sem=sV)
        am_state[("free", ab)] = (sV, sV.n)
        ph.add("vector", lambda e, scl=scl, t=t: e.tensor_scalar(
            scl[:, t:t + 1], amx_c[:, 0:1], 0.007874015748031496, None, ALU.mult),
            waits=[(sV, sV.n)], sem=sV)
        mi = mbase + t
        ph.add("vector", lambda e, mi=mi, scl=scl, t=t: e.reciprocal(
            mcols[:, mi:mi + 1], scl[:, t:t + 1]), waits=[(sV, sV.n)], sem=sV)
        amax_done[(kind, t)] = (sV, sV.n)

    def amax_step(ph):
        if len(am_state["pend"]) >= 2:
            _amax_reduce(ph)
        _amax_issue(ph)

    m_xfree = [None] * (n_xt + 4)
    m_trfree = [0] * (n_xt + 4)
    m_psffree = [0] * (n_xt + 4)
    m_hTdone = [0] * (n_xt + 4)
    m_ftokst = [None] * (n_xt + 4)

    for i in range(n_xt):
        amax_step(p0)
        pb = i % 2
        if i >= 2:
            w_ = [m_xfree[i - 2]]
        elif i == 0:
            w_ = [(dMW, dMW_all0), (sV, sv_bp), (sS, ss_opca), (sG, sg_mod),
                  (sT, sT.n)]
        else:
            w_ = []
        p0.add("sync", lambda e, i=i, pb=pb: e.dma_start(
            out=xb[pb][:], in_=x_hbm[128 * i:128 * (i + 1), :]),
            waits=w_, sem=dX[pb], inc=16)
        dxv = dX[pb].n

        st = stats[pb]
        w_ = [(dX[pb], dxv)]
        if i >= 2:
            w_.append(m_xfree[i - 2])
        p0.add("vector", lambda e, pb=pb, st=st: e.bn_stats(st[:, 0:6], xb[pb][:, 0:512]),
               waits=w_, sem=sV)
        p0.add("vector", lambda e, pb=pb, st=st: e.bn_stats(st[:, 6:12], xb[pb][:, 512:1024]),
               waits=[(sV, sV.n)], sem=sV)
        p0.add("vector", lambda e, st=st: e.bn_aggr(st[:, 12:14], st[:, 0:12]),
               waits=[(sV, sV.n)], sem=sV)
        p0.add("vector", lambda e, st=st: e.tensor_scalar(
            st[:, 14:15], st[:, 13:14], EPS, None, ALU.add), waits=[(sV, sV.n)], sem=sV)
        p0.add("vector", lambda e, st=st: e.reciprocal(st[:, 15:16], st[:, 14:15]),
               waits=[(sV, sV.n)], sem=sV)
        sv_rec = sV.n
        p0.add("scalar", lambda e, st=st: e.activation(st[:, 0:1], st[:, 15:16], AF.Sqrt),
               waits=[(sV, sv_rec)], sem=sS)
        ss_sq = sS.n
        p0.add("vector", lambda e, st=st: e.tensor_mul(st[:, 1:2], st[:, 0:1], st[:, 0:1]),
               waits=[(sS, ss_sq)], sem=sV)
        p0.add("vector", lambda e, st=st: e.tensor_mul(st[:, 2:3], st[:, 1:2], st[:, 14:15]),
               waits=[(sV, sV.n)], sem=sV)
        p0.add("vector", lambda e, st=st: e.tensor_scalar(
            st[:, 3:4], st[:, 2:3], -0.5, 1.5, ALU.mult, ALU.add), waits=[(sV, sV.n)], sem=sV)
        p0.add("vector", lambda e, st=st: e.tensor_mul(st[:, 4:5], st[:, 0:1], st[:, 3:4]),
               waits=[(sV, sV.n)], sem=sV)
        p0.add("vector", lambda e, st=st: e.tensor_mul(st[:, 5:6], st[:, 12:13], st[:, 4:5]),
               waits=[(sV, sV.n)], sem=sV)
        p0.add("vector", lambda e, st=st: e.tensor_scalar(
            st[:, 6:7], st[:, 5:6], -1.0, None, ALU.mult), waits=[(sV, sV.n)], sem=sV)
        sv_ln = sV.n

        w_ = [(sV, sv_ln)]
        if i >= 2:
            w_.append((sT, m_hTdone[i - 2]))
        p0.add("vector", lambda e, pb=pb, st=st: e.tensor_scalar(
            h_bf[pb][:], xb[pb][:], st[:, 4:5], st[:, 6:7], ALU.mult, ALU.add),
            waits=w_, sem=sV)
        m_xfree[i] = (sV, sV.n)
        ss_h = sV.n

        ss_evac = []
        for j in range(NKC):
            w_ = [(sV, ss_h)]
            if j < 2:
                w_.append((sG, g_identbf))
                if i > 0:
                    w_.append((sS, m_trfree[i - 1]))
                else:
                    w_.append((sS, ss_opca))
                    w_.append((sV, sv_bp))
                    w_.append((sG, sg_mod))
            else:
                w_.append((sS, ss_evac[j - 2]))
            p0.add("tensor", lambda e, pb=pb, j=j: e.matmul(
                ps_tr[j % 2][:], h_bf[pb][:, 128 * j:128 * (j + 1)], ident_bf[:],
                start=True, stop=True), waits=w_, sem=sT)
            mmv = sT.n
            p0.add("scalar", lambda e, pb=pb, j=j: e.activation(
                hT_bf[pb][:, 128 * j:128 * (j + 1)], ps_tr[j % 2][:], AF.Copy,
                scale=opca_feat[:, j:j + 1]),
                waits=[(sT, mmv)], sem=sS)
            ss_evac.append(sS.n)
        ss_hT = sS.n

        for s_ in range(2):
            for k in range(NKC):
                w_ = []
                if s_ == 0 and k == 0:
                    w_ = [(sS, ss_hT), (sV, sv_Wbf)]
                    if i >= 2:
                        w_.append((sV, m_psffree[i - 2]))
                p0.add("tensor", lambda e, pb=pb, s_=s_, k=k: e.matmul(
                    ps_f[pb][:, 512 * s_:512 * (s_ + 1)],
                    hT_bf[pb][:, 128 * k:128 * (k + 1)],
                    W_bf[0:128, D * k + 512 * s_:D * k + 512 * s_ + 512],
                    start=(k == 0), stop=(k == NKC - 1)), waits=w_, sem=sT)
        m_hTdone[i] = sT.n
        st_f = sT.n

        for s_ in range(2):
            w_ = [(sT, st_f), (dBC, dBC_bp)]
            if i >= 2:
                w_.append(m_ftokst[i - 2])
                w_.append((sT, m_hTdone[i - 1]))
            p0.add("vector", lambda e, pb=pb, s_=s_: e.tensor_add(
                ftk[pb][:, 512 * s_:512 * (s_ + 1)], ps_f[pb][:, 512 * s_:512 * (s_ + 1)],
                bp_tile[:, 512 * s_:512 * (s_ + 1)]),
                waits=w_ if s_ == 0 else [], sem=sV)
        m_psffree[i] = sV.n
        sv_ftk = sV.n

        p0.add("gpsimd", lambda e, pb=pb, i=i: e.dma_start(
            out=ftok_hbm[128 * i:128 * (i + 1), :], in_=ftk[pb][:]),
            waits=[(sV, sv_ftk)], sem=dSF[pb], inc=16)
        m_ftokst[i] = (dSF[pb], dSF[pb].n)

        ss_evac2 = []
        for j in range(NKC):
            w_ = [(sV, sv_ftk)]
            if j < 2:
                w_.append((sS, ss_hT))
            else:
                w_.append((sS, ss_evac2[j - 2]))
            p0.add("tensor", lambda e, pb=pb, j=j: e.matmul(
                ps_tr[j % 2][:], ftk[pb][:, 128 * j:128 * (j + 1)], ident_bf[:],
                start=True, stop=True), waits=w_, sem=sT)
            mmv = sT.n
            p0.add("scalar", lambda e, j=j, i=i: e.copy(
                f_feat[j][:, 128 * i:128 * (i + 1)], ps_tr[j % 2][:]),
                waits=[(sT, mmv)], sem=sS)
            ss_evac2.append(sS.n)
        m_trfree[i] = sS.n
        m_ftrdone = sT.n
        m_hTdone[i] = max(m_hTdone[i], m_ftrdone)

    while am_state["pend"] or am_state["i"] < len(amax_jobs1):
        _amax_reduce(p0)
        _amax_issue(p0)

    marks["p1_sS"] = sS.n
    marks["p1_sV"] = sV.n
    marks["p1_sT"] = sT.n

    if dev & 1:
        dDV = sem("dDV")
        for j in range(NKC):
            p0.add("gpsimd", lambda e, j=j: e.dma_start(
                out=dev_f[128 * j:128 * (j + 1), :], in_=f_feat[j][:]),
                waits=[(sS, marks["p1_sS"])], sem=dDV, inc=16)
        p0.add("gpsimd", lambda e: e.nop(), waits=[(dDV, dDV.n)])

    p0.add("scalar", lambda e: e.activation(band[:], band[:], AF.Exp),
           waits=[(sV, sV.n)], sem=sS)
    p0.add("gpsimd", lambda e: e.memset(amax1[:], 0.0), sem=sG)
    p0.add("gpsimd", lambda e: e.memset(amax2[:], 0.0), sem=sG)

    for eng, s_ in (("sync", None), ("scalar", sS), ("vector", sV),
                    ("tensor", sT), ("gpsimd", sG)):
        if s_ is not None:
            p0.add(eng, lambda e: e.nop(), sem=s_)
    marks["b1"] = {"sS": sS.n, "sV": sV.n, "sT": sT.n, "sG": sG.n}
    marks["dMW_all"] = dMW_all
    marks["dSF_final"] = [(dSF[0], dSF[0].n), (dSF[1], dSF[1].n)]

    p0.emit(nc)
    p0a.close()

    env = build_p2(nc, ctx, locals())
    env = build_p4(nc, ctx, env)
    env = build_p5(nc, ctx, env)
    stack.close()
    return nc


def build_p2(nc, ctx, env):
    S = env["S"]; Q = env["Q"]; n_qt = env["n_qt"]; dev = env["dev"]
    f_feat = env["f_feat"]; ident_bf = env["ident_bf"]; band = env["band"]
    amax1 = env["amax1"]
    x_hbm = env["x_hbm"]; ftok_hbm = env["ftok_hbm"]; mod_hbm = env["mod_hbm"]
    wg_hbm = env["wg_hbm"]; wu_hbm = env["wu_hbm"]; wd_hbm = env["wd_hbm"]
    sS = env["sS"]; sV = env["sV"]; sT = env["sT"]; sG = env["sG"]
    sem = ctx.sem
    marks = env["marks"]
    b1 = marks["b1"]
    PAD = 16
    n_tc = S // 512
    n_sc = S // 128
    n_pairs = n_qt // 2

    xp_hbm = nc.dram_tensor("xp_hbm", [Q, D], F32).ap()
    h2_hbm = nc.dram_tensor("h2_hbm", [Q, D], F32).ap()
    qg_hbm = nc.dram_tensor("qg_hbm", [22, 128, D], BF16).ap()
    qu_hbm = nc.dram_tensor("qu_hbm", [22, 128, D], BF16).ap()
    qd_hbm = nc.dram_tensor("qd_hbm", [22, 128, D], BF16).ap()
    bf1_hbm = nc.dram_tensor("bandfix1", [128, 16], F32, kind="ExternalInput").ap()
    bf2_hbm = nc.dram_tensor("bandfix2", [128, 16], F32, kind="ExternalInput").ap()

    scl_g = env["scl_g"]; scl_u = env["scl_u"]; scl_d = env["scl_d"]

    p2a = ExitStack()
    sb2 = lambda n, sh, dt: p2a.enter_context(nc.sbuf_tensor(n, sh, dt))
    ps2 = lambda n, sh: p2a.enter_context(nc.psum_tensor(n, sh, F32))

    p_sb = [sb2(f"p_sb{q}", [128, PAD + S], BF16) for q in range(2)]
    NMC = [sb2(f"NMC{q}", [128, n_tc], F32) for q in range(2)]
    Erow = [sb2(f"Erow{q}", [128, n_tc], F32) for q in range(2)]
    rsrow = [sb2(f"rs{q}", [128, n_tc], F32) for q in range(2)]
    smallq = [sb2(f"smallq{q}", [128, 8], F32) for q in range(2)]
    D_sb = [[sb2(f"Dsb{q}{j}", [128, 128], BF16) for j in range(2)] for q in range(2)]
    pT_bf = [[sb2(f"pTbf{q}{j}", [128, 128], BF16) for j in range(4)] for q in range(2)]
    fstr = [sb2(f"fstr{j}", [128, D], BF16) for j in range(2)]
    fstr2 = [sb2(f"fstr2{j}", [128, D], BF16) for j in range(2)]
    ga_tile = sb2("ga_tile", [128, D], BF16)
    onepcb_tile = sb2("onepcb_tile", [128, D], F32)
    sbb_tile = sb2("sbb_tile", [128, D], BF16)
    bfix1 = sb2("bfix1", [128, 16], F32)
    bfix2 = sb2("bfix2", [128, 16], F32)
    xe = sb2("xe", [128, D], F32)
    _xp1 = sb2("xp_sb", [128, D], F32)
    _h21 = sb2("h2_sb", [128, D], F32)
    xp_sb = [_xp1, _xp1]
    h2_sb = [_h21, _h21]
    st2 = [sb2(f"st2{q}", [128, 16], F32) for q in range(2)]
    wv = [sb2(f"wv{j}", [128, 512], F32) for j in range(2)]
    yq = [sb2(f"yq{j}", [128, 512], I32) for j in range(2)]
    NSTG = 3
    stg = [sb2(f"stg{j}", [128, 512], BF16) for j in range(NSTG)]
    mcols = env["mcols"]

    e_ps = [ps2(f"e_ps{j}", [128, 512]) for j in range(4)]
    pT_ps = [e_ps[0], e_ps[1]]
    o_ps = [ps2(f"o_ps{q}", [128, D]) for q in range(2)]

    p2 = Phase("p2")
    dBC2 = sem("dBC2")
    dFS = [sem("dFS0"), sem("dFS1"), sem("dFS2")]
    dXE = sem("dXE")
    dXP = [sem("dXP0"), sem("dXP1")]
    dXH = [sem("dXH0"), sem("dXH1")]
    dPW = [sem("dPW0"), sem("dPW1")]
    dQSs = [sem(f"dQSs{i}") for i in range(NSTG)]

    bar = [(sS, b1["sS"]), (sV, b1["sV"]), (sT, b1["sT"]), (sG, b1["sG"])]
    p2.add("gpsimd", lambda e: e.nop(), waits=bar + marks["dSF_final"])
    for t_ in (smallq[0], smallq[1], st2[0], st2[1], NMC[0], NMC[1],
               Erow[0], Erow[1]):
        p2.add("gpsimd", lambda e, t_=t_: e.memset(t_[:], 0.0), sem=sG)
    g_init = sG.n
    bar2 = bar + [(sG, g_init)]
    for eng in ("scalar", "vector", "tensor"):
        p2.add(eng, lambda e: e.nop(), waits=bar2)
    p2.add("sync", lambda e: e.nop(), waits=bar2 + marks["dSF_final"])

    dMW_all = marks["dMW_all"]
    dMW = env["dMW"]
    p2.add("sync", lambda e: e.dma_start(
        out=onepcb_tile[:], in_=mod_hbm[1:2, :].broadcast_to((128, D))),
        waits=[(dMW, dMW_all)], sem=dBC2, inc=16)
    p2.add("sync", lambda e: e.dma_start(out=bfix1[:], in_=bf1_hbm), sem=dBC2, inc=16)
    p2.add("sync", lambda e: e.dma_start(out=bfix2[:], in_=bf2_hbm), sem=dBC2, inc=16)
    dBC2g = sem("dBC2g")
    dBC2g2 = sem("dBC2g2")
    p2.add("gpsimd", lambda e: e.dma_start(
        out=sbb_tile[:], in_=mod_hbm[2:3, :].broadcast_to((128, D))),
        waits=[(dMW, dMW_all)], sem=dBC2g, inc=16)
    p2.add("gpsimd", lambda e: e.dma_start(
        out=ga_tile[:], in_=mod_hbm[0:1, :].broadcast_to((128, D))),
        waits=[(dMW, dMW_all)], sem=dBC2g2, inc=16)
    p2.add("sync", lambda e: e.nop(),
           waits=[(dBC2g, dBC2g.n), (dBC2g2, dBC2g2.n)], sem=dBC2, inc=1)
    dBC2_all = dBC2.n

    WKIND = {"g": (wg_hbm, qg_hbm, scl_g, 0, D), "u": (wu_hbm, qu_hbm, scl_u, 22, D),
             "d": (wd_hbm, qd_hbm, scl_d, 44, DI)}
    amax_jobs = [("g", t) for t in range(22)] + [("u", t) for t in range(22)] + \
                [("d", t) for t in range(8)]
    strip_jobs = []
    for kind, t in amax_jobs:
        Wd = WKIND[kind][4]
        for c0 in range(0, Wd, 512):
            strip_jobs.append((kind, t, c0, min(512, Wd - c0)))
    amax_done = env["amax_done"]
    aj = [len(amax_jobs)]
    sq_i = [0]
    free_slots = list(range(NSTG))
    pending = []
    quant_gdone = {}

    def prep_quant(budget):
        for _ in range(budget):
            sj = sq_i[0]
            if sj >= len(strip_jobs) or not free_slots:
                return
            kind, t, c0, wdt = strip_jobs[sj]
            if (kind, t) not in amax_done:
                return
            sq_i[0] += 1
            src, dst, scl, mbase, Wd = WKIND[kind]
            slot = free_slots.pop(0)
            bslot = sj % 2
            mi = mbase + t
            w_ = []
            if (sj - 2) in quant_gdone:
                w_.append(quant_gdone[sj - 2])
            if (kind, t) not in amax_done:
                pass
            p2.add("sync", lambda e, src=src, t=t, c0=c0, wdt=wdt, bslot=bslot: e.dma_start(
                out=wv[bslot][:, 0:wdt], in_=src[128 * t:128 * (t + 1), c0:c0 + wdt]),
                waits=w_, sem=dPW[bslot], inc=16)
            dv = dPW[bslot].n
            p2.add("vector", lambda e, bslot=bslot, mi=mi, wdt=wdt: e.tensor_scalar(
                wv[bslot][:, 0:wdt], wv[bslot][:, 0:wdt], mcols[:, mi:mi + 1], None,
                ALU.mult), waits=[(dPW[bslot], dv), amax_done[(kind, t)], (sV, sV.n)], sem=sV)
            p2.add("vector", lambda e, bslot=bslot, wdt=wdt: e.tensor_copy(
                yq[bslot][:, 0:wdt], wv[bslot][:, 0:wdt]), waits=[(sV, sV.n)], sem=sV)
            p2.add("vector", lambda e, bslot=bslot, wdt=wdt: e.tensor_sub(
                wv[bslot][:, 0:wdt], wv[bslot][:, 0:wdt], yq[bslot][:, 0:wdt]),
                waits=[(sV, sV.n)], sem=sV)
            w_c2 = [(sV, sV.n), marks.get(("stgfree", slot), (sV, 0))]
            p2.add("vector", lambda e, bslot=bslot, slot=slot, wdt=wdt: e.tensor_scalar(
                stg[slot][:, 0:wdt], wv[bslot][:, 0:wdt], -0.5, None, ALU.is_lt),
                waits=w_c2, sem=sV)
            p2.add("vector", lambda e, bslot=bslot, slot=slot, wdt=wdt: e.scalar_tensor_tensor(
                wv[bslot][:, 0:wdt], wv[bslot][:, 0:wdt], 0.5, stg[slot][:, 0:wdt],
                ALU.is_gt, ALU.subtract), waits=[(sV, sV.n)], sem=sV)
            p2.add("vector", lambda e, bslot=bslot, slot=slot, wdt=wdt: e.tensor_add(
                stg[slot][:, 0:wdt], wv[bslot][:, 0:wdt], yq[bslot][:, 0:wdt]),
                waits=[(sV, sV.n)], sem=sV)
            quant_gdone[sj] = (sV, sV.n)
            pending.append((sj, slot, (sV, sV.n)))

    def prep_transpose(budget, pt_gate):
        last_evac = None
        for _ in range(budget):
            if not pending:
                break
            sj, slot, qmark = pending.pop(0)
            kind, t, c0, wdt = strip_jobs[sj]
            src, dst, scl, mbase, Wd = WKIND[kind]
            nch = wdt // 128
            ev = []
            for j in range(nch):
                w_ = [qmark]
                if pt_gate is not None and len(ev) == 0:
                    w_.append(pt_gate)
                bm = marks.get(("obank", j % 2))
                if bm is not None:
                    w_.append(bm)
                p2.add("tensor", lambda e, slot=slot, j=j: e.matmul(
                    o_ps[j % 2][:, 0:128], stg[slot][:, 128 * j:128 * (j + 1)],
                    ident_bf[:], start=True, stop=True), waits=w_, sem=sT)
                mmv = sT.n
                p2.add("scalar", lambda e, slot=slot, j=j: e.copy(
                    stg[slot][:, 128 * j:128 * (j + 1)], o_ps[j % 2][:, 0:128]),
                    waits=[(sT, mmv)], sem=sS)
                marks[("obank", j % 2)] = (sS, sS.n)
                ev.append(sS.n)
            if kind in ("g", "u"):
                p2.add("gpsimd", lambda e, dst=dst, t=t, c0=c0, wdt=wdt, slot=slot: e.dma_start(
                    out=dst[t, :, c0:c0 + wdt], in_=stg[slot][:, 0:wdt]),
                    waits=[(sS, ev[-1])], sem=dQSs[slot], inc=16)
            else:
                for j in range(nch):
                    ic = (c0 + 128 * j) // 128
                    p2.add("gpsimd", lambda e, dst=dst, ic=ic, t=t, j=j, slot=slot: e.dma_start(
                        out=dst[ic, :, 128 * t:128 * (t + 1)],
                        in_=stg[slot][:, 128 * j:128 * (j + 1)]),
                        waits=[(sS, ev[-1])], sem=dQSs[slot], inc=16)
            marks[("stgfree", slot)] = (dQSs[slot], dQSs[slot].n)
            free_slots.append(slot)
            last_evac = (sS, ev[-1])
        return last_evac

    m_expdone = [[0] * n_tc for _ in range(2)]
    m_exphist = []
    m_pvlast = None
    m_epidone = [None, None]
    marks["pv_prev"] = 0

    for pr in range(n_pairs):
        qts = (2 * pr, 2 * pr + 1)
        prep_quant(2)

        ss_exp_last = [None, None]
        sv_p1done = [None, None]
        tr_mark = None
        for qi, qt in enumerate(qts):
            for tc in range(n_tc):
                bank = len(m_exphist) % 4
                if len(m_exphist) >= 4:
                    w_ = [(sS, m_exphist[-4])]
                else:
                    w_ = [(dBC2, dBC2_all)]
                for k in range(NKC):
                    p2.add("tensor", lambda e, k=k, qt=qt, tc=tc, bank=bank: e.matmul(
                        e_ps[bank][:], f_feat[k][:, 128 * qt:128 * (qt + 1)],
                        f_feat[k][:, 512 * tc:512 * (tc + 1)],
                        start=(k == 0), stop=(k == NKC - 1)),
                        waits=w_ if k == 0 else [], sem=sT)
                st_e = sT.n
                p2.add("vector", lambda e, qi=qi, tc=tc, bank=bank: e.reduce_max(
                    NMC[qi][:, tc:tc + 1], e_ps[bank][:], AX, negate=True),
                    waits=[(sT, st_e)], sem=sV)
                sv_m = sV.n
                w_ = [(sV, sv_m)]
                if tc == 0 and pr > 0:
                    w_.append((sT, marks["pv_prev"]))
                p2.add("scalar", lambda e, qi=qi, tc=tc, bank=bank: e.activation(
                    p_sb[qi][:, PAD + 512 * tc:PAD + 512 * (tc + 1)], e_ps[bank][:],
                    AF.Exp, bias=NMC[qi][:, tc:tc + 1]),
                    waits=w_, sem=sS)
                m_expdone[qi][tc] = sS.n
                m_exphist.append(sS.n)
                marks[("ptbank", bank)] = (sS, sS.n)
            ss_exp_last[qi] = sS.n

            sq = smallq[qi]
            p2.add("vector", lambda e, qi=qi, sq=sq: e.tensor_reduce(
                sq[:, 0:1], NMC[qi][:], AX, ALU.min), waits=[(sV, sV.n)], sem=sV)
            p2.add("vector", lambda e, qi=qi, sq=sq: e.tensor_scalar(
                Erow[qi][:], NMC[qi][:], -1.0, sq[:, 0:1], ALU.mult, ALU.add),
                waits=[(sV, sV.n)], sem=sV)
            sv_er = sV.n
            p2.add("scalar", lambda e, qi=qi: e.activation(
                Erow[qi][:], Erow[qi][:], AF.Exp),
                waits=[(sV, sv_er), (sS, ss_exp_last[qi])], sem=sS)
            ss_ebf = sS.n
            W0 = 128 * qt - 9
            lo = max(W0, 0)
            hi = W0 + 146
            tpl_lo = lo - W0
            p2.add("vector", lambda e, qi=qi, lo=lo, hi=hi, tpl_lo=tpl_lo: e.tensor_mul(
                p_sb[qi][:, PAD + lo:PAD + hi], p_sb[qi][:, PAD + lo:PAD + hi],
                band[:, tpl_lo:tpl_lo + (hi - lo)]),
                waits=[(sS, ss_exp_last[qi])], sem=sV)
            if qt == n_qt - 1:
                p2.add("vector", lambda e, qi=qi: e.tensor_mul(
                    p_sb[qi][:, PAD + Q:PAD + Q + 9], p_sb[qi][:, PAD + Q:PAD + Q + 9],
                    bfix1[:, 0:9]), waits=[(sV, sV.n)], sem=sV)
            if qt == 0:
                p2.add("vector", lambda e, qi=qi: e.tensor_mul(
                    p_sb[qi][:, PAD + S - 9:PAD + S], p_sb[qi][:, PAD + S - 9:PAD + S],
                    bfix2[:, 0:9]), waits=[(sV, sV.n), (dBC2, dBC2_all)], sem=sV)
            sv_band = sV.n
            for tc in range(n_tc):
                p2.add("vector", lambda e, qi=qi, tc=tc: e.reduce_sum(
                    rsrow[qi][:, tc:tc + 1],
                    p_sb[qi][:, PAD + 512 * tc:PAD + 512 * (tc + 1)], AX),
                    waits=[(sV, sv_band)] if tc == 0 else [], sem=sV)
            p2.add("vector", lambda e, qi=qi: e.tensor_mul(
                rsrow[qi][:], rsrow[qi][:], Erow[qi][:]),
                waits=[(sV, sV.n), (sS, ss_ebf)], sem=sV)
            p2.add("vector", lambda e, qi=qi, sq=sq: e.reduce_sum(
                sq[:, 1:2], rsrow[qi][:], AX), waits=[(sV, sV.n)], sem=sV)
            p2.add("vector", lambda e, qi=qi, sq=sq: e.reciprocal(
                sq[:, 2:3], sq[:, 1:2]), waits=[(sV, sV.n)], sem=sV)
            sv_p1done[qi] = sV.n

            gate = m_pvlast if qi == 0 else None
            tr = prep_transpose(3, gate if tr_mark is None else None)
            tr_mark = tr if tr is not None else tr_mark
            prep_quant(3)

        sv_D = {}
        ss_pT = {}
        mm_pT = {}
        st_pvcons = [0] * (n_sc + 8)
        dfs_mark = {}

        def emit_D(sc):
            tc = sc // 4
            for qi in range(2):
                w_ = [(sS, ss_ebf), (sV, sv_p1done[qi])]
                if tc >= 2 and (4 * (tc - 2) + 3) in mm_pT:
                    w_.append((sT, mm_pT[4 * (tc - 2) + 3]))
                p2.add("vector", lambda e, qi=qi, tc=tc: e.tensor_scalar(
                    D_sb[qi][tc % 2][:], ident_bf[:], Erow[qi][:, tc:tc + 1],
                    None, ALU.mult), waits=w_, sem=sV)
                sv_D[(qi, tc)] = sV.n

        def emit_pT(sc):
            tc = sc // 4
            par = sc % 4
            for qi in range(2):
                w_ = [(sV, sv_D[(qi, tc)]), (sV, sv_p1done[qi])]
                bm = marks.get(("ptbank", par))
                if bm is not None:
                    w_.append(bm)
                if sc >= 4:
                    w_.append((sS, ss_pT[(qi, sc - 4)]))
                p2.add("tensor", lambda e, qi=qi, sc=sc, par=par, tc=tc: e.matmul(
                    e_ps[par][:, 0:128], p_sb[qi][:, PAD + 128 * sc:PAD + 128 * (sc + 1)],
                    D_sb[qi][tc % 2][:], start=True, stop=True),
                    waits=w_, sem=sT)
                mmv = sT.n
                mm_pT[sc] = mmv
                p2.add("scalar", lambda e, qi=qi, par=par: e.copy(
                    pT_bf[qi][par][:], e_ps[par][:, 0:128]), waits=[(sT, mmv)], sem=sS)
                marks[("ptbank", par)] = (sS, sS.n)
                ss_pT[(qi, sc)] = sS.n

        emit_D(0)
        for _s in range(min(2, n_sc)):
            emit_pT(_s)

        dfs_val = {}

        def emit_fload(blk):
            buf = blk % 2
            w_ = []
            if blk >= 2:
                w_.append((sT, st_pvcons[2 * (blk - 2) + 1]))
            elif pr > 0:
                w_.append((sT, marks["pv_prev"]))
            for h, fb in ((0, fstr), (1, fstr2)):
                p2.add("sync", lambda e, blk=blk, h=h, fb=fb, buf=buf: e.dma_start(
                    out=fb[buf][:],
                    in_=ftok_hbm[128 * (2 * blk + h):128 * (2 * blk + h + 1), :]),
                    waits=w_ if h == 0 else [], sem=dFS[buf], inc=16)
            dfs_val[blk] = dFS[buf].n

        emit_fload(0)
        for sc in range(n_sc):
            par = sc % 4
            buf = (sc // 2) % 2
            if sc % 2 == 0:
                if sc // 2 + 1 < n_sc // 2:
                    emit_fload(sc // 2 + 1)
                dfs_mark[buf] = dfs_val[sc // 2]
            if (sc + 3) % 4 == 0 and sc + 3 < n_sc:
                emit_D(sc + 3)
            if sc + 2 < n_sc:
                emit_pT(sc + 2)
            fs = fstr if sc % 2 == 0 else fstr2
            for qi in range(2):
                w_ = [(sS, ss_pT[(qi, sc)]), (dFS[buf], dfs_mark[buf])]
                if sc == 0 and m_epidone[qi] is not None:
                    w_.append(m_epidone[qi])
                if sc == 0 and tr_mark is not None:
                    w_.append(tr_mark)
                for h in range(2):
                    p2.add("tensor", lambda e, qi=qi, h=h, par=par, buf=buf, fs=fs,
                           sc=sc: e.matmul(
                        o_ps[qi][:, 512 * h:512 * (h + 1)], pT_bf[qi][par][:],
                        fs[buf][:, 512 * h:512 * (h + 1)],
                        start=(sc == 0), stop=(sc == n_sc - 1)),
                        waits=w_ if h == 0 else [], sem=sT)
            st_pvcons[sc] = sT.n
        marks["pv_prev"] = sT.n
        m_pvlast = (sT, sT.n)
        st_pv = sT.n

        for qi, qt in enumerate(qts):
            sq = smallq[qi]
            w_ = [marks["xe_free"]] if "xe_free" in marks else []
            p2.add("sync", lambda e, qt=qt: e.dma_start(
                out=xe[:], in_=x_hbm[128 * qt:128 * (qt + 1), :]),
                waits=w_, sem=dXE, inc=16)
            dxe = dXE.n
            w_ = [(sT, st_pv), (dBC2, dBC2_all)]
            for m in m_epidone:
                if m is not None:
                    w_.append(m)
            if marks.get("xpstore") is not None:
                w_.append(marks["xpstore"])
            p2.add("vector", lambda e, qi=qi, sq=sq: e.scalar_tensor_tensor(
                xp_sb[qi][:], o_ps[qi][:], sq[:, 2:3], ga_tile[:], ALU.mult, ALU.mult),
                waits=w_, sem=sV)
            marks[("obank", qi)] = (sV, sV.n)
            p2.add("vector", lambda e, qi=qi: e.tensor_add(
                xp_sb[qi][:], xp_sb[qi][:], xe[:]), waits=[(sV, sV.n), (dXE, dxe)], sem=sV)
            sv_xp = sV.n
            marks["xe_free"] = (sV, sv_xp)
            p2.add("gpsimd", lambda e, qi=qi, qt=qt: e.dma_start(
                out=xp_hbm[128 * qt:128 * (qt + 1), :], in_=xp_sb[qi][:]),
                waits=[(sV, sv_xp)], sem=dXP[qi], inc=16)
            marks["xpstore"] = (dXP[qi], dXP[qi].n)
            st_ = st2[qi]
            p2.add("vector", lambda e, qi=qi, st_=st_: e.bn_stats(
                st_[:, 0:6], xp_sb[qi][:, 0:512]), waits=[(sV, sv_xp)], sem=sV)
            p2.add("vector", lambda e, qi=qi, st_=st_: e.bn_stats(
                st_[:, 6:12], xp_sb[qi][:, 512:1024]), waits=[(sV, sV.n)], sem=sV)
            p2.add("vector", lambda e, st_=st_: e.bn_aggr(
                st_[:, 12:14], st_[:, 0:12]), waits=[(sV, sV.n)], sem=sV)
            p2.add("vector", lambda e, st_=st_: e.tensor_scalar(
                st_[:, 14:15], st_[:, 13:14], EPS, None, ALU.add), waits=[(sV, sV.n)], sem=sV)
            p2.add("vector", lambda e, st_=st_: e.reciprocal(
                st_[:, 15:16], st_[:, 14:15]), waits=[(sV, sV.n)], sem=sV)
            sv_rec = sV.n
            p2.add("scalar", lambda e, st_=st_: e.activation(
                st_[:, 0:1], st_[:, 15:16], AF.Sqrt), waits=[(sV, sv_rec)], sem=sS)
            ss_sq2 = sS.n
            p2.add("vector", lambda e, st_=st_: e.tensor_mul(
                st_[:, 1:2], st_[:, 0:1], st_[:, 0:1]), waits=[(sS, ss_sq2)], sem=sV)
            p2.add("vector", lambda e, st_=st_: e.tensor_mul(
                st_[:, 2:3], st_[:, 1:2], st_[:, 14:15]), waits=[(sV, sV.n)], sem=sV)
            p2.add("vector", lambda e, st_=st_: e.tensor_scalar(
                st_[:, 3:4], st_[:, 2:3], -0.5, 1.5, ALU.mult, ALU.add),
                waits=[(sV, sV.n)], sem=sV)
            p2.add("vector", lambda e, st_=st_: e.tensor_mul(
                st_[:, 4:5], st_[:, 0:1], st_[:, 3:4]), waits=[(sV, sV.n)], sem=sV)
            p2.add("vector", lambda e, st_=st_: e.tensor_mul(
                st_[:, 5:6], st_[:, 12:13], st_[:, 4:5]), waits=[(sV, sV.n)], sem=sV)
            p2.add("vector", lambda e, st_=st_: e.tensor_scalar(
                st_[:, 6:7], st_[:, 5:6], -1.0, None, ALU.mult), waits=[(sV, sV.n)], sem=sV)
            w_ = [(sV, sV.n)]
            if m_epidone[qi] is not None:
                w_.append(m_epidone[qi])
            p2.add("vector", lambda e, qi=qi, st_=st_: e.tensor_scalar(
                h2_sb[qi][:], xp_sb[qi][:], st_[:, 4:5], st_[:, 6:7], ALU.mult, ALU.add),
                waits=w_, sem=sV)
            p2.add("vector", lambda e, qi=qi: e.tensor_mul(
                h2_sb[qi][:], h2_sb[qi][:], onepcb_tile[:]), waits=[(sV, sV.n)], sem=sV)
            p2.add("vector", lambda e, qi=qi: e.tensor_add(
                h2_sb[qi][:], h2_sb[qi][:], sbb_tile[:]), waits=[(sV, sV.n)], sem=sV)
            sv_h2 = sV.n
            p2.add("vector", lambda e, qi=qi, st_=st_: e.reduce_max(
                st_[:, 7:8], h2_sb[qi][:], AX, apply_absolute_value=True),
                waits=[(sV, sv_h2)], sem=sV)
            p2.add("vector", lambda e, st_=st_: e.tensor_max(
                amax1[:], amax1[:], st_[:, 7:8]), waits=[(sV, sV.n)], sem=sV)
            p2.add("gpsimd", lambda e, qi=qi, qt=qt: e.dma_start(
                out=h2_hbm[128 * qt:128 * (qt + 1), :], in_=h2_sb[qi][:]),
                waits=[(sV, sv_h2)], sem=dXH[qi], inc=16)
            m_epidone[qi] = (dXH[qi], dXH[qi].n)

    while sq_i[0] < len(strip_jobs) or pending:
        before = (sq_i[0], len(pending))
        prep_quant(3)
        prep_transpose(6, None)
        if (sq_i[0], len(pending)) == before:
            raise RuntimeError("weight prep drain stuck")

    for eng, s_ in (("scalar", sS), ("vector", sV), ("tensor", sT), ("gpsimd", sG)):
        p2.add(eng, lambda e: e.nop(), sem=s_)
    marks["b2"] = {"sS": sS.n, "sV": sV.n, "sT": sT.n, "sG": sG.n,
                   "h2st": list(m_epidone), "dQS": [(s_, s_.n) for s_ in dQSs],
                   "xpst": [(dXP[0], dXP[0].n), (dXP[1], dXP[1].n), (dXH[0], dXH[0].n), (dXH[1], dXH[1].n)]}

    p2.emit(nc)
    p2a.close()
    env.update(dict(xp_hbm=xp_hbm, h2_hbm=h2_hbm, qg_hbm=qg_hbm, qu_hbm=qu_hbm,
                    qd_hbm=qd_hbm, scl_g=scl_g, scl_u=scl_u, scl_d=scl_d))
    return env


def build_p4(nc, ctx, env):
    S = env["S"]; Q = env["Q"]; n_qt = env["n_qt"]
    ident_bf = env["ident_bf"]; ident_f32 = env["ident_f32"]
    amax1 = env["amax1"]; amax2 = env["amax2"]; clip_sb = env["clip_sb"]
    h2_hbm = env["h2_hbm"]; qg_hbm = env["qg_hbm"]; qu_hbm = env["qu_hbm"]
    scl_g = env["scl_g"]; scl_u = env["scl_u"]
    sS = env["sS"]; sV = env["sV"]; sT = env["sT"]; sG = env["sG"]
    sem = ctx.sem
    marks = env["marks"]
    b2 = marks["b2"]
    n_tt = Q // 128
    n_tb = Q // 512

    hmid_hbm = nc.dram_tensor("hmid_hbm", [DI, Q], F32).ap()
    scal_hbm = nc.dram_tensor("scal_hbm", [4, 8], F32).ap()
    cc1_in = nc.dram_tensor("cc1_in", [1, 8], F32)
    cc1_out = nc.dram_tensor("cc1_out", [1, 8], F32)

    env["ffstack"].close()
    p4a = ExitStack()
    sb4 = lambda n, sh, dt: p4a.enter_context(nc.sbuf_tensor(n, sh, dt))
    ps4 = lambda n, sh: p4a.enter_context(nc.psum_tensor(n, sh, F32))

    qg_sb = [sb4(f"qg_sb{t}", [128, D], BF16) for t in range(22)]
    qu_sb = [sb4(f"qu_sb{t}", [128, D], BF16) for t in range(22)]
    xqT = [sb4(f"xqT{k}", [128, Q], BF16) for k in range(NKC)]
    h2b = [sb4(f"h2b{j}", [128, D], F32) for j in range(2)]
    qbuf = [sb4(f"qbuf{j}", [128, D], F32) for j in range(2)]
    ybuf = [sb4(f"ybuf{j}", [128, D], I32) for j in range(2)]
    cbuf = [sb4(f"cbuf{j}", [128, D], F32) for j in range(2)]
    xq_bf = [sb4(f"xq_bf{j}", [128, D], BF16) for j in range(2)]
    gate_f = [sb4(f"gate_f{j}", [128, 512], F32) for j in range(2)]
    hm_sb = [sb4(f"hm_sb{j}", [128, 512], F32) for j in range(2)]
    gsc = sb4("gsc", [1, 8], F32)
    mxcol = sb4("mxcol", [128, 1], F32)
    sxcol = sb4("sxcol", [128, 1], F32)
    am4 = sb4("am4", [128, 2], F32)
    sgx = sb4("sgx", [128, 22], F32)
    sux = sb4("sux", [128, 22], F32)

    psA = ps4("psA", [1, 128])
    ps_tr2 = [ps4(f"ps_tr2{j}", [128, 512]) for j in range(2)]
    psG = [ps4(f"psG{j}", [128, 512]) for j in range(2)]
    psU = [ps4(f"psU{j}", [128, 512]) for j in range(2)]

    p4 = Phase("p4")
    dH2 = [sem("dH2_0"), sem("dH2_1")]
    dQR = sem("dQR")
    dCC1 = sem("dCC1")
    cc1s = sem("cc1s")
    dSC = sem("dSC")
    dSCB = sem("dSCB")
    dHM = [sem("dHM0"), sem("dHM1")]

    bar = [(sS, b2["sS"]), (sV, b2["sV"]), (sT, b2["sT"]), (sG, b2["sG"])]
    p4.add("gpsimd", lambda e: e.nop(), waits=bar)
    for t_ in (sgx, sux, gsc, mxcol, sxcol):
        p4.add("gpsimd", lambda e, t_=t_: e.memset(t_[:], 0.0), sem=sG)
    g_init = sG.n
    bar2 = bar + [(sG, g_init)]
    for eng in ("scalar", "vector", "tensor"):
        p4.add(eng, lambda e: e.nop(), waits=bar2)
    p4.add("sync", lambda e: e.nop(),
           waits=bar2 + b2["h2st"] + b2["xpst"] + b2["dQS"])

    p4.add("tensor", lambda e: e.matmul(
        psA[:], amax1[:], ident_f32[:], start=True, stop=True), sem=sT)
    st_a = sT.n
    p4.add("vector", lambda e: e.reduce_max(gsc[0:1, 0:1], psA[0:1, :], AX),
           waits=[(sT, st_a)], sem=sV)
    sv_g1 = sV.n
    p4.add("gpsimd", lambda e: e.dma_start(out=cc1_in[0:1, :], in_=gsc[0:1, :]),
           waits=[(sV, sv_g1)], sem=dCC1, inc=16)
    p4.add("gpsimd", lambda e: e.collective_compute(
        "AllReduce", ALU.max, replica_groups=[list(range(env["n_cores"]))],
        ins=[cc1_in.ap().opt()], outs=[cc1_out.ap().opt()]),
        waits=[(dCC1, dCC1.n)], sem=cc1s)
    p4.add("gpsimd", lambda e: e.dma_start(out=gsc[0:1, 1:2], in_=cc1_out[0:1, 0:1]),
           waits=[(cc1s, cc1s.n)], sem=dCC1, inc=16)
    d_gmax = dCC1.n
    p4.add("scalar", lambda e: e.activation(gsc[0:1, 2:3], clip_sb[0:1, 0:1], AF.Sigmoid),
           waits=[(dCC1, d_gmax)], sem=sS)
    ss_sig = sS.n
    p4.add("vector", lambda e: e.tensor_mul(gsc[0:1, 3:4], gsc[0:1, 1:2], gsc[0:1, 2:3]),
           waits=[(sS, ss_sig), (dCC1, d_gmax)], sem=sV)
    p4.add("vector", lambda e: e.tensor_scalar(
        gsc[0:1, 3:4], gsc[0:1, 3:4], 0.007874015748031496, None, ALU.mult), waits=[(sV, sV.n)], sem=sV)
    p4.add("vector", lambda e: e.reciprocal(gsc[0:1, 4:5], gsc[0:1, 3:4]),
           waits=[(sV, sV.n)], sem=sV)
    sv_mx = sV.n
    p4.add("gpsimd", lambda e: e.dma_start(out=scal_hbm[0:1, :], in_=gsc[0:1, :]),
           waits=[(sV, sv_mx)], sem=dSC, inc=16)
    p4.add("scalar", lambda e: e.dma_start(
        out=mxcol[:], in_=scal_hbm[0:1, 4:5].broadcast_to((128, 1))),
        waits=[(dSC, dSC.n)], sem=dSCB, inc=16)
    p4.add("scalar", lambda e: e.dma_start(
        out=sxcol[:], in_=scal_hbm[0:1, 3:4].broadcast_to((128, 1))),
        sem=dSCB, inc=16)
    d_bc = dSCB.n
    p4.add("vector", lambda e: e.tensor_scalar(
        sgx[:], scl_g[:], sxcol[:, 0:1], None, ALU.mult), waits=[(dSCB, d_bc)], sem=sV)
    p4.add("vector", lambda e: e.tensor_scalar(
        sux[:], scl_u[:], sxcol[:, 0:1], None, ALU.mult), waits=[(sV, sV.n)], sem=sV)
    sv_scales = sV.n

    m_quant = [None] * (n_tt + 4)
    m_h2free = [None] * (n_tt + 4)
    m_trbank = [None, None]
    quant_done = [0]

    def quant_tile(j):
        pb = j % 2
        w_ = [m_h2free[j - 2]] if j >= 2 else []
        p4.add("sync", lambda e, j=j, pb=pb: e.dma_start(
            out=h2b[pb][:], in_=h2_hbm[128 * j:128 * (j + 1), :]),
            waits=w_, sem=dH2[pb], inc=16)
        dv = dH2[pb].n
        w_ = [(dH2[pb], dv), (sV, sv_scales)]
        if j >= 2:
            w_.append(m_quant[j - 2])
        p4.add("vector", lambda e, pb=pb: e.tensor_scalar(
            qbuf[pb][:], h2b[pb][:], mxcol[:, 0:1], None, ALU.mult), waits=w_, sem=sV)
        m_h2free[j] = (sV, sV.n)
        p4.add("vector", lambda e, pb=pb: e.tensor_copy(ybuf[pb][:], qbuf[pb][:]),
               sem=sV)
        p4.add("vector", lambda e, pb=pb: e.tensor_sub(
            qbuf[pb][:], qbuf[pb][:], ybuf[pb][:]), sem=sV)
        p4.add("vector", lambda e, pb=pb: e.tensor_scalar(
            cbuf[pb][:], qbuf[pb][:], -0.5, None, ALU.is_lt), sem=sV)
        p4.add("vector", lambda e, pb=pb: e.scalar_tensor_tensor(
            qbuf[pb][:], qbuf[pb][:], 0.5, cbuf[pb][:], ALU.is_gt, ALU.subtract),
            sem=sV)
        p4.add("vector", lambda e, pb=pb: e.tensor_add(
            cbuf[pb][:], qbuf[pb][:], ybuf[pb][:]), sem=sV)
        p4.add("vector", lambda e, pb=pb: e.tensor_scalar(
            xq_bf[pb][:], cbuf[pb][:], 127.0, -128.0, ALU.min, ALU.max),
            sem=sV)
        sv_xq = sV.n
        for k in range(NKC):
            w_ = [(sV, sv_xq)]
            if m_trbank[k % 2] is not None:
                w_.append(m_trbank[k % 2])
            p4.add("tensor", lambda e, pb=pb, k=k: e.matmul(
                ps_tr2[k % 2][:, 0:128], xq_bf[pb][:, 128 * k:128 * (k + 1)],
                ident_bf[:], start=True, stop=True), waits=w_, sem=sT)
            mmv = sT.n
            p4.add("scalar", lambda e, k=k, j=j: e.copy(
                xqT[k][:, 128 * j:128 * (j + 1)], ps_tr2[k % 2][:, 0:128]),
                waits=[(sT, mmv)], sem=sS)
            m_trbank[k % 2] = (sS, sS.n)
        m_quant[j] = (sS, sS.n)

    quant_tile(0)
    quant_tile(1)
    quant_done[0] = 2
    dqr_t = []
    for t in range(22):
        p4.add("sync", lambda e, t=t: e.dma_start(out=qg_sb[t][:], in_=qg_hbm[t]),
               sem=dQR, inc=16)
        p4.add("sync", lambda e, t=t: e.dma_start(out=qu_sb[t][:], in_=qu_hbm[t]),
               sem=dQR, inc=16)
        dqr_t.append(dQR.n)
        if t % 2 == 1 and quant_done[0] < min(n_tt, 12):
            quant_tile(quant_done[0])
            quant_done[0] += 1

    mGU = {}
    m_gevac = {}
    m_hmst = {}
    for tb in range(n_tb):
        while quant_done[0] < min(n_tt, 4 * (tb + 1)):
            quant_tile(quant_done[0])
            quant_done[0] += 1
        for t in range(22):
            pb = t % 2
            w_ = [m_quant[4 * tb + 3] if 4 * tb + 3 < n_tt else m_quant[n_tt - 1],
                  (dQR, dqr_t[t])]
            if ("ge", pb) in m_gevac:
                w_.append(m_gevac[("ge", pb)])
            for k in range(NKC):
                p4.add("tensor", lambda e, t=t, tb=tb, k=k, pb=pb: e.matmul(
                    psG[pb][:], qg_sb[t][:, 128 * k:128 * (k + 1)],
                    xqT[k][:, 512 * tb:512 * (tb + 1)],
                    start=(k == 0), stop=(k == NKC - 1)),
                    waits=w_ if k == 0 else [], sem=sT)
            mGU[(tb, t, "g")] = (sT, sT.n)
            st_g = sT.n
            w_ = []
            if (("ue", pb)) in m_gevac:
                w_.append(m_gevac[("ue", pb)])
            for k in range(NKC):
                p4.add("tensor", lambda e, t=t, tb=tb, k=k, pb=pb: e.matmul(
                    psU[pb][:], qu_sb[t][:, 128 * k:128 * (k + 1)],
                    xqT[k][:, 512 * tb:512 * (tb + 1)],
                    start=(k == 0), stop=(k == NKC - 1)),
                    waits=w_ if k == 0 else [], sem=sT)
            st_u = sT.n
            w_ = [(sT, st_g)]
            if ("gf", pb) in m_gevac:
                w_.append(m_gevac[("gf", pb)])
            p4.add("scalar", lambda e, t=t, pb=pb: e.activation(
                gate_f[pb][:], psG[pb][:], AF.Silu, scale=sgx[:, t:t + 1]),
                waits=w_, sem=sS)
            m_gevac[("ge", pb)] = (sS, sS.n)
            ss_gf = sS.n
            w_ = [(sT, st_u), (sS, ss_gf)]
            if ("hm", pb) in m_gevac:
                w_.append(m_gevac[("hm", pb)])
            if ("hmst", pb) in m_gevac:
                w_.append(m_gevac[("hmst", pb)])
            p4.add("vector", lambda e, t=t, pb=pb: e.scalar_tensor_tensor(
                hm_sb[pb][:], psU[pb][:], sux[:, t:t + 1], gate_f[pb][:],
                ALU.mult, ALU.mult), waits=w_, sem=sV)
            m_gevac[("ue", pb)] = (sV, sV.n)
            m_gevac[("gf", pb)] = (sV, sV.n)
            sv_hm = sV.n
            p4.add("vector", lambda e, pb=pb: e.reduce_max(
                am4[:, 0:1], hm_sb[pb][:], AX, apply_absolute_value=True),
                waits=[(sV, sv_hm)], sem=sV)
            p4.add("vector", lambda e: e.tensor_max(
                amax2[:], amax2[:], am4[:, 0:1]), waits=[(sV, sV.n)], sem=sV)
            p4.add("gpsimd", lambda e, t=t, tb=tb, pb=pb: e.dma_start(
                out=hmid_hbm[128 * t:128 * (t + 1), 512 * tb:512 * (tb + 1)],
                in_=hm_sb[pb][:]), waits=[(sV, sv_hm)], sem=dHM[pb], inc=16)
            m_gevac[("hmst", pb)] = (dHM[pb], dHM[pb].n)

    for eng, s_ in (("scalar", sS), ("vector", sV), ("tensor", sT), ("gpsimd", sG)):
        p4.add(eng, lambda e: e.nop(), sem=s_)
    marks["b4"] = {"sS": sS.n, "sV": sV.n, "sT": sT.n, "sG": sG.n,
                   "hmst": [m_gevac.get(("hmst", 0)), m_gevac.get(("hmst", 1))],
                   "xpst": b2["xpst"]}

    p4.emit(nc)
    p4a.close()
    env.update(dict(hmid_hbm=hmid_hbm, scal_hbm=scal_hbm))
    return env


def build_p5(nc, ctx, env):
    S = env["S"]; Q = env["Q"]
    ident_f32 = env["ident_f32"]
    amax2 = env["amax2"]; clip_sb = env["clip_sb"]
    hmid_hbm = env["hmid_hbm"]; scal_hbm = env["scal_hbm"]
    qd_hbm = env["qd_hbm"]; xp_hbm = env["xp_hbm"]; mod_hbm = env["mod_hbm"]
    out_hbm = env["out_hbm"]
    scl_d = env["scl_d"]
    sS = env["sS"]; sV = env["sV"]; sT = env["sT"]; sG = env["sG"]
    sem = ctx.sem
    marks = env["marks"]
    b4 = marks["b4"]
    n_tt = Q // 128
    nhalf = Q // 2

    cc2_in = nc.dram_tensor("cc2_in", [1, 8], F32)
    cc2_out = nc.dram_tensor("cc2_out", [1, 8], F32)
    sd_hbm = nc.dram_tensor("sd_hbm", [8, 128], F32).ap()

    p5a = ExitStack()
    sb5 = lambda n, sh, dt: p5a.enter_context(nc.sbuf_tensor(n, sh, dt))
    ps5 = lambda n, sh: p5a.enter_context(nc.psum_tensor(n, sh, F32))

    qd_sb = [sb5(f"qd_sb{t}", [128, D], BF16) for t in range(22)]
    hq = [sb5(f"hq{t}", [128, nhalf], BF16) for t in range(22)]
    hmb = [sb5(f"hmb{j}", [128, nhalf], F32) for j in range(2)]
    y5 = sb5("y5", [128, nhalf], I32)
    c5 = sb5("c5", [128, nhalf], F32)
    xpb = [sb5(f"xpb{j}", [128, D], F32) for j in range(2)]
    outb = [sb5(f"outb{j}", [128, D], F32) for j in range(2)]
    gb_tile = sb5("gb_tile", [128, D], F32)
    sdb_tile = sb5("sdb_tile", [128, D], F32)
    sd_sb = sb5("sd_sb", [8, 128], F32)
    g5 = sb5("g5", [1, 8], F32)
    mdcol = sb5("mdcol", [128, 1], F32)
    sdcol = sb5("sdcol", [128, 1], F32)

    psA2 = ps5("psA2", [8, 128])
    psB2 = ps5("psB2", [1, 128])
    psD5 = [ps5(f"psD5{j}", [128, D]) for j in range(2)]

    p5 = Phase("p5")
    dHM5 = [sem("dHM5_0"), sem("dHM5_1")]
    dQD = sem("dQD")
    dCC2 = sem("dCC2")
    cc2s = sem("cc2s")
    dSD = sem("dSD")
    dSD2 = sem("dSD2")
    dSDB = sem("dSDB")
    dXPL = [sem("dXPL0"), sem("dXPL1")]
    dOUT = [sem("dOUT0"), sem("dOUT1")]

    bar = [(sS, b4["sS"]), (sV, b4["sV"]), (sT, b4["sT"]), (sG, b4["sG"])]
    p5.add("gpsimd", lambda e: e.nop(), waits=bar)
    for t_ in (g5, mdcol, sdcol, sd_sb):
        p5.add("gpsimd", lambda e, t_=t_: e.memset(t_[:], 0.0), sem=sG)
    g_init = sG.n
    bar2 = bar + [(sG, g_init)]
    for eng in ("scalar", "vector", "tensor"):
        p5.add(eng, lambda e: e.nop(), waits=bar2)
    p5.add("sync", lambda e: e.nop(),
           waits=bar2 + [m for m in b4["hmst"] if m] + b4["xpst"])

    for t in range(22):
        p5.add("sync", lambda e, t=t: e.dma_start(out=qd_sb[t][:], in_=qd_hbm[t]),
               sem=dQD, inc=16)
    dQD_all = dQD.n

    p5.add("tensor", lambda e: e.matmul(
        psB2[:], amax2[:], ident_f32[:], start=True, stop=True), sem=sT)
    st_a = sT.n
    p5.add("vector", lambda e: e.reduce_max(g5[0:1, 0:1], psB2[0:1, :], AX),
           waits=[(sT, st_a)], sem=sV)
    sv_g1 = sV.n
    p5.add("gpsimd", lambda e: e.dma_start(out=cc2_in[0:1, :], in_=g5[0:1, :]),
           waits=[(sV, sv_g1)], sem=dCC2, inc=16)
    p5.add("gpsimd", lambda e: e.collective_compute(
        "AllReduce", ALU.max, replica_groups=[list(range(env["n_cores"]))],
        ins=[cc2_in.ap().opt()], outs=[cc2_out.ap().opt()]),
        waits=[(dCC2, dCC2.n)], sem=cc2s)
    p5.add("gpsimd", lambda e: e.dma_start(out=g5[0:1, 1:2], in_=cc2_out[0:1, 0:1]),
           waits=[(cc2s, cc2s.n)], sem=dCC2, inc=16)
    d_gmax = dCC2.n
    p5.add("scalar", lambda e: e.activation(g5[0:1, 2:3], clip_sb[0:1, 1:2], AF.Sigmoid),
           waits=[(dCC2, d_gmax)], sem=sS)
    ss_sig = sS.n
    p5.add("vector", lambda e: e.tensor_mul(g5[0:1, 3:4], g5[0:1, 1:2], g5[0:1, 2:3]),
           waits=[(sS, ss_sig), (dCC2, d_gmax)], sem=sV)
    p5.add("vector", lambda e: e.tensor_scalar(
        g5[0:1, 3:4], g5[0:1, 3:4], 0.007874015748031496, None, ALU.mult), waits=[(sV, sV.n)], sem=sV)
    p5.add("vector", lambda e: e.reciprocal(g5[0:1, 4:5], g5[0:1, 3:4]),
           waits=[(sV, sV.n)], sem=sV)
    sv_md = sV.n
    p5.add("gpsimd", lambda e: e.dma_start(out=scal_hbm[1:2, :], in_=g5[0:1, :]),
           waits=[(sV, sv_md)], sem=dSD, inc=16)
    dsd1 = dSD.n
    p5.add("scalar", lambda e: e.dma_start(
        out=mdcol[:], in_=scal_hbm[1:2, 4:5].broadcast_to((128, 1))),
        waits=[(dSD, dsd1)], sem=dSDB, inc=16)
    p5.add("scalar", lambda e: e.dma_start(
        out=sdcol[:], in_=scal_hbm[1:2, 3:4].broadcast_to((128, 1))),
        sem=dSDB, inc=16)
    d_bc = dSDB.n

    p5.add("tensor", lambda e: e.matmul(
        psA2[:], scl_d[:], ident_f32[:], start=True, stop=True), sem=sT)
    st_sd = sT.n
    p5.add("scalar", lambda e: e.copy(sd_sb[:], psA2[:]), waits=[(sT, st_sd)], sem=sS)
    ss_sd = sS.n
    p5.add("gpsimd", lambda e: e.dma_start(out=sd_hbm, in_=sd_sb[:]),
           waits=[(sS, ss_sd)], sem=dSD2, inc=16)
    dsd2 = dSD2.n
    p5.add("scalar", lambda e: e.dma_start(
        out=sdb_tile[:],
        in_=sd_hbm.rearrange("a b -> (a b)").rearrange("(o d) -> o d", o=1).broadcast_to((128, D))),
        waits=[(dSD2, dsd2)], sem=dSDB, inc=16)
    p5.add("scalar", lambda e: e.dma_start(
        out=gb_tile[:], in_=mod_hbm[3:4, :].broadcast_to((128, D))),
        sem=dSDB, inc=16)
    d_bc2 = dSDB.n
    p5.add("vector", lambda e: e.tensor_mul(gb_tile[:], gb_tile[:], sdb_tile[:]),
           waits=[(dSDB, d_bc2)], sem=sV)
    p5.add("vector", lambda e: e.tensor_scalar(
        gb_tile[:], gb_tile[:], sdcol[:, 0:1], None, ALU.mult),
        waits=[(sV, sV.n)], sem=sV)
    sv_gp = sV.n

    m_outst = {}
    m_psd = {}
    for half in range(2):
        hq_marks = []
        for qtr in range(2):
            for ic in range(22):
                m = qtr * 22 + ic
                pb = m % 2
                w_ = []
                if m >= 2:
                    w_.append(hq_marks[m - 2])
                elif half == 1:
                    w_.append(marks["p5_half0_done"])
                c0 = nhalf * half + 1024 * qtr
                p5.add("sync", lambda e, ic=ic, c0=c0, pb=pb: e.dma_start(
                    out=hmb[pb][:, 0:1024],
                    in_=hmid_hbm[128 * ic:128 * (ic + 1), c0:c0 + 1024]),
                    waits=w_, sem=dHM5[pb], inc=16)
                dv = dHM5[pb].n
                w_ = [(dHM5[pb], dv), (sV, sV.n)]
                if m == 0 and half == 0:
                    w_.append((dSDB, d_bc2))
                p5.add("vector", lambda e, pb=pb: e.tensor_scalar(
                    hmb[pb][:, 0:1024], hmb[pb][:, 0:1024], mdcol[:, 0:1], None,
                    ALU.mult), waits=w_, sem=sV)
                p5.add("vector", lambda e, pb=pb: e.tensor_copy(
                    y5[:, 0:1024], hmb[pb][:, 0:1024]), sem=sV)
                p5.add("vector", lambda e, pb=pb: e.tensor_sub(
                    hmb[pb][:, 0:1024], hmb[pb][:, 0:1024], y5[:, 0:1024]), sem=sV)
                p5.add("vector", lambda e, pb=pb: e.tensor_scalar(
                    c5[:, 0:1024], hmb[pb][:, 0:1024], -0.5, None, ALU.is_lt), sem=sV)
                p5.add("vector", lambda e, pb=pb: e.scalar_tensor_tensor(
                    hmb[pb][:, 0:1024], hmb[pb][:, 0:1024], 0.5, c5[:, 0:1024],
                    ALU.is_gt, ALU.subtract), sem=sV)
                p5.add("vector", lambda e, pb=pb: e.tensor_add(
                    c5[:, 0:1024], hmb[pb][:, 0:1024], y5[:, 0:1024]), sem=sV)
                p5.add("vector", lambda e, ic=ic, qtr=qtr, pb=pb: e.tensor_scalar(
                    hq[ic][:, 1024 * qtr:1024 * (qtr + 1)], c5[:, 0:1024],
                    127.0, -128.0, ALU.min, ALU.max), sem=sV)
                hq_marks.append((sV, sV.n))

        for jj in range(n_tt // 2):
            j = (n_tt // 2) * half + jj
            qtr = jj // 8
            pb = jj % 2
            w_ = [m_outst[pb]] if pb in m_outst else []
            p5.add("scalar", lambda e, j=j, pb=pb: e.dma_start(
                out=xpb[pb][:], in_=xp_hbm[128 * j:128 * (j + 1), :]),
                waits=w_, sem=dXPL[pb], inc=16)
            dxp = dXPL[pb].n
            for hc in range(2):
                w_ = [(dQD, dQD_all)]
                if (pb, hc) in m_psd:
                    w_.append(m_psd[(pb, hc)])
                for ic in range(22):
                    if jj % 8 == 0 and hc == 0:
                        qm = hq_marks[qtr * 22 + ic]
                        wic = w_ + [qm] if ic == 0 else [qm]
                    else:
                        wic = w_ if ic == 0 else []
                    p5.add("tensor", lambda e, ic=ic, jj=jj, hc=hc, pb=pb: e.matmul(
                        psD5[pb][:, 512 * hc:512 * (hc + 1)],
                        hq[ic][:, 128 * jj:128 * (jj + 1)],
                        qd_sb[ic][:, 512 * hc:512 * (hc + 1)],
                        start=(ic == 0), stop=(ic == 21)),
                        waits=wic, sem=sT)
            st_d = sT.n
            w_ = [(sT, st_d), (sV, sv_gp), (dXPL[pb], dxp)]
            if pb in m_outst:
                w_.append(m_outst[pb])
            p5.add("vector", lambda e, pb=pb: e.tensor_mul(
                outb[pb][:], psD5[pb][:], gb_tile[:]), waits=w_, sem=sV)
            m_psd[(pb, 0)] = (sV, sV.n)
            m_psd[(pb, 1)] = (sV, sV.n)
            p5.add("vector", lambda e, pb=pb: e.tensor_add(
                outb[pb][:], outb[pb][:], xpb[pb][:]), waits=[(sV, sV.n)], sem=sV)
            sv_o = sV.n
            p5.add("gpsimd", lambda e, j=j, pb=pb: e.dma_start(
                out=out_hbm[128 * j:128 * (j + 1), :], in_=outb[pb][:]),
                waits=[(sV, sv_o)], sem=dOUT[pb], inc=16)
            m_outst[pb] = (dOUT[pb], dOUT[pb].n)
        marks["p5_half0_done"] = (sT, sT.n)

    p5.add("gpsimd", lambda e: e.nop(),
           waits=[(dOUT[0], dOUT[0].n), (dOUT[1], dOUT[1].n)])
    p5.emit(nc)
    p5a.close()
    return env



from concourse.bass_utils import run_bass_kernel_spmd

_S = 8192
_Q = _S // 2


def _bandfix_tiles(qh):
    bf1 = np.ones((128, 16), np.float32)
    bf2 = np.ones((128, 16), np.float32)
    if qh == 1:
        for r in range(128):
            for u in range(9):
                d = u + 128 - r
                bf1[r, u] = np.exp(0.01 * max(0, 10 - d), dtype=np.float64).astype(np.float32)
                dg = r + 9 - u
                bf2[r, u] = np.exp(-0.01 * max(0, 10 - dg), dtype=np.float64).astype(np.float32)
    return bf1, bf2


_NC_CACHE = {}


def kernel(L0_attn_x, L0_attn_W, L0_attn_b, L0_ffn_clip_x, L0_ffn_clip_down,
           L0_ffn_w_gate, L0_ffn_w_up, L0_ffn_w_down, L0_c, L0_wmod):
    x = np.ascontiguousarray(np.asarray(L0_attn_x, dtype=np.float32))
    W = np.ascontiguousarray(np.asarray(L0_attn_W, dtype=np.float32))
    battn = np.ascontiguousarray(np.asarray(L0_attn_b, dtype=np.float32)).reshape(1, D)
    cx = np.ascontiguousarray(np.asarray(L0_ffn_clip_x, dtype=np.float32)).reshape(1, 1)
    cd = np.ascontiguousarray(np.asarray(L0_ffn_clip_down, dtype=np.float32)).reshape(1, 1)
    wg = np.ascontiguousarray(np.asarray(L0_ffn_w_gate, dtype=np.float32))
    wu = np.ascontiguousarray(np.asarray(L0_ffn_w_up, dtype=np.float32))
    wd = np.ascontiguousarray(np.asarray(L0_ffn_w_down, dtype=np.float32))
    cmat = np.ascontiguousarray(np.asarray(L0_c, dtype=np.float32))
    wmod = np.ascontiguousarray(np.asarray(L0_wmod, dtype=np.float32))

    if "nc" not in _NC_CACHE:
        _NC_CACHE["nc"] = build_nc(S=_S, dev=0, n_cores=8)
    nc = _NC_CACHE["nc"]

    in_maps = []
    for corei in range(8):
        b_, qh = corei // 2, corei % 2
        xb = np.ascontiguousarray(np.roll(x[b_], -_Q * qh, axis=0))
        bf1, bf2 = _bandfix_tiles(qh)
        in_maps.append(dict(
            x_full=xb, W=W, attn_b=battn, c_row=cmat[b_:b_ + 1],
            wmod=wmod, clip_x=cx, clip_down=cd,
            w_gate=wg, w_up=wu, w_down=wd,
            bandfix1=bf1, bandfix2=bf2))
    res = run_bass_kernel_spmd(nc, in_maps, core_ids=list(range(8)))
    out = np.empty((4, _S, D), np.float32)
    for corei in range(8):
        b_, qh = corei // 2, corei % 2
        out[b_, _Q * qh:_Q * (qh + 1)] = res.results[corei]["out"]
    return out
